# version 01
# baseline (speedup 1.0000x reference)
import numpy as np
import ml_dtypes
from contextlib import ExitStack

import concourse.bass as bass
import concourse.mybir as mybir
from concourse.bass_utils import run_bass_kernel_spmd

BF = ml_dtypes.bfloat16
F32 = mybir.dt.float32
BF16 = mybir.dt.bfloat16
I32 = mybir.dt.int32
AF = mybir.ActivationFunctionType
ALU = mybir.AluOpType
AX = mybir.AxisListType

HD = 4096
NT = 32
TPC = 1024
NHEADS = 32
DH = 128
ASCALE = float(DH) ** -0.5
EPS_LN = 1e-5

_start = 2.0 ** (-(2.0 ** -(np.log2(NHEADS) - 3)))
SLOPES = np.array([_start ** (i + 1) for i in range(NHEADS)], dtype=np.float64)

def _chunk_for(slope):
    for c in (512, 256, 128, 64):
        if slope * (c - 1) <= 65.0:
            return c
    return 64

CHUNK = [_chunk_for(s) for s in SLOPES]
CUTOFF = [130.0 / s for s in SLOPES]


def _np_consts():
    c = {}
    c["ones_bf"] = np.ones((128, 1), dtype=BF)
    c["ones1x128_f"] = np.ones((1, 128), dtype=np.float32)
    c["ident_bf"] = np.eye(128, dtype=BF)
    c["ident_f"] = np.eye(128, dtype=np.float32)
    c["tri01_bf"] = np.triu(np.ones((128, 128), dtype=np.float32)).astype(BF)
    c["rmat_bf"] = np.eye(128, dtype=np.float32)[::-1].copy().astype(BF)
    c["onesm"] = np.ones((128, 128), dtype=BF)
    c["krepl8"] = np.repeat(np.eye(8, dtype=np.float32), 16, axis=1).astype(BF)
    c["krepl16x2"] = (2.0 * np.repeat(np.eye(16, dtype=np.float32), 8, axis=1)).astype(BF)
    c["iota16c"] = np.tile(np.arange(16, dtype=np.float32)[:, None], (1, 4))
    c["iota4r"] = np.tile(np.arange(16, dtype=np.float32)[None, :], (4, 1))
    a32 = np.arange(32)
    w32 = np.exp(2j * np.pi * np.outer(a32, a32) / 32.0)
    c["w32r"] = w32.real.astype(np.float32)
    c["w32i"] = w32.imag.astype(np.float32)
    c["w32in"] = (-w32.imag).astype(np.float32)
    p128 = np.arange(128)
    tw = np.exp(2j * np.pi * np.outer(p128, a32) / 4096.0)
    c["twr"] = tw.real.astype(np.float32)
    c["twi"] = tw.imag.astype(np.float32)
    w128 = np.exp(2j * np.pi * np.outer(p128, p128) / 128.0)
    c["c128"] = w128.real.astype(np.float32)
    c["s128n"] = (-w128.imag).astype(np.float32)
    k = (np.arange(32)[None, :] * 128 + np.arange(128)[:, None]).astype(np.float64)
    om = np.exp(-2j * np.pi * k / 4096.0)
    cc = 2.0 / (1.0 + om)
    c["ccr"] = cc.real.astype(np.float32)
    c["cci"] = cc.imag.astype(np.float32)
    tim = np.tan(np.pi * np.arange(4096, dtype=np.float64) / 4096.0)
    c["tim"] = tim[None, :].astype(np.float32)
    return c

NPC = _np_consts()


def attn_units(variant):
    k = variant
    Q = [512 * k, 3584 - 512 * k]
    units = []
    for h in range(NHEADS):
        c = CHUNK[h]
        cut = CUTOFF[h]
        for b in (0, 1):
            q0 = Q[b]
            for ch in range(512 // c):
                A = q0 + ch * c
                t_hi = (A + c - 1) // 128
                t_lo = max(0, int(np.floor((A - cut) / 128.0)))
                for t in range(t_lo, t_hi + 1):
                    off = max(0, t * 128 - A)
                    w = c - off
                    diag = (t * 128 + 127 >= A)
                    tri_shift = max(0, A - t * 128) if diag else 0
                    tri_w = min(128 - tri_shift, w) if diag else 0
                    units.append(dict(h=h, b=b, A=A, c=c, t=t, off=off, w=w,
                                      diag=diag, tri_shift=tri_shift,
                                      tri_w=tri_w))
    return units

N_UNITS = [len(attn_units(v)) for v in range(4)]
NU_MAX = max(N_UNITS)


def bias_table(core):
    v = core % 4
    units = attn_units(v)
    tab = np.zeros((128, NU_MAX), dtype=np.float32)
    p = np.arange(128, dtype=np.float64)
    for u, d in enumerate(units):
        kref = d["A"] + d["c"] - 1
        tab[:, u] = (SLOPES[d["h"]] * (d["t"] * 128 + p - kref)).astype(np.float32)
    return tab


ENGS = ("tensor", "vector", "scalar", "sync", "gpsimd")
ETYPE = {"tensor": mybir.EngineType.PE, "vector": mybir.EngineType.DVE,
         "scalar": mybir.EngineType.Activation, "sync": mybir.EngineType.SP,
         "gpsimd": mybir.EngineType.Pool}


class Sem:
    __slots__ = ("h", "n", "name")
    def __init__(self, h, name):
        self.h = h
        self.n = 0
        self.name = name


class KB:
    def __init__(self, nc, ctx):
        self.nc = nc
        self.ctx = ctx
        self.streams = {e: [] for e in ENGS}
        self._stack = []
        self.sems = {}
        self.sb = {}
        self.ps = {}

    def sem(self, name):
        s = Sem(self.ctx.enter_context(self.nc.semaphore(name)), name)
        self.sems[name] = s
        return s

    def sbuf(self, name, shape, dt):
        t = self.ctx.enter_context(self.nc.sbuf_tensor(name + "_s", shape, dt))
        self.sb[name] = t
        return t

    def psum(self, name, shape, dt):
        t = self.ctx.enter_context(self.nc.psum_tensor(name + "_p", shape, dt))
        self.ps[name] = t
        return t

    def emit(self, eng, fn):
        self.streams[eng].append(fn)

    def op(self, eng, fn, inc=None, inc2=None):
        if inc is not None:
            sem, k = inc if isinstance(inc, tuple) else (inc, 1)
            sem.n += k
            if inc2 is not None:
                s2, k2 = inc2 if isinstance(inc2, tuple) else (inc2, 1)
                s2.n += k2
                self.emit(eng, lambda: fn().then_inc(sem.h, k).then_inc(s2.h, k2))
            else:
                self.emit(eng, lambda: fn().then_inc(sem.h, k))
            return sem.n
        self.emit(eng, fn)
        return None

    def wait(self, eng, sem, val=None):
        v = sem.n if val is None else val
        h = sem.h
        eobj = getattr(self.nc, eng)
        self.emit(eng, lambda: eobj.wait_ge(h, v))

    def dma(self, eng, dst, src, sem):
        sem.n += 16
        eobj = getattr(self.nc, eng)
        d = dst if callable(dst) else (lambda: dst)
        s = src if callable(src) else (lambda: src)
        self.emit(eng, lambda: eobj.dma_start(d(), s()).then_inc(sem.h, 16))
        return sem.n

    def drain(self, eng):
        eobj = getattr(self.nc, eng)
        self.emit(eng, lambda: eobj.drain())

    def switch(self, nvar, emit_fn, engines):
        pre = {n: s.n for n, s in self.sems.items()}
        var_streams = []
        var_counts = []
        for v in range(nvar):
            for n, s in self.sems.items():
                s.n = pre[n]
            saved = self.streams
            self.streams = {e: [] for e in ENGS}
            emit_fn(v)
            var_streams.append(self.streams)
            self.streams = saved
            var_counts.append({n: s.n for n, s in self.sems.items()})
        final = {n: max(vc[n] for vc in var_counts) for n in self.sems}
        for n, s in self.sems.items():
            s.n = final[n]
        for v in range(nvar):
            for e in ENGS:
                if e not in engines:
                    assert not var_streams[v][e], (v, e, "undeclared switch engine")
        nc = self.nc
        for e in engines:
            owned = [n for n in self.sems
                     if any(var_counts[v][n] != pre[n] for v in range(nvar))
                     and SEM_OWNER.get(n) == e]
            bodies = [var_streams[v][e] for v in range(nvar)]
            deficits = [{n: final[n] - var_counts[v][n] for n in owned}
                        for v in range(nvar)]
            eobj = getattr(nc, e)
            et = ETYPE[e]
            semmap = {n: self.sems[n].h for n in owned}
            def do_switch(bodies=bodies, deficits=deficits, eobj=eobj, et=et,
                          semmap=semmap, nvar=nvar):
                pid = nc.partition_id(engines=[et])
                var = pid % nvar
                for i in eobj.Switch(var, nvar):
                    for f in bodies[i]:
                        f()
                    for n, d in deficits[i].items():
                        if d > 0:
                            eobj.sem_inc(semmap[n], d)
            self.emit(e, do_switch)

    def finalize_block(self):
        nc = self.nc
        block = self.ctx.enter_context(nc.Block())
        streams = self.streams
        @block.tensor
        def _(t):
            for f in streams["tensor"]:
                f()
        @block.vector
        def _(v):
            for f in streams["vector"]:
                f()
        @block.scalar
        def _(s):
            for f in streams["scalar"]:
                f()
        @block.sync
        def _(s):
            for f in streams["sync"]:
                f()
        @block.gpsimd
        def _(g):
            for f in streams["gpsimd"]:
                f()


SEM_OWNER = {}


def mk_sems(kb, names_owners):
    out = {}
    for name, owner in names_owners:
        out[name] = kb.sem(name)
        SEM_OWNER[name] = owner
    return out


def setup_common(kb, phases):
    nc = kb.nc
    D = {}
    def din(name, shape, dt=F32):
        D[name] = nc.dram_tensor(name, shape, dt, kind="ExternalInput")
        return D[name]
    din("ones_bf", [128, 1], BF16); din("ones1f", [1, 128])
    din("onesm", [128, 128], BF16)
    din("identb", [128, 128], BF16); din("identf", [128, 128])
    din("tri", [128, 128], BF16); din("rmat", [128, 128], BF16)
    if "p0" in phases:
        din("x0p", [HD, TPC]); din("xT", [HD, HD])
        din("wq", [HD, HD]); din("wkv", [HD, 256]); din("wo", [HD, HD])
        din("biastab", [128, NU_MAX])
    if "p1" in phases:
        din("wb0t", [HD, HD]); din("b0", [128, 32]); din("wgt", [HD, 8])
        din("a0t", [HD, 128]); din("b0f", [128, HD])
        din("kr8", [8, 128], BF16)
    if "p2" in phases:
        din("lam", [64, 2]); din("pri", [64, 2]); din("qri", [64, 2])
        din("vcrt", [64, 64]); din("vcit", [64, 64])
        din("ctri", [64, 2]); din("bp", [64, 1]); din("dd", [1, 1]); din("lstep", [1, 1])
        din("tim", [1, 4096]); din("ccr", [128, 32]); din("cci", [128, 32])
        din("w32r", [32, 32]); din("w32i", [32, 32]); din("w32in", [32, 32])
        din("twr", [128, 32]); din("twi", [128, 32])
        din("c128", [128, 128]); din("s128n", [128, 128])
    if "p3" in phases:
        din("wb1t", [HD, HD]); din("b1", [128, 32]); din("rwt", [HD, 16])
        din("la1t", [HD, 128]); din("lb1f", [128, HD])
        din("rb", [16, 1]); din("prior", [16, 1]); din("active", [1, 4], I32)
        din("kr16", [16, 128], BF16); din("io16c", [16, 4]); din("io4r", [4, 16])
    D["out"] = nc.dram_tensor("out", [HD, TPC], F32, kind="ExternalOutput")
    D["scr0"] = nc.dram_tensor("scr0", [HD, TPC], F32)
    D["scr1"] = nc.dram_tensor("scr1", [HD, TPC], F32)
    if "p2" in phases:
        D["kpad"] = nc.dram_tensor("kpad", [8192], BF16)
    kb.D = D

    kb.sbuf("bufA", [128, NT * TPC], BF16)
    kb.sbuf("bufB", [128, NT * TPC], BF16)
    kb.sbuf("wpool", [128, 8192], BF16)
    kb.sbuf("wsb", [128, 4096], BF16)
    kb.sbuf("scf", [128, 3072], F32)
    kb.sbuf("xpre", [128, 2048], F32)
    kb.sbuf("sqsb", [128, 2048], BF16)
    kb.sbuf("mbc", [128, 1024], F32)
    kb.sbuf("rbc2", [128, 1024], F32)
    kb.sbuf("cons_bf", [128, 1], BF16)
    kb.sbuf("onesm", [128, 128], BF16)
    kb.sbuf("cons1f", [1, 128], F32)
    kb.sbuf("identb", [128, 128], BF16)
    kb.sbuf("identf", [128, 128], F32)
    kb.sbuf("attnT", [128, 1024], BF16)
    if "p0" in phases:
        kb.sbuf("qsb", [128, 2048], BF16)
        kb.sbuf("rbcs", [128, 512], F32)
        kb.sbuf("biastab", [128, NU_MAX], F32)
        kb.sbuf("tri", [128, 128], BF16)
    if phases & {"p1", "p2", "p3"}:
        kb.sbuf("gw", [128, 48], F32)
        kb.sbuf("gwb", [128, 16], F32)
    if "p1" in phases or "p3" in phases:
        kb.sbuf("hwsb", [128, 1024], BF16)
        kb.sbuf("grow", [16, 1152], BF16)
        kb.sbuf("bias_sb", [128, 64], F32)
        kb.sbuf("kreps", [16, 256], BF16)
        kb.sbuf("wgs", [128, 512], BF16)
    if "p3" in phases:
        kb.sbuf("oh", [16, 64], F32)
        kb.sbuf("acti", [1, 4], I32)
        kb.sbuf("ioc", [16, 4], F32)
        kb.sbuf("ior", [4, 16], F32)
    if "p2" in phases:
        kb.sbuf("rmat", [128, 128], BF16)
        kb.sbuf("ktile", [128, 32], BF16)
    kb.alias = {}
    kb.psum("pA", [128, 1024], F32)
    kb.psum("pB", [128, 1024], F32)
    kb.psum("pC", [128, 1024], F32)
    kb.psum("pD", [128, 1024], F32)

    mk_sems(kb, [("dW", "gpsimd"), ("dXQ", "sync"), ("dSC", "sync"),
                 ("dOUT", "sync"), ("mT", "tensor"), ("vV", "vector"),
                 ("sS", "scalar"), ("gG", "gpsimd"), ("dSS", "scalar"),
                 ("aRs", "scalar"), ("aRv", "vector"),
                 ("dP", "gpsimd"),
                 ("dQ0", "gpsimd"), ("dQ1", "gpsimd"),
                 ("dQ2", "gpsimd"), ("dQ3", "gpsimd"),
                 ("dK0", "gpsimd"), ("dK1", "gpsimd"),
                 ("dX0", "sync"), ("dX1", "sync"),
                 ("dS0", "sync"), ("dS1", "sync"),
                 ("dO0", "sync"), ("dO1", "sync")])


def load_consts(kb, phases):
    D, S = kb.D, kb.sems
    kb.dma("sync", kb.sb["cons_bf"][:], D["ones_bf"].ap(), S["dXQ"])
    kb.dma("sync", kb.sb["onesm"][:], D["onesm"].ap(), S["dXQ"])
    kb.dma("sync", kb.sb["cons1f"][:], D["ones1f"].ap(), S["dXQ"])
    kb.dma("sync", kb.sb["identb"][:], D["identb"].ap(), S["dXQ"])
    kb.dma("sync", kb.sb["identf"][:], D["identf"].ap(), S["dXQ"])
    if "p0" in phases:
        kb.dma("sync", kb.sb["biastab"][:], D["biastab"].ap(), S["dXQ"])
        kb.dma("sync", kb.sb["tri"][:], D["tri"].ap(), S["dXQ"])
    if "p2" in phases:
        kb.dma("sync", kb.sb["rmat"][:], D["rmat"].ap(), S["dXQ"])
    if "p1" in phases:
        kb.dma("sync", kb.sb["kreps"][0:8, 0:128], D["kr8"].ap(), S["dXQ"])
        kb.dma("sync", kb.sb["bias_sb"][:, 0:32], D["b0"].ap(), S["dXQ"])
    if "p3" in phases:
        kb.dma("sync", kb.sb["kreps"][0:16, 128:256], D["kr16"].ap(), S["dXQ"])
        kb.dma("sync", kb.sb["bias_sb"][:, 32:64], D["b1"].ap(), S["dXQ"])
        kb.dma("sync", kb.sb["ioc"][:], D["io16c"].ap(), S["dXQ"])
        kb.dma("sync", kb.sb["ior"][:], D["io4r"].ap(), S["dXQ"])
        kb.dma("sync", kb.sb["acti"][:], D["active"].ap(), S["dXQ"])
        kb.dma("sync", kb.sb["oh"][:, 0:1], D["rb"].ap(), S["dXQ"])
        kb.dma("sync", kb.sb["oh"][:, 1:2], D["prior"].ap(), S["dXQ"])
    return S["dXQ"].n


def emit_evac_loop(kb, n_tiles, mm_emit, resid_load, normed, outbuf, scr_out,
                  bias_cols=None, psum_names=("pA", "pB")):
    S = kb.sems
    nc = kb.nc
    scf, xpre, sqsb = kb.sb["scf"], kb.sb["xpre"], kb.sb["sqsb"]
    ev1 = [0, 0]; wdone = [0, 0]; obn = [0, 0]; sqd = [0, 0]; scrd = [0, 0]
    stats_done = [0, 0]
    pending = []

    def emit_stats(t, sl, ob_ap, sq_ap):
        kb.wait("tensor", S["gG"], obn[sl])
        kb.wait("tensor", S["sS"], sqd[sl])
        for half in range(2):
            kb.op("tensor", lambda ob_ap=ob_ap, half=half, t=t:
                  nc.tensor.matmul(kb.ps["pC"][:, half * 512:(half + 1) * 512],
                                   kb.sb["onesm"][:],
                                   ob_ap[:, half * 512:(half + 1) * 512],
                                   start=(t == 0), stop=(t == n_tiles - 1)))
        for half in range(2):
            kb.op("tensor", lambda sq_ap=sq_ap, half=half, t=t:
                  nc.tensor.matmul(kb.ps["pD"][:, half * 512:(half + 1) * 512],
                                   kb.sb["onesm"][:],
                                   sq_ap[:, half * 512:(half + 1) * 512],
                                   start=(t == 0), stop=(t == n_tiles - 1)),
                  inc=(S["mT"], 1) if half == 1 else None)
        stats_done[sl] = S["mT"].n

    for t in range(n_tiles):
        sl = t % 2
        pt = kb.ps[psum_names[sl]]
        if t >= 2:
            kb.wait("tensor", S["sS"], ev1[sl])
        mm_done = mm_emit(kb, t, pt)
        if pending:
            emit_stats(*pending.pop())
        if t >= 2:
            kb.wait("sync", S["vV"], wdone[sl])
        xp = resid_load(kb, t, sl)
        if t >= 2:
            kb.wait("scalar", S["gG"], obn[sl])
            kb.wait("scalar", S["dS0" if sl == 0 else "dS1"], scrd[sl])
        kb.wait("scalar", S["mT"], mm_done)
        wout = scf[:, sl * 1024:(sl + 1) * 1024]
        kb.op("scalar", lambda pt=pt, wout=wout:
              nc.scalar.activation(wout, pt[:], AF.Copy), inc=S["sS"])
        ev1[sl] = S["sS"].n
        bap = bias_cols(t) if bias_cols is not None else None
        kb.wait("vector", S["sS"], ev1[sl])
        kb.wait("vector", S["dX0" if sl == 0 else "dX1"], xp)
        xs = xpre[:, sl * 1024:(sl + 1) * 1024]
        if normed:
            tmp = scf[:, 2048:3072]
            kb.op("vector", lambda xs=xs, tmp=tmp:
                  nc.vector.tensor_sub(tmp, xs, kb.sb["mbc"][:]))
            kb.drain("vector")
            kb.op("vector", lambda tmp=tmp:
                  nc.vector.tensor_mul(tmp, tmp, kb.sb["rbc2"][:]))
            kb.drain("vector")
            other = tmp
        else:
            other = xs
        if bap is not None:
            kb.op("vector", lambda other=other, wout=wout, bap=bap:
                  nc.vector.scalar_tensor_tensor(wout, wout, bap, other,
                                                 op0=ALU.add, op1=ALU.add),
                  inc=S["vV"])
        else:
            kb.op("vector", lambda other=other, wout=wout:
                  nc.vector.tensor_add(wout, other, wout), inc=S["vV"])
        wdone[sl] = S["vV"].n
        kb.wait("gpsimd", S["vV"], wdone[sl])
        ob_ap = outbuf[:, t * 1024:(t + 1) * 1024]
        kb.op("gpsimd", lambda ob_ap=ob_ap, wout=wout:
              nc.gpsimd.tensor_copy(ob_ap, wout), inc=S["gG"])
        obn[sl] = S["gG"].n
        if t >= 2:
            kb.wait("scalar", S["mT"], stats_done[sl])
        kb.wait("scalar", S["vV"], wdone[sl])
        sq_ap = sqsb[:, sl * 1024:(sl + 1) * 1024]
        kb.op("scalar", lambda sq_ap=sq_ap, wout=wout:
              nc.scalar.activation(sq_ap, wout, AF.Square), inc=S["sS"])
        sqd[sl] = S["sS"].n
        kb.wait("sync", S["vV"], wdone[sl])
        scrd[sl] = kb.dma("sync", lambda t=t: scr_out.ap()[t * 128:(t + 1) * 128, :],
                          lambda wout=wout: wout, S["dS0" if sl == 0 else "dS1"])
        pending.append((t, sl, ob_ap, sq_ap))
    emit_stats(*pending.pop())
    return S["mT"].n


def emit_stats_finalize(kb):
    S = kb.sems; nc = kb.nc
    scf = kb.sb["scf"]
    kb.wait("vector", S["mT"], S["mT"].n)
    kb.wait("vector", S["dS0"], S["dS0"].n)
    kb.wait("vector", S["dS1"], S["dS1"].n)
    MR = scf[0:1, 0:1024]; VR = scf[0:1, 1024:2048]; T1 = scf[0:1, 2048:3072]
    kb.op("vector", lambda: nc.vector.tensor_scalar(MR, kb.ps["pC"][0:1, :],
          1.0 / HD, None, op0=ALU.mult))
    kb.op("vector", lambda: nc.vector.tensor_scalar(VR, kb.ps["pD"][0:1, :],
          1.0 / HD, None, op0=ALU.mult))
    kb.drain("vector")
    kb.op("vector", lambda: nc.vector.tensor_mul(T1, MR, MR))
    kb.drain("vector")
    kb.op("vector", lambda: nc.vector.tensor_sub(VR, VR, T1), inc=S["vV"])
    kb.drain("vector")
    kb.op("vector", lambda: nc.vector.tensor_scalar(VR, VR, EPS_LN, None,
          op0=ALU.add), inc=S["vV"])
    vv = S["vV"].n
    kb.wait("scalar", S["vV"], vv)
    kb.op("scalar", lambda: nc.scalar.activation(T1, VR, AF.Ln))
    kb.drain("scalar")
    kb.op("scalar", lambda: nc.scalar.activation(VR, T1, AF.Exp, scale=-0.5),
          inc=S["sS"])
    sv = S["sS"].n
    kb.wait("tensor", S["sS"], sv)
    for half in range(2):
        kb.op("tensor", lambda half=half: nc.tensor.matmul(
            kb.ps["pA"][:, half * 512:(half + 1) * 512], kb.sb["cons1f"][:],
            MR[:, half * 512:(half + 1) * 512], start=True, stop=True))
    for half in range(2):
        kb.op("tensor", lambda half=half: nc.tensor.matmul(
            kb.ps["pB"][:, half * 512:(half + 1) * 512], kb.sb["cons1f"][:],
            VR[:, half * 512:(half + 1) * 512], start=True, stop=True),
            inc=(S["mT"], 1) if half == 1 else None)
    mm = S["mT"].n
    kb.wait("scalar", S["mT"], mm)
    kb.op("scalar", lambda: nc.scalar.activation(kb.sb["mbc"][:], kb.ps["pA"][:], AF.Copy))
    kb.op("scalar", lambda: nc.scalar.activation(kb.sb["rbc2"][:], kb.ps["pB"][:], AF.Copy),
          inc=S["sS"])
    return S["sS"].n


def emit_norm_pass(kb, buf):
    S = kb.sems; nc = kb.nc
    ntmp = kb.sb["sqsb"]
    sv0 = S["sS"].n
    kb.wait("vector", S["sS"], sv0)
    kb.wait("gpsimd", S["sS"], sv0)
    gdone = [0, 0]
    for t in range(NT):
        sl = t % 2
        bt = buf[:, t * 1024:(t + 1) * 1024]
        nt = ntmp[:, sl * 1024:(sl + 1) * 1024]
        if t >= 2:
            kb.wait("vector", S["gG"], gdone[sl])
        kb.op("vector", lambda bt=bt, nt=nt:
              nc.vector.tensor_sub(nt, bt, kb.sb["mbc"][:]), inc=S["vV"])
        vv = S["vV"].n
        kb.wait("gpsimd", S["vV"], vv)
        kb.op("gpsimd", lambda bt=bt, nt=nt:
              nc.gpsimd.tensor_mul(bt, nt, kb.sb["rbc2"][:]), inc=S["gG"])
        gdone[sl] = S["gG"].n
    return S["gG"].n


def emit_final_output(kb, scr_in, out_dram):
    S = kb.sems; nc = kb.nc
    xpre, scf = kb.sb["xpre"], kb.sb["scf"]
    vdone = [0, 0]; outd = [0, 0]
    sv0 = S["sS"].n
    kb.wait("vector", S["sS"], sv0)
    for t in range(NT):
        sl = t % 2
        if t >= 2:
            kb.wait("sync", S["dO0" if sl == 0 else "dO1"], outd[sl])
        xp = kb.dma("sync", lambda t=t, sl=sl: xpre[:, sl * 1024:(sl + 1) * 1024],
                    lambda t=t: scr_in.ap()[t * 128:(t + 1) * 128, :],
                    S["dX0" if sl == 0 else "dX1"])
        kb.wait("vector", S["dX0" if sl == 0 else "dX1"], xp)
        xs = xpre[:, sl * 1024:(sl + 1) * 1024]
        tmp = scf[:, sl * 1024:(sl + 1) * 1024]
        if t >= 2:
            kb.wait("vector", S["dO0" if sl == 0 else "dO1"], outd[sl])
        kb.op("vector", lambda xs=xs, tmp=tmp:
              nc.vector.tensor_sub(tmp, xs, kb.sb["mbc"][:]))
        kb.drain("vector")
        kb.op("vector", lambda tmp=tmp:
              nc.vector.tensor_mul(tmp, tmp, kb.sb["rbc2"][:]), inc=S["vV"])
        vdone[sl] = S["vV"].n
        kb.wait("sync", S["vV"], vdone[sl])
        outd[sl] = kb.dma("sync", lambda t=t: out_dram.ap()[t * 128:(t + 1) * 128, :],
                          lambda tmp=tmp: tmp, S["dO0" if sl == 0 else "dO1"])
    kb.wait("sync", S["dO0"], S["dO0"].n)
    kb.wait("sync", S["dO1"], S["dO1"].n)


def emit_cast_load_buf(kb, src_dram, buf):
    S = kb.sems
    nc = kb.nc
    for t in range(NT):
        kb.dma("gpsimd", lambda t=t: buf[:, t * 1024:(t + 1) * 1024],
               lambda t=t: src_dram.ap()[t * 128:(t + 1) * 128, :], S["dW"])
    return S["dW"].n


def emit_ffn(kb, layer, inbuf, outbuf, resid_load, normed, scr_out):
    S = kb.sems; nc = kb.nc; D = kb.D
    wpool, wgs, wsb = kb.sb["wpool"], kb.sb["wgs"], kb.sb["wsb"]
    smalls = kb.sb["scf"]
    gw, gwb, grow = kb.sb["gw"], kb.sb["gwb"], kb.sb["grow"]
    zsb = kb.sb["sqsb"]
    hwsb = kb.sb["hwsb"]
    identf = kb.sb["identf"]
    pA, pB, pC, pD = kb.ps["pA"], kb.ps["pB"], kb.ps["pC"], kb.ps["pD"]
    E = 8 if layer == 0 else 16

    at_name, bf_name, wb_name = (("a0t", "b0f", "wb0t") if layer == 0
                                 else ("la1t", "lb1f", "wb1t"))
    kb.dma("gpsimd", wpool[:, 0:4096],
           lambda: D[at_name].ap().rearrange("(t p) j -> p t j", p=128), S["dP"])
    kb.dma("gpsimd", wpool[:, 4096:8192], lambda: D[bf_name].ap(), S["dP"])
    if layer == 0:
        kb.dma("gpsimd", wgs[:, 0:256],
               lambda: D["wgt"].ap().rearrange("(t p) e -> p t e", p=128), S["dP"])
    else:
        kb.dma("gpsimd", wgs[:, 0:512],
               lambda: D["rwt"].ap().rearrange("(t p) e -> p t e", p=128), S["dP"])
    pool_dma = S["dP"].n

    if layer == 1:
        oh = kb.sb["oh"]
        kb.wait("vector", S["dXQ"], S["dXQ"].n)
        kb.op("vector", lambda: nc.vector.tensor_copy(oh[0:1, 2:6], kb.sb["acti"][:]),
              inc=S["vV"])
        va = S["vV"].n
        kb.wait("tensor", S["vV"], va)
        kb.op("tensor", lambda: nc.tensor.matmul(pA[0:16, 0:4], kb.sb["cons1f"][0:1, 0:16],
              oh[0:1, 2:6], start=True, stop=True))
        kb.op("tensor", lambda: nc.tensor.matmul(pB[0:4, 0:16], oh[0:1, 2:6],
              kb.sb["cons1f"][0:1, 0:16], start=True, stop=True), inc=S["mT"])
        m1 = S["mT"].n
        kb.wait("vector", S["mT"], m1)
        kb.op("vector", lambda: nc.vector.tensor_tensor(oh[0:16, 8:12], kb.sb["ioc"][:],
              pA[0:16, 0:4], op=ALU.is_equal))
        kb.op("vector", lambda: nc.vector.tensor_tensor(oh[0:4, 16:32], kb.sb["ior"][:],
              pB[0:4, 0:16], op=ALU.is_equal), inc=S["vV"])
        v1 = S["vV"].n
        kb.wait("tensor", S["vV"], v1)
        kb.op("tensor", lambda: nc.tensor.matmul(pC[0:4, 0:1], oh[0:16, 8:12],
              oh[0:16, 1:2], start=True, stop=True), inc=S["mT"])
        m2 = S["mT"].n
        kb.wait("vector", S["mT"], m2)
        kb.op("vector", lambda: nc.vector.tensor_scalar(oh[0:4, 33:34],
              pC[0:4, 0:1], 1e-12, None, op0=ALU.add), inc=S["vV"])
        kb.wait("scalar", S["vV"], S["vV"].n)
        kb.op("scalar", lambda: nc.scalar.activation(oh[0:4, 32:33],
              oh[0:4, 33:34], AF.Ln))
        kb.drain("scalar")
        kb.op("scalar", lambda: nc.scalar.activation(grow[0:4, 0:16], oh[0:4, 16:32],
              AF.Copy), inc=S["sS"])
        onehot_done = S["sS"].n

    kb.wait("tensor", S["dP"], pool_dma)
    if layer == 1:
        kb.wait("tensor", S["sS"], onehot_done)
    for ht in range(NT):
        for half in range(2):
            kb.op("tensor", lambda ht=ht, half=half: nc.tensor.matmul(
                pC[0:E, half * 512:(half + 1) * 512],
                wgs[:, E * ht:E * (ht + 1)],
                inbuf[:, ht * 1024 + half * 512: ht * 1024 + (half + 1) * 512],
                start=(ht == 0), stop=(ht == NT - 1)),
                inc=S["mT"] if (ht == NT - 1 and half == 1) else None)
    lg_mm = S["mT"].n
    kb.wait("scalar", S["mT"], lg_mm)
    if layer == 0:
        kb.op("scalar", lambda: nc.scalar.activation(smalls[0:E, 0:1024], pC[0:E, :],
              AF.Copy), inc=S["sS"])
    else:
        kb.wait("vector", S["mT"], lg_mm)
        kb.op("vector", lambda: nc.vector.tensor_single_scalar(
            smalls[0:E, 0:1024], pC[0:E, :], kb.sb["oh"][0:16, 0:1],
            op=ALU.add), inc=S["vV"])
        kb.wait("scalar", S["vV"], S["vV"].n)
        kb.op("scalar", lambda: nc.scalar.drain(), inc=S["sS"])
    lg_ev = S["sS"].n

    if layer == 1:
        kb.wait("tensor", S["sS"], lg_ev)
        for half in range(2):
            kb.op("tensor", lambda half=half: nc.tensor.matmul(
                pD[0:4, half * 512:(half + 1) * 512], kb.sb["oh"][0:16, 8:12],
                smalls[0:16, half * 512:(half + 1) * 512], start=True, stop=True),
                inc=S["mT"] if half == 1 else None)
        sl_mm = S["mT"].n
        kb.wait("vector", S["mT"], sl_mm)
        kb.op("vector", lambda: nc.vector.tensor_single_scalar(
            smalls[0:4, 1024:2048], pD[0:4, :], kb.sb["oh"][0:4, 32:33],
            op=ALU.add), inc=S["vV"])
        kb.wait("scalar", S["vV"], S["vV"].n)
        kb.op("scalar", lambda: nc.scalar.drain(), inc=S["sS"])
        lg_ev = S["sS"].n

    nsl = E if layer == 0 else 4
    soff = 0 if layer == 0 else 1024
    kb.wait("tensor", S["sS"], lg_ev)
    for i in range(8):
        kb.op("tensor", lambda i=i, nsl=nsl, soff=soff: nc.tensor.transpose(
            pD[:, 64 + nsl * i: 64 + nsl * (i + 1)],
            smalls[0:nsl, soff + 128 * i: soff + 128 * (i + 1)],
            identf[0:nsl, 0:nsl]),
            inc=S["mT"] if i == 7 else None)
    tr_mm = S["mT"].n
    kb.wait("vector", S["mT"], tr_mm)
    M1, NM1, SS, RS = gw[:, 0:1], gw[:, 1:2], gw[:, 2:3], gw[:, 3:4]
    M2 = gw[:, 4:5]
    EQ, KEEP, EE, WW = gw[:, 8:16], gw[:, 16:24], gw[:, 24:32], gw[:, 32:40]
    tb_done = 0
    for i in range(8):
        lgt = pD[:, 64 + nsl * i: 64 + nsl * (i + 1)]
        if i > 0:
            kb.wait("vector", S["mT"], tb_done)
        kb.op("vector", lambda lgt=lgt: nc.vector.reduce_max(M1, lgt, axis=AX.X))
        kb.drain("vector")
        kb.op("vector", lambda: nc.vector.tensor_scalar(NM1, M1, -1.0, None, op0=ALU.mult))
        if layer == 0:
            kb.op("vector", lambda lgt=lgt: nc.vector.tensor_scalar(
                EQ[:, 0:nsl], lgt, M1[:, 0:1], None, op0=ALU.is_equal))
            kb.drain("vector")
            kb.op("vector", lambda lgt=lgt: nc.vector.scalar_tensor_tensor(
                EQ[:, 0:nsl], EQ[:, 0:nsl], -1e30, lgt, op0=ALU.mult, op1=ALU.add))
            kb.drain("vector")
            kb.op("vector", lambda: nc.vector.reduce_max(M2, EQ[:, 0:nsl], axis=AX.X))
        kb.op("vector", lambda: nc.vector.drain(), inc=S["vV"])
        vv = S["vV"].n
        kb.wait("scalar", S["vV"], vv)
        kb.op("scalar", lambda lgt=lgt: nc.scalar.activation(
            EE[:, 0:nsl], lgt, AF.Exp, bias=NM1[:, 0:1]), inc=S["sS"])
        se = S["sS"].n
        kb.wait("vector", S["sS"], se)
        if layer == 0:
            kb.op("vector", lambda lgt=lgt: nc.vector.tensor_scalar(
                KEEP[:, 0:nsl], lgt, M2[:, 0:1], None, op0=ALU.is_ge))
            kb.drain("vector")
            kb.op("vector", lambda: nc.vector.tensor_mul(WW[:, 0:nsl], EE[:, 0:nsl],
                  KEEP[:, 0:nsl]))
        else:
            kb.op("vector", lambda: nc.vector.tensor_copy(WW[:, 0:nsl], EE[:, 0:nsl]))
        kb.drain("vector")
        kb.op("vector", lambda: nc.vector.reduce_sum(SS, WW[:, 0:nsl], axis=AX.X))
        kb.drain("vector")
        kb.op("vector", lambda: nc.vector.reciprocal(RS, SS))
        kb.drain("vector")
        kb.op("vector", lambda: nc.vector.tensor_scalar(
            gwb[:, 0:nsl], WW[:, 0:nsl], RS[:, 0:1], None, op0=ALU.mult), inc=S["vV"])
        vr = S["vV"].n
        kb.wait("tensor", S["vV"], vr)
        kb.op("tensor", lambda i=i, nsl=nsl: nc.tensor.transpose(
            pB[0:nsl, 128 * i:128 * (i + 1)], gwb[:, 0:nsl], identf[:]),
            inc=S["mT"])
        tb_done = S["mT"].n
    kb.wait("scalar", S["mT"], tb_done)
    if layer == 0:
        kb.op("scalar", lambda: nc.scalar.activation(grow[0:8, 64:1088],
              pB[0:8, 0:1024], AF.Copy), inc=S["sS"])
    else:
        kb.op("scalar", lambda: nc.scalar.activation(kb.sb["attnT"][0:4, 0:1024],
              pB[0:4, 0:1024], AF.Copy), inc=S["sS"])
    rw_ready = S["sS"].n

    if layer == 1:
        kb.wait("tensor", S["sS"], rw_ready)
        for half in range(2):
            kb.op("tensor", lambda half=half: nc.tensor.matmul(
                pC[0:16, half * 512:(half + 1) * 512], grow[0:4, 0:16],
                kb.sb["attnT"][0:4, half * 512:(half + 1) * 512],
                start=True, stop=True), inc=S["mT"] if half == 1 else None)
        af_mm = S["mT"].n
        kb.wait("scalar", S["mT"], af_mm)
        kb.op("scalar", lambda: nc.scalar.activation(grow[0:16, 64:1088],
              pC[0:16, 0:1024], AF.Copy), inc=S["sS"])
        rw_ready = S["sS"].n

    kb.wait("tensor", S["vV"], S["vV"].n)
    for ht in range(NT):
        for half in range(2):
            kb.op("tensor", lambda ht=ht, half=half: nc.tensor.matmul(
                pD[:, half * 512:(half + 1) * 512], wpool[:, 128 * ht:128 * (ht + 1)],
                inbuf[:, ht * 1024 + half * 512: ht * 1024 + (half + 1) * 512],
                start=(ht == 0), stop=(ht == NT - 1)),
                inc=S["mT"] if (ht == NT - 1 and half == 1) else None)
    z_mm = S["mT"].n
    kb.wait("scalar", S["mT"], z_mm)
    kb.op("scalar", lambda: nc.scalar.activation(zsb[:, 0:1024], pD[:], AF.Copy), inc=S["sS"])
    z_ev = S["sS"].n
    kb.wait("tensor", S["sS"], rw_ready)
    krap = kb.sb["kreps"][0:8, 0:128] if layer == 0 else kb.sb["kreps"][0:16, 128:256]
    nK = 8 if layer == 0 else 16
    for half in range(2):
        kb.op("tensor", lambda half=half, krap=krap, nK=nK: nc.tensor.matmul(
            pC[:, half * 512:(half + 1) * 512], krap,
            grow[0:nK, 64 + half * 512: 64 + (half + 1) * 512],
            start=True, stop=True), inc=S["mT"] if half == 1 else None)
    bc_mm = S["mT"].n
    kb.wait("vector", S["mT"], bc_mm)
    kb.wait("vector", S["sS"], z_ev)
    kb.op("vector", lambda: nc.vector.tensor_mul(hwsb[:], zsb[:, 0:1024], pC[:]), inc=S["vV"])
    hw_done = S["vV"].n

    wb_dram = D[wb_name]
    mm_emit = make_wstream_mm_emit(kb, wb_dram, inbuf, extra=("delta", hw_done))
    bcol = 0 if layer == 0 else 32
    bias_cols = lambda t: kb.sb["bias_sb"][:, bcol + t: bcol + t + 1]
    emit_evac_loop(kb, NT, mm_emit, resid_load, normed, outbuf, scr_out,
                   bias_cols=bias_cols)


def make_wstream_mm_emit(kb, wb_dram, inbuf, extra=None):
    S = kb.sems; nc = kb.nc
    wsb, wpool = kb.sb["wsb"], kb.sb["wpool"]
    hwsb = kb.sb.get("hwsb")
    q_mm = [S["mT"].n] * 4
    q_dma = [0, 0, 0, 0]
    def mm_emit(kb2, t, pt):
        for q in range(4):
            kb.wait("gpsimd", S["mT"], q_mm[q])
            q_dma[q] = kb.dma(
                "gpsimd",
                lambda q=q: wsb[:, q * 1024:(q + 1) * 1024],
                lambda t=t, q=q: wb_dram.ap()[q * 1024:(q + 1) * 1024,
                                              128 * t:128 * (t + 1)]
                .rearrange("(u p) o -> p u o", p=128), S["dQ" + str(q)])
        if t == 0 and extra is not None:
            kb.wait("tensor", S["vV"], extra[1])
        for q in range(4):
            kb.wait("tensor", S["dQ" + str(q)], q_dma[q])
            for j in range(8):
                u = 8 * q + j
                for half in range(2):
                    kb.op("tensor", lambda u=u, half=half, pt=pt:
                          nc.tensor.matmul(
                              pt[:, half * 512:(half + 1) * 512],
                              wsb[:, u * 128:(u + 1) * 128],
                              inbuf[:, u * 1024 + half * 512:
                                    u * 1024 + (half + 1) * 512],
                              start=(u == 0),
                              stop=(extra is None and u == NT - 1)),
                          inc=S["mT"] if (j == 7 and half == 1) else None)
            q_mm[q] = S["mT"].n
        if extra is not None:
            for half in range(2):
                kb.op("tensor", lambda t=t, half=half, pt=pt: nc.tensor.matmul(
                    pt[:, half * 512:(half + 1) * 512],
                    wpool[:, 4096 + 128 * t: 4096 + 128 * (t + 1)],
                    hwsb[:, half * 512:(half + 1) * 512],
                    start=False, stop=True),
                    inc=S["mT"] if half == 1 else None)
        return S["mT"].n
    return mm_emit


def resid_from_dram(kb, dram):
    S = kb.sems
    def load(kb2, t, sl):
        return kb.dma("sync",
                      lambda sl=sl: kb.sb["xpre"][:, sl * 1024:(sl + 1) * 1024],
                      lambda t=t: dram.ap()[t * 128:(t + 1) * 128, :],
                      S["dX0" if sl == 0 else "dX1"])
    return load


def build_ffn_phase_program(layer):
    nc = bass.Bass("TRN2", target_bir_lowering=False)
    ctx = ExitStack()
    with ctx:
        kb = KB(nc, ctx)
        phase = "p1" if layer == 0 else "p3"
        setup_common(kb, {phase})
        kb.D["xnin"] = nc.dram_tensor("xnin", [HD, TPC], F32, kind="ExternalInput")
        load_consts(kb, {phase})
        emit_cast_load_buf(kb, kb.D["xnin"], kb.sb["bufA"])
        kb.wait("tensor", kb.sems["dW"], kb.sems["dW"].n)
        kb.wait("tensor", kb.sems["dXQ"], kb.sems["dXQ"].n)
        emit_ffn(kb, layer, kb.sb["bufA"], kb.sb["bufB"],
                 resid_from_dram(kb, kb.D["xnin"]), False, kb.D["scr0"])
        emit_stats_finalize(kb)
        emit_final_output(kb, kb.D["scr0"], kb.D["out"])
        kb.finalize_block()
    return nc


def ffn_phase_inputs(layer, inputs, core):
    m = dict(
        ones_bf=NPC["ones_bf"], ones1f=NPC["ones1x128_f"], onesm=NPC["onesm"],
        identb=NPC["ident_bf"], identf=NPC["ident_f"],
        tri=NPC["tri01_bf"], rmat=NPC["rmat_bf"],
    )
    if layer == 0:
        m["wb0t"] = np.ascontiguousarray(inputs["L0_ffn_W_base"].T)
        m["b0"] = np.ascontiguousarray(inputs["L0_ffn_b_base"].reshape(32, 128).T)
        m["wgt"] = np.ascontiguousarray(inputs["L0_ffn_W_gate"].T)
        m["a0t"] = np.ascontiguousarray(
            inputs["L0_ffn_A"].transpose(2, 0, 1).reshape(HD, 128))
        m["b0f"] = np.ascontiguousarray(
            inputs["L0_ffn_B"].transpose(0, 2, 1).reshape(128, HD))
        m["kr8"] = NPC["krepl8"]
    else:
        m["wb1t"] = np.ascontiguousarray(inputs["L1_ffn_W_base"].T)
        m["b1"] = np.ascontiguousarray(inputs["L1_ffn_b_base"].reshape(32, 128).T)
        m["rwt"] = np.ascontiguousarray(inputs["L1_ffn_router_W"].T)
        m["la1t"] = np.ascontiguousarray(
            inputs["L1_ffn_lora_A"].transpose(2, 0, 1).reshape(HD, 128))
        m["lb1f"] = np.ascontiguousarray(
            inputs["L1_ffn_lora_B"].transpose(0, 2, 1).reshape(128, HD))
        m["rb"] = np.ascontiguousarray(inputs["L1_ffn_router_b"].reshape(16, 1))
        m["prior"] = np.ascontiguousarray(inputs["L1_ffn_domain_prior"].reshape(16, 1))
        m["active"] = np.ascontiguousarray(inputs["L1_ffn_active"].reshape(1, 4))
        m["kr16"] = NPC["krepl16x2"]
        m["io16c"] = NPC["iota16c"]
        m["io4r"] = NPC["iota4r"]
    return m


def emit_kbuild(kb):
    S = kb.sems; nc = kb.nc; D = kb.D
    gw, gwb, xpre, scf = kb.sb["gw"], kb.sb["gwb"], kb.sb["xpre"], kb.sb["scf"]
    pA, pB, pD = kb.ps["pA"], kb.ps["pB"], kb.ps["pD"]
    ktile = kb.sb["ktile"]
    C1f = kb.sb["cons1f"]

    LR, LI = gw[0:64, 0:1], gw[0:64, 1:2]
    PR, PI = gw[0:64, 2:3], gw[0:64, 3:4]
    QR, QI = gw[0:64, 4:5], gw[0:64, 5:6]
    CTR, CTI = gw[0:64, 6:7], gw[0:64, 7:8]
    BP = gw[0:64, 8:9]
    NLR, NLI, LR2 = gw[0:64, 9:10], gw[0:64, 10:11], gw[0:64, 11:12]
    BR, BI = gw[0:64, 12:13], gw[0:64, 13:14]
    VV = gw[0:64, 16:24]
    STP = gw[0:1, 32:33]
    RS2 = gw[0:1, 33:34]
    DD = gw[0:1, 34:35]

    kb.op("vector", lambda: nc.vector.memset(xpre[:, 0:32], 0.0), inc=S["vV"])
    zv = S["vV"].n
    kb.wait("gpsimd", S["vV"], zv)
    zkp = kb.dma("gpsimd",
                 lambda: bass.AP(D["kpad"], 0, [[32, 128], [1, 32]]),
                 lambda: xpre[:, 0:32], S["dW"])

    kb.dma("sync", gw[0:64, 0:2], D["lam"].ap(), S["dXQ"])
    kb.dma("sync", gw[0:64, 2:4], D["pri"].ap(), S["dXQ"])
    kb.dma("sync", gw[0:64, 4:6], D["qri"].ap(), S["dXQ"])
    kb.dma("sync", gw[0:64, 6:8], D["ctri"].ap(), S["dXQ"])
    kb.dma("sync", gw[0:64, 8:9], D["bp"].ap(), S["dXQ"])
    kb.dma("sync", gw[0:1, 34:35], D["dd"].ap(), S["dXQ"])
    kb.dma("sync", gw[0:1, 35:36], D["lstep"].ap(), S["dXQ"])
    kb.dma("sync", scf[0:64, 2400:2464], D["vcrt"].ap(), S["dXQ"])
    kb.dma("sync", scf[0:64, 2464:2528], D["vcit"].ap(), S["dXQ"])
    kb.dma("sync", scf[:, 2528:2560], D["ccr"].ap(), S["dXQ"])
    kb.dma("sync", scf[:, 2560:2592], D["cci"].ap(), S["dXQ"])
    kb.dma("sync", scf[0:32, 2592:2624], D["w32r"].ap(), S["dXQ"])
    kb.dma("sync", scf[0:32, 2624:2656], D["w32i"].ap(), S["dXQ"])
    kb.dma("sync", scf[0:32, 2656:2688], D["w32in"].ap(), S["dXQ"])
    kb.dma("sync", scf[:, 2688:2720], D["twr"].ap(), S["dXQ"])
    kb.dma("sync", scf[:, 2720:2752], D["twi"].ap(), S["dXQ"])
    kb.dma("sync", scf[:, 2752:2880], D["c128"].ap(), S["dXQ"])
    kb.dma("sync", scf[:, 2880:3008], D["s128n"].ap(), S["dXQ"])
    pdma = S["dXQ"].n

    kb.wait("scalar", S["dXQ"], pdma)
    kb.op("scalar", lambda: nc.scalar.activation(STP, gw[0:1, 35:36], AF.Exp),
          inc=S["sS"])
    kb.wait("vector", S["sS"], S["sS"].n)
    kb.op("vector", lambda: nc.vector.reciprocal(RS2, STP))
    kb.drain("vector")
    kb.op("vector", lambda: nc.vector.tensor_scalar(RS2, RS2, 2.0, None, op0=ALU.mult))
    kb.drain("vector")
    kb.op("vector", lambda: nc.vector.tensor_scalar(NLR, LR, -1.0, None, op0=ALU.mult))
    kb.op("vector", lambda: nc.vector.tensor_scalar(NLI, LI, -1.0, None, op0=ALU.mult))
    kb.op("vector", lambda: nc.vector.tensor_mul(LR2, LR, LR), inc=S["vV"])
    vpar = S["vV"].n

    kb.wait("tensor", S["dXQ"], pdma)
    kb.op("tensor", lambda: nc.tensor.matmul(pA[0:64, 0:1], scf[0:64, 2400:2464],
          BP, start=True, stop=True))
    kb.op("tensor", lambda: nc.tensor.matmul(pA[0:64, 1:2], scf[0:64, 2464:2528],
          BP, start=True, stop=True), inc=S["mT"])
    kb.wait("vector", S["mT"], S["mT"].n)
    kb.op("vector", lambda: nc.vector.tensor_copy(BR, pA[0:64, 0:1]))
    kb.op("vector", lambda: nc.vector.tensor_copy(BI, pA[0:64, 1:2]))
    kb.drain("vector")
    def cmul_conj(ar, ai, br_, bi_, outr, outi):
        kb.op("vector", lambda: nc.vector.tensor_mul(outr, ar, br_))
        kb.op("vector", lambda: nc.vector.tensor_mul(outi, ar, bi_))
        kb.drain("vector")
        kb.op("vector", lambda: nc.vector.scalar_tensor_tensor(
            gw[0:64, 14:15], ai, 1.0, bi_, op0=ALU.mult, op1=ALU.mult))
        kb.op("vector", lambda: nc.vector.scalar_tensor_tensor(
            gw[0:64, 15:16], ai, 1.0, br_, op0=ALU.mult, op1=ALU.mult))
        kb.drain("vector")
        kb.op("vector", lambda: nc.vector.tensor_add(outr, outr, gw[0:64, 14:15]))
        kb.op("vector", lambda: nc.vector.tensor_sub(outi, outi, gw[0:64, 15:16]))
        kb.drain("vector")
    cmul_conj(CTR, CTI, BR, BI, VV[:, 0:1], VV[:, 1:2])
    cmul_conj(CTR, CTI, PR, PI, VV[:, 2:3], VV[:, 3:4])
    cmul_conj(QR, QI, BR, BI, VV[:, 4:5], VV[:, 5:6])
    cmul_conj(QR, QI, PR, PI, VV[:, 6:7], VV[:, 7:8])
    for j in range(4):
        kb.op("vector", lambda j=j: nc.vector.tensor_copy(
            gwb[0:64, 2 * j:2 * j + 1], VV[:, 2 * j:2 * j + 1]))
        kb.op("vector", lambda j=j: nc.vector.tensor_copy(
            gwb[0:64, 2 * j + 1:2 * j + 2], VV[:, 2 * j + 1:2 * j + 2]))
        kb.op("vector", lambda j=j: nc.vector.tensor_scalar(
            gwb[0:64, 8 + 2 * j:9 + 2 * j], VV[:, 2 * j + 1:2 * j + 2], -1.0,
            None, op0=ALU.mult))
        kb.op("vector", lambda j=j: nc.vector.tensor_copy(
            gwb[0:64, 9 + 2 * j:10 + 2 * j], VV[:, 2 * j:2 * j + 1]))
    kb.wait("vector", S["dW"], zkp)
    kb.op("vector", lambda: nc.vector.drain(), inc=S["vV"])
    vl_done = S["vV"].n

    tr_n = 0
    for kh in range(2):
        if kh == 1:
            kb.wait("sync", S["mT"], tr_n)
        th = kb.dma("sync", scf[0:1, 0:2048],
                    D["tim"].ap()[0:1, 2048 * kh:2048 * (kh + 1)], S["dX0"])
        kb.wait("vector", S["dX0"], th)
        kb.wait("vector", S["vV"], vl_done)
        kb.op("vector", lambda: nc.vector.tensor_scalar(
            scf[0:1, 0:2048], scf[0:1, 0:2048], RS2, None, op0=ALU.mult),
            inc=S["vV"])
        gv = S["vV"].n
        for c in range(4):
            kb.wait("tensor", S["vV"], gv)
            if c > 0:
                kb.wait("tensor", S["vV"], S["vV"].n)
            kb.op("tensor", lambda c=c: nc.tensor.matmul(
                pA[0:64, 0:512], C1f[0:1, 0:64],
                scf[0:1, 512 * c:512 * (c + 1)], start=True, stop=True),
                inc=S["mT"])
            kb.wait("vector", S["mT"], S["mT"].n)
            kb.op("vector", lambda c=c: nc.vector.tensor_single_scalar(
                xpre[0:64, 512 * c:512 * (c + 1)], pA[0:64, 0:512], NLI,
                op=ALU.add), inc=S["vV"])
        dIn = S["vV"].n
        kb.wait("scalar", S["vV"], dIn)
        kb.op("scalar", lambda: nc.scalar.activation(
            scf[0:64, 0:2048], xpre[0:64, 0:2048], AF.Square), inc=S["sS"])
        sqn = S["sS"].n
        kb.wait("vector", S["sS"], sqn)
        kb.op("vector", lambda: nc.vector.tensor_scalar(
            scf[0:64, 0:2048], scf[0:64, 0:2048], LR2, None, op0=ALU.add))
        kb.drain("vector")
        kb.op("vector", lambda: nc.vector.reciprocal(
            scf[0:64, 0:2048], scf[0:64, 0:2048]))
        kb.drain("vector")
        kb.op("vector", lambda: nc.vector.scalar_tensor_tensor(
            xpre[0:64, 0:2048], xpre[0:64, 0:2048], -1.0, scf[0:64, 0:2048],
            op0=ALU.mult, op1=ALU.mult))
        kb.drain("vector")
        kb.op("vector", lambda: nc.vector.tensor_scalar(
            scf[0:64, 0:2048], scf[0:64, 0:2048], NLR, None, op0=ALU.mult),
            inc=S["vV"])
        rdone = S["vV"].n
        for c in range(4):
            kb.wait("tensor", S["vV"], rdone)
            kb.wait("tensor", S["sS"], S["sS"].n)
            kb.op("tensor", lambda c=c: nc.tensor.matmul(
                pB[0:8, 0:512], gwb[0:64, 0:8],
                scf[0:64, 512 * c:512 * (c + 1)], start=True, stop=False))
            kb.op("tensor", lambda c=c: nc.tensor.matmul(
                pB[0:8, 0:512], gwb[0:64, 8:16],
                xpre[0:64, 512 * c:512 * (c + 1)], start=False, stop=True),
                inc=S["mT"])
            kb.wait("scalar", S["mT"], S["mT"].n)
            kb.op("scalar", lambda c=c: nc.scalar.activation(
                xpre[0:8, 512 * c:512 * (c + 1)], pB[0:8, 0:512], AF.Copy),
                inc=S["sS"])
        kmn = S["sS"].n
        kb.wait("tensor", S["sS"], kmn)
        for j in range(16):
            F = 16 * kh + j
            kb.op("tensor", lambda j=j, F=F: nc.tensor.transpose(
                pD[:, 8 * F:8 * F + 8], xpre[0:8, 128 * j:128 * (j + 1)],
                kb.sb["identf"][0:8, 0:8]),
                inc=S["mT"] if j == 15 else None)
        tr_n = S["mT"].n

    kb.wait("vector", S["mT"], tr_n)
    AT = lambda j: pD[:, 8 * 0 + j: 256 + j: 8]
    atR, atI = scf[:, 0:32], scf[:, 32:64]
    T1, T2, T3, T4 = (scf[:, 64:96], scf[:, 96:128], scf[:, 128:160],
                      scf[:, 160:192])
    kb.op("vector", lambda: nc.vector.tensor_scalar(T1, AT(6), 1.0, None, op0=ALU.add))
    kb.op("vector", lambda: nc.vector.tensor_copy(T2, AT(7)))
    kb.drain("vector")
    kb.op("vector", lambda: nc.vector.tensor_mul(T3, T1, T1))
    kb.op("vector", lambda: nc.vector.tensor_mul(T4, T2, T2))
    kb.drain("vector")
    kb.op("vector", lambda: nc.vector.tensor_add(T3, T3, T4))
    kb.drain("vector")
    kb.op("vector", lambda: nc.vector.reciprocal(T3, T3))
    kb.drain("vector")
    QRv, QIv = scf[:, 192:224], scf[:, 224:256]
    kb.op("vector", lambda: nc.vector.tensor_mul(QRv, AT(4), T1))
    kb.op("vector", lambda: nc.vector.tensor_mul(QIv, AT(5), T1))
    kb.drain("vector")
    kb.op("vector", lambda: nc.vector.scalar_tensor_tensor(
        T4, AT(5), 1.0, T2, op0=ALU.mult, op1=ALU.mult))
    kb.drain("vector")
    kb.op("vector", lambda: nc.vector.tensor_add(QRv, QRv, T4))
    kb.drain("vector")
    kb.op("vector", lambda: nc.vector.scalar_tensor_tensor(
        T4, AT(4), 1.0, T2, op0=ALU.mult, op1=ALU.mult))
    kb.drain("vector")
    kb.op("vector", lambda: nc.vector.tensor_sub(QIv, QIv, T4))
    kb.drain("vector")
    kb.op("vector", lambda: nc.vector.tensor_mul(QRv, QRv, T3))
    kb.op("vector", lambda: nc.vector.tensor_mul(QIv, QIv, T3))
    kb.drain("vector")
    kb.op("vector", lambda: nc.vector.tensor_mul(T1, AT(2), QRv))
    kb.op("vector", lambda: nc.vector.tensor_mul(T2, AT(3), QIv))
    kb.op("vector", lambda: nc.vector.tensor_mul(T3, AT(2), QIv))
    kb.op("vector", lambda: nc.vector.tensor_mul(T4, AT(3), QRv))
    kb.drain("vector")
    kb.op("vector", lambda: nc.vector.tensor_sub(T1, T1, T2))
    kb.op("vector", lambda: nc.vector.tensor_add(T3, T3, T4))
    kb.drain("vector")
    kb.op("vector", lambda: nc.vector.tensor_sub(T1, AT(0), T1))
    kb.op("vector", lambda: nc.vector.tensor_sub(T3, AT(1), T3))
    kb.drain("vector")
    kb.op("vector", lambda: nc.vector.tensor_mul(atR, scf[:, 2528:2560], T1))
    kb.op("vector", lambda: nc.vector.tensor_mul(atI, scf[:, 2528:2560], T3))
    kb.drain("vector")
    kb.op("vector", lambda: nc.vector.scalar_tensor_tensor(
        T2, scf[:, 2560:2592], 1.0, T3, op0=ALU.mult, op1=ALU.mult))
    kb.op("vector", lambda: nc.vector.scalar_tensor_tensor(
        T4, scf[:, 2560:2592], 1.0, T1, op0=ALU.mult, op1=ALU.mult))
    kb.drain("vector")
    kb.op("vector", lambda: nc.vector.tensor_sub(atR, atR, T2))
    kb.op("vector", lambda: nc.vector.tensor_add(atI, atI, T4), inc=S["vV"])
    at_done = S["vV"].n

    kb.wait("tensor", S["vV"], at_done)
    kb.op("tensor", lambda: nc.tensor.transpose(pA[0:32, 0:128], atR,
          kb.sb["identf"][:]))
    kb.op("tensor", lambda: nc.tensor.transpose(pA[0:32, 128:256], atI,
          kb.sb["identf"][:]), inc=S["mT"])
    kb.wait("scalar", S["mT"], S["mT"].n)
    kb.op("scalar", lambda: nc.scalar.activation(scf[0:32, 256:384],
          pA[0:32, 0:128], AF.Copy))
    kb.op("scalar", lambda: nc.scalar.activation(scf[0:32, 384:512],
          pA[0:32, 128:256], AF.Copy), inc=S["sS"])
    art_done = S["sS"].n
    kb.wait("tensor", S["sS"], art_done)
    kb.op("tensor", lambda: nc.tensor.matmul(pB[:, 0:32], scf[0:32, 256:384],
          scf[0:32, 2592:2624], start=True, stop=False))
    kb.op("tensor", lambda: nc.tensor.matmul(pB[:, 0:32], scf[0:32, 384:512],
          scf[0:32, 2656:2688], start=False, stop=True))
    kb.op("tensor", lambda: nc.tensor.matmul(pB[:, 32:64], scf[0:32, 256:384],
          scf[0:32, 2624:2656], start=True, stop=False))
    kb.op("tensor", lambda: nc.tensor.matmul(pB[:, 32:64], scf[0:32, 384:512],
          scf[0:32, 2592:2624], start=False, stop=True), inc=S["mT"])
    kb.wait("scalar", S["mT"], S["mT"].n)
    kb.op("scalar", lambda: nc.scalar.activation(scf[:, 512:544], pB[:, 0:32],
          AF.Copy))
    kb.op("scalar", lambda: nc.scalar.activation(scf[:, 544:576], pB[:, 32:64],
          AF.Copy), inc=S["sS"])
    h_done = S["sS"].n
    kb.wait("vector", S["sS"], h_done)
    HR, HI = scf[:, 512:544], scf[:, 544:576]
    HpR, HpI = scf[:, 576:608], scf[:, 608:640]
    kb.op("vector", lambda: nc.vector.tensor_mul(HpR, HR, scf[:, 2688:2720]))
    kb.op("vector", lambda: nc.vector.tensor_mul(HpI, HR, scf[:, 2720:2752]))
    kb.drain("vector")
    kb.op("vector", lambda: nc.vector.scalar_tensor_tensor(
        T1, HI, 1.0, scf[:, 2720:2752], op0=ALU.mult, op1=ALU.mult))
    kb.op("vector", lambda: nc.vector.scalar_tensor_tensor(
        T2, HI, 1.0, scf[:, 2688:2720], op0=ALU.mult, op1=ALU.mult))
    kb.drain("vector")
    kb.op("vector", lambda: nc.vector.tensor_sub(HpR, HpR, T1))
    kb.op("vector", lambda: nc.vector.tensor_add(HpI, HpI, T2), inc=S["vV"])
    hp_done = S["vV"].n
    kb.wait("tensor", S["vV"], hp_done)
    kb.op("tensor", lambda: nc.tensor.matmul(pA[:, 0:32], scf[:, 2752:2880],
          HpR, start=True, stop=False))
    kb.op("tensor", lambda: nc.tensor.matmul(pA[:, 0:32], scf[:, 2880:3008],
          HpI, start=False, stop=True), inc=S["mT"])
    k2_done = S["mT"].n
    kb.wait("scalar", S["mT"], k2_done)
    kb.op("scalar", lambda: nc.scalar.activation(ktile[:], pA[:, 0:32],
          AF.Copy, scale=1.0 / 4096.0), inc=S["sS"])
    kt_done = S["sS"].n
    kb.wait("vector", S["sS"], kt_done)
    kb.op("vector", lambda: nc.vector.tensor_scalar(
        ktile[0:1, 0:1], ktile[0:1, 0:1], DD, None, op0=ALU.add), inc=S["vV"])
    kd = S["vV"].n
    kb.wait("gpsimd", S["vV"], kd)
    kb.wait("gpsimd", S["dW"], zkp)
    kpd = kb.dma("gpsimd",
                 lambda: bass.AP(D["kpad"], 4096, [[32, 128], [1, 32]]),
                 lambda: ktile[:], S["dW"])
    return kpd


def emit_conv(kb, inbuf, outbuf, resid_load, normed, scr_out):
    S = kb.sems; nc = kb.nc; D = kb.D
    wsb, rmat = kb.sb["wsb"], kb.sb["rmat"]
    pA, pB = kb.ps["pA"], kb.ps["pB"]

    kpd = emit_kbuild(kb)

    kb.wait("tensor", S["sS"], S["sS"].n)
    fl_mm = [0, 0]; fl_ev = [0, 0]
    for j in range(NT):
        sl = j % 2
        pt = kb.ps["pA" if sl == 0 else "pB"]
        if j >= 2:
            kb.wait("tensor", S["sS"], fl_ev[sl])
        for half in range(2):
            kb.op("tensor", lambda j=j, half=half, pt=pt: nc.tensor.matmul(
                pt[:, half * 512:(half + 1) * 512], rmat[:],
                inbuf[:, j * 1024 + half * 512: j * 1024 + (half + 1) * 512],
                start=True, stop=True),
                inc=S["mT"] if half == 1 else None)
        fl_mm[sl] = S["mT"].n
        kb.wait("scalar", S["mT"], fl_mm[sl])
        kb.op("scalar", lambda j=j, pt=pt: nc.scalar.activation(
            inbuf[:, j * 1024:(j + 1) * 1024], pt[:], AF.Copy), inc=S["sS"])
        fl_ev[sl] = S["sS"].n
    flip_done = S["sS"].n

    kb.wait("sync", S["dW"], kpd)
    bnd = kb.dma("sync", wsb[:, 0:4096],
                 lambda: bass.AP(D["kpad"], 3969, [[1, 128], [128, 32], [1, 128]]),
                 S["dX1"])

    def mm_emit(kb2, t, pt):
        if t == 0:
            kb.wait("tensor", S["dX1"], bnd)
            kb.wait("tensor", S["sS"], flip_done)
        for j in range(t + 1):
            for half in range(2):
                kb.op("tensor", lambda t=t, j=j, half=half, pt=pt:
                      nc.tensor.matmul(
                          pt[:, half * 512:(half + 1) * 512],
                          wsb[:, (t - j) * 128:(t - j + 1) * 128],
                          inbuf[:, j * 1024 + half * 512:
                                j * 1024 + (half + 1) * 512],
                          start=(j == 0), stop=(j == t)),
                      inc=S["mT"] if (j == t and half == 1) else None)
        return S["mT"].n

    emit_evac_loop(kb, NT, mm_emit, resid_load, normed, outbuf, scr_out)


def build_conv_phase_program():
    nc = bass.Bass("TRN2", target_bir_lowering=False)
    ctx = ExitStack()
    with ctx:
        kb = KB(nc, ctx)
        setup_common(kb, {"p2"})
        kb.D["xnin"] = nc.dram_tensor("xnin", [HD, TPC], F32, kind="ExternalInput")
        load_consts(kb, {"p2"})
        emit_cast_load_buf(kb, kb.D["xnin"], kb.sb["bufA"])
        kb.wait("tensor", kb.sems["dW"], kb.sems["dW"].n)
        kb.wait("tensor", kb.sems["dXQ"], kb.sems["dXQ"].n)
        emit_conv(kb, kb.sb["bufA"], kb.sb["bufB"],
                  resid_from_dram(kb, kb.D["xnin"]), False, kb.D["scr0"])
        emit_stats_finalize(kb)
        emit_final_output(kb, kb.D["scr0"], kb.D["out"])
        kb.finalize_block()
    return nc


def conv_phase_inputs(inputs):
    m = dict(
        ones_bf=NPC["ones_bf"], ones1f=NPC["ones1x128_f"], onesm=NPC["onesm"],
        identb=NPC["ident_bf"], identf=NPC["ident_f"],
        tri=NPC["tri01_bf"], rmat=NPC["rmat_bf"],
        lam=np.ascontiguousarray(inputs["L1_attn_Lambda_ri"]),
        pri=np.ascontiguousarray(inputs["L1_attn_p_ri"]),
        qri=np.ascontiguousarray(inputs["L1_attn_q_ri"]),
        vcrt=np.ascontiguousarray(inputs["L1_attn_Vc_ri"][:, :, 0].T),
        vcit=np.ascontiguousarray(inputs["L1_attn_Vc_ri"][:, :, 1].T),
        ctri=np.ascontiguousarray(inputs["L1_attn_Ct_ri"].reshape(64, 2)),
        bp=np.ascontiguousarray(inputs["L1_attn_Bp"]),
        dd=np.ascontiguousarray(inputs["L1_attn_D"].reshape(1, 1)),
        lstep=np.ascontiguousarray(inputs["L1_attn_log_step"].reshape(1, 1)),
        tim=NPC["tim"], ccr=NPC["ccr"], cci=NPC["cci"],
        w32r=NPC["w32r"], w32i=NPC["w32i"], w32in=NPC["w32in"],
        twr=NPC["twr"], twi=NPC["twi"], c128=NPC["c128"], s128n=NPC["s128n"],
    )
    return m


ATTN_HEADS = NHEADS
ATTN_SKIP_SWITCH = False
ATTN_PARTS = 8


def emit_attn(kb, outbuf, AO, resid_load, scr_out):
    S = kb.sems; nc = kb.nc; D = kb.D
    wsb, wpool = kb.sb["wsb"], kb.sb["wpool"]
    qsb, attnT, rbcs = kb.sb["qsb"], kb.sb["attnT"], kb.sb["rbcs"]
    biastab, tri = kb.sb["biastab"], kb.sb["tri"]
    scf = kb.sb["scf"]
    pA, pB, pC, pD = kb.ps["pA"], kb.ps["pB"], kb.ps["pC"], kb.ps["pD"]
    x0buf = outbuf

    xsb = [wsb[:, 0:1024], wsb[:, 1024:2048]]
    wkvs = [wsb[:, 2048:2304], wsb[:, 2304:2560]]
    kv_dma = [0, 0]; kv_mm = [0, 0]
    ev_k = 0; ev_v = 0; tr_ev = [0, 0]
    for kc in range(4):
        for ht in range(NT):
            sl = ht % 2
            if kc + ht > 0:
                kb.wait("gpsimd", S["mT"], kv_mm[sl])
            ksem = S["dK0" if sl == 0 else "dK1"]
            kb.dma("gpsimd", lambda sl=sl: xsb[sl],
                   lambda kc=kc, ht=ht: D["xT"].ap()[128 * ht:128 * (ht + 1),
                                                     1024 * kc:1024 * (kc + 1)],
                   ksem)
            kv_dma[sl] = kb.dma("gpsimd", lambda sl=sl: wkvs[sl],
                                lambda ht=ht: D["wkv"].ap()[128 * ht:128 * (ht + 1), :],
                                ksem)
            kb.wait("tensor", ksem, kv_dma[sl])
            if ht == 0:
                kb.wait("tensor", S["sS"], ev_v)
            for half in range(2):
                kb.op("tensor", lambda sl=sl, half=half, ht=ht: nc.tensor.matmul(
                    pA[:, half * 512:(half + 1) * 512], wkvs[sl][:, 0:128],
                    xsb[sl][:, half * 512:(half + 1) * 512],
                    start=(ht == 0), stop=(ht == NT - 1)))
            for half in range(2):
                kb.op("tensor", lambda sl=sl, half=half, ht=ht: nc.tensor.matmul(
                    pB[:, half * 512:(half + 1) * 512], wkvs[sl][:, 128:256],
                    xsb[sl][:, half * 512:(half + 1) * 512],
                    start=(ht == 0), stop=(ht == NT - 1)),
                    inc=S["mT"] if half == 1 else None)
            kv_mm[sl] = S["mT"].n
        kb.wait("scalar", S["mT"], kv_mm[(NT - 1) % 2])
        kb.op("scalar", lambda kc=kc: nc.scalar.activation(
            wpool[:, 1024 * kc:1024 * (kc + 1)], pA[:], AF.Copy))
        kb.op("scalar", lambda: nc.scalar.activation(
            scf[:, 0:1024], pB[:], AF.Copy), inc=S["sS"])
        ev_v = S["sS"].n
        for i in range(8):
            tsl = i % 2
            kb.wait("tensor", S["sS"], max(ev_v, tr_ev[tsl]))
            kb.op("tensor", lambda i=i, tsl=tsl: nc.tensor.transpose(
                pD[:, 512 * tsl:512 * tsl + 128], scf[:, 128 * i:128 * (i + 1)],
                kb.sb["identf"][:]), inc=S["mT"])
            kb.wait("scalar", S["mT"], S["mT"].n)
            kb.op("scalar", lambda kc=kc, i=i, tsl=tsl: nc.scalar.activation(
                wpool[:, 4096 + (8 * kc + i) * 128: 4096 + (8 * kc + i + 1) * 128],
                pD[:, 512 * tsl:512 * tsl + 128], AF.Copy), inc=S["sS"])
            tr_ev[tsl] = S["sS"].n
    kv_done_s = S["sS"].n

    def emit_variant(v):
        units = attn_units(v)
        by_head = {}
        for uid, d in enumerate(units):
            by_head.setdefault(d["h"], []).append((uid, d))
        Q = [512 * v, 3584 - 512 * v]
        q_ev = [kv_done_s, kv_done_s]
        q_ev_last = kv_done_s
        att_done = [0, 0]
        exp_n = [0, 0]
        trv_n = [0, 0]
        avden = [0, 0]
        avev = [S["vV"].n, S["vV"].n]
        rbcev = [kv_done_s, kv_done_s]
        qq_mm = [S["mT"].n] * 4
        useq = 0
        for h in range(ATTN_HEADS):
            hsl = h % 2
            for q in range(4):
                kb.wait("gpsimd", S["mT"], qq_mm[q])
                qd = kb.dma("gpsimd", lambda q=q: wsb[:, q * 1024:(q + 1) * 1024],
                            lambda h=h, q=q: D["wq"].ap()[1024 * q:1024 * (q + 1),
                                                          128 * h:128 * (h + 1)]
                            .rearrange("(u p) o -> p u o", p=128), S["dQ" + str(q)])
                kb.wait("tensor", S["dQ" + str(q)], qd)
                if q == 0:
                    kb.wait("tensor", S["sS"], q_ev_last)
                for j in range(8):
                    u = 8 * q + j
                    for half in range(2):
                        kb.op("tensor", lambda u=u, half=half: nc.tensor.matmul(
                            pC[:, half * 512:(half + 1) * 512],
                            wsb[:, u * 128:(u + 1) * 128],
                            x0buf[:, u * 1024 + half * 512: u * 1024 + (half + 1) * 512],
                            start=(u == 0), stop=(u == NT - 1)),
                            inc=S["mT"] if (j == 7 and half == 1) else None)
                qq_mm[q] = S["mT"].n
            qmm = S["mT"].n
            kb.wait("scalar", S["mT"], max(qmm, att_done[hsl]))
            kb.op("scalar", lambda hsl=hsl: nc.scalar.activation(
                qsb[:, 1024 * hsl:1024 * (hsl + 1)], pC[:], AF.Copy), inc=S["sS"])
            q_ev[hsl] = S["sS"].n
            q_ev_last = S["sS"].n
            blocks = {}
            for uid, d in by_head[h]:
                blocks.setdefault(d["b"], []).append((uid, d))
            for b, ulist in sorted(blocks.items()):
                bcol = 512 * b
                first = True
                nu = len(ulist)
                for i, (uid, d) in enumerate(ulist):
                    last = (i == nu - 1)
                    usl = useq % 2
                    useq += 1
                    w = d["w"]
                    if i == 0:
                        kb.wait("tensor", S["vV"], max(avev[0], avev[1]))
                        kb.wait("tensor", S["sS"], max(rbcev[0], rbcev[1]))
                    kb.wait("tensor", S["sS"], q_ev[hsl])
                    kb.wait("tensor", S["aRs"], exp_n[usl])
                    kb.op("tensor", lambda d=d, usl=usl, w=w, hsl=hsl, b=b, Q=Q:
                          nc.tensor.matmul(
                              pA[:, 512 * usl:512 * usl + w],
                              wpool[:, d["t"] * 128:(d["t"] + 1) * 128],
                              qsb[:, 1024 * hsl + 512 * b + (d["A"] + d["off"] - Q[b]):
                                  1024 * hsl + 512 * b + (d["A"] + d["off"] - Q[b]) + w],
                              start=True, stop=True), inc=S["mT"])
                    sc_n = S["mT"].n
                    if ATTN_PARTS < 2:
                        exp_n[usl] = S["aRs"].n
                        first = False
                        continue
                    kb.wait("scalar", S["mT"], max(sc_n, avden[usl]))
                    kb.op("scalar", lambda uid=uid, usl=usl, w=w:
                          nc.scalar.activation(
                              attnT[:, 512 * usl:512 * usl + w],
                              pA[:, 512 * usl:512 * usl + w], AF.Exp,
                              bias=biastab[:, uid:uid + 1], scale=ASCALE),
                          inc=S["aRs"])
                    exp_n[usl] = S["aRs"].n
                    if ATTN_PARTS < 3:
                        first = False
                        continue
                    if ATTN_PARTS >= 3 and d["diag"]:
                        kb.wait("vector", S["aRs"], exp_n[usl])
                        kb.op("vector", lambda d=d, usl=usl:
                              nc.vector.tensor_mul(
                                  attnT[:, 512 * usl:512 * usl + d["tri_w"]],
                                  attnT[:, 512 * usl:512 * usl + d["tri_w"]],
                                  tri[:, d["tri_shift"]:d["tri_shift"] + d["tri_w"]]),
                              inc=S["aRv"])
                        trv_n[usl] = S["aRv"].n
                    if ATTN_PARTS < 4:
                        first = False
                        continue
                    kb.wait("tensor", S["aRs"], exp_n[usl])
                    if d["diag"]:
                        kb.wait("tensor", S["aRv"], trv_n[usl])
                    if first:
                        kb.wait("tensor", S["vV"], max(avev[0], avev[1]))
                        kb.wait("tensor", S["sS"], max(rbcev[0], rbcev[1]))
                    col0 = bcol + (d["A"] + d["off"] - Q[b])
                    kb.op("tensor", lambda d=d, usl=usl, w=w, col0=col0, b=b, first=first, last=last:
                          nc.tensor.matmul(
                              pB[:, col0 - 512 * b:col0 - 512 * b + w],
                              wpool[:, 4096 + d["t"] * 128: 4096 + (d["t"] + 1) * 128],
                              attnT[:, 512 * usl:512 * usl + w],
                              start=first, stop=last))
                    dcol = col0 - 512 * b
                    kb.op("tensor", lambda usl=usl, w=w, dcol=dcol, first=first, last=last:
                          nc.tensor.matmul(
                              pD[:, dcol:dcol + w],
                              kb.sb["onesm"][:],
                              attnT[:, 512 * usl:512 * usl + w],
                              start=first, stop=last), inc=S["mT"])
                    avden[usl] = S["mT"].n
                    first = False
                att_done[hsl] = S["mT"].n
                avsl = scf[:, 1024 * b:1024 * b + 512]
                dnsl = scf[:, 1024 * b + 512:1024 * b + 1024]
                rcp = scf[:, 2048:2560]
                kb.wait("scalar", S["mT"], att_done[hsl])
                kb.wait("scalar", S["vV"], avev[b])
                kb.op("scalar", lambda avsl=avsl: nc.scalar.activation(
                    avsl, pB[:, 0:512], AF.Copy))
                kb.op("scalar", lambda dnsl=dnsl: nc.scalar.activation(
                    dnsl, pD[:, 0:512], AF.Copy), inc=S["sS"])
                rbcev[b] = S["sS"].n
                kb.wait("vector", S["sS"], rbcev[b])
                kb.op("vector", lambda dnsl=dnsl, rcp=rcp:
                      nc.vector.reciprocal(rcp, dnsl))
                kb.drain("vector")
                kb.op("vector", lambda h=h, b=b, avsl=avsl, rcp=rcp:
                      nc.vector.tensor_mul(
                          AO[:, 1024 * h + 512 * b: 1024 * h + 512 * (b + 1)],
                          avsl, rcp), inc=S["vV"])
                avev[b] = S["vV"].n

    if not ATTN_SKIP_SWITCH:
        kb.switch(4, emit_variant, ["tensor", "vector", "scalar", "gpsimd"])

    kb.wait("tensor", S["vV"], S["vV"].n)
    mm_emit = make_wstream_mm_emit(kb, D["wo"], AO, extra=None)
    emit_evac_loop(kb, NT, mm_emit, resid_load, False, outbuf, scr_out)


def build_attn_phase_program():
    nc = bass.Bass("TRN2", target_bir_lowering=False)
    ctx = ExitStack()
    with ctx:
        kb = KB(nc, ctx)
        setup_common(kb, {"p0"})
        load_consts(kb, {"p0"})
        emit_cast_load_buf(kb, kb.D["x0p"], kb.sb["bufA"])
        kb.wait("tensor", kb.sems["dW"], kb.sems["dW"].n)
        kb.wait("tensor", kb.sems["dXQ"], kb.sems["dXQ"].n)
        emit_attn(kb, kb.sb["bufA"], kb.sb["bufB"],
                  resid_from_dram(kb, kb.D["x0p"]), kb.D["scr0"])
        emit_stats_finalize(kb)
        emit_final_output(kb, kb.D["scr0"], kb.D["out"])
        kb.finalize_block()
    return nc


def attn_phase_inputs(inputs, core, x_full_T=None):
    if x_full_T is None:
        x_full_T = np.ascontiguousarray(
            inputs["L0_attn_hidden_states"].transpose(0, 2, 1))
    batch = core // 4
    v = core % 4
    lo, hi = 512 * v, 3584 - 512 * v
    xT = x_full_T[batch]
    x0p = np.concatenate([xT[:, lo:lo + 512], xT[:, hi:hi + 512]], axis=1)
    m = dict(
        ones_bf=NPC["ones_bf"], ones1f=NPC["ones1x128_f"], onesm=NPC["onesm"],
        identb=NPC["ident_bf"], identf=NPC["ident_f"],
        tri=NPC["tri01_bf"], rmat=NPC["rmat_bf"],
        x0p=np.ascontiguousarray(x0p), xT=np.ascontiguousarray(xT),
        wq=np.ascontiguousarray(inputs["L0_attn_Wq"]),
        wkv=np.ascontiguousarray(inputs["L0_attn_Wkv"]),
        wo=np.ascontiguousarray(inputs["L0_attn_Wo"]),
        biastab=bias_table(core),
    )
    return m


_PROG_CACHE = {}


def _get_prog(name, builder):
    if name not in _PROG_CACHE:
        _PROG_CACHE[name] = builder()
    return _PROG_CACHE[name]


def kernel(**inputs):
    x_full_T = np.ascontiguousarray(
        np.asarray(inputs["L0_attn_hidden_states"]).transpose(0, 2, 1))
    cores = list(range(8))

    nc0 = _get_prog("p0", build_attn_phase_program)
    maps = [attn_phase_inputs(inputs, c, x_full_T) for c in cores]
    r = run_bass_kernel_spmd(nc0, maps, core_ids=cores)
    xn = [np.asarray(r.results[c]["out"]) for c in cores]

    nc1 = _get_prog("p1", lambda: build_ffn_phase_program(0))
    maps = []
    for c in cores:
        m = ffn_phase_inputs(0, inputs, c)
        m["xnin"] = xn[c]
        maps.append(m)
    r = run_bass_kernel_spmd(nc1, maps, core_ids=cores)
    xn = [np.asarray(r.results[c]["out"]) for c in cores]

    nc2 = _get_prog("p2", build_conv_phase_program)
    maps = []
    for c in cores:
        m = conv_phase_inputs(inputs)
        m["xnin"] = xn[c]
        maps.append(m)
    r = run_bass_kernel_spmd(nc2, maps, core_ids=cores)
    xn = [np.asarray(r.results[c]["out"]) for c in cores]

    nc3 = _get_prog("p3", lambda: build_ffn_phase_program(1))
    maps = []
    for c in cores:
        m = ffn_phase_inputs(1, inputs, c)
        m["xnin"] = xn[c]
        maps.append(m)
    r = run_bass_kernel_spmd(nc3, maps, core_ids=cores)

    yT = np.empty((2, HD, HD), dtype=np.float32)
    for c in cores:
        out = np.asarray(r.results[c]["out"])
        b, v = c // 4, c % 4
        lo, hi = 512 * v, 3584 - 512 * v
        yT[b][:, lo:lo + 512] = out[:, 0:512]
        yT[b][:, hi:hi + 512] = out[:, 512:1024]
    return np.ascontiguousarray(yT.transpose(0, 2, 1))


# version 02
# speedup vs baseline: 1.0268x; 1.0268x over previous
import numpy as np
import ml_dtypes
from contextlib import ExitStack

import concourse.bass as bass
import concourse.mybir as mybir
from concourse.bass_utils import run_bass_kernel_spmd

BF = ml_dtypes.bfloat16
F32 = mybir.dt.float32
BF16 = mybir.dt.bfloat16
I32 = mybir.dt.int32
AF = mybir.ActivationFunctionType
ALU = mybir.AluOpType
AX = mybir.AxisListType

HD = 4096
NT = 32
TPC = 1024
NHEADS = 32
DH = 128
ASCALE = float(DH) ** -0.5
EPS_LN = 1e-5

_start = 2.0 ** (-(2.0 ** -(np.log2(NHEADS) - 3)))
SLOPES = np.array([_start ** (i + 1) for i in range(NHEADS)], dtype=np.float64)

def _chunk_for(slope):
    for c in (512, 256, 128, 64):
        if slope * (c - 1) <= 65.0:
            return c
    return 64

CHUNK = [_chunk_for(s) for s in SLOPES]
CUTOFF = [130.0 / s for s in SLOPES]


def _np_consts():
    c = {}
    c["ones_bf"] = np.ones((128, 1), dtype=BF)
    c["ones1x128_f"] = np.ones((1, 128), dtype=np.float32)
    c["ident_bf"] = np.eye(128, dtype=BF)
    c["ident_f"] = np.eye(128, dtype=np.float32)
    c["tri01_bf"] = np.triu(np.ones((128, 128), dtype=np.float32)).astype(BF)
    c["rmat_bf"] = np.eye(128, dtype=np.float32)[::-1].copy().astype(BF)
    c["onesm"] = np.ones((128, 128), dtype=BF)
    c["krepl8"] = np.repeat(np.eye(8, dtype=np.float32), 16, axis=1).astype(BF)
    c["krepl16x2"] = (2.0 * np.repeat(np.eye(16, dtype=np.float32), 8, axis=1)).astype(BF)
    c["iota16c"] = np.tile(np.arange(16, dtype=np.float32)[:, None], (1, 4))
    c["iota4r"] = np.tile(np.arange(16, dtype=np.float32)[None, :], (4, 1))
    a32 = np.arange(32)
    w32 = np.exp(2j * np.pi * np.outer(a32, a32) / 32.0)
    c["w32r"] = w32.real.astype(np.float32)
    c["w32i"] = w32.imag.astype(np.float32)
    c["w32in"] = (-w32.imag).astype(np.float32)
    p128 = np.arange(128)
    tw = np.exp(2j * np.pi * np.outer(p128, a32) / 4096.0)
    c["twr"] = tw.real.astype(np.float32)
    c["twi"] = tw.imag.astype(np.float32)
    w128 = np.exp(2j * np.pi * np.outer(p128, p128) / 128.0)
    c["c128"] = w128.real.astype(np.float32)
    c["s128n"] = (-w128.imag).astype(np.float32)
    k = (np.arange(32)[None, :] * 128 + np.arange(128)[:, None]).astype(np.float64)
    om = np.exp(-2j * np.pi * k / 4096.0)
    cc = 2.0 / (1.0 + om)
    c["ccr"] = cc.real.astype(np.float32)
    c["cci"] = cc.imag.astype(np.float32)
    tim = np.tan(np.pi * np.arange(4096, dtype=np.float64) / 4096.0)
    c["tim"] = tim[None, :].astype(np.float32)
    return c

NPC = _np_consts()


def attn_units(variant):
    k = variant
    Q = [512 * k, 3584 - 512 * k]
    units = []
    for h in range(NHEADS):
        c = CHUNK[h]
        cut = CUTOFF[h]
        for b in (0, 1):
            q0 = Q[b]
            for ch in range(512 // c):
                A = q0 + ch * c
                t_hi = (A + c - 1) // 128
                t_lo = max(0, int(np.floor((A - cut) / 128.0)))
                for t in range(t_lo, t_hi + 1):
                    off = max(0, t * 128 - A)
                    w = c - off
                    diag = (t * 128 + 127 >= A)
                    tri_shift = max(0, A - t * 128) if diag else 0
                    tri_w = min(128 - tri_shift, w) if diag else 0
                    units.append(dict(h=h, b=b, A=A, c=c, t=t, off=off, w=w,
                                      diag=diag, tri_shift=tri_shift,
                                      tri_w=tri_w))
    return units

N_UNITS = [len(attn_units(v)) for v in range(4)]
NU_MAX = max(N_UNITS)


def bias_table(core):
    v = core % 4
    units = attn_units(v)
    tab = np.zeros((128, NU_MAX), dtype=np.float32)
    p = np.arange(128, dtype=np.float64)
    for u, d in enumerate(units):
        kref = d["A"] + d["c"] - 1
        tab[:, u] = (SLOPES[d["h"]] * (d["t"] * 128 + p - kref)).astype(np.float32)
    return tab


ENGS = ("tensor", "vector", "scalar", "sync", "gpsimd")
ETYPE = {"tensor": mybir.EngineType.PE, "vector": mybir.EngineType.DVE,
         "scalar": mybir.EngineType.Activation, "sync": mybir.EngineType.SP,
         "gpsimd": mybir.EngineType.Pool}


class Sem:
    __slots__ = ("h", "n", "name")
    def __init__(self, h, name):
        self.h = h
        self.n = 0
        self.name = name


class KB:
    def __init__(self, nc, ctx):
        self.nc = nc
        self.ctx = ctx
        self.streams = {e: [] for e in ENGS}
        self._stack = []
        self.sems = {}
        self.sb = {}
        self.ps = {}

    def sem(self, name):
        s = Sem(self.ctx.enter_context(self.nc.semaphore(name)), name)
        self.sems[name] = s
        return s

    def sbuf(self, name, shape, dt):
        t = self.ctx.enter_context(self.nc.sbuf_tensor(name + "_s", shape, dt))
        self.sb[name] = t
        return t

    def psum(self, name, shape, dt):
        t = self.ctx.enter_context(self.nc.psum_tensor(name + "_p", shape, dt))
        self.ps[name] = t
        return t

    def emit(self, eng, fn):
        self.streams[eng].append(fn)

    def op(self, eng, fn, inc=None, inc2=None):
        if inc is not None:
            sem, k = inc if isinstance(inc, tuple) else (inc, 1)
            sem.n += k
            if inc2 is not None:
                s2, k2 = inc2 if isinstance(inc2, tuple) else (inc2, 1)
                s2.n += k2
                self.emit(eng, lambda: fn().then_inc(sem.h, k).then_inc(s2.h, k2))
            else:
                self.emit(eng, lambda: fn().then_inc(sem.h, k))
            return sem.n
        self.emit(eng, fn)
        return None

    def wait(self, eng, sem, val=None):
        v = sem.n if val is None else val
        h = sem.h
        eobj = getattr(self.nc, eng)
        self.emit(eng, lambda: eobj.wait_ge(h, v))

    def dma(self, eng, dst, src, sem):
        sem.n += 16
        eobj = getattr(self.nc, eng)
        d = dst if callable(dst) else (lambda: dst)
        s = src if callable(src) else (lambda: src)
        self.emit(eng, lambda: eobj.dma_start(d(), s()).then_inc(sem.h, 16))
        return sem.n

    def drain(self, eng):
        eobj = getattr(self.nc, eng)
        self.emit(eng, lambda: eobj.drain())

    def switch(self, nvar, emit_fn, engines):
        pre = {n: s.n for n, s in self.sems.items()}
        var_streams = []
        var_counts = []
        for v in range(nvar):
            for n, s in self.sems.items():
                s.n = pre[n]
            saved = self.streams
            self.streams = {e: [] for e in ENGS}
            emit_fn(v)
            var_streams.append(self.streams)
            self.streams = saved
            var_counts.append({n: s.n for n, s in self.sems.items()})
        final = {n: max(vc[n] for vc in var_counts) for n in self.sems}
        for n, s in self.sems.items():
            s.n = final[n]
        for v in range(nvar):
            for e in ENGS:
                if e not in engines:
                    assert not var_streams[v][e], (v, e, "undeclared switch engine")
        nc = self.nc
        for e in engines:
            owned = [n for n in self.sems
                     if any(var_counts[v][n] != pre[n] for v in range(nvar))
                     and SEM_OWNER.get(n) == e]
            bodies = [var_streams[v][e] for v in range(nvar)]
            deficits = [{n: final[n] - var_counts[v][n] for n in owned}
                        for v in range(nvar)]
            eobj = getattr(nc, e)
            et = ETYPE[e]
            semmap = {n: self.sems[n].h for n in owned}
            def do_switch(bodies=bodies, deficits=deficits, eobj=eobj, et=et,
                          semmap=semmap, nvar=nvar):
                pid = nc.partition_id(engines=[et])
                var = pid % nvar
                for i in eobj.Switch(var, nvar):
                    for f in bodies[i]:
                        f()
                    for n, d in deficits[i].items():
                        if d > 0:
                            eobj.sem_inc(semmap[n], d)
            self.emit(e, do_switch)

    def finalize_block(self):
        nc = self.nc
        block = self.ctx.enter_context(nc.Block())
        streams = self.streams
        @block.tensor
        def _(t):
            for f in streams["tensor"]:
                f()
        @block.vector
        def _(v):
            for f in streams["vector"]:
                f()
        @block.scalar
        def _(s):
            for f in streams["scalar"]:
                f()
        @block.sync
        def _(s):
            for f in streams["sync"]:
                f()
        @block.gpsimd
        def _(g):
            for f in streams["gpsimd"]:
                f()


SEM_OWNER = {}


def mk_sems(kb, names_owners):
    out = {}
    for name, owner in names_owners:
        out[name] = kb.sem(name)
        SEM_OWNER[name] = owner
    return out


def setup_common(kb, phases):
    nc = kb.nc
    D = {}
    def din(name, shape, dt=F32):
        D[name] = nc.dram_tensor(name, shape, dt, kind="ExternalInput")
        return D[name]
    din("ones_bf", [128, 1], BF16); din("ones1f", [1, 128])
    din("onesm", [128, 128], BF16)
    din("identb", [128, 128], BF16); din("identf", [128, 128])
    din("tri", [128, 128], BF16); din("rmat", [128, 128], BF16)
    if "p0" in phases:
        din("x0p", [HD, TPC]); din("xT", [HD, HD])
        din("wq", [HD, HD]); din("wkv", [HD, 256]); din("wo", [HD, HD])
        din("biastab", [128, NU_MAX])
    if "p1" in phases:
        din("wb0t", [HD, HD]); din("b0", [128, 32]); din("wgt", [HD, 8])
        din("a0t", [HD, 128]); din("b0f", [128, HD])
        din("kr8", [8, 128], BF16)
    if "p2" in phases:
        din("lam", [64, 2]); din("pri", [64, 2]); din("qri", [64, 2])
        din("vcrt", [64, 64]); din("vcit", [64, 64])
        din("ctri", [64, 2]); din("bp", [64, 1]); din("dd", [1, 1]); din("lstep", [1, 1])
        din("tim", [1, 4096]); din("ccr", [128, 32]); din("cci", [128, 32])
        din("w32r", [32, 32]); din("w32i", [32, 32]); din("w32in", [32, 32])
        din("twr", [128, 32]); din("twi", [128, 32])
        din("c128", [128, 128]); din("s128n", [128, 128])
    if "p3" in phases:
        din("wb1t", [HD, HD]); din("b1", [128, 32]); din("rwt", [HD, 16])
        din("la1t", [HD, 128]); din("lb1f", [128, HD])
        din("rb", [16, 1]); din("prior", [16, 1]); din("active", [1, 4], I32)
        din("kr16", [16, 128], BF16); din("io16c", [16, 4]); din("io4r", [4, 16])
    D["out"] = nc.dram_tensor("out", [HD, TPC], F32, kind="ExternalOutput")
    D["scr0"] = nc.dram_tensor("scr0", [HD, TPC], F32)
    D["scr1"] = nc.dram_tensor("scr1", [HD, TPC], F32)
    if "p2" in phases:
        D["kpad"] = nc.dram_tensor("kpad", [8192], BF16)
    kb.D = D

    kb.sbuf("bufA", [128, NT * TPC], BF16)
    kb.sbuf("bufB", [128, NT * TPC], BF16)
    kb.sbuf("wpool", [128, 8192], BF16)
    kb.sbuf("wsb", [128, 4096], BF16)
    kb.sbuf("scf", [128, 3072], F32)
    kb.sbuf("xpre", [128, 2048], F32)
    kb.sbuf("sqsb", [128, 2048], BF16)
    kb.sbuf("mbc", [128, 1024], F32)
    kb.sbuf("rbc2", [128, 1024], F32)
    kb.sbuf("cons_bf", [128, 1], BF16)
    kb.sbuf("onesm", [128, 128], BF16)
    kb.sbuf("cons1f", [1, 128], F32)
    kb.sbuf("identb", [128, 128], BF16)
    kb.sbuf("identf", [128, 128], F32)
    kb.sbuf("attnT", [128, 1024], BF16)
    if "p0" in phases:
        kb.sbuf("qsb", [128, 2048], BF16)
        kb.sbuf("rbcs", [128, 512], F32)
        kb.sbuf("biastab", [128, NU_MAX], F32)
        kb.sbuf("tri", [128, 128], BF16)
    if phases & {"p1", "p2", "p3"}:
        kb.sbuf("gw", [128, 48], F32)
        kb.sbuf("gwb", [128, 16], F32)
    if "p1" in phases or "p3" in phases:
        kb.sbuf("hwsb", [128, 1024], BF16)
        kb.sbuf("grow", [16, 1152], BF16)
        kb.sbuf("bias_sb", [128, 64], F32)
        kb.sbuf("kreps", [16, 256], BF16)
        kb.sbuf("wgs", [128, 512], BF16)
    if "p3" in phases:
        kb.sbuf("oh", [16, 64], F32)
        kb.sbuf("acti", [1, 4], I32)
        kb.sbuf("ioc", [16, 4], F32)
        kb.sbuf("ior", [4, 16], F32)
    if "p2" in phases:
        kb.sbuf("rmat", [128, 128], BF16)
        kb.sbuf("ktile", [128, 32], BF16)
    kb.alias = {}
    kb.psum("pA", [128, 1024], F32)
    kb.psum("pB", [128, 1024], F32)
    kb.psum("pC", [128, 1024], F32)
    kb.psum("pD", [128, 1024], F32)

    mk_sems(kb, [("dW", "gpsimd"), ("dXQ", "sync"), ("dSC", "sync"),
                 ("dOUT", "sync"), ("mT", "tensor"), ("vV", "vector"),
                 ("sS", "scalar"), ("gG", "gpsimd"), ("dSS", "scalar"),
                 ("aRs", "scalar"), ("aRv", "vector"),
                 ("dP", "gpsimd"),
                 ("dQ0", "gpsimd"), ("dQ1", "gpsimd"),
                 ("dQ2", "gpsimd"), ("dQ3", "gpsimd"),
                 ("dK0", "gpsimd"), ("dK1", "gpsimd"),
                 ("dX0", "sync"), ("dX1", "sync"),
                 ("dS0", "sync"), ("dS1", "sync"),
                 ("dO0", "sync"), ("dO1", "sync")])


def load_consts(kb, phases):
    D, S = kb.D, kb.sems
    kb.dma("sync", kb.sb["cons_bf"][:], D["ones_bf"].ap(), S["dXQ"])
    kb.dma("sync", kb.sb["onesm"][:], D["onesm"].ap(), S["dXQ"])
    kb.dma("sync", kb.sb["cons1f"][:], D["ones1f"].ap(), S["dXQ"])
    kb.dma("sync", kb.sb["identb"][:], D["identb"].ap(), S["dXQ"])
    kb.dma("sync", kb.sb["identf"][:], D["identf"].ap(), S["dXQ"])
    if "p0" in phases:
        kb.dma("sync", kb.sb["biastab"][:], D["biastab"].ap(), S["dXQ"])
        kb.dma("sync", kb.sb["tri"][:], D["tri"].ap(), S["dXQ"])
    if "p2" in phases:
        kb.dma("sync", kb.sb["rmat"][:], D["rmat"].ap(), S["dXQ"])
    if "p1" in phases:
        kb.dma("sync", kb.sb["kreps"][0:8, 0:128], D["kr8"].ap(), S["dXQ"])
        kb.dma("sync", kb.sb["bias_sb"][:, 0:32], D["b0"].ap(), S["dXQ"])
    if "p3" in phases:
        kb.dma("sync", kb.sb["kreps"][0:16, 128:256], D["kr16"].ap(), S["dXQ"])
        kb.dma("sync", kb.sb["bias_sb"][:, 32:64], D["b1"].ap(), S["dXQ"])
        kb.dma("sync", kb.sb["ioc"][:], D["io16c"].ap(), S["dXQ"])
        kb.dma("sync", kb.sb["ior"][:], D["io4r"].ap(), S["dXQ"])
        kb.dma("sync", kb.sb["acti"][:], D["active"].ap(), S["dXQ"])
        kb.dma("sync", kb.sb["oh"][:, 0:1], D["rb"].ap(), S["dXQ"])
        kb.dma("sync", kb.sb["oh"][:, 1:2], D["prior"].ap(), S["dXQ"])
    return S["dXQ"].n


def emit_evac_loop(kb, n_tiles, mm_emit, resid_load, normed, outbuf, scr_out,
                  bias_cols=None, psum_names=("pA", "pB")):
    S = kb.sems
    nc = kb.nc
    scf, xpre, sqsb = kb.sb["scf"], kb.sb["xpre"], kb.sb["sqsb"]
    ev1 = [0, 0]; wdone = [0, 0]; obn = [0, 0]; sqd = [0, 0]; scrd = [0, 0]
    stats_done = [0, 0]
    pending = []

    def emit_stats(t, sl, ob_ap, sq_ap):
        kb.wait("tensor", S["gG"], obn[sl])
        kb.wait("tensor", S["sS"], sqd[sl])
        for half in range(2):
            kb.op("tensor", lambda ob_ap=ob_ap, half=half, t=t:
                  nc.tensor.matmul(kb.ps["pC"][:, half * 512:(half + 1) * 512],
                                   kb.sb["onesm"][:],
                                   ob_ap[:, half * 512:(half + 1) * 512],
                                   start=(t == 0), stop=(t == n_tiles - 1)))
        for half in range(2):
            kb.op("tensor", lambda sq_ap=sq_ap, half=half, t=t:
                  nc.tensor.matmul(kb.ps["pD"][:, half * 512:(half + 1) * 512],
                                   kb.sb["onesm"][:],
                                   sq_ap[:, half * 512:(half + 1) * 512],
                                   start=(t == 0), stop=(t == n_tiles - 1)),
                  inc=(S["mT"], 1) if half == 1 else None)
        stats_done[sl] = S["mT"].n

    for t in range(n_tiles):
        sl = t % 2
        pt = kb.ps[psum_names[sl]]
        if t >= 2:
            kb.wait("tensor", S["sS"], ev1[sl])
        mm_done = mm_emit(kb, t, pt)
        if pending:
            emit_stats(*pending.pop())
        if t >= 2:
            kb.wait("sync", S["vV"], wdone[sl])
        xp = resid_load(kb, t, sl)
        if t >= 2:
            kb.wait("scalar", S["gG"], obn[sl])
            kb.wait("scalar", S["dS0" if sl == 0 else "dS1"], scrd[sl])
        kb.wait("scalar", S["mT"], mm_done)
        wout = scf[:, sl * 1024:(sl + 1) * 1024]
        kb.op("scalar", lambda pt=pt, wout=wout:
              nc.scalar.activation(wout, pt[:], AF.Copy), inc=S["sS"])
        ev1[sl] = S["sS"].n
        bap = bias_cols(t) if bias_cols is not None else None
        kb.wait("vector", S["sS"], ev1[sl])
        kb.wait("vector", S["dX0" if sl == 0 else "dX1"], xp)
        xs = xpre[:, sl * 1024:(sl + 1) * 1024]
        if normed:
            tmp = scf[:, 2048:3072]
            kb.op("vector", lambda xs=xs, tmp=tmp:
                  nc.vector.tensor_sub(tmp, xs, kb.sb["mbc"][:]))
            kb.drain("vector")
            kb.op("vector", lambda tmp=tmp:
                  nc.vector.tensor_mul(tmp, tmp, kb.sb["rbc2"][:]))
            kb.drain("vector")
            other = tmp
        else:
            other = xs
        if bap is not None:
            kb.op("vector", lambda other=other, wout=wout, bap=bap:
                  nc.vector.scalar_tensor_tensor(wout, wout, bap, other,
                                                 op0=ALU.add, op1=ALU.add),
                  inc=S["vV"])
        else:
            kb.op("vector", lambda other=other, wout=wout:
                  nc.vector.tensor_add(wout, other, wout), inc=S["vV"])
        wdone[sl] = S["vV"].n
        kb.wait("gpsimd", S["vV"], wdone[sl])
        ob_ap = outbuf[:, t * 1024:(t + 1) * 1024]
        kb.op("gpsimd", lambda ob_ap=ob_ap, wout=wout:
              nc.gpsimd.tensor_copy(ob_ap, wout), inc=S["gG"])
        obn[sl] = S["gG"].n
        if t >= 2:
            kb.wait("scalar", S["mT"], stats_done[sl])
        kb.wait("scalar", S["vV"], wdone[sl])
        sq_ap = sqsb[:, sl * 1024:(sl + 1) * 1024]
        kb.op("scalar", lambda sq_ap=sq_ap, wout=wout:
              nc.scalar.activation(sq_ap, wout, AF.Square), inc=S["sS"])
        sqd[sl] = S["sS"].n
        kb.wait("sync", S["vV"], wdone[sl])
        scrd[sl] = kb.dma("sync", lambda t=t: scr_out.ap()[t * 128:(t + 1) * 128, :],
                          lambda wout=wout: wout, S["dS0" if sl == 0 else "dS1"])
        pending.append((t, sl, ob_ap, sq_ap))
    emit_stats(*pending.pop())
    return S["mT"].n


def emit_stats_finalize(kb):
    S = kb.sems; nc = kb.nc
    scf = kb.sb["scf"]
    kb.wait("vector", S["mT"], S["mT"].n)
    kb.wait("vector", S["dS0"], S["dS0"].n)
    kb.wait("vector", S["dS1"], S["dS1"].n)
    MR = scf[0:1, 0:1024]; VR = scf[0:1, 1024:2048]; T1 = scf[0:1, 2048:3072]
    kb.op("vector", lambda: nc.vector.tensor_scalar(MR, kb.ps["pC"][0:1, :],
          1.0 / HD, None, op0=ALU.mult))
    kb.op("vector", lambda: nc.vector.tensor_scalar(VR, kb.ps["pD"][0:1, :],
          1.0 / HD, None, op0=ALU.mult))
    kb.drain("vector")
    kb.op("vector", lambda: nc.vector.tensor_mul(T1, MR, MR))
    kb.drain("vector")
    kb.op("vector", lambda: nc.vector.tensor_sub(VR, VR, T1), inc=S["vV"])
    kb.drain("vector")
    kb.op("vector", lambda: nc.vector.tensor_scalar(VR, VR, EPS_LN, None,
          op0=ALU.add), inc=S["vV"])
    vv = S["vV"].n
    kb.wait("scalar", S["vV"], vv)
    kb.op("scalar", lambda: nc.scalar.activation(T1, VR, AF.Ln))
    kb.drain("scalar")
    kb.op("scalar", lambda: nc.scalar.activation(VR, T1, AF.Exp, scale=-0.5),
          inc=S["sS"])
    sv = S["sS"].n
    kb.wait("tensor", S["sS"], sv)
    for half in range(2):
        kb.op("tensor", lambda half=half: nc.tensor.matmul(
            kb.ps["pA"][:, half * 512:(half + 1) * 512], kb.sb["cons1f"][:],
            MR[:, half * 512:(half + 1) * 512], start=True, stop=True))
    for half in range(2):
        kb.op("tensor", lambda half=half: nc.tensor.matmul(
            kb.ps["pB"][:, half * 512:(half + 1) * 512], kb.sb["cons1f"][:],
            VR[:, half * 512:(half + 1) * 512], start=True, stop=True),
            inc=(S["mT"], 1) if half == 1 else None)
    mm = S["mT"].n
    kb.wait("scalar", S["mT"], mm)
    kb.op("scalar", lambda: nc.scalar.activation(kb.sb["mbc"][:], kb.ps["pA"][:], AF.Copy))
    kb.op("scalar", lambda: nc.scalar.activation(kb.sb["rbc2"][:], kb.ps["pB"][:], AF.Copy),
          inc=S["sS"])
    return S["sS"].n


def emit_norm_pass(kb, buf):
    S = kb.sems; nc = kb.nc
    ntmp = kb.sb["sqsb"]
    sv0 = S["sS"].n
    kb.wait("vector", S["sS"], sv0)
    kb.wait("gpsimd", S["sS"], sv0)
    gdone = [0, 0]
    for t in range(NT):
        sl = t % 2
        bt = buf[:, t * 1024:(t + 1) * 1024]
        nt = ntmp[:, sl * 1024:(sl + 1) * 1024]
        if t >= 2:
            kb.wait("vector", S["gG"], gdone[sl])
        kb.op("vector", lambda bt=bt, nt=nt:
              nc.vector.tensor_sub(nt, bt, kb.sb["mbc"][:]), inc=S["vV"])
        vv = S["vV"].n
        kb.wait("gpsimd", S["vV"], vv)
        kb.op("gpsimd", lambda bt=bt, nt=nt:
              nc.gpsimd.tensor_mul(bt, nt, kb.sb["rbc2"][:]), inc=S["gG"])
        gdone[sl] = S["gG"].n
    return S["gG"].n


def emit_final_output(kb, scr_in, out_dram):
    S = kb.sems; nc = kb.nc
    xpre, scf = kb.sb["xpre"], kb.sb["scf"]
    vdone = [0, 0]; outd = [0, 0]
    sv0 = S["sS"].n
    kb.wait("vector", S["sS"], sv0)
    for t in range(NT):
        sl = t % 2
        if t >= 2:
            kb.wait("sync", S["dO0" if sl == 0 else "dO1"], outd[sl])
        xp = kb.dma("sync", lambda t=t, sl=sl: xpre[:, sl * 1024:(sl + 1) * 1024],
                    lambda t=t: scr_in.ap()[t * 128:(t + 1) * 128, :],
                    S["dX0" if sl == 0 else "dX1"])
        kb.wait("vector", S["dX0" if sl == 0 else "dX1"], xp)
        xs = xpre[:, sl * 1024:(sl + 1) * 1024]
        tmp = scf[:, sl * 1024:(sl + 1) * 1024]
        if t >= 2:
            kb.wait("vector", S["dO0" if sl == 0 else "dO1"], outd[sl])
        kb.op("vector", lambda xs=xs, tmp=tmp:
              nc.vector.tensor_sub(tmp, xs, kb.sb["mbc"][:]))
        kb.drain("vector")
        kb.op("vector", lambda tmp=tmp:
              nc.vector.tensor_mul(tmp, tmp, kb.sb["rbc2"][:]), inc=S["vV"])
        vdone[sl] = S["vV"].n
        kb.wait("sync", S["vV"], vdone[sl])
        outd[sl] = kb.dma("sync", lambda t=t: out_dram.ap()[t * 128:(t + 1) * 128, :],
                          lambda tmp=tmp: tmp, S["dO0" if sl == 0 else "dO1"])
    kb.wait("sync", S["dO0"], S["dO0"].n)
    kb.wait("sync", S["dO1"], S["dO1"].n)


def emit_cast_load_buf(kb, src_dram, buf):
    S = kb.sems
    nc = kb.nc
    for t in range(NT):
        kb.dma("gpsimd", lambda t=t: buf[:, t * 1024:(t + 1) * 1024],
               lambda t=t: src_dram.ap()[t * 128:(t + 1) * 128, :], S["dW"])
    return S["dW"].n


def emit_ffn(kb, layer, inbuf, outbuf, resid_load, normed, scr_out):
    S = kb.sems; nc = kb.nc; D = kb.D
    wpool, wgs, wsb = kb.sb["wpool"], kb.sb["wgs"], kb.sb["wsb"]
    smalls = kb.sb["scf"]
    gw, gwb, grow = kb.sb["gw"], kb.sb["gwb"], kb.sb["grow"]
    zsb = kb.sb["sqsb"]
    hwsb = kb.sb["hwsb"]
    identf = kb.sb["identf"]
    pA, pB, pC, pD = kb.ps["pA"], kb.ps["pB"], kb.ps["pC"], kb.ps["pD"]
    E = 8 if layer == 0 else 16

    at_name, bf_name, wb_name = (("a0t", "b0f", "wb0t") if layer == 0
                                 else ("la1t", "lb1f", "wb1t"))
    kb.dma("gpsimd", wpool[:, 0:4096],
           lambda: D[at_name].ap().rearrange("(t p) j -> p t j", p=128), S["dP"])
    kb.dma("gpsimd", wpool[:, 4096:8192], lambda: D[bf_name].ap(), S["dP"])
    if layer == 0:
        kb.dma("gpsimd", wgs[:, 0:256],
               lambda: D["wgt"].ap().rearrange("(t p) e -> p t e", p=128), S["dP"])
    else:
        kb.dma("gpsimd", wgs[:, 0:512],
               lambda: D["rwt"].ap().rearrange("(t p) e -> p t e", p=128), S["dP"])
    pool_dma = S["dP"].n

    if layer == 1:
        oh = kb.sb["oh"]
        kb.wait("vector", S["dXQ"], S["dXQ"].n)
        kb.op("vector", lambda: nc.vector.tensor_copy(oh[0:1, 2:6], kb.sb["acti"][:]),
              inc=S["vV"])
        va = S["vV"].n
        kb.wait("tensor", S["vV"], va)
        kb.op("tensor", lambda: nc.tensor.matmul(pA[0:16, 0:4], kb.sb["cons1f"][0:1, 0:16],
              oh[0:1, 2:6], start=True, stop=True))
        kb.op("tensor", lambda: nc.tensor.matmul(pB[0:4, 0:16], oh[0:1, 2:6],
              kb.sb["cons1f"][0:1, 0:16], start=True, stop=True), inc=S["mT"])
        m1 = S["mT"].n
        kb.wait("vector", S["mT"], m1)
        kb.op("vector", lambda: nc.vector.tensor_tensor(oh[0:16, 8:12], kb.sb["ioc"][:],
              pA[0:16, 0:4], op=ALU.is_equal))
        kb.op("vector", lambda: nc.vector.tensor_tensor(oh[0:4, 16:32], kb.sb["ior"][:],
              pB[0:4, 0:16], op=ALU.is_equal), inc=S["vV"])
        v1 = S["vV"].n
        kb.wait("tensor", S["vV"], v1)
        kb.op("tensor", lambda: nc.tensor.matmul(pC[0:4, 0:1], oh[0:16, 8:12],
              oh[0:16, 1:2], start=True, stop=True), inc=S["mT"])
        m2 = S["mT"].n
        kb.wait("vector", S["mT"], m2)
        kb.op("vector", lambda: nc.vector.tensor_scalar(oh[0:4, 33:34],
              pC[0:4, 0:1], 1e-12, None, op0=ALU.add), inc=S["vV"])
        kb.wait("scalar", S["vV"], S["vV"].n)
        kb.op("scalar", lambda: nc.scalar.activation(oh[0:4, 32:33],
              oh[0:4, 33:34], AF.Ln))
        kb.drain("scalar")
        kb.op("scalar", lambda: nc.scalar.activation(grow[0:4, 0:16], oh[0:4, 16:32],
              AF.Copy), inc=S["sS"])
        onehot_done = S["sS"].n

    kb.wait("tensor", S["dP"], pool_dma)
    if layer == 1:
        kb.wait("tensor", S["sS"], onehot_done)
    for ht in range(NT):
        for half in range(2):
            kb.op("tensor", lambda ht=ht, half=half: nc.tensor.matmul(
                pC[0:E, half * 512:(half + 1) * 512],
                wgs[:, E * ht:E * (ht + 1)],
                inbuf[:, ht * 1024 + half * 512: ht * 1024 + (half + 1) * 512],
                start=(ht == 0), stop=(ht == NT - 1)),
                inc=S["mT"] if (ht == NT - 1 and half == 1) else None)
    lg_mm = S["mT"].n
    kb.wait("scalar", S["mT"], lg_mm)
    if layer == 0:
        kb.op("scalar", lambda: nc.scalar.activation(smalls[0:E, 0:1024], pC[0:E, :],
              AF.Copy), inc=S["sS"])
    else:
        kb.wait("vector", S["mT"], lg_mm)
        kb.op("vector", lambda: nc.vector.tensor_single_scalar(
            smalls[0:E, 0:1024], pC[0:E, :], kb.sb["oh"][0:16, 0:1],
            op=ALU.add), inc=S["vV"])
        kb.wait("scalar", S["vV"], S["vV"].n)
        kb.op("scalar", lambda: nc.scalar.drain(), inc=S["sS"])
    lg_ev = S["sS"].n

    if layer == 1:
        kb.wait("tensor", S["sS"], lg_ev)
        for half in range(2):
            kb.op("tensor", lambda half=half: nc.tensor.matmul(
                pD[0:4, half * 512:(half + 1) * 512], kb.sb["oh"][0:16, 8:12],
                smalls[0:16, half * 512:(half + 1) * 512], start=True, stop=True),
                inc=S["mT"] if half == 1 else None)
        sl_mm = S["mT"].n
        kb.wait("vector", S["mT"], sl_mm)
        kb.op("vector", lambda: nc.vector.tensor_single_scalar(
            smalls[0:4, 1024:2048], pD[0:4, :], kb.sb["oh"][0:4, 32:33],
            op=ALU.add), inc=S["vV"])
        kb.wait("scalar", S["vV"], S["vV"].n)
        kb.op("scalar", lambda: nc.scalar.drain(), inc=S["sS"])
        lg_ev = S["sS"].n

    nsl = E if layer == 0 else 4
    soff = 0 if layer == 0 else 1024
    kb.wait("tensor", S["sS"], lg_ev)
    for i in range(8):
        kb.op("tensor", lambda i=i, nsl=nsl, soff=soff: nc.tensor.transpose(
            pD[:, 64 + nsl * i: 64 + nsl * (i + 1)],
            smalls[0:nsl, soff + 128 * i: soff + 128 * (i + 1)],
            identf[0:nsl, 0:nsl]),
            inc=S["mT"] if i == 7 else None)
    tr_mm = S["mT"].n
    kb.wait("vector", S["mT"], tr_mm)
    M1, NM1, SS, RS = gw[:, 0:1], gw[:, 1:2], gw[:, 2:3], gw[:, 3:4]
    M2 = gw[:, 4:5]
    EQ, KEEP, EE, WW = gw[:, 8:16], gw[:, 16:24], gw[:, 24:32], gw[:, 32:40]
    tb_done = 0
    for i in range(8):
        lgt = pD[:, 64 + nsl * i: 64 + nsl * (i + 1)]
        if i > 0:
            kb.wait("vector", S["mT"], tb_done)
        kb.op("vector", lambda lgt=lgt: nc.vector.reduce_max(M1, lgt, axis=AX.X))
        kb.drain("vector")
        kb.op("vector", lambda: nc.vector.tensor_scalar(NM1, M1, -1.0, None, op0=ALU.mult))
        if layer == 0:
            kb.op("vector", lambda lgt=lgt: nc.vector.tensor_scalar(
                EQ[:, 0:nsl], lgt, M1[:, 0:1], None, op0=ALU.is_equal))
            kb.drain("vector")
            kb.op("vector", lambda lgt=lgt: nc.vector.scalar_tensor_tensor(
                EQ[:, 0:nsl], EQ[:, 0:nsl], -1e30, lgt, op0=ALU.mult, op1=ALU.add))
            kb.drain("vector")
            kb.op("vector", lambda: nc.vector.reduce_max(M2, EQ[:, 0:nsl], axis=AX.X))
        kb.op("vector", lambda: nc.vector.drain(), inc=S["vV"])
        vv = S["vV"].n
        kb.wait("scalar", S["vV"], vv)
        kb.op("scalar", lambda lgt=lgt: nc.scalar.activation(
            EE[:, 0:nsl], lgt, AF.Exp, bias=NM1[:, 0:1]), inc=S["sS"])
        se = S["sS"].n
        kb.wait("vector", S["sS"], se)
        if layer == 0:
            kb.op("vector", lambda lgt=lgt: nc.vector.tensor_scalar(
                KEEP[:, 0:nsl], lgt, M2[:, 0:1], None, op0=ALU.is_ge))
            kb.drain("vector")
            kb.op("vector", lambda: nc.vector.tensor_mul(WW[:, 0:nsl], EE[:, 0:nsl],
                  KEEP[:, 0:nsl]))
        else:
            kb.op("vector", lambda: nc.vector.tensor_copy(WW[:, 0:nsl], EE[:, 0:nsl]))
        kb.drain("vector")
        kb.op("vector", lambda: nc.vector.reduce_sum(SS, WW[:, 0:nsl], axis=AX.X))
        kb.drain("vector")
        kb.op("vector", lambda: nc.vector.reciprocal(RS, SS))
        kb.drain("vector")
        kb.op("vector", lambda: nc.vector.tensor_scalar(
            gwb[:, 0:nsl], WW[:, 0:nsl], RS[:, 0:1], None, op0=ALU.mult), inc=S["vV"])
        vr = S["vV"].n
        kb.wait("tensor", S["vV"], vr)
        kb.op("tensor", lambda i=i, nsl=nsl: nc.tensor.transpose(
            pB[0:nsl, 128 * i:128 * (i + 1)], gwb[:, 0:nsl], identf[:]),
            inc=S["mT"])
        tb_done = S["mT"].n
    kb.wait("scalar", S["mT"], tb_done)
    if layer == 0:
        kb.op("scalar", lambda: nc.scalar.activation(grow[0:8, 64:1088],
              pB[0:8, 0:1024], AF.Copy), inc=S["sS"])
    else:
        kb.op("scalar", lambda: nc.scalar.activation(kb.sb["attnT"][0:4, 0:1024],
              pB[0:4, 0:1024], AF.Copy), inc=S["sS"])
    rw_ready = S["sS"].n

    if layer == 1:
        kb.wait("tensor", S["sS"], rw_ready)
        for half in range(2):
            kb.op("tensor", lambda half=half: nc.tensor.matmul(
                pC[0:16, half * 512:(half + 1) * 512], grow[0:4, 0:16],
                kb.sb["attnT"][0:4, half * 512:(half + 1) * 512],
                start=True, stop=True), inc=S["mT"] if half == 1 else None)
        af_mm = S["mT"].n
        kb.wait("scalar", S["mT"], af_mm)
        kb.op("scalar", lambda: nc.scalar.activation(grow[0:16, 64:1088],
              pC[0:16, 0:1024], AF.Copy), inc=S["sS"])
        rw_ready = S["sS"].n

    kb.wait("tensor", S["vV"], S["vV"].n)
    for ht in range(NT):
        for half in range(2):
            kb.op("tensor", lambda ht=ht, half=half: nc.tensor.matmul(
                pD[:, half * 512:(half + 1) * 512], wpool[:, 128 * ht:128 * (ht + 1)],
                inbuf[:, ht * 1024 + half * 512: ht * 1024 + (half + 1) * 512],
                start=(ht == 0), stop=(ht == NT - 1)),
                inc=S["mT"] if (ht == NT - 1 and half == 1) else None)
    z_mm = S["mT"].n
    kb.wait("scalar", S["mT"], z_mm)
    kb.op("scalar", lambda: nc.scalar.activation(zsb[:, 0:1024], pD[:], AF.Copy), inc=S["sS"])
    z_ev = S["sS"].n
    kb.wait("tensor", S["sS"], rw_ready)
    krap = kb.sb["kreps"][0:8, 0:128] if layer == 0 else kb.sb["kreps"][0:16, 128:256]
    nK = 8 if layer == 0 else 16
    for half in range(2):
        kb.op("tensor", lambda half=half, krap=krap, nK=nK: nc.tensor.matmul(
            pC[:, half * 512:(half + 1) * 512], krap,
            grow[0:nK, 64 + half * 512: 64 + (half + 1) * 512],
            start=True, stop=True), inc=S["mT"] if half == 1 else None)
    bc_mm = S["mT"].n
    kb.wait("vector", S["mT"], bc_mm)
    kb.wait("vector", S["sS"], z_ev)
    kb.op("vector", lambda: nc.vector.tensor_mul(hwsb[:], zsb[:, 0:1024], pC[:]), inc=S["vV"])
    hw_done = S["vV"].n

    wb_dram = D[wb_name]
    mm_emit = make_wstream_mm_emit(kb, wb_dram, inbuf, extra=("delta", hw_done))
    bcol = 0 if layer == 0 else 32
    bias_cols = lambda t: kb.sb["bias_sb"][:, bcol + t: bcol + t + 1]
    emit_evac_loop(kb, NT, mm_emit, resid_load, normed, outbuf, scr_out,
                   bias_cols=bias_cols)


def make_wstream_mm_emit(kb, wb_dram, inbuf, extra=None):
    S = kb.sems; nc = kb.nc
    wsb, wpool = kb.sb["wsb"], kb.sb["wpool"]
    hwsb = kb.sb.get("hwsb")
    q_mm = [S["mT"].n] * 4
    q_dma = [0, 0, 0, 0]
    def mm_emit(kb2, t, pt):
        for q in range(4):
            kb.wait("gpsimd", S["mT"], q_mm[q])
            q_dma[q] = kb.dma(
                "gpsimd",
                lambda q=q: wsb[:, q * 1024:(q + 1) * 1024],
                lambda t=t, q=q: wb_dram.ap()[q * 1024:(q + 1) * 1024,
                                              128 * t:128 * (t + 1)]
                .rearrange("(u p) o -> p u o", p=128), S["dQ" + str(q)])
        if t == 0 and extra is not None:
            kb.wait("tensor", S["vV"], extra[1])
        for q in range(4):
            kb.wait("tensor", S["dQ" + str(q)], q_dma[q])
            for j in range(8):
                u = 8 * q + j
                for half in range(2):
                    kb.op("tensor", lambda u=u, half=half, pt=pt:
                          nc.tensor.matmul(
                              pt[:, half * 512:(half + 1) * 512],
                              wsb[:, u * 128:(u + 1) * 128],
                              inbuf[:, u * 1024 + half * 512:
                                    u * 1024 + (half + 1) * 512],
                              start=(u == 0),
                              stop=(extra is None and u == NT - 1)),
                          inc=S["mT"] if (j == 7 and half == 1) else None)
            q_mm[q] = S["mT"].n
        if extra is not None:
            for half in range(2):
                kb.op("tensor", lambda t=t, half=half, pt=pt: nc.tensor.matmul(
                    pt[:, half * 512:(half + 1) * 512],
                    wpool[:, 4096 + 128 * t: 4096 + 128 * (t + 1)],
                    hwsb[:, half * 512:(half + 1) * 512],
                    start=False, stop=True),
                    inc=S["mT"] if half == 1 else None)
        return S["mT"].n
    return mm_emit


def resid_from_dram(kb, dram):
    S = kb.sems
    def load(kb2, t, sl):
        return kb.dma("sync",
                      lambda sl=sl: kb.sb["xpre"][:, sl * 1024:(sl + 1) * 1024],
                      lambda t=t: dram.ap()[t * 128:(t + 1) * 128, :],
                      S["dX0" if sl == 0 else "dX1"])
    return load


def build_ffn_phase_program(layer):
    nc = bass.Bass("TRN2", target_bir_lowering=False)
    ctx = ExitStack()
    with ctx:
        kb = KB(nc, ctx)
        phase = "p1" if layer == 0 else "p3"
        setup_common(kb, {phase})
        kb.D["xnin"] = nc.dram_tensor("xnin", [HD, TPC], F32, kind="ExternalInput")
        load_consts(kb, {phase})
        emit_cast_load_buf(kb, kb.D["xnin"], kb.sb["bufA"])
        kb.wait("tensor", kb.sems["dW"], kb.sems["dW"].n)
        kb.wait("tensor", kb.sems["dXQ"], kb.sems["dXQ"].n)
        emit_ffn(kb, layer, kb.sb["bufA"], kb.sb["bufB"],
                 resid_from_dram(kb, kb.D["xnin"]), False, kb.D["scr0"])
        emit_stats_finalize(kb)
        emit_final_output(kb, kb.D["scr0"], kb.D["out"])
        kb.finalize_block()
    return nc


def ffn_phase_inputs(layer, inputs, core):
    m = dict(
        ones_bf=NPC["ones_bf"], ones1f=NPC["ones1x128_f"], onesm=NPC["onesm"],
        identb=NPC["ident_bf"], identf=NPC["ident_f"],
        tri=NPC["tri01_bf"], rmat=NPC["rmat_bf"],
    )
    if layer == 0:
        m["wb0t"] = np.ascontiguousarray(inputs["L0_ffn_W_base"].T)
        m["b0"] = np.ascontiguousarray(inputs["L0_ffn_b_base"].reshape(32, 128).T)
        m["wgt"] = np.ascontiguousarray(inputs["L0_ffn_W_gate"].T)
        m["a0t"] = np.ascontiguousarray(
            inputs["L0_ffn_A"].transpose(2, 0, 1).reshape(HD, 128))
        m["b0f"] = np.ascontiguousarray(
            inputs["L0_ffn_B"].transpose(0, 2, 1).reshape(128, HD))
        m["kr8"] = NPC["krepl8"]
    else:
        m["wb1t"] = np.ascontiguousarray(inputs["L1_ffn_W_base"].T)
        m["b1"] = np.ascontiguousarray(inputs["L1_ffn_b_base"].reshape(32, 128).T)
        m["rwt"] = np.ascontiguousarray(inputs["L1_ffn_router_W"].T)
        m["la1t"] = np.ascontiguousarray(
            inputs["L1_ffn_lora_A"].transpose(2, 0, 1).reshape(HD, 128))
        m["lb1f"] = np.ascontiguousarray(
            inputs["L1_ffn_lora_B"].transpose(0, 2, 1).reshape(128, HD))
        m["rb"] = np.ascontiguousarray(inputs["L1_ffn_router_b"].reshape(16, 1))
        m["prior"] = np.ascontiguousarray(inputs["L1_ffn_domain_prior"].reshape(16, 1))
        m["active"] = np.ascontiguousarray(inputs["L1_ffn_active"].reshape(1, 4))
        m["kr16"] = NPC["krepl16x2"]
        m["io16c"] = NPC["iota16c"]
        m["io4r"] = NPC["iota4r"]
    return m


def emit_kbuild(kb):
    S = kb.sems; nc = kb.nc; D = kb.D
    gw, gwb, xpre, scf = kb.sb["gw"], kb.sb["gwb"], kb.sb["xpre"], kb.sb["scf"]
    pA, pB, pD = kb.ps["pA"], kb.ps["pB"], kb.ps["pD"]
    ktile = kb.sb["ktile"]
    C1f = kb.sb["cons1f"]

    LR, LI = gw[0:64, 0:1], gw[0:64, 1:2]
    PR, PI = gw[0:64, 2:3], gw[0:64, 3:4]
    QR, QI = gw[0:64, 4:5], gw[0:64, 5:6]
    CTR, CTI = gw[0:64, 6:7], gw[0:64, 7:8]
    BP = gw[0:64, 8:9]
    NLR, NLI, LR2 = gw[0:64, 9:10], gw[0:64, 10:11], gw[0:64, 11:12]
    BR, BI = gw[0:64, 12:13], gw[0:64, 13:14]
    VV = gw[0:64, 16:24]
    STP = gw[0:1, 32:33]
    RS2 = gw[0:1, 33:34]
    DD = gw[0:1, 34:35]

    kb.op("vector", lambda: nc.vector.memset(xpre[:, 0:32], 0.0), inc=S["vV"])
    zv = S["vV"].n
    kb.wait("gpsimd", S["vV"], zv)
    zkp = kb.dma("gpsimd",
                 lambda: bass.AP(D["kpad"], 0, [[32, 128], [1, 32]]),
                 lambda: xpre[:, 0:32], S["dW"])

    kb.dma("sync", gw[0:64, 0:2], D["lam"].ap(), S["dXQ"])
    kb.dma("sync", gw[0:64, 2:4], D["pri"].ap(), S["dXQ"])
    kb.dma("sync", gw[0:64, 4:6], D["qri"].ap(), S["dXQ"])
    kb.dma("sync", gw[0:64, 6:8], D["ctri"].ap(), S["dXQ"])
    kb.dma("sync", gw[0:64, 8:9], D["bp"].ap(), S["dXQ"])
    kb.dma("sync", gw[0:1, 34:35], D["dd"].ap(), S["dXQ"])
    kb.dma("sync", gw[0:1, 35:36], D["lstep"].ap(), S["dXQ"])
    kb.dma("sync", scf[0:64, 2400:2464], D["vcrt"].ap(), S["dXQ"])
    kb.dma("sync", scf[0:64, 2464:2528], D["vcit"].ap(), S["dXQ"])
    kb.dma("sync", scf[:, 2528:2560], D["ccr"].ap(), S["dXQ"])
    kb.dma("sync", scf[:, 2560:2592], D["cci"].ap(), S["dXQ"])
    kb.dma("sync", scf[0:32, 2592:2624], D["w32r"].ap(), S["dXQ"])
    kb.dma("sync", scf[0:32, 2624:2656], D["w32i"].ap(), S["dXQ"])
    kb.dma("sync", scf[0:32, 2656:2688], D["w32in"].ap(), S["dXQ"])
    kb.dma("sync", scf[:, 2688:2720], D["twr"].ap(), S["dXQ"])
    kb.dma("sync", scf[:, 2720:2752], D["twi"].ap(), S["dXQ"])
    kb.dma("sync", scf[:, 2752:2880], D["c128"].ap(), S["dXQ"])
    kb.dma("sync", scf[:, 2880:3008], D["s128n"].ap(), S["dXQ"])
    pdma = S["dXQ"].n

    kb.wait("scalar", S["dXQ"], pdma)
    kb.op("scalar", lambda: nc.scalar.activation(STP, gw[0:1, 35:36], AF.Exp),
          inc=S["sS"])
    kb.wait("vector", S["sS"], S["sS"].n)
    kb.op("vector", lambda: nc.vector.reciprocal(RS2, STP))
    kb.drain("vector")
    kb.op("vector", lambda: nc.vector.tensor_scalar(RS2, RS2, 2.0, None, op0=ALU.mult))
    kb.drain("vector")
    kb.op("vector", lambda: nc.vector.tensor_scalar(NLR, LR, -1.0, None, op0=ALU.mult))
    kb.op("vector", lambda: nc.vector.tensor_scalar(NLI, LI, -1.0, None, op0=ALU.mult))
    kb.op("vector", lambda: nc.vector.tensor_mul(LR2, LR, LR), inc=S["vV"])
    vpar = S["vV"].n

    kb.wait("tensor", S["dXQ"], pdma)
    kb.op("tensor", lambda: nc.tensor.matmul(pA[0:64, 0:1], scf[0:64, 2400:2464],
          BP, start=True, stop=True))
    kb.op("tensor", lambda: nc.tensor.matmul(pA[0:64, 1:2], scf[0:64, 2464:2528],
          BP, start=True, stop=True), inc=S["mT"])
    kb.wait("vector", S["mT"], S["mT"].n)
    kb.op("vector", lambda: nc.vector.tensor_copy(BR, pA[0:64, 0:1]))
    kb.op("vector", lambda: nc.vector.tensor_copy(BI, pA[0:64, 1:2]))
    kb.drain("vector")
    def cmul_conj(ar, ai, br_, bi_, outr, outi):
        kb.op("vector", lambda: nc.vector.tensor_mul(outr, ar, br_))
        kb.op("vector", lambda: nc.vector.tensor_mul(outi, ar, bi_))
        kb.drain("vector")
        kb.op("vector", lambda: nc.vector.scalar_tensor_tensor(
            gw[0:64, 14:15], ai, 1.0, bi_, op0=ALU.mult, op1=ALU.mult))
        kb.op("vector", lambda: nc.vector.scalar_tensor_tensor(
            gw[0:64, 15:16], ai, 1.0, br_, op0=ALU.mult, op1=ALU.mult))
        kb.drain("vector")
        kb.op("vector", lambda: nc.vector.tensor_add(outr, outr, gw[0:64, 14:15]))
        kb.op("vector", lambda: nc.vector.tensor_sub(outi, outi, gw[0:64, 15:16]))
        kb.drain("vector")
    cmul_conj(CTR, CTI, BR, BI, VV[:, 0:1], VV[:, 1:2])
    cmul_conj(CTR, CTI, PR, PI, VV[:, 2:3], VV[:, 3:4])
    cmul_conj(QR, QI, BR, BI, VV[:, 4:5], VV[:, 5:6])
    cmul_conj(QR, QI, PR, PI, VV[:, 6:7], VV[:, 7:8])
    for j in range(4):
        kb.op("vector", lambda j=j: nc.vector.tensor_copy(
            gwb[0:64, 2 * j:2 * j + 1], VV[:, 2 * j:2 * j + 1]))
        kb.op("vector", lambda j=j: nc.vector.tensor_copy(
            gwb[0:64, 2 * j + 1:2 * j + 2], VV[:, 2 * j + 1:2 * j + 2]))
        kb.op("vector", lambda j=j: nc.vector.tensor_scalar(
            gwb[0:64, 8 + 2 * j:9 + 2 * j], VV[:, 2 * j + 1:2 * j + 2], -1.0,
            None, op0=ALU.mult))
        kb.op("vector", lambda j=j: nc.vector.tensor_copy(
            gwb[0:64, 9 + 2 * j:10 + 2 * j], VV[:, 2 * j:2 * j + 1]))
    kb.wait("vector", S["dW"], zkp)
    kb.op("vector", lambda: nc.vector.drain(), inc=S["vV"])
    vl_done = S["vV"].n

    tr_n = 0
    for kh in range(2):
        if kh == 1:
            kb.wait("sync", S["mT"], tr_n)
        th = kb.dma("sync", scf[0:1, 0:2048],
                    D["tim"].ap()[0:1, 2048 * kh:2048 * (kh + 1)], S["dX0"])
        kb.wait("vector", S["dX0"], th)
        kb.wait("vector", S["vV"], vl_done)
        kb.op("vector", lambda: nc.vector.tensor_scalar(
            scf[0:1, 0:2048], scf[0:1, 0:2048], RS2, None, op0=ALU.mult),
            inc=S["vV"])
        gv = S["vV"].n
        for c in range(4):
            kb.wait("tensor", S["vV"], gv)
            if c > 0:
                kb.wait("tensor", S["vV"], S["vV"].n)
            kb.op("tensor", lambda c=c: nc.tensor.matmul(
                pA[0:64, 0:512], C1f[0:1, 0:64],
                scf[0:1, 512 * c:512 * (c + 1)], start=True, stop=True),
                inc=S["mT"])
            kb.wait("vector", S["mT"], S["mT"].n)
            kb.op("vector", lambda c=c: nc.vector.tensor_single_scalar(
                xpre[0:64, 512 * c:512 * (c + 1)], pA[0:64, 0:512], NLI,
                op=ALU.add), inc=S["vV"])
        dIn = S["vV"].n
        kb.wait("scalar", S["vV"], dIn)
        kb.op("scalar", lambda: nc.scalar.activation(
            scf[0:64, 0:2048], xpre[0:64, 0:2048], AF.Square), inc=S["sS"])
        sqn = S["sS"].n
        kb.wait("vector", S["sS"], sqn)
        kb.op("vector", lambda: nc.vector.tensor_scalar(
            scf[0:64, 0:2048], scf[0:64, 0:2048], LR2, None, op0=ALU.add))
        kb.drain("vector")
        kb.op("vector", lambda: nc.vector.reciprocal(
            scf[0:64, 0:2048], scf[0:64, 0:2048]))
        kb.drain("vector")
        kb.op("vector", lambda: nc.vector.scalar_tensor_tensor(
            xpre[0:64, 0:2048], xpre[0:64, 0:2048], -1.0, scf[0:64, 0:2048],
            op0=ALU.mult, op1=ALU.mult))
        kb.drain("vector")
        kb.op("vector", lambda: nc.vector.tensor_scalar(
            scf[0:64, 0:2048], scf[0:64, 0:2048], NLR, None, op0=ALU.mult),
            inc=S["vV"])
        rdone = S["vV"].n
        for c in range(4):
            kb.wait("tensor", S["vV"], rdone)
            kb.wait("tensor", S["sS"], S["sS"].n)
            kb.op("tensor", lambda c=c: nc.tensor.matmul(
                pB[0:8, 0:512], gwb[0:64, 0:8],
                scf[0:64, 512 * c:512 * (c + 1)], start=True, stop=False))
            kb.op("tensor", lambda c=c: nc.tensor.matmul(
                pB[0:8, 0:512], gwb[0:64, 8:16],
                xpre[0:64, 512 * c:512 * (c + 1)], start=False, stop=True),
                inc=S["mT"])
            kb.wait("scalar", S["mT"], S["mT"].n)
            kb.op("scalar", lambda c=c: nc.scalar.activation(
                xpre[0:8, 512 * c:512 * (c + 1)], pB[0:8, 0:512], AF.Copy),
                inc=S["sS"])
        kmn = S["sS"].n
        kb.wait("tensor", S["sS"], kmn)
        for j in range(16):
            F = 16 * kh + j
            kb.op("tensor", lambda j=j, F=F: nc.tensor.transpose(
                pD[:, 8 * F:8 * F + 8], xpre[0:8, 128 * j:128 * (j + 1)],
                kb.sb["identf"][0:8, 0:8]),
                inc=S["mT"] if j == 15 else None)
        tr_n = S["mT"].n

    kb.wait("vector", S["mT"], tr_n)
    AT = lambda j: pD[:, 8 * 0 + j: 256 + j: 8]
    atR, atI = scf[:, 0:32], scf[:, 32:64]
    T1, T2, T3, T4 = (scf[:, 64:96], scf[:, 96:128], scf[:, 128:160],
                      scf[:, 160:192])
    kb.op("vector", lambda: nc.vector.tensor_scalar(T1, AT(6), 1.0, None, op0=ALU.add))
    kb.op("vector", lambda: nc.vector.tensor_copy(T2, AT(7)))
    kb.drain("vector")
    kb.op("vector", lambda: nc.vector.tensor_mul(T3, T1, T1))
    kb.op("vector", lambda: nc.vector.tensor_mul(T4, T2, T2))
    kb.drain("vector")
    kb.op("vector", lambda: nc.vector.tensor_add(T3, T3, T4))
    kb.drain("vector")
    kb.op("vector", lambda: nc.vector.reciprocal(T3, T3))
    kb.drain("vector")
    QRv, QIv = scf[:, 192:224], scf[:, 224:256]
    kb.op("vector", lambda: nc.vector.tensor_mul(QRv, AT(4), T1))
    kb.op("vector", lambda: nc.vector.tensor_mul(QIv, AT(5), T1))
    kb.drain("vector")
    kb.op("vector", lambda: nc.vector.scalar_tensor_tensor(
        T4, AT(5), 1.0, T2, op0=ALU.mult, op1=ALU.mult))
    kb.drain("vector")
    kb.op("vector", lambda: nc.vector.tensor_add(QRv, QRv, T4))
    kb.drain("vector")
    kb.op("vector", lambda: nc.vector.scalar_tensor_tensor(
        T4, AT(4), 1.0, T2, op0=ALU.mult, op1=ALU.mult))
    kb.drain("vector")
    kb.op("vector", lambda: nc.vector.tensor_sub(QIv, QIv, T4))
    kb.drain("vector")
    kb.op("vector", lambda: nc.vector.tensor_mul(QRv, QRv, T3))
    kb.op("vector", lambda: nc.vector.tensor_mul(QIv, QIv, T3))
    kb.drain("vector")
    kb.op("vector", lambda: nc.vector.tensor_mul(T1, AT(2), QRv))
    kb.op("vector", lambda: nc.vector.tensor_mul(T2, AT(3), QIv))
    kb.op("vector", lambda: nc.vector.tensor_mul(T3, AT(2), QIv))
    kb.op("vector", lambda: nc.vector.tensor_mul(T4, AT(3), QRv))
    kb.drain("vector")
    kb.op("vector", lambda: nc.vector.tensor_sub(T1, T1, T2))
    kb.op("vector", lambda: nc.vector.tensor_add(T3, T3, T4))
    kb.drain("vector")
    kb.op("vector", lambda: nc.vector.tensor_sub(T1, AT(0), T1))
    kb.op("vector", lambda: nc.vector.tensor_sub(T3, AT(1), T3))
    kb.drain("vector")
    kb.op("vector", lambda: nc.vector.tensor_mul(atR, scf[:, 2528:2560], T1))
    kb.op("vector", lambda: nc.vector.tensor_mul(atI, scf[:, 2528:2560], T3))
    kb.drain("vector")
    kb.op("vector", lambda: nc.vector.scalar_tensor_tensor(
        T2, scf[:, 2560:2592], 1.0, T3, op0=ALU.mult, op1=ALU.mult))
    kb.op("vector", lambda: nc.vector.scalar_tensor_tensor(
        T4, scf[:, 2560:2592], 1.0, T1, op0=ALU.mult, op1=ALU.mult))
    kb.drain("vector")
    kb.op("vector", lambda: nc.vector.tensor_sub(atR, atR, T2))
    kb.op("vector", lambda: nc.vector.tensor_add(atI, atI, T4), inc=S["vV"])
    at_done = S["vV"].n

    kb.wait("tensor", S["vV"], at_done)
    kb.op("tensor", lambda: nc.tensor.transpose(pA[0:32, 0:128], atR,
          kb.sb["identf"][:]))
    kb.op("tensor", lambda: nc.tensor.transpose(pA[0:32, 128:256], atI,
          kb.sb["identf"][:]), inc=S["mT"])
    kb.wait("scalar", S["mT"], S["mT"].n)
    kb.op("scalar", lambda: nc.scalar.activation(scf[0:32, 256:384],
          pA[0:32, 0:128], AF.Copy))
    kb.op("scalar", lambda: nc.scalar.activation(scf[0:32, 384:512],
          pA[0:32, 128:256], AF.Copy), inc=S["sS"])
    art_done = S["sS"].n
    kb.wait("tensor", S["sS"], art_done)
    kb.op("tensor", lambda: nc.tensor.matmul(pB[:, 0:32], scf[0:32, 256:384],
          scf[0:32, 2592:2624], start=True, stop=False))
    kb.op("tensor", lambda: nc.tensor.matmul(pB[:, 0:32], scf[0:32, 384:512],
          scf[0:32, 2656:2688], start=False, stop=True))
    kb.op("tensor", lambda: nc.tensor.matmul(pB[:, 32:64], scf[0:32, 256:384],
          scf[0:32, 2624:2656], start=True, stop=False))
    kb.op("tensor", lambda: nc.tensor.matmul(pB[:, 32:64], scf[0:32, 384:512],
          scf[0:32, 2592:2624], start=False, stop=True), inc=S["mT"])
    kb.wait("scalar", S["mT"], S["mT"].n)
    kb.op("scalar", lambda: nc.scalar.activation(scf[:, 512:544], pB[:, 0:32],
          AF.Copy))
    kb.op("scalar", lambda: nc.scalar.activation(scf[:, 544:576], pB[:, 32:64],
          AF.Copy), inc=S["sS"])
    h_done = S["sS"].n
    kb.wait("vector", S["sS"], h_done)
    HR, HI = scf[:, 512:544], scf[:, 544:576]
    HpR, HpI = scf[:, 576:608], scf[:, 608:640]
    kb.op("vector", lambda: nc.vector.tensor_mul(HpR, HR, scf[:, 2688:2720]))
    kb.op("vector", lambda: nc.vector.tensor_mul(HpI, HR, scf[:, 2720:2752]))
    kb.drain("vector")
    kb.op("vector", lambda: nc.vector.scalar_tensor_tensor(
        T1, HI, 1.0, scf[:, 2720:2752], op0=ALU.mult, op1=ALU.mult))
    kb.op("vector", lambda: nc.vector.scalar_tensor_tensor(
        T2, HI, 1.0, scf[:, 2688:2720], op0=ALU.mult, op1=ALU.mult))
    kb.drain("vector")
    kb.op("vector", lambda: nc.vector.tensor_sub(HpR, HpR, T1))
    kb.op("vector", lambda: nc.vector.tensor_add(HpI, HpI, T2), inc=S["vV"])
    hp_done = S["vV"].n
    kb.wait("tensor", S["vV"], hp_done)
    kb.op("tensor", lambda: nc.tensor.matmul(pA[:, 0:32], scf[:, 2752:2880],
          HpR, start=True, stop=False))
    kb.op("tensor", lambda: nc.tensor.matmul(pA[:, 0:32], scf[:, 2880:3008],
          HpI, start=False, stop=True), inc=S["mT"])
    k2_done = S["mT"].n
    kb.wait("scalar", S["mT"], k2_done)
    kb.op("scalar", lambda: nc.scalar.activation(ktile[:], pA[:, 0:32],
          AF.Copy, scale=1.0 / 4096.0), inc=S["sS"])
    kt_done = S["sS"].n
    kb.wait("vector", S["sS"], kt_done)
    kb.op("vector", lambda: nc.vector.tensor_scalar(
        ktile[0:1, 0:1], ktile[0:1, 0:1], DD, None, op0=ALU.add), inc=S["vV"])
    kd = S["vV"].n
    kb.wait("gpsimd", S["vV"], kd)
    kb.wait("gpsimd", S["dW"], zkp)
    kpd = kb.dma("gpsimd",
                 lambda: bass.AP(D["kpad"], 4096, [[32, 128], [1, 32]]),
                 lambda: ktile[:], S["dW"])
    return kpd


def emit_conv(kb, inbuf, outbuf, resid_load, normed, scr_out):
    S = kb.sems; nc = kb.nc; D = kb.D
    wsb, rmat = kb.sb["wsb"], kb.sb["rmat"]
    pA, pB = kb.ps["pA"], kb.ps["pB"]

    kpd = emit_kbuild(kb)

    kb.wait("tensor", S["sS"], S["sS"].n)
    fl_mm = [0, 0]; fl_ev = [0, 0]
    for j in range(NT):
        sl = j % 2
        pt = kb.ps["pA" if sl == 0 else "pB"]
        if j >= 2:
            kb.wait("tensor", S["sS"], fl_ev[sl])
        for half in range(2):
            kb.op("tensor", lambda j=j, half=half, pt=pt: nc.tensor.matmul(
                pt[:, half * 512:(half + 1) * 512], rmat[:],
                inbuf[:, j * 1024 + half * 512: j * 1024 + (half + 1) * 512],
                start=True, stop=True),
                inc=S["mT"] if half == 1 else None)
        fl_mm[sl] = S["mT"].n
        kb.wait("scalar", S["mT"], fl_mm[sl])
        kb.op("scalar", lambda j=j, pt=pt: nc.scalar.activation(
            inbuf[:, j * 1024:(j + 1) * 1024], pt[:], AF.Copy), inc=S["sS"])
        fl_ev[sl] = S["sS"].n
    flip_done = S["sS"].n

    kb.wait("sync", S["dW"], kpd)
    bnd = kb.dma("sync", wsb[:, 0:4096],
                 lambda: bass.AP(D["kpad"], 3969, [[1, 128], [128, 32], [1, 128]]),
                 S["dX1"])

    def mm_emit(kb2, t, pt):
        if t == 0:
            kb.wait("tensor", S["dX1"], bnd)
            kb.wait("tensor", S["sS"], flip_done)
        for j in range(t + 1):
            for half in range(2):
                kb.op("tensor", lambda t=t, j=j, half=half, pt=pt:
                      nc.tensor.matmul(
                          pt[:, half * 512:(half + 1) * 512],
                          wsb[:, (t - j) * 128:(t - j + 1) * 128],
                          inbuf[:, j * 1024 + half * 512:
                                j * 1024 + (half + 1) * 512],
                          start=(j == 0), stop=(j == t)),
                      inc=S["mT"] if (j == t and half == 1) else None)
        return S["mT"].n

    emit_evac_loop(kb, NT, mm_emit, resid_load, normed, outbuf, scr_out)


def build_conv_phase_program():
    nc = bass.Bass("TRN2", target_bir_lowering=False)
    ctx = ExitStack()
    with ctx:
        kb = KB(nc, ctx)
        setup_common(kb, {"p2"})
        kb.D["xnin"] = nc.dram_tensor("xnin", [HD, TPC], F32, kind="ExternalInput")
        load_consts(kb, {"p2"})
        emit_cast_load_buf(kb, kb.D["xnin"], kb.sb["bufA"])
        kb.wait("tensor", kb.sems["dW"], kb.sems["dW"].n)
        kb.wait("tensor", kb.sems["dXQ"], kb.sems["dXQ"].n)
        emit_conv(kb, kb.sb["bufA"], kb.sb["bufB"],
                  resid_from_dram(kb, kb.D["xnin"]), False, kb.D["scr0"])
        emit_stats_finalize(kb)
        emit_final_output(kb, kb.D["scr0"], kb.D["out"])
        kb.finalize_block()
    return nc


def conv_phase_inputs(inputs):
    m = dict(
        ones_bf=NPC["ones_bf"], ones1f=NPC["ones1x128_f"], onesm=NPC["onesm"],
        identb=NPC["ident_bf"], identf=NPC["ident_f"],
        tri=NPC["tri01_bf"], rmat=NPC["rmat_bf"],
        lam=np.ascontiguousarray(inputs["L1_attn_Lambda_ri"]),
        pri=np.ascontiguousarray(inputs["L1_attn_p_ri"]),
        qri=np.ascontiguousarray(inputs["L1_attn_q_ri"]),
        vcrt=np.ascontiguousarray(inputs["L1_attn_Vc_ri"][:, :, 0].T),
        vcit=np.ascontiguousarray(inputs["L1_attn_Vc_ri"][:, :, 1].T),
        ctri=np.ascontiguousarray(inputs["L1_attn_Ct_ri"].reshape(64, 2)),
        bp=np.ascontiguousarray(inputs["L1_attn_Bp"]),
        dd=np.ascontiguousarray(inputs["L1_attn_D"].reshape(1, 1)),
        lstep=np.ascontiguousarray(inputs["L1_attn_log_step"].reshape(1, 1)),
        tim=NPC["tim"], ccr=NPC["ccr"], cci=NPC["cci"],
        w32r=NPC["w32r"], w32i=NPC["w32i"], w32in=NPC["w32in"],
        twr=NPC["twr"], twi=NPC["twi"], c128=NPC["c128"], s128n=NPC["s128n"],
    )
    return m


ATTN_HEADS = NHEADS
ATTN_SKIP_SWITCH = False
ATTN_PARTS = 8


def emit_attn(kb, outbuf, AO, resid_load, scr_out):
    S = kb.sems; nc = kb.nc; D = kb.D
    wsb, wpool = kb.sb["wsb"], kb.sb["wpool"]
    qsb, attnT, rbcs = kb.sb["qsb"], kb.sb["attnT"], kb.sb["rbcs"]
    biastab, tri = kb.sb["biastab"], kb.sb["tri"]
    scf = kb.sb["scf"]
    pA, pB, pC, pD = kb.ps["pA"], kb.ps["pB"], kb.ps["pC"], kb.ps["pD"]
    x0buf = outbuf

    xsb = [wsb[:, 0:1024], wsb[:, 1024:2048]]
    wkvs = [wsb[:, 2048:2304], wsb[:, 2304:2560]]
    kv_dma = [0, 0]; kv_mm = [0, 0]
    ev_k = 0; ev_v = 0; tr_ev = [0, 0]
    for kc in range(4):
        for ht in range(NT):
            sl = ht % 2
            if kc + ht > 0:
                kb.wait("gpsimd", S["mT"], kv_mm[sl])
            ksem = S["dK0" if sl == 0 else "dK1"]
            kb.dma("gpsimd", lambda sl=sl: xsb[sl],
                   lambda kc=kc, ht=ht: D["xT"].ap()[128 * ht:128 * (ht + 1),
                                                     1024 * kc:1024 * (kc + 1)],
                   ksem)
            kv_dma[sl] = kb.dma("gpsimd", lambda sl=sl: wkvs[sl],
                                lambda ht=ht: D["wkv"].ap()[128 * ht:128 * (ht + 1), :],
                                ksem)
            kb.wait("tensor", ksem, kv_dma[sl])
            if ht == 0:
                kb.wait("tensor", S["sS"], ev_v)
            for half in range(2):
                kb.op("tensor", lambda sl=sl, half=half, ht=ht: nc.tensor.matmul(
                    pA[:, half * 512:(half + 1) * 512], wkvs[sl][:, 0:128],
                    xsb[sl][:, half * 512:(half + 1) * 512],
                    start=(ht == 0), stop=(ht == NT - 1)))
            for half in range(2):
                kb.op("tensor", lambda sl=sl, half=half, ht=ht: nc.tensor.matmul(
                    pB[:, half * 512:(half + 1) * 512], wkvs[sl][:, 128:256],
                    xsb[sl][:, half * 512:(half + 1) * 512],
                    start=(ht == 0), stop=(ht == NT - 1)),
                    inc=S["mT"] if half == 1 else None)
            kv_mm[sl] = S["mT"].n
        kb.wait("scalar", S["mT"], kv_mm[(NT - 1) % 2])
        kb.op("scalar", lambda kc=kc: nc.scalar.activation(
            wpool[:, 1024 * kc:1024 * (kc + 1)], pA[:], AF.Copy))
        kb.op("scalar", lambda: nc.scalar.activation(
            scf[:, 0:1024], pB[:], AF.Copy), inc=S["sS"])
        ev_v = S["sS"].n
        for i in range(8):
            tsl = i % 2
            kb.wait("tensor", S["sS"], max(ev_v, tr_ev[tsl]))
            kb.op("tensor", lambda i=i, tsl=tsl: nc.tensor.transpose(
                pD[:, 512 * tsl:512 * tsl + 128], scf[:, 128 * i:128 * (i + 1)],
                kb.sb["identf"][:]), inc=S["mT"])
            kb.wait("scalar", S["mT"], S["mT"].n)
            kb.op("scalar", lambda kc=kc, i=i, tsl=tsl: nc.scalar.activation(
                wpool[:, 4096 + (8 * kc + i) * 128: 4096 + (8 * kc + i + 1) * 128],
                pD[:, 512 * tsl:512 * tsl + 128], AF.Copy), inc=S["sS"])
            tr_ev[tsl] = S["sS"].n
    kv_done_s = S["sS"].n

    def emit_variant(v):
        units = attn_units(v)
        by_head = {}
        for uid, d in enumerate(units):
            by_head.setdefault(d["h"], []).append((uid, d))
        Q = [512 * v, 3584 - 512 * v]
        q_ev = [kv_done_s, kv_done_s]
        q_ev_last = kv_done_s
        att_done = [0, 0]
        exp_n = [0, 0]
        trv_n = [0, 0]
        avden = [0, 0]
        avev = [S["vV"].n, S["vV"].n]
        rbcev = [kv_done_s, kv_done_s]
        qq_mm = [S["mT"].n] * 4
        useq = 0
        for h in range(ATTN_HEADS):
            hsl = h % 2
            for q in range(4):
                kb.wait("gpsimd", S["mT"], qq_mm[q])
                qd = kb.dma("gpsimd", lambda q=q: wsb[:, q * 1024:(q + 1) * 1024],
                            lambda h=h, q=q: D["wq"].ap()[1024 * q:1024 * (q + 1),
                                                          128 * h:128 * (h + 1)]
                            .rearrange("(u p) o -> p u o", p=128), S["dQ" + str(q)])
                kb.wait("tensor", S["dQ" + str(q)], qd)
                if q == 0:
                    kb.wait("tensor", S["sS"], q_ev_last)
                for j in range(8):
                    u = 8 * q + j
                    for half in range(2):
                        kb.op("tensor", lambda u=u, half=half: nc.tensor.matmul(
                            pC[:, half * 512:(half + 1) * 512],
                            wsb[:, u * 128:(u + 1) * 128],
                            x0buf[:, u * 1024 + half * 512: u * 1024 + (half + 1) * 512],
                            start=(u == 0), stop=(u == NT - 1)),
                            inc=S["mT"] if (j == 7 and half == 1) else None)
                qq_mm[q] = S["mT"].n
            qmm = S["mT"].n
            kb.wait("scalar", S["mT"], max(qmm, att_done[hsl]))
            kb.op("scalar", lambda hsl=hsl: nc.scalar.activation(
                qsb[:, 1024 * hsl:1024 * (hsl + 1)], pC[:], AF.Copy), inc=S["sS"])
            q_ev[hsl] = S["sS"].n
            q_ev_last = S["sS"].n
            blocks = {}
            for uid, d in by_head[h]:
                blocks.setdefault(d["b"], []).append((uid, d))
            for b, ulist in sorted(blocks.items()):
                bcol = 512 * b
                first = True
                nu = len(ulist)
                for i, (uid, d) in enumerate(ulist):
                    last = (i == nu - 1)
                    usl = useq % 2
                    useq += 1
                    w = d["w"]
                    if i == 0:
                        kb.wait("tensor", S["vV"], max(avev[0], avev[1]))
                        kb.wait("tensor", S["sS"], max(rbcev[0], rbcev[1]))
                    kb.wait("tensor", S["sS"], q_ev[hsl])
                    kb.wait("tensor", S["aRs"], exp_n[usl])
                    kb.op("tensor", lambda d=d, usl=usl, w=w, hsl=hsl, b=b, Q=Q:
                          nc.tensor.matmul(
                              pA[:, 512 * usl:512 * usl + w],
                              wpool[:, d["t"] * 128:(d["t"] + 1) * 128],
                              qsb[:, 1024 * hsl + 512 * b + (d["A"] + d["off"] - Q[b]):
                                  1024 * hsl + 512 * b + (d["A"] + d["off"] - Q[b]) + w],
                              start=True, stop=True), inc=S["mT"])
                    sc_n = S["mT"].n
                    if ATTN_PARTS < 2:
                        exp_n[usl] = S["aRs"].n
                        first = False
                        continue
                    kb.wait("scalar", S["mT"], max(sc_n, avden[usl]))
                    kb.op("scalar", lambda uid=uid, usl=usl, w=w:
                          nc.scalar.activation(
                              attnT[:, 512 * usl:512 * usl + w],
                              pA[:, 512 * usl:512 * usl + w], AF.Exp,
                              bias=biastab[:, uid:uid + 1], scale=ASCALE),
                          inc=S["aRs"])
                    exp_n[usl] = S["aRs"].n
                    if ATTN_PARTS < 3:
                        first = False
                        continue
                    if ATTN_PARTS >= 3 and d["diag"]:
                        kb.wait("vector", S["aRs"], exp_n[usl])
                        kb.op("vector", lambda d=d, usl=usl:
                              nc.vector.tensor_mul(
                                  attnT[:, 512 * usl:512 * usl + d["tri_w"]],
                                  attnT[:, 512 * usl:512 * usl + d["tri_w"]],
                                  tri[:, d["tri_shift"]:d["tri_shift"] + d["tri_w"]]),
                              inc=S["aRv"])
                        trv_n[usl] = S["aRv"].n
                    if ATTN_PARTS < 4:
                        first = False
                        continue
                    kb.wait("tensor", S["aRs"], exp_n[usl])
                    if d["diag"]:
                        kb.wait("tensor", S["aRv"], trv_n[usl])
                    if first:
                        kb.wait("tensor", S["vV"], max(avev[0], avev[1]))
                        kb.wait("tensor", S["sS"], max(rbcev[0], rbcev[1]))
                    col0 = bcol + (d["A"] + d["off"] - Q[b])
                    kb.op("tensor", lambda d=d, usl=usl, w=w, col0=col0, b=b, first=first, last=last:
                          nc.tensor.matmul(
                              pB[:, col0 - 512 * b:col0 - 512 * b + w],
                              wpool[:, 4096 + d["t"] * 128: 4096 + (d["t"] + 1) * 128],
                              attnT[:, 512 * usl:512 * usl + w],
                              start=first, stop=last))
                    dcol = col0 - 512 * b
                    kb.op("tensor", lambda usl=usl, w=w, dcol=dcol, first=first, last=last:
                          nc.tensor.matmul(
                              pD[:, dcol:dcol + w],
                              kb.sb["onesm"][:],
                              attnT[:, 512 * usl:512 * usl + w],
                              start=first, stop=last), inc=S["mT"])
                    avden[usl] = S["mT"].n
                    first = False
                att_done[hsl] = S["mT"].n
                avsl = scf[:, 1024 * b:1024 * b + 512]
                dnsl = scf[:, 1024 * b + 512:1024 * b + 1024]
                rcp = scf[:, 2048:2560]
                kb.wait("scalar", S["mT"], att_done[hsl])
                kb.wait("scalar", S["vV"], avev[b])
                kb.op("scalar", lambda avsl=avsl: nc.scalar.activation(
                    avsl, pB[:, 0:512], AF.Copy))
                kb.op("scalar", lambda dnsl=dnsl: nc.scalar.activation(
                    dnsl, pD[:, 0:512], AF.Copy), inc=S["sS"])
                rbcev[b] = S["sS"].n
                kb.wait("vector", S["sS"], rbcev[b])
                kb.op("vector", lambda dnsl=dnsl, rcp=rcp:
                      nc.vector.reciprocal(rcp, dnsl))
                kb.drain("vector")
                kb.op("vector", lambda h=h, b=b, avsl=avsl, rcp=rcp:
                      nc.vector.tensor_mul(
                          AO[:, 1024 * h + 512 * b: 1024 * h + 512 * (b + 1)],
                          avsl, rcp), inc=S["vV"])
                avev[b] = S["vV"].n

    if not ATTN_SKIP_SWITCH:
        kb.switch(4, emit_variant, ["tensor", "vector", "scalar", "gpsimd"])

    kb.wait("tensor", S["vV"], S["vV"].n)
    mm_emit = make_wstream_mm_emit(kb, D["wo"], AO, extra=None)
    emit_evac_loop(kb, NT, mm_emit, resid_load, False, outbuf, scr_out)


def build_attn_phase_program():
    nc = bass.Bass("TRN2", target_bir_lowering=False)
    ctx = ExitStack()
    with ctx:
        kb = KB(nc, ctx)
        setup_common(kb, {"p0"})
        load_consts(kb, {"p0"})
        emit_cast_load_buf(kb, kb.D["x0p"], kb.sb["bufA"])
        kb.wait("tensor", kb.sems["dW"], kb.sems["dW"].n)
        kb.wait("tensor", kb.sems["dXQ"], kb.sems["dXQ"].n)
        emit_attn(kb, kb.sb["bufA"], kb.sb["bufB"],
                  resid_from_dram(kb, kb.D["x0p"]), kb.D["scr0"])
        emit_stats_finalize(kb)
        emit_final_output(kb, kb.D["scr0"], kb.D["out"])
        kb.finalize_block()
    return nc


def attn_phase_inputs(inputs, core, x_full_T=None):
    if x_full_T is None:
        x_full_T = np.ascontiguousarray(
            inputs["L0_attn_hidden_states"].transpose(0, 2, 1))
    batch = core // 4
    v = core % 4
    lo, hi = 512 * v, 3584 - 512 * v
    xT = x_full_T[batch]
    x0p = np.concatenate([xT[:, lo:lo + 512], xT[:, hi:hi + 512]], axis=1)
    m = dict(
        ones_bf=NPC["ones_bf"], ones1f=NPC["ones1x128_f"], onesm=NPC["onesm"],
        identb=NPC["ident_bf"], identf=NPC["ident_f"],
        tri=NPC["tri01_bf"], rmat=NPC["rmat_bf"],
        x0p=np.ascontiguousarray(x0p), xT=np.ascontiguousarray(xT),
        wq=np.ascontiguousarray(inputs["L0_attn_Wq"]),
        wkv=np.ascontiguousarray(inputs["L0_attn_Wkv"]),
        wo=np.ascontiguousarray(inputs["L0_attn_Wo"]),
        biastab=bias_table(core),
    )
    return m


_PROG_CACHE = {}


def _get_prog(name, builder):
    if name not in _PROG_CACHE:
        _PROG_CACHE[name] = builder()
    return _PROG_CACHE[name]


def kernel_unfused(**inputs):
    x_full_T = np.ascontiguousarray(
        np.asarray(inputs["L0_attn_hidden_states"]).transpose(0, 2, 1))
    cores = list(range(8))

    nc0 = _get_prog("p0", build_attn_phase_program)
    maps = [attn_phase_inputs(inputs, c, x_full_T) for c in cores]
    r = run_bass_kernel_spmd(nc0, maps, core_ids=cores)
    xn = [np.asarray(r.results[c]["out"]) for c in cores]

    nc1 = _get_prog("p1", lambda: build_ffn_phase_program(0))
    maps = []
    for c in cores:
        m = ffn_phase_inputs(0, inputs, c)
        m["xnin"] = xn[c]
        maps.append(m)
    r = run_bass_kernel_spmd(nc1, maps, core_ids=cores)
    xn = [np.asarray(r.results[c]["out"]) for c in cores]

    nc2 = _get_prog("p2", build_conv_phase_program)
    maps = []
    for c in cores:
        m = conv_phase_inputs(inputs)
        m["xnin"] = xn[c]
        maps.append(m)
    r = run_bass_kernel_spmd(nc2, maps, core_ids=cores)
    xn = [np.asarray(r.results[c]["out"]) for c in cores]

    nc3 = _get_prog("p3", lambda: build_ffn_phase_program(1))
    maps = []
    for c in cores:
        m = ffn_phase_inputs(1, inputs, c)
        m["xnin"] = xn[c]
        maps.append(m)
    r = run_bass_kernel_spmd(nc3, maps, core_ids=cores)

    yT = np.empty((2, HD, HD), dtype=np.float32)
    for c in cores:
        out = np.asarray(r.results[c]["out"])
        b, v = c // 4, c % 4
        lo, hi = 512 * v, 3584 - 512 * v
        yT[b][:, lo:lo + 512] = out[:, 0:512]
        yT[b][:, hi:hi + 512] = out[:, 512:1024]
    return np.ascontiguousarray(yT.transpose(0, 2, 1))


def _phase_barrier(kb):
    S = kb.sems
    snap = [(n, s.n) for n, s in S.items()]
    for eng in ENGS:
        for n, v in snap:
            if v > 0:
                kb.wait(eng, S[n], v)


def build_fused_program():
    nc = bass.Bass("TRN2", target_bir_lowering=False)
    ctx = ExitStack()
    with ctx:
        kb = KB(nc, ctx)
        phases = {"p0", "p1", "p2", "p3"}
        setup_common(kb, phases)
        load_consts(kb, phases)
        D = kb.D
        bufA, bufB = kb.sb["bufA"], kb.sb["bufB"]
        emit_cast_load_buf(kb, D["x0p"], bufA)
        kb.wait("tensor", kb.sems["dW"], kb.sems["dW"].n)
        kb.wait("tensor", kb.sems["dXQ"], kb.sems["dXQ"].n)
        emit_attn(kb, bufA, bufB, resid_from_dram(kb, D["x0p"]), D["scr0"])
        emit_stats_finalize(kb)
        emit_norm_pass(kb, bufA)
        _phase_barrier(kb)
        emit_ffn(kb, 0, bufA, bufB, resid_from_scr(kb, D["scr0"]), True, D["scr1"])
        emit_stats_finalize(kb)
        emit_norm_pass(kb, bufB)
        _phase_barrier(kb)
        emit_conv(kb, bufB, bufA, resid_from_scr(kb, D["scr1"]), True, D["scr0"])
        emit_stats_finalize(kb)
        emit_norm_pass(kb, bufA)
        _phase_barrier(kb)
        emit_ffn(kb, 1, bufA, bufB, resid_from_scr(kb, D["scr0"]), True, D["scr1"])
        emit_stats_finalize(kb)
        _phase_barrier(kb)
        emit_final_output(kb, D["scr1"], D["out"])
        kb.finalize_block()
    return nc


def resid_from_scr(kb, dram):
    return resid_from_dram(kb, dram)


def fused_inputs(inputs, core, x_full_T):
    m = attn_phase_inputs(inputs, core, x_full_T)
    m.update(ffn_phase_inputs(0, inputs, core))
    m.update(ffn_phase_inputs(1, inputs, core))
    m.update(conv_phase_inputs(inputs))
    return m


def kernel(**inputs):
    x_full_T = np.ascontiguousarray(
        np.asarray(inputs["L0_attn_hidden_states"]).transpose(0, 2, 1))
    cores = list(range(8))
    ncf = _get_prog("fused", build_fused_program)
    maps = [fused_inputs(inputs, c, x_full_T) for c in cores]
    r = run_bass_kernel_spmd(ncf, maps, core_ids=cores)
    yT = np.empty((2, HD, HD), dtype=np.float32)
    for c in cores:
        out = np.asarray(r.results[c]["out"])
        b, v = c // 4, c % 4
        lo, hi = 512 * v, 3584 - 512 * v
        yT[b][:, lo:lo + 512] = out[:, 0:512]
        yT[b][:, hi:hi + 512] = out[:, 512:1024]
    return np.ascontiguousarray(yT.transpose(0, 2, 1))


# version 03
# speedup vs baseline: 1.0511x; 1.0237x over previous
import numpy as np
import ml_dtypes
from contextlib import ExitStack

import concourse.bass as bass
import concourse.mybir as mybir
from concourse.bass_utils import run_bass_kernel_spmd

BF = ml_dtypes.bfloat16
F32 = mybir.dt.float32
BF16 = mybir.dt.bfloat16
I32 = mybir.dt.int32
AF = mybir.ActivationFunctionType
ALU = mybir.AluOpType
AX = mybir.AxisListType

HD = 4096
NT = 32
TPC = 1024
NHEADS = 32
DH = 128
ASCALE = float(DH) ** -0.5
EPS_LN = 1e-5

_start = 2.0 ** (-(2.0 ** -(np.log2(NHEADS) - 3)))
SLOPES = np.array([_start ** (i + 1) for i in range(NHEADS)], dtype=np.float64)

def _chunk_for(slope):
    for c in (512, 256, 128, 64):
        if slope * (c - 1) <= 65.0:
            return c
    return 64

CHUNK = [_chunk_for(s) for s in SLOPES]
CUTOFF = [130.0 / s for s in SLOPES]


def _np_consts():
    c = {}
    c["ones_bf"] = np.ones((128, 1), dtype=BF)
    c["ones1x128_f"] = np.ones((1, 128), dtype=np.float32)
    c["ident_bf"] = np.eye(128, dtype=BF)
    c["ident_f"] = np.eye(128, dtype=np.float32)
    c["tri01_bf"] = np.triu(np.ones((128, 128), dtype=np.float32)).astype(BF)
    c["rmat_bf"] = np.eye(128, dtype=np.float32)[::-1].copy().astype(BF)
    c["onesm"] = np.ones((128, 128), dtype=BF)
    c["krepl8"] = np.repeat(np.eye(8, dtype=np.float32), 16, axis=1).astype(BF)
    c["krepl16x2"] = (2.0 * np.repeat(np.eye(16, dtype=np.float32), 8, axis=1)).astype(BF)
    c["iota16c"] = np.tile(np.arange(16, dtype=np.float32)[:, None], (1, 4))
    c["iota4r"] = np.tile(np.arange(16, dtype=np.float32)[None, :], (4, 1))
    a32 = np.arange(32)
    w32 = np.exp(2j * np.pi * np.outer(a32, a32) / 32.0)
    c["w32r"] = w32.real.astype(np.float32)
    c["w32i"] = w32.imag.astype(np.float32)
    c["w32in"] = (-w32.imag).astype(np.float32)
    p128 = np.arange(128)
    tw = np.exp(2j * np.pi * np.outer(p128, a32) / 4096.0)
    c["twr"] = tw.real.astype(np.float32)
    c["twi"] = tw.imag.astype(np.float32)
    w128 = np.exp(2j * np.pi * np.outer(p128, p128) / 128.0)
    c["c128"] = w128.real.astype(np.float32)
    c["s128n"] = (-w128.imag).astype(np.float32)
    k = (np.arange(32)[None, :] * 128 + np.arange(128)[:, None]).astype(np.float64)
    om = np.exp(-2j * np.pi * k / 4096.0)
    cc = 2.0 / (1.0 + om)
    c["ccr"] = cc.real.astype(np.float32)
    c["cci"] = cc.imag.astype(np.float32)
    tim = np.tan(np.pi * np.arange(4096, dtype=np.float64) / 4096.0)
    c["tim"] = tim[None, :].astype(np.float32)
    return c

NPC = _np_consts()


def attn_units(variant):
    k = variant
    Q = [512 * k, 3584 - 512 * k]
    units = []
    for h in range(NHEADS):
        c = CHUNK[h]
        cut = CUTOFF[h]
        for b in (0, 1):
            q0 = Q[b]
            for ch in range(512 // c):
                A = q0 + ch * c
                t_hi = (A + c - 1) // 128
                t_lo = max(0, int(np.floor((A - cut) / 128.0)))
                for t in range(t_lo, t_hi + 1):
                    off = max(0, t * 128 - A)
                    w = c - off
                    diag = (t * 128 + 127 >= A)
                    tri_shift = max(0, A - t * 128) if diag else 0
                    tri_w = min(128 - tri_shift, w) if diag else 0
                    units.append(dict(h=h, b=b, A=A, c=c, t=t, off=off, w=w,
                                      diag=diag, tri_shift=tri_shift,
                                      tri_w=tri_w))
    return units

N_UNITS = [len(attn_units(v)) for v in range(4)]
NU_MAX = max(N_UNITS)


def pack_w(w):
    return np.ascontiguousarray(
        np.asarray(w).reshape(4, 8, 128, 32, 128).transpose(3, 0, 2, 1, 4))


def bias_table(core):
    v = core % 4
    units = attn_units(v)
    tab = np.zeros((128, NU_MAX), dtype=np.float32)
    p = np.arange(128, dtype=np.float64)
    for u, d in enumerate(units):
        kref = d["A"] + d["c"] - 1
        tab[:, u] = (SLOPES[d["h"]] * (d["t"] * 128 + p - kref)).astype(np.float32)
    return tab


ENGS = ("tensor", "vector", "scalar", "sync", "gpsimd")
ETYPE = {"tensor": mybir.EngineType.PE, "vector": mybir.EngineType.DVE,
         "scalar": mybir.EngineType.Activation, "sync": mybir.EngineType.SP,
         "gpsimd": mybir.EngineType.Pool}


class Sem:
    __slots__ = ("h", "n", "name")
    def __init__(self, h, name):
        self.h = h
        self.n = 0
        self.name = name


class KB:
    def __init__(self, nc, ctx):
        self.nc = nc
        self.ctx = ctx
        self.streams = {e: [] for e in ENGS}
        self._stack = []
        self.sems = {}
        self.sb = {}
        self.ps = {}

    def sem(self, name):
        s = Sem(self.ctx.enter_context(self.nc.semaphore(name)), name)
        self.sems[name] = s
        return s

    def sbuf(self, name, shape, dt):
        t = self.ctx.enter_context(self.nc.sbuf_tensor(name + "_s", shape, dt))
        self.sb[name] = t
        return t

    def psum(self, name, shape, dt):
        t = self.ctx.enter_context(self.nc.psum_tensor(name + "_p", shape, dt))
        self.ps[name] = t
        return t

    def emit(self, eng, fn):
        self.streams[eng].append(fn)

    def op(self, eng, fn, inc=None, inc2=None):
        if inc is not None:
            sem, k = inc if isinstance(inc, tuple) else (inc, 1)
            sem.n += k
            if inc2 is not None:
                s2, k2 = inc2 if isinstance(inc2, tuple) else (inc2, 1)
                s2.n += k2
                self.emit(eng, lambda: fn().then_inc(sem.h, k).then_inc(s2.h, k2))
            else:
                self.emit(eng, lambda: fn().then_inc(sem.h, k))
            return sem.n
        self.emit(eng, fn)
        return None

    def wait(self, eng, sem, val=None):
        v = sem.n if val is None else val
        h = sem.h
        eobj = getattr(self.nc, eng)
        self.emit(eng, lambda: eobj.wait_ge(h, v))

    def dma(self, eng, dst, src, sem):
        sem.n += 16
        eobj = getattr(self.nc, eng)
        d = dst if callable(dst) else (lambda: dst)
        s = src if callable(src) else (lambda: src)
        self.emit(eng, lambda: eobj.dma_start(d(), s()).then_inc(sem.h, 16))
        return sem.n

    def drain(self, eng):
        eobj = getattr(self.nc, eng)
        self.emit(eng, lambda: eobj.drain())

    def switch(self, nvar, emit_fn, engines):
        pre = {n: s.n for n, s in self.sems.items()}
        var_streams = []
        var_counts = []
        for v in range(nvar):
            for n, s in self.sems.items():
                s.n = pre[n]
            saved = self.streams
            self.streams = {e: [] for e in ENGS}
            emit_fn(v)
            var_streams.append(self.streams)
            self.streams = saved
            var_counts.append({n: s.n for n, s in self.sems.items()})
        final = {n: max(vc[n] for vc in var_counts) for n in self.sems}
        for n, s in self.sems.items():
            s.n = final[n]
        for v in range(nvar):
            for e in ENGS:
                if e not in engines:
                    assert not var_streams[v][e], (v, e, "undeclared switch engine")
        nc = self.nc
        for e in engines:
            owned = [n for n in self.sems
                     if any(var_counts[v][n] != pre[n] for v in range(nvar))
                     and SEM_OWNER.get(n) == e]
            bodies = [var_streams[v][e] for v in range(nvar)]
            deficits = [{n: final[n] - var_counts[v][n] for n in owned}
                        for v in range(nvar)]
            eobj = getattr(nc, e)
            et = ETYPE[e]
            semmap = {n: self.sems[n].h for n in owned}
            def do_switch(bodies=bodies, deficits=deficits, eobj=eobj, et=et,
                          semmap=semmap, nvar=nvar):
                pid = nc.partition_id(engines=[et])
                var = pid % nvar
                for i in eobj.Switch(var, nvar):
                    for f in bodies[i]:
                        f()
                    for n, d in deficits[i].items():
                        if d > 0:
                            eobj.sem_inc(semmap[n], d)
            self.emit(e, do_switch)

    def finalize_block(self):
        nc = self.nc
        block = self.ctx.enter_context(nc.Block())
        streams = self.streams
        @block.tensor
        def _(t):
            for f in streams["tensor"]:
                f()
        @block.vector
        def _(v):
            for f in streams["vector"]:
                f()
        @block.scalar
        def _(s):
            for f in streams["scalar"]:
                f()
        @block.sync
        def _(s):
            for f in streams["sync"]:
                f()
        @block.gpsimd
        def _(g):
            for f in streams["gpsimd"]:
                f()


SEM_OWNER = {}


def mk_sems(kb, names_owners):
    out = {}
    for name, owner in names_owners:
        out[name] = kb.sem(name)
        SEM_OWNER[name] = owner
    return out


def setup_common(kb, phases):
    nc = kb.nc
    D = {}
    def din(name, shape, dt=F32):
        D[name] = nc.dram_tensor(name, shape, dt, kind="ExternalInput")
        return D[name]
    din("ones_bf", [128, 1], BF16); din("ones1f", [1, 128])
    din("onesm", [128, 128], BF16)
    din("identb", [128, 128], BF16); din("identf", [128, 128])
    din("tri", [128, 128], BF16); din("rmat", [128, 128], BF16)
    if "p0" in phases:
        din("x0p", [HD, TPC]); din("xT", [HD, HD])
        din("wq", [NT, 4, 128, 8, 128]); din("wkv", [HD, 256]); din("wo", [NT, 4, 128, 8, 128])
        din("biastab", [128, NU_MAX])
    if "p1" in phases:
        din("wb0t", [NT, 4, 128, 8, 128]); din("b0", [128, 32]); din("wgt", [HD, 8])
        din("a0t", [HD, 128]); din("b0f", [128, HD])
        din("kr8", [8, 128], BF16)
    if "p2" in phases:
        din("lam", [64, 2]); din("pri", [64, 2]); din("qri", [64, 2])
        din("vcrt", [64, 64]); din("vcit", [64, 64])
        din("ctri", [64, 2]); din("bp", [64, 1]); din("dd", [1, 1]); din("lstep", [1, 1])
        din("tim", [1, 4096]); din("ccr", [128, 32]); din("cci", [128, 32])
        din("w32r", [32, 32]); din("w32i", [32, 32]); din("w32in", [32, 32])
        din("twr", [128, 32]); din("twi", [128, 32])
        din("c128", [128, 128]); din("s128n", [128, 128])
    if "p3" in phases:
        din("wb1t", [NT, 4, 128, 8, 128]); din("b1", [128, 32]); din("rwt", [HD, 16])
        din("la1t", [HD, 128]); din("lb1f", [128, HD])
        din("rb", [16, 1]); din("prior", [16, 1]); din("active", [1, 4], I32)
        din("kr16", [16, 128], BF16); din("io16c", [16, 4]); din("io4r", [4, 16])
    D["out"] = nc.dram_tensor("out", [HD, TPC], F32, kind="ExternalOutput")
    D["scr0"] = nc.dram_tensor("scr0", [HD, TPC], F32)
    D["scr1"] = nc.dram_tensor("scr1", [HD, TPC], F32)
    if "p2" in phases:
        D["kpad"] = nc.dram_tensor("kpad", [8192], BF16)
    kb.D = D

    kb.sbuf("bufA", [128, NT * TPC], BF16)
    kb.sbuf("bufB", [128, NT * TPC], BF16)
    kb.sbuf("wpool", [128, 8192], BF16)
    kb.sbuf("wsb", [128, 4096], BF16)
    kb.sbuf("scf", [128, 3072], F32)
    kb.sbuf("xpre", [128, 2048], F32)
    kb.sbuf("sqsb", [128, 2048], BF16)
    kb.sbuf("mbc", [128, 1024], F32)
    kb.sbuf("rbc2", [128, 1024], F32)
    kb.sbuf("cons_bf", [128, 1], BF16)
    kb.sbuf("onesm", [128, 128], BF16)
    kb.sbuf("cons1f", [1, 128], F32)
    kb.sbuf("identb", [128, 128], BF16)
    kb.sbuf("identf", [128, 128], F32)
    kb.sbuf("attnT", [128, 1024], BF16)
    if "p0" in phases:
        kb.sbuf("qsb", [128, 2048], BF16)
        kb.sbuf("rbcs", [128, 512], F32)
        kb.sbuf("biastab", [128, NU_MAX], F32)
        kb.sbuf("tri", [128, 128], BF16)
    if phases & {"p1", "p2", "p3"}:
        kb.sbuf("gw", [128, 48], F32)
        kb.sbuf("gwb", [128, 16], F32)
    if "p1" in phases or "p3" in phases:
        kb.sbuf("hwsb", [128, 1024], BF16)
        kb.sbuf("grow", [16, 1152], BF16)
        kb.sbuf("bias_sb", [128, 64], F32)
        kb.sbuf("kreps", [16, 256], BF16)
        kb.sbuf("wgs", [128, 512], BF16)
    if "p3" in phases:
        kb.sbuf("oh", [16, 64], F32)
        kb.sbuf("acti", [1, 4], I32)
        kb.sbuf("ioc", [16, 4], F32)
        kb.sbuf("ior", [4, 16], F32)
    if "p2" in phases:
        kb.sbuf("rmat", [128, 128], BF16)
        kb.sbuf("ktile", [128, 32], BF16)
    kb.alias = {}
    kb.psum("pA", [128, 1024], F32)
    kb.psum("pB", [128, 1024], F32)
    kb.psum("pC", [128, 1024], F32)
    kb.psum("pD", [128, 1024], F32)

    mk_sems(kb, [("dW", "gpsimd"), ("dXQ", "sync"), ("dSC", "sync"),
                 ("dOUT", "sync"), ("mT", "tensor"), ("vV", "vector"),
                 ("sS", "scalar"), ("gG", "gpsimd"), ("dSS", "scalar"),
                 ("aRs", "scalar"), ("aRv", "vector"),
                 ("dP", "gpsimd"),
                 ("dQ0", "gpsimd"), ("dQ1", "gpsimd"),
                 ("dQ2", "gpsimd"), ("dQ3", "gpsimd"),
                 ("dK0", "gpsimd"), ("dK1", "gpsimd"),
                 ("dX0", "sync"), ("dX1", "sync"),
                 ("dS0", "sync"), ("dS1", "sync"),
                 ("dO0", "sync"), ("dO1", "sync")])


def load_consts(kb, phases):
    D, S = kb.D, kb.sems
    kb.dma("sync", kb.sb["cons_bf"][:], D["ones_bf"].ap(), S["dXQ"])
    kb.dma("sync", kb.sb["onesm"][:], D["onesm"].ap(), S["dXQ"])
    kb.dma("sync", kb.sb["cons1f"][:], D["ones1f"].ap(), S["dXQ"])
    kb.dma("sync", kb.sb["identb"][:], D["identb"].ap(), S["dXQ"])
    kb.dma("sync", kb.sb["identf"][:], D["identf"].ap(), S["dXQ"])
    if "p0" in phases:
        kb.dma("sync", kb.sb["biastab"][:], D["biastab"].ap(), S["dXQ"])
        kb.dma("sync", kb.sb["tri"][:], D["tri"].ap(), S["dXQ"])
    if "p2" in phases:
        kb.dma("sync", kb.sb["rmat"][:], D["rmat"].ap(), S["dXQ"])
    if "p1" in phases:
        kb.dma("sync", kb.sb["kreps"][0:8, 0:128], D["kr8"].ap(), S["dXQ"])
        kb.dma("sync", kb.sb["bias_sb"][:, 0:32], D["b0"].ap(), S["dXQ"])
    if "p3" in phases:
        kb.dma("sync", kb.sb["kreps"][0:16, 128:256], D["kr16"].ap(), S["dXQ"])
        kb.dma("sync", kb.sb["bias_sb"][:, 32:64], D["b1"].ap(), S["dXQ"])
        kb.dma("sync", kb.sb["ioc"][:], D["io16c"].ap(), S["dXQ"])
        kb.dma("sync", kb.sb["ior"][:], D["io4r"].ap(), S["dXQ"])
        kb.dma("sync", kb.sb["acti"][:], D["active"].ap(), S["dXQ"])
        kb.dma("sync", kb.sb["oh"][:, 0:1], D["rb"].ap(), S["dXQ"])
        kb.dma("sync", kb.sb["oh"][:, 1:2], D["prior"].ap(), S["dXQ"])
    return S["dXQ"].n


def emit_evac_loop(kb, n_tiles, mm_emit, resid_load, normed, outbuf, scr_out,
                  bias_cols=None, psum_names=("pA", "pB")):
    S = kb.sems
    nc = kb.nc
    scf, xpre, sqsb = kb.sb["scf"], kb.sb["xpre"], kb.sb["sqsb"]
    ev1 = [0, 0]; wdone = [0, 0]; obn = [0, 0]; sqd = [0, 0]; scrd = [0, 0]
    stats_done = [0, 0]
    pending = []

    def emit_stats(t, sl, ob_ap, sq_ap):
        kb.wait("tensor", S["gG"], obn[sl])
        kb.wait("tensor", S["sS"], sqd[sl])
        for half in range(2):
            kb.op("tensor", lambda ob_ap=ob_ap, half=half, t=t:
                  nc.tensor.matmul(kb.ps["pC"][:, half * 512:(half + 1) * 512],
                                   kb.sb["onesm"][:],
                                   ob_ap[:, half * 512:(half + 1) * 512],
                                   start=(t == 0), stop=(t == n_tiles - 1)))
        for half in range(2):
            kb.op("tensor", lambda sq_ap=sq_ap, half=half, t=t:
                  nc.tensor.matmul(kb.ps["pD"][:, half * 512:(half + 1) * 512],
                                   kb.sb["onesm"][:],
                                   sq_ap[:, half * 512:(half + 1) * 512],
                                   start=(t == 0), stop=(t == n_tiles - 1)),
                  inc=(S["mT"], 1) if half == 1 else None)
        stats_done[sl] = S["mT"].n

    for t in range(n_tiles):
        sl = t % 2
        pt = kb.ps[psum_names[sl]]
        if t >= 2:
            kb.wait("tensor", S["sS"], ev1[sl])
        mm_done = mm_emit(kb, t, pt)
        if pending:
            emit_stats(*pending.pop())
        if t >= 2:
            kb.wait("sync", S["vV"], wdone[sl])
        xp = resid_load(kb, t, sl)
        if t >= 2:
            kb.wait("scalar", S["gG"], obn[sl])
            kb.wait("scalar", S["dS0" if sl == 0 else "dS1"], scrd[sl])
        kb.wait("scalar", S["mT"], mm_done)
        wout = scf[:, sl * 1024:(sl + 1) * 1024]
        kb.op("scalar", lambda pt=pt, wout=wout:
              nc.scalar.activation(wout, pt[:], AF.Copy), inc=S["sS"])
        ev1[sl] = S["sS"].n
        bap = bias_cols(t) if bias_cols is not None else None
        kb.wait("vector", S["sS"], ev1[sl])
        kb.wait("vector", S["dX0" if sl == 0 else "dX1"], xp)
        xs = xpre[:, sl * 1024:(sl + 1) * 1024]
        if normed:
            tmp = scf[:, 2048:3072]
            kb.op("vector", lambda xs=xs, tmp=tmp:
                  nc.vector.tensor_sub(tmp, xs, kb.sb["mbc"][:]))
            kb.drain("vector")
            kb.op("vector", lambda tmp=tmp:
                  nc.vector.tensor_mul(tmp, tmp, kb.sb["rbc2"][:]))
            kb.drain("vector")
            other = tmp
        else:
            other = xs
        if bap is not None:
            kb.op("vector", lambda other=other, wout=wout, bap=bap:
                  nc.vector.scalar_tensor_tensor(wout, wout, bap, other,
                                                 op0=ALU.add, op1=ALU.add),
                  inc=S["vV"])
        else:
            kb.op("vector", lambda other=other, wout=wout:
                  nc.vector.tensor_add(wout, other, wout), inc=S["vV"])
        wdone[sl] = S["vV"].n
        kb.wait("gpsimd", S["vV"], wdone[sl])
        ob_ap = outbuf[:, t * 1024:(t + 1) * 1024]
        kb.op("gpsimd", lambda ob_ap=ob_ap, wout=wout:
              nc.gpsimd.tensor_copy(ob_ap, wout), inc=S["gG"])
        obn[sl] = S["gG"].n
        if t >= 2:
            kb.wait("scalar", S["mT"], stats_done[sl])
        kb.wait("scalar", S["vV"], wdone[sl])
        sq_ap = sqsb[:, sl * 1024:(sl + 1) * 1024]
        kb.op("scalar", lambda sq_ap=sq_ap, wout=wout:
              nc.scalar.activation(sq_ap, wout, AF.Square), inc=S["sS"])
        sqd[sl] = S["sS"].n
        kb.wait("sync", S["vV"], wdone[sl])
        scrd[sl] = kb.dma("sync", lambda t=t: scr_out.ap()[t * 128:(t + 1) * 128, :],
                          lambda wout=wout: wout, S["dS0" if sl == 0 else "dS1"])
        pending.append((t, sl, ob_ap, sq_ap))
    emit_stats(*pending.pop())
    return S["mT"].n


def emit_stats_finalize(kb):
    S = kb.sems; nc = kb.nc
    scf = kb.sb["scf"]
    kb.wait("vector", S["mT"], S["mT"].n)
    kb.wait("vector", S["dS0"], S["dS0"].n)
    kb.wait("vector", S["dS1"], S["dS1"].n)
    MR = scf[0:1, 0:1024]; VR = scf[0:1, 1024:2048]; T1 = scf[0:1, 2048:3072]
    kb.op("vector", lambda: nc.vector.tensor_scalar(MR, kb.ps["pC"][0:1, :],
          1.0 / HD, None, op0=ALU.mult))
    kb.op("vector", lambda: nc.vector.tensor_scalar(VR, kb.ps["pD"][0:1, :],
          1.0 / HD, None, op0=ALU.mult))
    kb.drain("vector")
    kb.op("vector", lambda: nc.vector.tensor_mul(T1, MR, MR))
    kb.drain("vector")
    kb.op("vector", lambda: nc.vector.tensor_sub(VR, VR, T1), inc=S["vV"])
    kb.drain("vector")
    kb.op("vector", lambda: nc.vector.tensor_scalar(VR, VR, EPS_LN, None,
          op0=ALU.add), inc=S["vV"])
    vv = S["vV"].n
    kb.wait("scalar", S["vV"], vv)
    kb.op("scalar", lambda: nc.scalar.activation(T1, VR, AF.Ln))
    kb.drain("scalar")
    kb.op("scalar", lambda: nc.scalar.activation(VR, T1, AF.Exp, scale=-0.5),
          inc=S["sS"])
    sv = S["sS"].n
    kb.wait("tensor", S["sS"], sv)
    for half in range(2):
        kb.op("tensor", lambda half=half: nc.tensor.matmul(
            kb.ps["pA"][:, half * 512:(half + 1) * 512], kb.sb["cons1f"][:],
            MR[:, half * 512:(half + 1) * 512], start=True, stop=True))
    for half in range(2):
        kb.op("tensor", lambda half=half: nc.tensor.matmul(
            kb.ps["pB"][:, half * 512:(half + 1) * 512], kb.sb["cons1f"][:],
            VR[:, half * 512:(half + 1) * 512], start=True, stop=True),
            inc=(S["mT"], 1) if half == 1 else None)
    mm = S["mT"].n
    kb.wait("scalar", S["mT"], mm)
    kb.op("scalar", lambda: nc.scalar.activation(kb.sb["mbc"][:], kb.ps["pA"][:], AF.Copy))
    kb.op("scalar", lambda: nc.scalar.activation(kb.sb["rbc2"][:], kb.ps["pB"][:], AF.Copy),
          inc=S["sS"])
    return S["sS"].n


def emit_norm_pass(kb, buf):
    S = kb.sems; nc = kb.nc
    ntmp = kb.sb["sqsb"]
    sv0 = S["sS"].n
    kb.wait("vector", S["sS"], sv0)
    kb.wait("gpsimd", S["sS"], sv0)
    gdone = [0, 0]
    for t in range(NT):
        sl = t % 2
        bt = buf[:, t * 1024:(t + 1) * 1024]
        nt = ntmp[:, sl * 1024:(sl + 1) * 1024]
        if t >= 2:
            kb.wait("vector", S["gG"], gdone[sl])
        kb.op("vector", lambda bt=bt, nt=nt:
              nc.vector.tensor_sub(nt, bt, kb.sb["mbc"][:]), inc=S["vV"])
        vv = S["vV"].n
        kb.wait("gpsimd", S["vV"], vv)
        kb.op("gpsimd", lambda bt=bt, nt=nt:
              nc.gpsimd.tensor_mul(bt, nt, kb.sb["rbc2"][:]), inc=S["gG"])
        gdone[sl] = S["gG"].n
    return S["gG"].n


def emit_final_output(kb, scr_in, out_dram):
    S = kb.sems; nc = kb.nc
    xpre, scf = kb.sb["xpre"], kb.sb["scf"]
    vdone = [0, 0]; outd = [0, 0]
    sv0 = S["sS"].n
    kb.wait("vector", S["sS"], sv0)
    for t in range(NT):
        sl = t % 2
        if t >= 2:
            kb.wait("sync", S["dO0" if sl == 0 else "dO1"], outd[sl])
        xp = kb.dma("sync", lambda t=t, sl=sl: xpre[:, sl * 1024:(sl + 1) * 1024],
                    lambda t=t: scr_in.ap()[t * 128:(t + 1) * 128, :],
                    S["dX0" if sl == 0 else "dX1"])
        kb.wait("vector", S["dX0" if sl == 0 else "dX1"], xp)
        xs = xpre[:, sl * 1024:(sl + 1) * 1024]
        tmp = scf[:, sl * 1024:(sl + 1) * 1024]
        if t >= 2:
            kb.wait("vector", S["dO0" if sl == 0 else "dO1"], outd[sl])
        kb.op("vector", lambda xs=xs, tmp=tmp:
              nc.vector.tensor_sub(tmp, xs, kb.sb["mbc"][:]))
        kb.drain("vector")
        kb.op("vector", lambda tmp=tmp:
              nc.vector.tensor_mul(tmp, tmp, kb.sb["rbc2"][:]), inc=S["vV"])
        vdone[sl] = S["vV"].n
        kb.wait("sync", S["vV"], vdone[sl])
        outd[sl] = kb.dma("sync", lambda t=t: out_dram.ap()[t * 128:(t + 1) * 128, :],
                          lambda tmp=tmp: tmp, S["dO0" if sl == 0 else "dO1"])
    kb.wait("sync", S["dO0"], S["dO0"].n)
    kb.wait("sync", S["dO1"], S["dO1"].n)


def emit_cast_load_buf(kb, src_dram, buf):
    S = kb.sems
    nc = kb.nc
    for t in range(NT):
        kb.dma("gpsimd", lambda t=t: buf[:, t * 1024:(t + 1) * 1024],
               lambda t=t: src_dram.ap()[t * 128:(t + 1) * 128, :], S["dW"])
    return S["dW"].n


def emit_ffn(kb, layer, inbuf, outbuf, resid_load, normed, scr_out):
    S = kb.sems; nc = kb.nc; D = kb.D
    wpool, wgs, wsb = kb.sb["wpool"], kb.sb["wgs"], kb.sb["wsb"]
    smalls = kb.sb["scf"]
    gw, gwb, grow = kb.sb["gw"], kb.sb["gwb"], kb.sb["grow"]
    zsb = kb.sb["sqsb"]
    hwsb = kb.sb["hwsb"]
    identf = kb.sb["identf"]
    pA, pB, pC, pD = kb.ps["pA"], kb.ps["pB"], kb.ps["pC"], kb.ps["pD"]
    E = 8 if layer == 0 else 16

    at_name, bf_name, wb_name = (("a0t", "b0f", "wb0t") if layer == 0
                                 else ("la1t", "lb1f", "wb1t"))
    kb.dma("gpsimd", wpool[:, 0:4096],
           lambda: D[at_name].ap().rearrange("(t p) j -> p t j", p=128), S["dP"])
    kb.dma("gpsimd", wpool[:, 4096:8192], lambda: D[bf_name].ap(), S["dP"])
    if layer == 0:
        kb.dma("gpsimd", wgs[:, 0:256],
               lambda: D["wgt"].ap().rearrange("(t p) e -> p t e", p=128), S["dP"])
    else:
        kb.dma("gpsimd", wgs[:, 0:512],
               lambda: D["rwt"].ap().rearrange("(t p) e -> p t e", p=128), S["dP"])
    pool_dma = S["dP"].n

    if layer == 1:
        oh = kb.sb["oh"]
        kb.wait("vector", S["dXQ"], S["dXQ"].n)
        kb.op("vector", lambda: nc.vector.tensor_copy(oh[0:1, 2:6], kb.sb["acti"][:]),
              inc=S["vV"])
        va = S["vV"].n
        kb.wait("tensor", S["vV"], va)
        kb.op("tensor", lambda: nc.tensor.matmul(pA[0:16, 0:4], kb.sb["cons1f"][0:1, 0:16],
              oh[0:1, 2:6], start=True, stop=True))
        kb.op("tensor", lambda: nc.tensor.matmul(pB[0:4, 0:16], oh[0:1, 2:6],
              kb.sb["cons1f"][0:1, 0:16], start=True, stop=True), inc=S["mT"])
        m1 = S["mT"].n
        kb.wait("vector", S["mT"], m1)
        kb.op("vector", lambda: nc.vector.tensor_tensor(oh[0:16, 8:12], kb.sb["ioc"][:],
              pA[0:16, 0:4], op=ALU.is_equal))
        kb.op("vector", lambda: nc.vector.tensor_tensor(oh[0:4, 16:32], kb.sb["ior"][:],
              pB[0:4, 0:16], op=ALU.is_equal), inc=S["vV"])
        v1 = S["vV"].n
        kb.wait("tensor", S["vV"], v1)
        kb.op("tensor", lambda: nc.tensor.matmul(pC[0:4, 0:1], oh[0:16, 8:12],
              oh[0:16, 1:2], start=True, stop=True), inc=S["mT"])
        m2 = S["mT"].n
        kb.wait("vector", S["mT"], m2)
        kb.op("vector", lambda: nc.vector.tensor_scalar(oh[0:4, 33:34],
              pC[0:4, 0:1], 1e-12, None, op0=ALU.add), inc=S["vV"])
        kb.wait("scalar", S["vV"], S["vV"].n)
        kb.op("scalar", lambda: nc.scalar.activation(oh[0:4, 32:33],
              oh[0:4, 33:34], AF.Ln))
        kb.drain("scalar")
        kb.op("scalar", lambda: nc.scalar.activation(grow[0:4, 0:16], oh[0:4, 16:32],
              AF.Copy), inc=S["sS"])
        onehot_done = S["sS"].n

    kb.wait("tensor", S["dP"], pool_dma)
    if layer == 1:
        kb.wait("tensor", S["sS"], onehot_done)
    for ht in range(NT):
        for half in range(2):
            kb.op("tensor", lambda ht=ht, half=half: nc.tensor.matmul(
                pC[0:E, half * 512:(half + 1) * 512],
                wgs[:, E * ht:E * (ht + 1)],
                inbuf[:, ht * 1024 + half * 512: ht * 1024 + (half + 1) * 512],
                start=(ht == 0), stop=(ht == NT - 1)),
                inc=S["mT"] if (ht == NT - 1 and half == 1) else None)
    lg_mm = S["mT"].n
    kb.wait("scalar", S["mT"], lg_mm)
    if layer == 0:
        kb.op("scalar", lambda: nc.scalar.activation(smalls[0:E, 0:1024], pC[0:E, :],
              AF.Copy), inc=S["sS"])
    else:
        kb.wait("vector", S["mT"], lg_mm)
        kb.op("vector", lambda: nc.vector.tensor_single_scalar(
            smalls[0:E, 0:1024], pC[0:E, :], kb.sb["oh"][0:16, 0:1],
            op=ALU.add), inc=S["vV"])
        kb.wait("scalar", S["vV"], S["vV"].n)
        kb.op("scalar", lambda: nc.scalar.drain(), inc=S["sS"])
    lg_ev = S["sS"].n

    if layer == 1:
        kb.wait("tensor", S["sS"], lg_ev)
        for half in range(2):
            kb.op("tensor", lambda half=half: nc.tensor.matmul(
                pD[0:4, half * 512:(half + 1) * 512], kb.sb["oh"][0:16, 8:12],
                smalls[0:16, half * 512:(half + 1) * 512], start=True, stop=True),
                inc=S["mT"] if half == 1 else None)
        sl_mm = S["mT"].n
        kb.wait("vector", S["mT"], sl_mm)
        kb.op("vector", lambda: nc.vector.tensor_single_scalar(
            smalls[0:4, 1024:2048], pD[0:4, :], kb.sb["oh"][0:4, 32:33],
            op=ALU.add), inc=S["vV"])
        kb.wait("scalar", S["vV"], S["vV"].n)
        kb.op("scalar", lambda: nc.scalar.drain(), inc=S["sS"])
        lg_ev = S["sS"].n

    nsl = E if layer == 0 else 4
    soff = 0 if layer == 0 else 1024
    kb.wait("tensor", S["sS"], lg_ev)
    for i in range(8):
        kb.op("tensor", lambda i=i, nsl=nsl, soff=soff: nc.tensor.transpose(
            pD[:, 64 + nsl * i: 64 + nsl * (i + 1)],
            smalls[0:nsl, soff + 128 * i: soff + 128 * (i + 1)],
            identf[0:nsl, 0:nsl]),
            inc=S["mT"] if i == 7 else None)
    tr_mm = S["mT"].n
    kb.wait("vector", S["mT"], tr_mm)
    M1, NM1, SS, RS = gw[:, 0:1], gw[:, 1:2], gw[:, 2:3], gw[:, 3:4]
    M2 = gw[:, 4:5]
    EQ, KEEP, EE, WW = gw[:, 8:16], gw[:, 16:24], gw[:, 24:32], gw[:, 32:40]
    tb_done = 0
    for i in range(8):
        lgt = pD[:, 64 + nsl * i: 64 + nsl * (i + 1)]
        if i > 0:
            kb.wait("vector", S["mT"], tb_done)
        kb.op("vector", lambda lgt=lgt: nc.vector.reduce_max(M1, lgt, axis=AX.X))
        kb.drain("vector")
        kb.op("vector", lambda: nc.vector.tensor_scalar(NM1, M1, -1.0, None, op0=ALU.mult))
        if layer == 0:
            kb.op("vector", lambda lgt=lgt: nc.vector.tensor_scalar(
                EQ[:, 0:nsl], lgt, M1[:, 0:1], None, op0=ALU.is_equal))
            kb.drain("vector")
            kb.op("vector", lambda lgt=lgt: nc.vector.scalar_tensor_tensor(
                EQ[:, 0:nsl], EQ[:, 0:nsl], -1e30, lgt, op0=ALU.mult, op1=ALU.add))
            kb.drain("vector")
            kb.op("vector", lambda: nc.vector.reduce_max(M2, EQ[:, 0:nsl], axis=AX.X))
        kb.op("vector", lambda: nc.vector.drain(), inc=S["vV"])
        vv = S["vV"].n
        kb.wait("scalar", S["vV"], vv)
        kb.op("scalar", lambda lgt=lgt: nc.scalar.activation(
            EE[:, 0:nsl], lgt, AF.Exp, bias=NM1[:, 0:1]), inc=S["sS"])
        se = S["sS"].n
        kb.wait("vector", S["sS"], se)
        if layer == 0:
            kb.op("vector", lambda lgt=lgt: nc.vector.tensor_scalar(
                KEEP[:, 0:nsl], lgt, M2[:, 0:1], None, op0=ALU.is_ge))
            kb.drain("vector")
            kb.op("vector", lambda: nc.vector.tensor_mul(WW[:, 0:nsl], EE[:, 0:nsl],
                  KEEP[:, 0:nsl]))
        else:
            kb.op("vector", lambda: nc.vector.tensor_copy(WW[:, 0:nsl], EE[:, 0:nsl]))
        kb.drain("vector")
        kb.op("vector", lambda: nc.vector.reduce_sum(SS, WW[:, 0:nsl], axis=AX.X))
        kb.drain("vector")
        kb.op("vector", lambda: nc.vector.reciprocal(RS, SS))
        kb.drain("vector")
        kb.op("vector", lambda: nc.vector.tensor_scalar(
            gwb[:, 0:nsl], WW[:, 0:nsl], RS[:, 0:1], None, op0=ALU.mult), inc=S["vV"])
        vr = S["vV"].n
        kb.wait("tensor", S["vV"], vr)
        kb.op("tensor", lambda i=i, nsl=nsl: nc.tensor.transpose(
            pB[0:nsl, 128 * i:128 * (i + 1)], gwb[:, 0:nsl], identf[:]),
            inc=S["mT"])
        tb_done = S["mT"].n
    kb.wait("scalar", S["mT"], tb_done)
    if layer == 0:
        kb.op("scalar", lambda: nc.scalar.activation(grow[0:8, 64:1088],
              pB[0:8, 0:1024], AF.Copy), inc=S["sS"])
    else:
        kb.op("scalar", lambda: nc.scalar.activation(kb.sb["attnT"][0:4, 0:1024],
              pB[0:4, 0:1024], AF.Copy), inc=S["sS"])
    rw_ready = S["sS"].n

    if layer == 1:
        kb.wait("tensor", S["sS"], rw_ready)
        for half in range(2):
            kb.op("tensor", lambda half=half: nc.tensor.matmul(
                pC[0:16, half * 512:(half + 1) * 512], grow[0:4, 0:16],
                kb.sb["attnT"][0:4, half * 512:(half + 1) * 512],
                start=True, stop=True), inc=S["mT"] if half == 1 else None)
        af_mm = S["mT"].n
        kb.wait("scalar", S["mT"], af_mm)
        kb.op("scalar", lambda: nc.scalar.activation(grow[0:16, 64:1088],
              pC[0:16, 0:1024], AF.Copy), inc=S["sS"])
        rw_ready = S["sS"].n

    kb.wait("tensor", S["vV"], S["vV"].n)
    for ht in range(NT):
        for half in range(2):
            kb.op("tensor", lambda ht=ht, half=half: nc.tensor.matmul(
                pD[:, half * 512:(half + 1) * 512], wpool[:, 128 * ht:128 * (ht + 1)],
                inbuf[:, ht * 1024 + half * 512: ht * 1024 + (half + 1) * 512],
                start=(ht == 0), stop=(ht == NT - 1)),
                inc=S["mT"] if (ht == NT - 1 and half == 1) else None)
    z_mm = S["mT"].n
    kb.wait("scalar", S["mT"], z_mm)
    kb.op("scalar", lambda: nc.scalar.activation(zsb[:, 0:1024], pD[:], AF.Copy), inc=S["sS"])
    z_ev = S["sS"].n
    kb.wait("tensor", S["sS"], rw_ready)
    krap = kb.sb["kreps"][0:8, 0:128] if layer == 0 else kb.sb["kreps"][0:16, 128:256]
    nK = 8 if layer == 0 else 16
    for half in range(2):
        kb.op("tensor", lambda half=half, krap=krap, nK=nK: nc.tensor.matmul(
            pC[:, half * 512:(half + 1) * 512], krap,
            grow[0:nK, 64 + half * 512: 64 + (half + 1) * 512],
            start=True, stop=True), inc=S["mT"] if half == 1 else None)
    bc_mm = S["mT"].n
    kb.wait("vector", S["mT"], bc_mm)
    kb.wait("vector", S["sS"], z_ev)
    kb.op("vector", lambda: nc.vector.tensor_mul(hwsb[:], zsb[:, 0:1024], pC[:]), inc=S["vV"])
    hw_done = S["vV"].n

    wb_dram = D[wb_name]
    mm_emit = make_wstream_mm_emit(kb, wb_dram, inbuf, extra=("delta", hw_done))
    bcol = 0 if layer == 0 else 32
    bias_cols = lambda t: kb.sb["bias_sb"][:, bcol + t: bcol + t + 1]
    emit_evac_loop(kb, NT, mm_emit, resid_load, normed, outbuf, scr_out,
                   bias_cols=bias_cols)


def make_wstream_mm_emit(kb, wb_dram, inbuf, extra=None):
    S = kb.sems; nc = kb.nc
    wsb, wpool = kb.sb["wsb"], kb.sb["wpool"]
    hwsb = kb.sb.get("hwsb")
    q_mm = [S["mT"].n] * 4
    q_dma = [0, 0, 0, 0]
    def mm_emit(kb2, t, pt):
        for q in range(4):
            kb.wait("gpsimd", S["mT"], q_mm[q])
            q_dma[q] = kb.dma(
                "gpsimd",
                lambda q=q: wsb[:, q * 1024:(q + 1) * 1024],
                lambda t=t, q=q: wb_dram.ap()[t, q], S["dQ" + str(q)])
        if t == 0 and extra is not None:
            kb.wait("tensor", S["vV"], extra[1])
        for q in range(4):
            kb.wait("tensor", S["dQ" + str(q)], q_dma[q])
            for j in range(8):
                u = 8 * q + j
                for half in range(2):
                    kb.op("tensor", lambda u=u, half=half, pt=pt:
                          nc.tensor.matmul(
                              pt[:, half * 512:(half + 1) * 512],
                              wsb[:, u * 128:(u + 1) * 128],
                              inbuf[:, u * 1024 + half * 512:
                                    u * 1024 + (half + 1) * 512],
                              start=(u == 0),
                              stop=(extra is None and u == NT - 1)),
                          inc=S["mT"] if (j == 7 and half == 1) else None)
            q_mm[q] = S["mT"].n
        if extra is not None:
            for half in range(2):
                kb.op("tensor", lambda t=t, half=half, pt=pt: nc.tensor.matmul(
                    pt[:, half * 512:(half + 1) * 512],
                    wpool[:, 4096 + 128 * t: 4096 + 128 * (t + 1)],
                    hwsb[:, half * 512:(half + 1) * 512],
                    start=False, stop=True),
                    inc=S["mT"] if half == 1 else None)
        return S["mT"].n
    return mm_emit


def resid_from_dram(kb, dram):
    S = kb.sems
    def load(kb2, t, sl):
        return kb.dma("sync",
                      lambda sl=sl: kb.sb["xpre"][:, sl * 1024:(sl + 1) * 1024],
                      lambda t=t: dram.ap()[t * 128:(t + 1) * 128, :],
                      S["dX0" if sl == 0 else "dX1"])
    return load


def build_ffn_phase_program(layer):
    nc = bass.Bass("TRN2", target_bir_lowering=False)
    ctx = ExitStack()
    with ctx:
        kb = KB(nc, ctx)
        phase = "p1" if layer == 0 else "p3"
        setup_common(kb, {phase})
        kb.D["xnin"] = nc.dram_tensor("xnin", [HD, TPC], F32, kind="ExternalInput")
        load_consts(kb, {phase})
        emit_cast_load_buf(kb, kb.D["xnin"], kb.sb["bufA"])
        kb.wait("tensor", kb.sems["dW"], kb.sems["dW"].n)
        kb.wait("tensor", kb.sems["dXQ"], kb.sems["dXQ"].n)
        emit_ffn(kb, layer, kb.sb["bufA"], kb.sb["bufB"],
                 resid_from_dram(kb, kb.D["xnin"]), False, kb.D["scr0"])
        emit_stats_finalize(kb)
        emit_final_output(kb, kb.D["scr0"], kb.D["out"])
        kb.finalize_block()
    return nc


def ffn_phase_inputs(layer, inputs, core):
    m = dict(
        ones_bf=NPC["ones_bf"], ones1f=NPC["ones1x128_f"], onesm=NPC["onesm"],
        identb=NPC["ident_bf"], identf=NPC["ident_f"],
        tri=NPC["tri01_bf"], rmat=NPC["rmat_bf"],
    )
    if layer == 0:
        m["wb0t"] = pack_w(np.asarray(inputs["L0_ffn_W_base"]).T)
        m["b0"] = np.ascontiguousarray(inputs["L0_ffn_b_base"].reshape(32, 128).T)
        m["wgt"] = np.ascontiguousarray(inputs["L0_ffn_W_gate"].T)
        m["a0t"] = np.ascontiguousarray(
            inputs["L0_ffn_A"].transpose(2, 0, 1).reshape(HD, 128))
        m["b0f"] = np.ascontiguousarray(
            inputs["L0_ffn_B"].transpose(0, 2, 1).reshape(128, HD))
        m["kr8"] = NPC["krepl8"]
    else:
        m["wb1t"] = pack_w(np.asarray(inputs["L1_ffn_W_base"]).T)
        m["b1"] = np.ascontiguousarray(inputs["L1_ffn_b_base"].reshape(32, 128).T)
        m["rwt"] = np.ascontiguousarray(inputs["L1_ffn_router_W"].T)
        m["la1t"] = np.ascontiguousarray(
            inputs["L1_ffn_lora_A"].transpose(2, 0, 1).reshape(HD, 128))
        m["lb1f"] = np.ascontiguousarray(
            inputs["L1_ffn_lora_B"].transpose(0, 2, 1).reshape(128, HD))
        m["rb"] = np.ascontiguousarray(inputs["L1_ffn_router_b"].reshape(16, 1))
        m["prior"] = np.ascontiguousarray(inputs["L1_ffn_domain_prior"].reshape(16, 1))
        m["active"] = np.ascontiguousarray(inputs["L1_ffn_active"].reshape(1, 4))
        m["kr16"] = NPC["krepl16x2"]
        m["io16c"] = NPC["iota16c"]
        m["io4r"] = NPC["iota4r"]
    return m


def emit_kbuild(kb):
    S = kb.sems; nc = kb.nc; D = kb.D
    gw, gwb, xpre, scf = kb.sb["gw"], kb.sb["gwb"], kb.sb["xpre"], kb.sb["scf"]
    pA, pB, pD = kb.ps["pA"], kb.ps["pB"], kb.ps["pD"]
    ktile = kb.sb["ktile"]
    C1f = kb.sb["cons1f"]

    LR, LI = gw[0:64, 0:1], gw[0:64, 1:2]
    PR, PI = gw[0:64, 2:3], gw[0:64, 3:4]
    QR, QI = gw[0:64, 4:5], gw[0:64, 5:6]
    CTR, CTI = gw[0:64, 6:7], gw[0:64, 7:8]
    BP = gw[0:64, 8:9]
    NLR, NLI, LR2 = gw[0:64, 9:10], gw[0:64, 10:11], gw[0:64, 11:12]
    BR, BI = gw[0:64, 12:13], gw[0:64, 13:14]
    VV = gw[0:64, 16:24]
    STP = gw[0:1, 32:33]
    RS2 = gw[0:1, 33:34]
    DD = gw[0:1, 34:35]

    kb.op("vector", lambda: nc.vector.memset(xpre[:, 0:32], 0.0), inc=S["vV"])
    zv = S["vV"].n
    kb.wait("gpsimd", S["vV"], zv)
    zkp = kb.dma("gpsimd",
                 lambda: bass.AP(D["kpad"], 0, [[32, 128], [1, 32]]),
                 lambda: xpre[:, 0:32], S["dW"])

    kb.dma("sync", gw[0:64, 0:2], D["lam"].ap(), S["dXQ"])
    kb.dma("sync", gw[0:64, 2:4], D["pri"].ap(), S["dXQ"])
    kb.dma("sync", gw[0:64, 4:6], D["qri"].ap(), S["dXQ"])
    kb.dma("sync", gw[0:64, 6:8], D["ctri"].ap(), S["dXQ"])
    kb.dma("sync", gw[0:64, 8:9], D["bp"].ap(), S["dXQ"])
    kb.dma("sync", gw[0:1, 34:35], D["dd"].ap(), S["dXQ"])
    kb.dma("sync", gw[0:1, 35:36], D["lstep"].ap(), S["dXQ"])
    kb.dma("sync", scf[0:64, 2400:2464], D["vcrt"].ap(), S["dXQ"])
    kb.dma("sync", scf[0:64, 2464:2528], D["vcit"].ap(), S["dXQ"])
    kb.dma("sync", scf[:, 2528:2560], D["ccr"].ap(), S["dXQ"])
    kb.dma("sync", scf[:, 2560:2592], D["cci"].ap(), S["dXQ"])
    kb.dma("sync", scf[0:32, 2592:2624], D["w32r"].ap(), S["dXQ"])
    kb.dma("sync", scf[0:32, 2624:2656], D["w32i"].ap(), S["dXQ"])
    kb.dma("sync", scf[0:32, 2656:2688], D["w32in"].ap(), S["dXQ"])
    kb.dma("sync", scf[:, 2688:2720], D["twr"].ap(), S["dXQ"])
    kb.dma("sync", scf[:, 2720:2752], D["twi"].ap(), S["dXQ"])
    kb.dma("sync", scf[:, 2752:2880], D["c128"].ap(), S["dXQ"])
    kb.dma("sync", scf[:, 2880:3008], D["s128n"].ap(), S["dXQ"])
    pdma = S["dXQ"].n

    kb.wait("scalar", S["dXQ"], pdma)
    kb.op("scalar", lambda: nc.scalar.activation(STP, gw[0:1, 35:36], AF.Exp),
          inc=S["sS"])
    kb.wait("vector", S["sS"], S["sS"].n)
    kb.op("vector", lambda: nc.vector.reciprocal(RS2, STP))
    kb.drain("vector")
    kb.op("vector", lambda: nc.vector.tensor_scalar(RS2, RS2, 2.0, None, op0=ALU.mult))
    kb.drain("vector")
    kb.op("vector", lambda: nc.vector.tensor_scalar(NLR, LR, -1.0, None, op0=ALU.mult))
    kb.op("vector", lambda: nc.vector.tensor_scalar(NLI, LI, -1.0, None, op0=ALU.mult))
    kb.op("vector", lambda: nc.vector.tensor_mul(LR2, LR, LR), inc=S["vV"])
    vpar = S["vV"].n

    kb.wait("tensor", S["dXQ"], pdma)
    kb.op("tensor", lambda: nc.tensor.matmul(pA[0:64, 0:1], scf[0:64, 2400:2464],
          BP, start=True, stop=True))
    kb.op("tensor", lambda: nc.tensor.matmul(pA[0:64, 1:2], scf[0:64, 2464:2528],
          BP, start=True, stop=True), inc=S["mT"])
    kb.wait("vector", S["mT"], S["mT"].n)
    kb.op("vector", lambda: nc.vector.tensor_copy(BR, pA[0:64, 0:1]))
    kb.op("vector", lambda: nc.vector.tensor_copy(BI, pA[0:64, 1:2]))
    kb.drain("vector")
    def cmul_conj(ar, ai, br_, bi_, outr, outi):
        kb.op("vector", lambda: nc.vector.tensor_mul(outr, ar, br_))
        kb.op("vector", lambda: nc.vector.tensor_mul(outi, ar, bi_))
        kb.drain("vector")
        kb.op("vector", lambda: nc.vector.scalar_tensor_tensor(
            gw[0:64, 14:15], ai, 1.0, bi_, op0=ALU.mult, op1=ALU.mult))
        kb.op("vector", lambda: nc.vector.scalar_tensor_tensor(
            gw[0:64, 15:16], ai, 1.0, br_, op0=ALU.mult, op1=ALU.mult))
        kb.drain("vector")
        kb.op("vector", lambda: nc.vector.tensor_add(outr, outr, gw[0:64, 14:15]))
        kb.op("vector", lambda: nc.vector.tensor_sub(outi, outi, gw[0:64, 15:16]))
        kb.drain("vector")
    cmul_conj(CTR, CTI, BR, BI, VV[:, 0:1], VV[:, 1:2])
    cmul_conj(CTR, CTI, PR, PI, VV[:, 2:3], VV[:, 3:4])
    cmul_conj(QR, QI, BR, BI, VV[:, 4:5], VV[:, 5:6])
    cmul_conj(QR, QI, PR, PI, VV[:, 6:7], VV[:, 7:8])
    for j in range(4):
        kb.op("vector", lambda j=j: nc.vector.tensor_copy(
            gwb[0:64, 2 * j:2 * j + 1], VV[:, 2 * j:2 * j + 1]))
        kb.op("vector", lambda j=j: nc.vector.tensor_copy(
            gwb[0:64, 2 * j + 1:2 * j + 2], VV[:, 2 * j + 1:2 * j + 2]))
        kb.op("vector", lambda j=j: nc.vector.tensor_scalar(
            gwb[0:64, 8 + 2 * j:9 + 2 * j], VV[:, 2 * j + 1:2 * j + 2], -1.0,
            None, op0=ALU.mult))
        kb.op("vector", lambda j=j: nc.vector.tensor_copy(
            gwb[0:64, 9 + 2 * j:10 + 2 * j], VV[:, 2 * j:2 * j + 1]))
    kb.wait("vector", S["dW"], zkp)
    kb.op("vector", lambda: nc.vector.drain(), inc=S["vV"])
    vl_done = S["vV"].n

    tr_n = 0
    for kh in range(2):
        if kh == 1:
            kb.wait("sync", S["mT"], tr_n)
        th = kb.dma("sync", scf[0:1, 0:2048],
                    D["tim"].ap()[0:1, 2048 * kh:2048 * (kh + 1)], S["dX0"])
        kb.wait("vector", S["dX0"], th)
        kb.wait("vector", S["vV"], vl_done)
        kb.op("vector", lambda: nc.vector.tensor_scalar(
            scf[0:1, 0:2048], scf[0:1, 0:2048], RS2, None, op0=ALU.mult),
            inc=S["vV"])
        gv = S["vV"].n
        for c in range(4):
            kb.wait("tensor", S["vV"], gv)
            if c > 0:
                kb.wait("tensor", S["vV"], S["vV"].n)
            kb.op("tensor", lambda c=c: nc.tensor.matmul(
                pA[0:64, 0:512], C1f[0:1, 0:64],
                scf[0:1, 512 * c:512 * (c + 1)], start=True, stop=True),
                inc=S["mT"])
            kb.wait("vector", S["mT"], S["mT"].n)
            kb.op("vector", lambda c=c: nc.vector.tensor_single_scalar(
                xpre[0:64, 512 * c:512 * (c + 1)], pA[0:64, 0:512], NLI,
                op=ALU.add), inc=S["vV"])
        dIn = S["vV"].n
        kb.wait("scalar", S["vV"], dIn)
        kb.op("scalar", lambda: nc.scalar.activation(
            scf[0:64, 0:2048], xpre[0:64, 0:2048], AF.Square), inc=S["sS"])
        sqn = S["sS"].n
        kb.wait("vector", S["sS"], sqn)
        kb.op("vector", lambda: nc.vector.tensor_scalar(
            scf[0:64, 0:2048], scf[0:64, 0:2048], LR2, None, op0=ALU.add))
        kb.drain("vector")
        kb.op("vector", lambda: nc.vector.reciprocal(
            scf[0:64, 0:2048], scf[0:64, 0:2048]))
        kb.drain("vector")
        kb.op("vector", lambda: nc.vector.scalar_tensor_tensor(
            xpre[0:64, 0:2048], xpre[0:64, 0:2048], -1.0, scf[0:64, 0:2048],
            op0=ALU.mult, op1=ALU.mult))
        kb.drain("vector")
        kb.op("vector", lambda: nc.vector.tensor_scalar(
            scf[0:64, 0:2048], scf[0:64, 0:2048], NLR, None, op0=ALU.mult),
            inc=S["vV"])
        rdone = S["vV"].n
        for c in range(4):
            kb.wait("tensor", S["vV"], rdone)
            kb.wait("tensor", S["sS"], S["sS"].n)
            kb.op("tensor", lambda c=c: nc.tensor.matmul(
                pB[0:8, 0:512], gwb[0:64, 0:8],
                scf[0:64, 512 * c:512 * (c + 1)], start=True, stop=False))
            kb.op("tensor", lambda c=c: nc.tensor.matmul(
                pB[0:8, 0:512], gwb[0:64, 8:16],
                xpre[0:64, 512 * c:512 * (c + 1)], start=False, stop=True),
                inc=S["mT"])
            kb.wait("scalar", S["mT"], S["mT"].n)
            kb.op("scalar", lambda c=c: nc.scalar.activation(
                xpre[0:8, 512 * c:512 * (c + 1)], pB[0:8, 0:512], AF.Copy),
                inc=S["sS"])
        kmn = S["sS"].n
        kb.wait("tensor", S["sS"], kmn)
        for j in range(16):
            F = 16 * kh + j
            kb.op("tensor", lambda j=j, F=F: nc.tensor.transpose(
                pD[:, 8 * F:8 * F + 8], xpre[0:8, 128 * j:128 * (j + 1)],
                kb.sb["identf"][0:8, 0:8]),
                inc=S["mT"] if j == 15 else None)
        tr_n = S["mT"].n

    kb.wait("vector", S["mT"], tr_n)
    AT = lambda j: pD[:, 8 * 0 + j: 256 + j: 8]
    atR, atI = scf[:, 0:32], scf[:, 32:64]
    T1, T2, T3, T4 = (scf[:, 64:96], scf[:, 96:128], scf[:, 128:160],
                      scf[:, 160:192])
    kb.op("vector", lambda: nc.vector.tensor_scalar(T1, AT(6), 1.0, None, op0=ALU.add))
    kb.op("vector", lambda: nc.vector.tensor_copy(T2, AT(7)))
    kb.drain("vector")
    kb.op("vector", lambda: nc.vector.tensor_mul(T3, T1, T1))
    kb.op("vector", lambda: nc.vector.tensor_mul(T4, T2, T2))
    kb.drain("vector")
    kb.op("vector", lambda: nc.vector.tensor_add(T3, T3, T4))
    kb.drain("vector")
    kb.op("vector", lambda: nc.vector.reciprocal(T3, T3))
    kb.drain("vector")
    QRv, QIv = scf[:, 192:224], scf[:, 224:256]
    kb.op("vector", lambda: nc.vector.tensor_mul(QRv, AT(4), T1))
    kb.op("vector", lambda: nc.vector.tensor_mul(QIv, AT(5), T1))
    kb.drain("vector")
    kb.op("vector", lambda: nc.vector.scalar_tensor_tensor(
        T4, AT(5), 1.0, T2, op0=ALU.mult, op1=ALU.mult))
    kb.drain("vector")
    kb.op("vector", lambda: nc.vector.tensor_add(QRv, QRv, T4))
    kb.drain("vector")
    kb.op("vector", lambda: nc.vector.scalar_tensor_tensor(
        T4, AT(4), 1.0, T2, op0=ALU.mult, op1=ALU.mult))
    kb.drain("vector")
    kb.op("vector", lambda: nc.vector.tensor_sub(QIv, QIv, T4))
    kb.drain("vector")
    kb.op("vector", lambda: nc.vector.tensor_mul(QRv, QRv, T3))
    kb.op("vector", lambda: nc.vector.tensor_mul(QIv, QIv, T3))
    kb.drain("vector")
    kb.op("vector", lambda: nc.vector.tensor_mul(T1, AT(2), QRv))
    kb.op("vector", lambda: nc.vector.tensor_mul(T2, AT(3), QIv))
    kb.op("vector", lambda: nc.vector.tensor_mul(T3, AT(2), QIv))
    kb.op("vector", lambda: nc.vector.tensor_mul(T4, AT(3), QRv))
    kb.drain("vector")
    kb.op("vector", lambda: nc.vector.tensor_sub(T1, T1, T2))
    kb.op("vector", lambda: nc.vector.tensor_add(T3, T3, T4))
    kb.drain("vector")
    kb.op("vector", lambda: nc.vector.tensor_sub(T1, AT(0), T1))
    kb.op("vector", lambda: nc.vector.tensor_sub(T3, AT(1), T3))
    kb.drain("vector")
    kb.op("vector", lambda: nc.vector.tensor_mul(atR, scf[:, 2528:2560], T1))
    kb.op("vector", lambda: nc.vector.tensor_mul(atI, scf[:, 2528:2560], T3))
    kb.drain("vector")
    kb.op("vector", lambda: nc.vector.scalar_tensor_tensor(
        T2, scf[:, 2560:2592], 1.0, T3, op0=ALU.mult, op1=ALU.mult))
    kb.op("vector", lambda: nc.vector.scalar_tensor_tensor(
        T4, scf[:, 2560:2592], 1.0, T1, op0=ALU.mult, op1=ALU.mult))
    kb.drain("vector")
    kb.op("vector", lambda: nc.vector.tensor_sub(atR, atR, T2))
    kb.op("vector", lambda: nc.vector.tensor_add(atI, atI, T4), inc=S["vV"])
    at_done = S["vV"].n

    kb.wait("tensor", S["vV"], at_done)
    kb.op("tensor", lambda: nc.tensor.transpose(pA[0:32, 0:128], atR,
          kb.sb["identf"][:]))
    kb.op("tensor", lambda: nc.tensor.transpose(pA[0:32, 128:256], atI,
          kb.sb["identf"][:]), inc=S["mT"])
    kb.wait("scalar", S["mT"], S["mT"].n)
    kb.op("scalar", lambda: nc.scalar.activation(scf[0:32, 256:384],
          pA[0:32, 0:128], AF.Copy))
    kb.op("scalar", lambda: nc.scalar.activation(scf[0:32, 384:512],
          pA[0:32, 128:256], AF.Copy), inc=S["sS"])
    art_done = S["sS"].n
    kb.wait("tensor", S["sS"], art_done)
    kb.op("tensor", lambda: nc.tensor.matmul(pB[:, 0:32], scf[0:32, 256:384],
          scf[0:32, 2592:2624], start=True, stop=False))
    kb.op("tensor", lambda: nc.tensor.matmul(pB[:, 0:32], scf[0:32, 384:512],
          scf[0:32, 2656:2688], start=False, stop=True))
    kb.op("tensor", lambda: nc.tensor.matmul(pB[:, 32:64], scf[0:32, 256:384],
          scf[0:32, 2624:2656], start=True, stop=False))
    kb.op("tensor", lambda: nc.tensor.matmul(pB[:, 32:64], scf[0:32, 384:512],
          scf[0:32, 2592:2624], start=False, stop=True), inc=S["mT"])
    kb.wait("scalar", S["mT"], S["mT"].n)
    kb.op("scalar", lambda: nc.scalar.activation(scf[:, 512:544], pB[:, 0:32],
          AF.Copy))
    kb.op("scalar", lambda: nc.scalar.activation(scf[:, 544:576], pB[:, 32:64],
          AF.Copy), inc=S["sS"])
    h_done = S["sS"].n
    kb.wait("vector", S["sS"], h_done)
    HR, HI = scf[:, 512:544], scf[:, 544:576]
    HpR, HpI = scf[:, 576:608], scf[:, 608:640]
    kb.op("vector", lambda: nc.vector.tensor_mul(HpR, HR, scf[:, 2688:2720]))
    kb.op("vector", lambda: nc.vector.tensor_mul(HpI, HR, scf[:, 2720:2752]))
    kb.drain("vector")
    kb.op("vector", lambda: nc.vector.scalar_tensor_tensor(
        T1, HI, 1.0, scf[:, 2720:2752], op0=ALU.mult, op1=ALU.mult))
    kb.op("vector", lambda: nc.vector.scalar_tensor_tensor(
        T2, HI, 1.0, scf[:, 2688:2720], op0=ALU.mult, op1=ALU.mult))
    kb.drain("vector")
    kb.op("vector", lambda: nc.vector.tensor_sub(HpR, HpR, T1))
    kb.op("vector", lambda: nc.vector.tensor_add(HpI, HpI, T2), inc=S["vV"])
    hp_done = S["vV"].n
    kb.wait("tensor", S["vV"], hp_done)
    kb.op("tensor", lambda: nc.tensor.matmul(pA[:, 0:32], scf[:, 2752:2880],
          HpR, start=True, stop=False))
    kb.op("tensor", lambda: nc.tensor.matmul(pA[:, 0:32], scf[:, 2880:3008],
          HpI, start=False, stop=True), inc=S["mT"])
    k2_done = S["mT"].n
    kb.wait("scalar", S["mT"], k2_done)
    kb.op("scalar", lambda: nc.scalar.activation(ktile[:], pA[:, 0:32],
          AF.Copy, scale=1.0 / 4096.0), inc=S["sS"])
    kt_done = S["sS"].n
    kb.wait("vector", S["sS"], kt_done)
    kb.op("vector", lambda: nc.vector.tensor_scalar(
        ktile[0:1, 0:1], ktile[0:1, 0:1], DD, None, op0=ALU.add), inc=S["vV"])
    kd = S["vV"].n
    kb.wait("gpsimd", S["vV"], kd)
    kb.wait("gpsimd", S["dW"], zkp)
    kpd = kb.dma("gpsimd",
                 lambda: bass.AP(D["kpad"], 4096, [[32, 128], [1, 32]]),
                 lambda: ktile[:], S["dW"])
    return kpd


def emit_conv(kb, inbuf, outbuf, resid_load, normed, scr_out):
    S = kb.sems; nc = kb.nc; D = kb.D
    wsb, rmat = kb.sb["wsb"], kb.sb["rmat"]
    pA, pB = kb.ps["pA"], kb.ps["pB"]

    kpd = emit_kbuild(kb)

    kb.wait("tensor", S["sS"], S["sS"].n)
    fl_mm = [0, 0]; fl_ev = [0, 0]
    for j in range(NT):
        sl = j % 2
        pt = kb.ps["pA" if sl == 0 else "pB"]
        if j >= 2:
            kb.wait("tensor", S["sS"], fl_ev[sl])
        for half in range(2):
            kb.op("tensor", lambda j=j, half=half, pt=pt: nc.tensor.matmul(
                pt[:, half * 512:(half + 1) * 512], rmat[:],
                inbuf[:, j * 1024 + half * 512: j * 1024 + (half + 1) * 512],
                start=True, stop=True),
                inc=S["mT"] if half == 1 else None)
        fl_mm[sl] = S["mT"].n
        kb.wait("scalar", S["mT"], fl_mm[sl])
        kb.op("scalar", lambda j=j, pt=pt: nc.scalar.activation(
            inbuf[:, j * 1024:(j + 1) * 1024], pt[:], AF.Copy), inc=S["sS"])
        fl_ev[sl] = S["sS"].n
    flip_done = S["sS"].n

    kb.wait("sync", S["dW"], kpd)
    bnd = kb.dma("sync", wsb[:, 0:4096],
                 lambda: bass.AP(D["kpad"], 3969, [[1, 128], [128, 32], [1, 128]]),
                 S["dX1"])

    def mm_emit(kb2, t, pt):
        if t == 0:
            kb.wait("tensor", S["dX1"], bnd)
            kb.wait("tensor", S["sS"], flip_done)
        for j in range(t + 1):
            for half in range(2):
                kb.op("tensor", lambda t=t, j=j, half=half, pt=pt:
                      nc.tensor.matmul(
                          pt[:, half * 512:(half + 1) * 512],
                          wsb[:, (t - j) * 128:(t - j + 1) * 128],
                          inbuf[:, j * 1024 + half * 512:
                                j * 1024 + (half + 1) * 512],
                          start=(j == 0), stop=(j == t)),
                      inc=S["mT"] if (j == t and half == 1) else None)
        return S["mT"].n

    emit_evac_loop(kb, NT, mm_emit, resid_load, normed, outbuf, scr_out)


def build_conv_phase_program():
    nc = bass.Bass("TRN2", target_bir_lowering=False)
    ctx = ExitStack()
    with ctx:
        kb = KB(nc, ctx)
        setup_common(kb, {"p2"})
        kb.D["xnin"] = nc.dram_tensor("xnin", [HD, TPC], F32, kind="ExternalInput")
        load_consts(kb, {"p2"})
        emit_cast_load_buf(kb, kb.D["xnin"], kb.sb["bufA"])
        kb.wait("tensor", kb.sems["dW"], kb.sems["dW"].n)
        kb.wait("tensor", kb.sems["dXQ"], kb.sems["dXQ"].n)
        emit_conv(kb, kb.sb["bufA"], kb.sb["bufB"],
                  resid_from_dram(kb, kb.D["xnin"]), False, kb.D["scr0"])
        emit_stats_finalize(kb)
        emit_final_output(kb, kb.D["scr0"], kb.D["out"])
        kb.finalize_block()
    return nc


def conv_phase_inputs(inputs):
    m = dict(
        ones_bf=NPC["ones_bf"], ones1f=NPC["ones1x128_f"], onesm=NPC["onesm"],
        identb=NPC["ident_bf"], identf=NPC["ident_f"],
        tri=NPC["tri01_bf"], rmat=NPC["rmat_bf"],
        lam=np.ascontiguousarray(inputs["L1_attn_Lambda_ri"]),
        pri=np.ascontiguousarray(inputs["L1_attn_p_ri"]),
        qri=np.ascontiguousarray(inputs["L1_attn_q_ri"]),
        vcrt=np.ascontiguousarray(inputs["L1_attn_Vc_ri"][:, :, 0].T),
        vcit=np.ascontiguousarray(inputs["L1_attn_Vc_ri"][:, :, 1].T),
        ctri=np.ascontiguousarray(inputs["L1_attn_Ct_ri"].reshape(64, 2)),
        bp=np.ascontiguousarray(inputs["L1_attn_Bp"]),
        dd=np.ascontiguousarray(inputs["L1_attn_D"].reshape(1, 1)),
        lstep=np.ascontiguousarray(inputs["L1_attn_log_step"].reshape(1, 1)),
        tim=NPC["tim"], ccr=NPC["ccr"], cci=NPC["cci"],
        w32r=NPC["w32r"], w32i=NPC["w32i"], w32in=NPC["w32in"],
        twr=NPC["twr"], twi=NPC["twi"], c128=NPC["c128"], s128n=NPC["s128n"],
    )
    return m


ATTN_HEADS = NHEADS
ATTN_SKIP_SWITCH = False
ATTN_PARTS = 8


def emit_attn(kb, outbuf, AO, resid_load, scr_out):
    S = kb.sems; nc = kb.nc; D = kb.D
    wsb, wpool = kb.sb["wsb"], kb.sb["wpool"]
    qsb, attnT, rbcs = kb.sb["qsb"], kb.sb["attnT"], kb.sb["rbcs"]
    biastab, tri = kb.sb["biastab"], kb.sb["tri"]
    scf = kb.sb["scf"]
    pA, pB, pC, pD = kb.ps["pA"], kb.ps["pB"], kb.ps["pC"], kb.ps["pD"]
    x0buf = outbuf

    xsb = [wsb[:, 0:1024], wsb[:, 1024:2048]]
    wkvs = [wsb[:, 2048:2304], wsb[:, 2304:2560]]
    kv_dma = [0, 0]; kv_mm = [0, 0]
    ev_k = 0; ev_v = 0; tr_ev = [0, 0]
    for kc in range(4):
        for ht in range(NT):
            sl = ht % 2
            if kc + ht > 0:
                kb.wait("gpsimd", S["mT"], kv_mm[sl])
            ksem = S["dK0" if sl == 0 else "dK1"]
            kb.dma("gpsimd", lambda sl=sl: xsb[sl],
                   lambda kc=kc, ht=ht: D["xT"].ap()[128 * ht:128 * (ht + 1),
                                                     1024 * kc:1024 * (kc + 1)],
                   ksem)
            kv_dma[sl] = kb.dma("gpsimd", lambda sl=sl: wkvs[sl],
                                lambda ht=ht: D["wkv"].ap()[128 * ht:128 * (ht + 1), :],
                                ksem)
            kb.wait("tensor", ksem, kv_dma[sl])
            if ht == 0:
                kb.wait("tensor", S["sS"], ev_v)
            for half in range(2):
                kb.op("tensor", lambda sl=sl, half=half, ht=ht: nc.tensor.matmul(
                    pA[:, half * 512:(half + 1) * 512], wkvs[sl][:, 0:128],
                    xsb[sl][:, half * 512:(half + 1) * 512],
                    start=(ht == 0), stop=(ht == NT - 1)))
            for half in range(2):
                kb.op("tensor", lambda sl=sl, half=half, ht=ht: nc.tensor.matmul(
                    pB[:, half * 512:(half + 1) * 512], wkvs[sl][:, 128:256],
                    xsb[sl][:, half * 512:(half + 1) * 512],
                    start=(ht == 0), stop=(ht == NT - 1)),
                    inc=S["mT"] if half == 1 else None)
            kv_mm[sl] = S["mT"].n
        kb.wait("scalar", S["mT"], kv_mm[(NT - 1) % 2])
        kb.op("scalar", lambda kc=kc: nc.scalar.activation(
            wpool[:, 1024 * kc:1024 * (kc + 1)], pA[:], AF.Copy))
        kb.op("scalar", lambda: nc.scalar.activation(
            scf[:, 0:1024], pB[:], AF.Copy), inc=S["sS"])
        ev_v = S["sS"].n
        for i in range(8):
            tsl = i % 2
            kb.wait("tensor", S["sS"], max(ev_v, tr_ev[tsl]))
            kb.op("tensor", lambda i=i, tsl=tsl: nc.tensor.transpose(
                pD[:, 512 * tsl:512 * tsl + 128], scf[:, 128 * i:128 * (i + 1)],
                kb.sb["identf"][:]), inc=S["mT"])
            kb.wait("scalar", S["mT"], S["mT"].n)
            kb.op("scalar", lambda kc=kc, i=i, tsl=tsl: nc.scalar.activation(
                wpool[:, 4096 + (8 * kc + i) * 128: 4096 + (8 * kc + i + 1) * 128],
                pD[:, 512 * tsl:512 * tsl + 128], AF.Copy), inc=S["sS"])
            tr_ev[tsl] = S["sS"].n
    kv_done_s = S["sS"].n

    def emit_variant(v):
        units = attn_units(v)
        by_head = {}
        for uid, d in enumerate(units):
            by_head.setdefault(d["h"], []).append((uid, d))
        Q = [512 * v, 3584 - 512 * v]
        q_ev = [kv_done_s, kv_done_s]
        q_ev_last = kv_done_s
        att_done = [0, 0]
        exp_n = [0, 0]
        trv_n = [0, 0]
        avden = [0, 0]
        avev = [S["vV"].n, S["vV"].n]
        rbcev = [kv_done_s, kv_done_s]
        qq_mm = [S["mT"].n] * 4
        useq = 0
        for h in range(ATTN_HEADS):
            hsl = h % 2
            for q in range(4):
                kb.wait("gpsimd", S["mT"], qq_mm[q])
                qd = kb.dma("gpsimd", lambda q=q: wsb[:, q * 1024:(q + 1) * 1024],
                            lambda h=h, q=q: D["wq"].ap()[h, q], S["dQ" + str(q)])
                kb.wait("tensor", S["dQ" + str(q)], qd)
                if q == 0:
                    kb.wait("tensor", S["sS"], q_ev_last)
                for j in range(8):
                    u = 8 * q + j
                    for half in range(2):
                        kb.op("tensor", lambda u=u, half=half: nc.tensor.matmul(
                            pC[:, half * 512:(half + 1) * 512],
                            wsb[:, u * 128:(u + 1) * 128],
                            x0buf[:, u * 1024 + half * 512: u * 1024 + (half + 1) * 512],
                            start=(u == 0), stop=(u == NT - 1)),
                            inc=S["mT"] if (j == 7 and half == 1) else None)
                qq_mm[q] = S["mT"].n
            qmm = S["mT"].n
            kb.wait("scalar", S["mT"], max(qmm, att_done[hsl]))
            kb.op("scalar", lambda hsl=hsl: nc.scalar.activation(
                qsb[:, 1024 * hsl:1024 * (hsl + 1)], pC[:], AF.Copy), inc=S["sS"])
            q_ev[hsl] = S["sS"].n
            q_ev_last = S["sS"].n
            blocks = {}
            for uid, d in by_head[h]:
                blocks.setdefault(d["b"], []).append((uid, d))
            for b, ulist in sorted(blocks.items()):
                bcol = 512 * b
                first = True
                nu = len(ulist)
                for i, (uid, d) in enumerate(ulist):
                    last = (i == nu - 1)
                    usl = useq % 2
                    useq += 1
                    w = d["w"]
                    if i == 0:
                        kb.wait("tensor", S["vV"], max(avev[0], avev[1]))
                        kb.wait("tensor", S["sS"], max(rbcev[0], rbcev[1]))
                    kb.wait("tensor", S["sS"], q_ev[hsl])
                    kb.wait("tensor", S["aRs"], exp_n[usl])
                    kb.op("tensor", lambda d=d, usl=usl, w=w, hsl=hsl, b=b, Q=Q:
                          nc.tensor.matmul(
                              pA[:, 512 * usl:512 * usl + w],
                              wpool[:, d["t"] * 128:(d["t"] + 1) * 128],
                              qsb[:, 1024 * hsl + 512 * b + (d["A"] + d["off"] - Q[b]):
                                  1024 * hsl + 512 * b + (d["A"] + d["off"] - Q[b]) + w],
                              start=True, stop=True), inc=S["mT"])
                    sc_n = S["mT"].n
                    if ATTN_PARTS < 2:
                        exp_n[usl] = S["aRs"].n
                        first = False
                        continue
                    kb.wait("scalar", S["mT"], max(sc_n, avden[usl]))
                    kb.op("scalar", lambda uid=uid, usl=usl, w=w:
                          nc.scalar.activation(
                              attnT[:, 512 * usl:512 * usl + w],
                              pA[:, 512 * usl:512 * usl + w], AF.Exp,
                              bias=biastab[:, uid:uid + 1], scale=ASCALE),
                          inc=S["aRs"])
                    exp_n[usl] = S["aRs"].n
                    if ATTN_PARTS < 3:
                        first = False
                        continue
                    if ATTN_PARTS >= 3 and d["diag"]:
                        kb.wait("vector", S["aRs"], exp_n[usl])
                        kb.op("vector", lambda d=d, usl=usl:
                              nc.vector.tensor_mul(
                                  attnT[:, 512 * usl:512 * usl + d["tri_w"]],
                                  attnT[:, 512 * usl:512 * usl + d["tri_w"]],
                                  tri[:, d["tri_shift"]:d["tri_shift"] + d["tri_w"]]),
                              inc=S["aRv"])
                        trv_n[usl] = S["aRv"].n
                    if ATTN_PARTS < 4:
                        first = False
                        continue
                    kb.wait("tensor", S["aRs"], exp_n[usl])
                    if d["diag"]:
                        kb.wait("tensor", S["aRv"], trv_n[usl])
                    if first:
                        kb.wait("tensor", S["vV"], max(avev[0], avev[1]))
                        kb.wait("tensor", S["sS"], max(rbcev[0], rbcev[1]))
                    col0 = bcol + (d["A"] + d["off"] - Q[b])
                    kb.op("tensor", lambda d=d, usl=usl, w=w, col0=col0, b=b, first=first, last=last:
                          nc.tensor.matmul(
                              pB[:, col0 - 512 * b:col0 - 512 * b + w],
                              wpool[:, 4096 + d["t"] * 128: 4096 + (d["t"] + 1) * 128],
                              attnT[:, 512 * usl:512 * usl + w],
                              start=first, stop=last))
                    dcol = col0 - 512 * b
                    kb.op("tensor", lambda usl=usl, w=w, dcol=dcol, first=first, last=last:
                          nc.tensor.matmul(
                              pD[:, dcol:dcol + w],
                              kb.sb["onesm"][:],
                              attnT[:, 512 * usl:512 * usl + w],
                              start=first, stop=last), inc=S["mT"])
                    avden[usl] = S["mT"].n
                    first = False
                att_done[hsl] = S["mT"].n
                avsl = scf[:, 1024 * b:1024 * b + 512]
                dnsl = scf[:, 1024 * b + 512:1024 * b + 1024]
                rcp = scf[:, 2048:2560]
                kb.wait("scalar", S["mT"], att_done[hsl])
                kb.wait("scalar", S["vV"], avev[b])
                kb.op("scalar", lambda avsl=avsl: nc.scalar.activation(
                    avsl, pB[:, 0:512], AF.Copy))
                kb.op("scalar", lambda dnsl=dnsl: nc.scalar.activation(
                    dnsl, pD[:, 0:512], AF.Copy), inc=S["sS"])
                rbcev[b] = S["sS"].n
                kb.wait("vector", S["sS"], rbcev[b])
                kb.op("vector", lambda dnsl=dnsl, rcp=rcp:
                      nc.vector.reciprocal(rcp, dnsl))
                kb.drain("vector")
                kb.op("vector", lambda h=h, b=b, avsl=avsl, rcp=rcp:
                      nc.vector.tensor_mul(
                          AO[:, 1024 * h + 512 * b: 1024 * h + 512 * (b + 1)],
                          avsl, rcp), inc=S["vV"])
                avev[b] = S["vV"].n

    if not ATTN_SKIP_SWITCH:
        kb.switch(4, emit_variant, ["tensor", "vector", "scalar", "gpsimd"])

    kb.wait("tensor", S["vV"], S["vV"].n)
    mm_emit = make_wstream_mm_emit(kb, D["wo"], AO, extra=None)
    emit_evac_loop(kb, NT, mm_emit, resid_load, False, outbuf, scr_out)


def build_attn_phase_program():
    nc = bass.Bass("TRN2", target_bir_lowering=False)
    ctx = ExitStack()
    with ctx:
        kb = KB(nc, ctx)
        setup_common(kb, {"p0"})
        load_consts(kb, {"p0"})
        emit_cast_load_buf(kb, kb.D["x0p"], kb.sb["bufA"])
        kb.wait("tensor", kb.sems["dW"], kb.sems["dW"].n)
        kb.wait("tensor", kb.sems["dXQ"], kb.sems["dXQ"].n)
        emit_attn(kb, kb.sb["bufA"], kb.sb["bufB"],
                  resid_from_dram(kb, kb.D["x0p"]), kb.D["scr0"])
        emit_stats_finalize(kb)
        emit_final_output(kb, kb.D["scr0"], kb.D["out"])
        kb.finalize_block()
    return nc


def attn_phase_inputs(inputs, core, x_full_T=None):
    if x_full_T is None:
        x_full_T = np.ascontiguousarray(
            inputs["L0_attn_hidden_states"].transpose(0, 2, 1))
    batch = core // 4
    v = core % 4
    lo, hi = 512 * v, 3584 - 512 * v
    xT = x_full_T[batch]
    x0p = np.concatenate([xT[:, lo:lo + 512], xT[:, hi:hi + 512]], axis=1)
    m = dict(
        ones_bf=NPC["ones_bf"], ones1f=NPC["ones1x128_f"], onesm=NPC["onesm"],
        identb=NPC["ident_bf"], identf=NPC["ident_f"],
        tri=NPC["tri01_bf"], rmat=NPC["rmat_bf"],
        x0p=np.ascontiguousarray(x0p), xT=np.ascontiguousarray(xT),
        wq=pack_w(inputs["L0_attn_Wq"]),
        wkv=np.ascontiguousarray(inputs["L0_attn_Wkv"]),
        wo=pack_w(inputs["L0_attn_Wo"]),
        biastab=bias_table(core),
    )
    return m


_PROG_CACHE = {}


def _get_prog(name, builder):
    if name not in _PROG_CACHE:
        _PROG_CACHE[name] = builder()
    return _PROG_CACHE[name]


def kernel_unfused(**inputs):
    x_full_T = np.ascontiguousarray(
        np.asarray(inputs["L0_attn_hidden_states"]).transpose(0, 2, 1))
    cores = list(range(8))

    nc0 = _get_prog("p0", build_attn_phase_program)
    maps = [attn_phase_inputs(inputs, c, x_full_T) for c in cores]
    r = run_bass_kernel_spmd(nc0, maps, core_ids=cores)
    xn = [np.asarray(r.results[c]["out"]) for c in cores]

    nc1 = _get_prog("p1", lambda: build_ffn_phase_program(0))
    maps = []
    for c in cores:
        m = ffn_phase_inputs(0, inputs, c)
        m["xnin"] = xn[c]
        maps.append(m)
    r = run_bass_kernel_spmd(nc1, maps, core_ids=cores)
    xn = [np.asarray(r.results[c]["out"]) for c in cores]

    nc2 = _get_prog("p2", build_conv_phase_program)
    maps = []
    for c in cores:
        m = conv_phase_inputs(inputs)
        m["xnin"] = xn[c]
        maps.append(m)
    r = run_bass_kernel_spmd(nc2, maps, core_ids=cores)
    xn = [np.asarray(r.results[c]["out"]) for c in cores]

    nc3 = _get_prog("p3", lambda: build_ffn_phase_program(1))
    maps = []
    for c in cores:
        m = ffn_phase_inputs(1, inputs, c)
        m["xnin"] = xn[c]
        maps.append(m)
    r = run_bass_kernel_spmd(nc3, maps, core_ids=cores)

    yT = np.empty((2, HD, HD), dtype=np.float32)
    for c in cores:
        out = np.asarray(r.results[c]["out"])
        b, v = c // 4, c % 4
        lo, hi = 512 * v, 3584 - 512 * v
        yT[b][:, lo:lo + 512] = out[:, 0:512]
        yT[b][:, hi:hi + 512] = out[:, 512:1024]
    return np.ascontiguousarray(yT.transpose(0, 2, 1))


def _phase_barrier(kb):
    S = kb.sems
    snap = [(n, s.n) for n, s in S.items()]
    for eng in ENGS:
        for n, v in snap:
            if v > 0:
                kb.wait(eng, S[n], v)


def build_fused_program():
    nc = bass.Bass("TRN2", target_bir_lowering=False)
    ctx = ExitStack()
    with ctx:
        kb = KB(nc, ctx)
        phases = {"p0", "p1", "p2", "p3"}
        setup_common(kb, phases)
        load_consts(kb, phases)
        D = kb.D
        bufA, bufB = kb.sb["bufA"], kb.sb["bufB"]
        emit_cast_load_buf(kb, D["x0p"], bufA)
        kb.wait("tensor", kb.sems["dW"], kb.sems["dW"].n)
        kb.wait("tensor", kb.sems["dXQ"], kb.sems["dXQ"].n)
        emit_attn(kb, bufA, bufB, resid_from_dram(kb, D["x0p"]), D["scr0"])
        emit_stats_finalize(kb)
        emit_norm_pass(kb, bufA)
        _phase_barrier(kb)
        emit_ffn(kb, 0, bufA, bufB, resid_from_scr(kb, D["scr0"]), True, D["scr1"])
        emit_stats_finalize(kb)
        emit_norm_pass(kb, bufB)
        _phase_barrier(kb)
        emit_conv(kb, bufB, bufA, resid_from_scr(kb, D["scr1"]), True, D["scr0"])
        emit_stats_finalize(kb)
        emit_norm_pass(kb, bufA)
        _phase_barrier(kb)
        emit_ffn(kb, 1, bufA, bufB, resid_from_scr(kb, D["scr0"]), True, D["scr1"])
        emit_stats_finalize(kb)
        _phase_barrier(kb)
        emit_final_output(kb, D["scr1"], D["out"])
        kb.finalize_block()
    return nc


def resid_from_scr(kb, dram):
    return resid_from_dram(kb, dram)


def fused_inputs(inputs, core, x_full_T):
    m = attn_phase_inputs(inputs, core, x_full_T)
    m.update(ffn_phase_inputs(0, inputs, core))
    m.update(ffn_phase_inputs(1, inputs, core))
    m.update(conv_phase_inputs(inputs))
    return m


def kernel(**inputs):
    x_full_T = np.ascontiguousarray(
        np.asarray(inputs["L0_attn_hidden_states"]).transpose(0, 2, 1))
    cores = list(range(8))
    ncf = _get_prog("fused", build_fused_program)
    maps = [fused_inputs(inputs, c, x_full_T) for c in cores]
    r = run_bass_kernel_spmd(ncf, maps, core_ids=cores)
    yT = np.empty((2, HD, HD), dtype=np.float32)
    for c in cores:
        out = np.asarray(r.results[c]["out"])
        b, v = c // 4, c % 4
        lo, hi = 512 * v, 3584 - 512 * v
        yT[b][:, lo:lo + 512] = out[:, 0:512]
        yT[b][:, hi:hi + 512] = out[:, 512:1024]
    return np.ascontiguousarray(yT.transpose(0, 2, 1))


# version 04
# speedup vs baseline: 1.2920x; 1.2292x over previous
import numpy as np
import ml_dtypes
from contextlib import ExitStack

import concourse.bass as bass
import concourse.mybir as mybir
from concourse.bass_utils import run_bass_kernel_spmd

BF = ml_dtypes.bfloat16
F32 = mybir.dt.float32
BF16 = mybir.dt.bfloat16
I32 = mybir.dt.int32
AF = mybir.ActivationFunctionType
ALU = mybir.AluOpType
AX = mybir.AxisListType

HD = 4096
NT = 32
TPC = 1024
NHEADS = 32
DH = 128
ASCALE = float(DH) ** -0.5
EPS_LN = 1e-5

_start = 2.0 ** (-(2.0 ** -(np.log2(NHEADS) - 3)))
SLOPES = np.array([_start ** (i + 1) for i in range(NHEADS)], dtype=np.float64)

def _chunk_for(slope):
    for c in (512, 256, 128, 64):
        if slope * (c - 1) <= 65.0:
            return c
    return 64

CHUNK = [_chunk_for(s) for s in SLOPES]
CUTOFF = [130.0 / s for s in SLOPES]


def _np_consts():
    c = {}
    c["ones_bf"] = np.ones((128, 1), dtype=BF)
    c["ones1x128_f"] = np.ones((1, 128), dtype=np.float32)
    c["ident_bf"] = np.eye(128, dtype=BF)
    c["ident_f"] = np.eye(128, dtype=np.float32)
    c["tri01_bf"] = np.triu(np.ones((128, 128), dtype=np.float32)).astype(BF)
    c["rmat_bf"] = np.eye(128, dtype=np.float32)[::-1].copy().astype(BF)
    c["onesm"] = np.ones((128, 128), dtype=BF)
    c["krepl8"] = np.repeat(np.eye(8, dtype=np.float32), 16, axis=1).astype(BF)
    c["krepl16x2"] = (2.0 * np.repeat(np.eye(16, dtype=np.float32), 8, axis=1)).astype(BF)
    c["iota16c"] = np.tile(np.arange(16, dtype=np.float32)[:, None], (1, 4))
    c["iota4r"] = np.tile(np.arange(16, dtype=np.float32)[None, :], (4, 1))
    a32 = np.arange(32)
    w32 = np.exp(2j * np.pi * np.outer(a32, a32) / 32.0)
    c["w32r"] = w32.real.astype(np.float32)
    c["w32i"] = w32.imag.astype(np.float32)
    c["w32in"] = (-w32.imag).astype(np.float32)
    p128 = np.arange(128)
    tw = np.exp(2j * np.pi * np.outer(p128, a32) / 4096.0)
    c["twr"] = tw.real.astype(np.float32)
    c["twi"] = tw.imag.astype(np.float32)
    w128 = np.exp(2j * np.pi * np.outer(p128, p128) / 128.0)
    c["c128"] = w128.real.astype(np.float32)
    c["s128n"] = (-w128.imag).astype(np.float32)
    k = (np.arange(32)[None, :] * 128 + np.arange(128)[:, None]).astype(np.float64)
    om = np.exp(-2j * np.pi * k / 4096.0)
    cc = 2.0 / (1.0 + om)
    c["ccr"] = cc.real.astype(np.float32)
    c["cci"] = cc.imag.astype(np.float32)
    tim = np.tan(np.pi * np.arange(4096, dtype=np.float64) / 4096.0)
    c["tim"] = tim[None, :].astype(np.float32)
    return c

NPC = _np_consts()


def attn_units(variant):
    k = variant
    Q = [512 * k, 3584 - 512 * k]
    units = []
    for h in range(NHEADS):
        c = CHUNK[h]
        cut = CUTOFF[h]
        for b in (0, 1):
            q0 = Q[b]
            for ch in range(512 // c):
                A = q0 + ch * c
                t_hi = (A + c - 1) // 128
                t_lo = max(0, int(np.floor((A - cut) / 128.0)))
                for t in range(t_lo, t_hi + 1):
                    off = max(0, t * 128 - A)
                    w = c - off
                    diag = (t * 128 + 127 >= A)
                    tri_shift = max(0, A - t * 128) if diag else 0
                    tri_w = min(128 - tri_shift, w) if diag else 0
                    units.append(dict(h=h, b=b, A=A, c=c, t=t, off=off, w=w,
                                      diag=diag, tri_shift=tri_shift,
                                      tri_w=tri_w))
    return units

N_UNITS = [len(attn_units(v)) for v in range(4)]
NU_MAX = max(N_UNITS)


def pack_w(w):
    return np.ascontiguousarray(
        np.asarray(w).reshape(4, 8, 128, 32, 128).transpose(3, 0, 2, 1, 4))


def bias_table(core):
    v = core % 4
    units = attn_units(v)
    tab = np.zeros((128, NU_MAX), dtype=np.float32)
    p = np.arange(128, dtype=np.float64)
    for u, d in enumerate(units):
        kref = d["A"] + d["c"] - 1
        tab[:, u] = (SLOPES[d["h"]] * (d["t"] * 128 + p - kref)).astype(np.float32)
    return tab


ENGS = ("tensor", "vector", "scalar", "sync", "gpsimd")
ETYPE = {"tensor": mybir.EngineType.PE, "vector": mybir.EngineType.DVE,
         "scalar": mybir.EngineType.Activation, "sync": mybir.EngineType.SP,
         "gpsimd": mybir.EngineType.Pool}


class Sem:
    __slots__ = ("h", "n", "name")
    def __init__(self, h, name):
        self.h = h
        self.n = 0
        self.name = name


class KB:
    def __init__(self, nc, ctx):
        self.nc = nc
        self.ctx = ctx
        self.streams = {e: [] for e in ENGS}
        self._stack = []
        self.sems = {}
        self.sb = {}
        self.ps = {}

    def sem(self, name):
        s = Sem(self.ctx.enter_context(self.nc.semaphore(name)), name)
        self.sems[name] = s
        return s

    def sbuf(self, name, shape, dt):
        t = self.ctx.enter_context(self.nc.sbuf_tensor(name + "_s", shape, dt))
        self.sb[name] = t
        return t

    def psum(self, name, shape, dt):
        t = self.ctx.enter_context(self.nc.psum_tensor(name + "_p", shape, dt))
        self.ps[name] = t
        return t

    def emit(self, eng, fn):
        self.streams[eng].append(fn)

    def op(self, eng, fn, inc=None, inc2=None):
        if inc is not None:
            sem, k = inc if isinstance(inc, tuple) else (inc, 1)
            sem.n += k
            if inc2 is not None:
                s2, k2 = inc2 if isinstance(inc2, tuple) else (inc2, 1)
                s2.n += k2
                self.emit(eng, lambda: fn().then_inc(sem.h, k).then_inc(s2.h, k2))
            else:
                self.emit(eng, lambda: fn().then_inc(sem.h, k))
            return sem.n
        self.emit(eng, fn)
        return None

    def wait(self, eng, sem, val=None):
        v = sem.n if val is None else val
        h = sem.h
        eobj = getattr(self.nc, eng)
        self.emit(eng, lambda: eobj.wait_ge(h, v))

    def dma(self, eng, dst, src, sem):
        sem.n += 16
        eobj = getattr(self.nc, eng)
        d = dst if callable(dst) else (lambda: dst)
        s = src if callable(src) else (lambda: src)
        self.emit(eng, lambda: eobj.dma_start(d(), s()).then_inc(sem.h, 16))
        return sem.n

    def drain(self, eng):
        eobj = getattr(self.nc, eng)
        self.emit(eng, lambda: eobj.drain())

    def switch(self, nvar, emit_fn, engines):
        pre = {n: s.n for n, s in self.sems.items()}
        var_streams = []
        var_counts = []
        for v in range(nvar):
            for n, s in self.sems.items():
                s.n = pre[n]
            saved = self.streams
            self.streams = {e: [] for e in ENGS}
            emit_fn(v)
            var_streams.append(self.streams)
            self.streams = saved
            var_counts.append({n: s.n for n, s in self.sems.items()})
        final = {n: max(vc[n] for vc in var_counts) for n in self.sems}
        for n, s in self.sems.items():
            s.n = final[n]
        for v in range(nvar):
            for e in ENGS:
                if e not in engines:
                    assert not var_streams[v][e], (v, e, "undeclared switch engine")
        nc = self.nc
        for e in engines:
            owned = [n for n in self.sems
                     if any(var_counts[v][n] != pre[n] for v in range(nvar))
                     and SEM_OWNER.get(n) == e]
            bodies = [var_streams[v][e] for v in range(nvar)]
            deficits = [{n: final[n] - var_counts[v][n] for n in owned}
                        for v in range(nvar)]
            eobj = getattr(nc, e)
            et = ETYPE[e]
            semmap = {n: self.sems[n].h for n in owned}
            def do_switch(bodies=bodies, deficits=deficits, eobj=eobj, et=et,
                          semmap=semmap, nvar=nvar):
                pid = nc.partition_id(engines=[et])
                var = pid % nvar
                for i in eobj.Switch(var, nvar):
                    for f in bodies[i]:
                        f()
                    for n, d in deficits[i].items():
                        if d > 0:
                            eobj.sem_inc(semmap[n], d)
            self.emit(e, do_switch)

    def finalize_block(self):
        nc = self.nc
        block = self.ctx.enter_context(nc.Block())
        streams = self.streams
        @block.tensor
        def _(t):
            for f in streams["tensor"]:
                f()
        @block.vector
        def _(v):
            for f in streams["vector"]:
                f()
        @block.scalar
        def _(s):
            for f in streams["scalar"]:
                f()
        @block.sync
        def _(s):
            for f in streams["sync"]:
                f()
        @block.gpsimd
        def _(g):
            for f in streams["gpsimd"]:
                f()


SEM_OWNER = {}


def mk_sems(kb, names_owners):
    out = {}
    for name, owner in names_owners:
        out[name] = kb.sem(name)
        SEM_OWNER[name] = owner
    return out


def setup_common(kb, phases):
    nc = kb.nc
    D = {}
    def din(name, shape, dt=F32):
        D[name] = nc.dram_tensor(name, shape, dt, kind="ExternalInput")
        return D[name]
    din("ones_bf", [128, 1], BF16); din("ones1f", [1, 128])
    din("onesm", [128, 128], BF16)
    din("identb", [128, 128], BF16); din("identf", [128, 128])
    din("tri", [128, 128], BF16); din("rmat", [128, 128], BF16)
    if "p0" in phases:
        din("x0p", [HD, TPC]); din("xT", [HD, HD])
        din("wq", [NT, 4, 128, 8, 128]); din("wkv", [HD, 256]); din("wo", [NT, 4, 128, 8, 128])
        din("biastab", [128, NU_MAX])
    if "p1" in phases:
        din("wb0t", [NT, 4, 128, 8, 128]); din("b0", [128, 32]); din("wgt", [HD, 8])
        din("a0t", [HD, 128]); din("b0f", [128, HD])
        din("kr8", [8, 128], BF16)
    if "p2" in phases:
        din("lam", [64, 2]); din("pri", [64, 2]); din("qri", [64, 2])
        din("vcrt", [64, 64]); din("vcit", [64, 64])
        din("ctri", [64, 2]); din("bp", [64, 1]); din("dd", [1, 1]); din("lstep", [1, 1])
        din("tim", [1, 4096]); din("ccr", [128, 32]); din("cci", [128, 32])
        din("w32r", [32, 32]); din("w32i", [32, 32]); din("w32in", [32, 32])
        din("twr", [128, 32]); din("twi", [128, 32])
        din("c128", [128, 128]); din("s128n", [128, 128])
    if "p3" in phases:
        din("wb1t", [NT, 4, 128, 8, 128]); din("b1", [128, 32]); din("rwt", [HD, 16])
        din("la1t", [HD, 128]); din("lb1f", [128, HD])
        din("rb", [16, 1]); din("prior", [16, 1]); din("active", [1, 4], I32)
        din("kr16", [16, 128], BF16); din("io16c", [16, 4]); din("io4r", [4, 16])
    D["out"] = nc.dram_tensor("out", [HD, TPC], F32, kind="ExternalOutput")
    D["scr0"] = nc.dram_tensor("scr0", [HD, TPC], F32)
    D["scr1"] = nc.dram_tensor("scr1", [HD, TPC], F32)
    if "p2" in phases:
        D["kpad"] = nc.dram_tensor("kpad", [8192], BF16)
    kb.D = D

    kb.sbuf("bufA", [128, NT * TPC], BF16)
    kb.sbuf("bufB", [128, NT * TPC], BF16)
    kb.sbuf("wpool", [128, 8192], BF16)
    kb.sbuf("wsb", [128, 4096], BF16)
    kb.sbuf("scf", [128, 3072], F32)
    kb.sbuf("xpre", [128, 2048], F32)
    kb.sbuf("sqsb", [128, 2048], BF16)
    kb.sbuf("mbc", [128, 1024], F32)
    kb.sbuf("rbc2", [128, 1024], F32)
    kb.sbuf("cons_bf", [128, 1], BF16)
    kb.sbuf("onesm", [128, 128], BF16)
    kb.sbuf("cons1f", [1, 128], F32)
    kb.sbuf("identb", [128, 128], BF16)
    kb.sbuf("identf", [128, 128], F32)
    kb.sbuf("attnT", [128, 1024], BF16)
    if "p0" in phases:
        kb.sbuf("qsb", [128, 2048], BF16)
        kb.sbuf("rbcs", [128, 512], F32)
        kb.sbuf("biastab", [128, NU_MAX], F32)
        kb.sbuf("tri", [128, 128], BF16)
    if phases & {"p1", "p2", "p3"}:
        kb.sbuf("gw", [128, 48], F32)
        kb.sbuf("gwb", [128, 16], F32)
    if "p1" in phases or "p3" in phases:
        kb.sbuf("hwsb", [128, 1024], BF16)
        kb.sbuf("grow", [16, 1152], BF16)
        kb.sbuf("bias_sb", [128, 64], F32)
        kb.sbuf("kreps", [16, 256], BF16)
        kb.sbuf("wgs", [128, 512], BF16)
    if "p3" in phases:
        kb.sbuf("oh", [16, 64], F32)
        kb.sbuf("acti", [1, 4], I32)
        kb.sbuf("ioc", [16, 4], F32)
        kb.sbuf("ior", [4, 16], F32)
    if "p2" in phases:
        kb.sbuf("rmat", [128, 128], BF16)
        kb.sbuf("ktile", [128, 32], BF16)
    kb.alias = {}
    kb.psum("pA", [128, 1024], F32)
    kb.psum("pB", [128, 1024], F32)
    kb.psum("pC", [128, 1024], F32)
    kb.psum("pD", [128, 1024], F32)

    mk_sems(kb, [("dW", "gpsimd"), ("dXQ", "sync"), ("dSC", "sync"),
                 ("dOUT", "sync"), ("mT", "tensor"), ("vV", "vector"),
                 ("sS", "scalar"), ("gG", "gpsimd"), ("dSS", "scalar"),
                 ("aRs", "scalar"), ("aRv", "vector"),
                 ("dP", "gpsimd"),
                 ("dQ0", "gpsimd"), ("dQ1", "gpsimd"),
                 ("dQ2", "gpsimd"), ("dQ3", "gpsimd"),
                 ("dK0", "gpsimd"), ("dK1", "gpsimd"),
                 ("dX0", "sync"), ("dX1", "sync"),
                 ("dS0", "sync"), ("dS1", "sync"),
                 ("dO0", "sync"), ("dO1", "sync")])


def load_consts(kb, phases):
    D, S = kb.D, kb.sems
    kb.dma("sync", kb.sb["cons_bf"][:], D["ones_bf"].ap(), S["dXQ"])
    kb.dma("sync", kb.sb["onesm"][:], D["onesm"].ap(), S["dXQ"])
    kb.dma("sync", kb.sb["cons1f"][:], D["ones1f"].ap(), S["dXQ"])
    kb.dma("sync", kb.sb["identb"][:], D["identb"].ap(), S["dXQ"])
    kb.dma("sync", kb.sb["identf"][:], D["identf"].ap(), S["dXQ"])
    if "p0" in phases:
        kb.dma("sync", kb.sb["biastab"][:], D["biastab"].ap(), S["dXQ"])
        kb.dma("sync", kb.sb["tri"][:], D["tri"].ap(), S["dXQ"])
    if "p2" in phases:
        kb.dma("sync", kb.sb["rmat"][:], D["rmat"].ap(), S["dXQ"])
    if "p1" in phases:
        kb.dma("sync", kb.sb["kreps"][0:8, 0:128], D["kr8"].ap(), S["dXQ"])
        kb.dma("sync", kb.sb["bias_sb"][:, 0:32], D["b0"].ap(), S["dXQ"])
    if "p3" in phases:
        kb.dma("sync", kb.sb["kreps"][0:16, 128:256], D["kr16"].ap(), S["dXQ"])
        kb.dma("sync", kb.sb["bias_sb"][:, 32:64], D["b1"].ap(), S["dXQ"])
        kb.dma("sync", kb.sb["ioc"][:], D["io16c"].ap(), S["dXQ"])
        kb.dma("sync", kb.sb["ior"][:], D["io4r"].ap(), S["dXQ"])
        kb.dma("sync", kb.sb["acti"][:], D["active"].ap(), S["dXQ"])
        kb.dma("sync", kb.sb["oh"][:, 0:1], D["rb"].ap(), S["dXQ"])
        kb.dma("sync", kb.sb["oh"][:, 1:2], D["prior"].ap(), S["dXQ"])
    return S["dXQ"].n


def emit_evac_loop(kb, n_tiles, mm_emit, resid_load, normed, outbuf, scr_out,
                  bias_cols=None, psum_names=("pA", "pB")):
    S = kb.sems
    nc = kb.nc
    scf, xpre, sqsb = kb.sb["scf"], kb.sb["xpre"], kb.sb["sqsb"]
    ev1 = [0, 0]; wdone = [0, 0]; obn = [0, 0]; sqd = [0, 0]; scrd = [0, 0]
    stats_done = [0, 0]
    pending = []

    def emit_stats(t, sl, ob_ap, sq_ap):
        kb.wait("tensor", S["sS"], sqd[sl])
        for half in range(2):
            kb.op("tensor", lambda ob_ap=ob_ap, half=half, t=t:
                  nc.tensor.matmul(kb.ps["pC"][:, half * 512:(half + 1) * 512],
                                   kb.sb["onesm"][:],
                                   ob_ap[:, half * 512:(half + 1) * 512],
                                   start=(t == 0), stop=(t == n_tiles - 1)))
        for half in range(2):
            kb.op("tensor", lambda sq_ap=sq_ap, half=half, t=t:
                  nc.tensor.matmul(kb.ps["pD"][:, half * 512:(half + 1) * 512],
                                   kb.sb["onesm"][:],
                                   sq_ap[:, half * 512:(half + 1) * 512],
                                   start=(t == 0), stop=(t == n_tiles - 1)),
                  inc=(S["mT"], 1) if half == 1 else None)
        stats_done[sl] = S["mT"].n

    for t in range(n_tiles):
        sl = t % 2
        pt = kb.ps[psum_names[sl]]
        if t >= 2:
            kb.wait("tensor", S["sS"], ev1[sl])
        mm_done = mm_emit(kb, t, pt)
        if pending:
            emit_stats(*pending.pop())
        if t >= 2:
            kb.wait("sync", S["vV"], wdone[sl])
        xp = resid_load(kb, t, sl)
        if t >= 2:
            kb.wait("scalar", S["dS0" if sl == 0 else "dS1"], scrd[sl])
        kb.wait("scalar", S["mT"], mm_done)
        wout = scf[:, sl * 1024:(sl + 1) * 1024]
        kb.op("scalar", lambda pt=pt, wout=wout:
              nc.scalar.activation(wout, pt[:], AF.Copy), inc=S["sS"])
        ev1[sl] = S["sS"].n
        bap = bias_cols(t) if bias_cols is not None else None
        kb.wait("vector", S["sS"], ev1[sl])
        kb.wait("vector", S["dX0" if sl == 0 else "dX1"], xp)
        xs = xpre[:, sl * 1024:(sl + 1) * 1024]
        if normed:
            tmp = scf[:, 2048:3072]
            kb.op("vector", lambda xs=xs, tmp=tmp:
                  nc.vector.tensor_sub(tmp, xs, kb.sb["mbc"][:]))
            kb.drain("vector")
            kb.op("vector", lambda tmp=tmp:
                  nc.vector.tensor_mul(tmp, tmp, kb.sb["rbc2"][:]))
            kb.drain("vector")
            other = tmp
        else:
            other = xs
        if bap is not None:
            kb.op("vector", lambda other=other, wout=wout, bap=bap:
                  nc.vector.scalar_tensor_tensor(wout, wout, bap, other,
                                                 op0=ALU.add, op1=ALU.add),
                  inc=S["vV"])
        else:
            kb.op("vector", lambda other=other, wout=wout:
                  nc.vector.tensor_add(wout, other, wout), inc=S["vV"])
        wdone[sl] = S["vV"].n
        ob_ap = outbuf[:, t * 1024:(t + 1) * 1024]
        if t >= 2:
            kb.wait("scalar", S["mT"], stats_done[sl])
        kb.wait("scalar", S["vV"], wdone[sl])
        kb.op("scalar", lambda ob_ap=ob_ap, wout=wout:
              nc.scalar.activation(ob_ap, wout, AF.Copy))
        sq_ap = sqsb[:, sl * 1024:(sl + 1) * 1024]
        kb.op("scalar", lambda sq_ap=sq_ap, wout=wout:
              nc.scalar.activation(sq_ap, wout, AF.Square), inc=S["sS"])
        obn[sl] = S["sS"].n
        sqd[sl] = S["sS"].n
        kb.wait("sync", S["vV"], wdone[sl])
        scrd[sl] = kb.dma("sync", lambda t=t: scr_out.ap()[t * 128:(t + 1) * 128, :],
                          lambda wout=wout: wout, S["dS0" if sl == 0 else "dS1"])
        pending.append((t, sl, ob_ap, sq_ap))
    emit_stats(*pending.pop())
    return S["mT"].n


def emit_stats_finalize(kb):
    S = kb.sems; nc = kb.nc
    scf = kb.sb["scf"]
    kb.wait("vector", S["mT"], S["mT"].n)
    kb.wait("vector", S["dS0"], S["dS0"].n)
    kb.wait("vector", S["dS1"], S["dS1"].n)
    MR = scf[0:1, 0:1024]; VR = scf[0:1, 1024:2048]; T1 = scf[0:1, 2048:3072]
    kb.op("vector", lambda: nc.vector.tensor_scalar(MR, kb.ps["pC"][0:1, :],
          1.0 / HD, None, op0=ALU.mult))
    kb.op("vector", lambda: nc.vector.tensor_scalar(VR, kb.ps["pD"][0:1, :],
          1.0 / HD, None, op0=ALU.mult))
    kb.drain("vector")
    kb.op("vector", lambda: nc.vector.tensor_mul(T1, MR, MR))
    kb.drain("vector")
    kb.op("vector", lambda: nc.vector.tensor_sub(VR, VR, T1), inc=S["vV"])
    kb.drain("vector")
    kb.op("vector", lambda: nc.vector.tensor_scalar(VR, VR, EPS_LN, None,
          op0=ALU.add), inc=S["vV"])
    vv = S["vV"].n
    kb.wait("scalar", S["vV"], vv)
    kb.op("scalar", lambda: nc.scalar.activation(T1, VR, AF.Ln))
    kb.drain("scalar")
    kb.op("scalar", lambda: nc.scalar.activation(VR, T1, AF.Exp, scale=-0.5),
          inc=S["sS"])
    sv = S["sS"].n
    kb.wait("tensor", S["sS"], sv)
    for half in range(2):
        kb.op("tensor", lambda half=half: nc.tensor.matmul(
            kb.ps["pA"][:, half * 512:(half + 1) * 512], kb.sb["cons1f"][:],
            MR[:, half * 512:(half + 1) * 512], start=True, stop=True))
    for half in range(2):
        kb.op("tensor", lambda half=half: nc.tensor.matmul(
            kb.ps["pB"][:, half * 512:(half + 1) * 512], kb.sb["cons1f"][:],
            VR[:, half * 512:(half + 1) * 512], start=True, stop=True),
            inc=(S["mT"], 1) if half == 1 else None)
    mm = S["mT"].n
    kb.wait("scalar", S["mT"], mm)
    kb.op("scalar", lambda: nc.scalar.activation(kb.sb["mbc"][:], kb.ps["pA"][:], AF.Copy))
    kb.op("scalar", lambda: nc.scalar.activation(kb.sb["rbc2"][:], kb.ps["pB"][:], AF.Copy),
          inc=S["sS"])
    return S["sS"].n


def emit_norm_pass(kb, buf):
    S = kb.sems; nc = kb.nc
    ntmp = kb.sb["sqsb"]
    sv0 = S["sS"].n
    kb.wait("vector", S["sS"], sv0)
    kb.wait("gpsimd", S["sS"], sv0)
    gdone = [0, 0]
    for t in range(NT):
        sl = t % 2
        bt = buf[:, t * 1024:(t + 1) * 1024]
        nt = ntmp[:, sl * 1024:(sl + 1) * 1024]
        if t >= 2:
            kb.wait("vector", S["gG"], gdone[sl])
        kb.op("vector", lambda bt=bt, nt=nt:
              nc.vector.tensor_sub(nt, bt, kb.sb["mbc"][:]), inc=S["vV"])
        vv = S["vV"].n
        kb.wait("gpsimd", S["vV"], vv)
        kb.op("gpsimd", lambda bt=bt, nt=nt:
              nc.gpsimd.tensor_mul(bt, nt, kb.sb["rbc2"][:]), inc=S["gG"])
        gdone[sl] = S["gG"].n
    return S["gG"].n


def emit_final_output(kb, scr_in, out_dram):
    S = kb.sems; nc = kb.nc
    xpre, scf = kb.sb["xpre"], kb.sb["scf"]
    vdone = [0, 0]; outd = [0, 0]
    sv0 = S["sS"].n
    kb.wait("vector", S["sS"], sv0)
    for t in range(NT):
        sl = t % 2
        if t >= 2:
            kb.wait("sync", S["dO0" if sl == 0 else "dO1"], outd[sl])
        xp = kb.dma("sync", lambda t=t, sl=sl: xpre[:, sl * 1024:(sl + 1) * 1024],
                    lambda t=t: scr_in.ap()[t * 128:(t + 1) * 128, :],
                    S["dX0" if sl == 0 else "dX1"])
        kb.wait("vector", S["dX0" if sl == 0 else "dX1"], xp)
        xs = xpre[:, sl * 1024:(sl + 1) * 1024]
        tmp = scf[:, sl * 1024:(sl + 1) * 1024]
        if t >= 2:
            kb.wait("vector", S["dO0" if sl == 0 else "dO1"], outd[sl])
        kb.op("vector", lambda xs=xs, tmp=tmp:
              nc.vector.tensor_sub(tmp, xs, kb.sb["mbc"][:]))
        kb.drain("vector")
        kb.op("vector", lambda tmp=tmp:
              nc.vector.tensor_mul(tmp, tmp, kb.sb["rbc2"][:]), inc=S["vV"])
        vdone[sl] = S["vV"].n
        kb.wait("sync", S["vV"], vdone[sl])
        outd[sl] = kb.dma("sync", lambda t=t: out_dram.ap()[t * 128:(t + 1) * 128, :],
                          lambda tmp=tmp: tmp, S["dO0" if sl == 0 else "dO1"])
    kb.wait("sync", S["dO0"], S["dO0"].n)
    kb.wait("sync", S["dO1"], S["dO1"].n)


def emit_cast_load_buf(kb, src_dram, buf):
    S = kb.sems
    nc = kb.nc
    for t in range(NT):
        kb.dma("gpsimd", lambda t=t: buf[:, t * 1024:(t + 1) * 1024],
               lambda t=t: src_dram.ap()[t * 128:(t + 1) * 128, :], S["dW"])
    return S["dW"].n


def emit_ffn(kb, layer, inbuf, outbuf, resid_load, normed, scr_out):
    S = kb.sems; nc = kb.nc; D = kb.D
    wpool, wgs, wsb = kb.sb["wpool"], kb.sb["wgs"], kb.sb["wsb"]
    smalls = kb.sb["scf"]
    gw, gwb, grow = kb.sb["gw"], kb.sb["gwb"], kb.sb["grow"]
    zsb = kb.sb["sqsb"]
    hwsb = kb.sb["hwsb"]
    identf = kb.sb["identf"]
    pA, pB, pC, pD = kb.ps["pA"], kb.ps["pB"], kb.ps["pC"], kb.ps["pD"]
    E = 8 if layer == 0 else 16

    at_name, bf_name, wb_name = (("a0t", "b0f", "wb0t") if layer == 0
                                 else ("la1t", "lb1f", "wb1t"))
    kb.dma("gpsimd", wpool[:, 0:4096],
           lambda: D[at_name].ap().rearrange("(t p) j -> p t j", p=128), S["dP"])
    kb.dma("gpsimd", wpool[:, 4096:8192], lambda: D[bf_name].ap(), S["dP"])
    if layer == 0:
        kb.dma("gpsimd", wgs[:, 0:256],
               lambda: D["wgt"].ap().rearrange("(t p) e -> p t e", p=128), S["dP"])
    else:
        kb.dma("gpsimd", wgs[:, 0:512],
               lambda: D["rwt"].ap().rearrange("(t p) e -> p t e", p=128), S["dP"])
    pool_dma = S["dP"].n

    if layer == 1:
        oh = kb.sb["oh"]
        kb.wait("vector", S["dXQ"], S["dXQ"].n)
        kb.op("vector", lambda: nc.vector.tensor_copy(oh[0:1, 2:6], kb.sb["acti"][:]),
              inc=S["vV"])
        va = S["vV"].n
        kb.wait("tensor", S["vV"], va)
        kb.op("tensor", lambda: nc.tensor.matmul(pA[0:16, 0:4], kb.sb["cons1f"][0:1, 0:16],
              oh[0:1, 2:6], start=True, stop=True))
        kb.op("tensor", lambda: nc.tensor.matmul(pB[0:4, 0:16], oh[0:1, 2:6],
              kb.sb["cons1f"][0:1, 0:16], start=True, stop=True), inc=S["mT"])
        m1 = S["mT"].n
        kb.wait("vector", S["mT"], m1)
        kb.op("vector", lambda: nc.vector.tensor_tensor(oh[0:16, 8:12], kb.sb["ioc"][:],
              pA[0:16, 0:4], op=ALU.is_equal))
        kb.op("vector", lambda: nc.vector.tensor_tensor(oh[0:4, 16:32], kb.sb["ior"][:],
              pB[0:4, 0:16], op=ALU.is_equal), inc=S["vV"])
        v1 = S["vV"].n
        kb.wait("tensor", S["vV"], v1)
        kb.op("tensor", lambda: nc.tensor.matmul(pC[0:4, 0:1], oh[0:16, 8:12],
              oh[0:16, 1:2], start=True, stop=True), inc=S["mT"])
        m2 = S["mT"].n
        kb.wait("vector", S["mT"], m2)
        kb.op("vector", lambda: nc.vector.tensor_scalar(oh[0:4, 33:34],
              pC[0:4, 0:1], 1e-12, None, op0=ALU.add), inc=S["vV"])
        kb.wait("scalar", S["vV"], S["vV"].n)
        kb.op("scalar", lambda: nc.scalar.activation(oh[0:4, 32:33],
              oh[0:4, 33:34], AF.Ln))
        kb.drain("scalar")
        kb.op("scalar", lambda: nc.scalar.activation(grow[0:4, 0:16], oh[0:4, 16:32],
              AF.Copy), inc=S["sS"])
        onehot_done = S["sS"].n

    kb.wait("tensor", S["dP"], pool_dma)
    if layer == 1:
        kb.wait("tensor", S["sS"], onehot_done)
    for ht in range(NT):
        for half in range(2):
            kb.op("tensor", lambda ht=ht, half=half: nc.tensor.matmul(
                pC[0:E, half * 512:(half + 1) * 512],
                wgs[:, E * ht:E * (ht + 1)],
                inbuf[:, ht * 1024 + half * 512: ht * 1024 + (half + 1) * 512],
                start=(ht == 0), stop=(ht == NT - 1)),
                inc=S["mT"] if (ht == NT - 1 and half == 1) else None)
    lg_mm = S["mT"].n
    kb.wait("scalar", S["mT"], lg_mm)
    if layer == 0:
        kb.op("scalar", lambda: nc.scalar.activation(smalls[0:E, 0:1024], pC[0:E, :],
              AF.Copy), inc=S["sS"])
    else:
        kb.wait("vector", S["mT"], lg_mm)
        kb.op("vector", lambda: nc.vector.tensor_single_scalar(
            smalls[0:E, 0:1024], pC[0:E, :], kb.sb["oh"][0:16, 0:1],
            op=ALU.add), inc=S["vV"])
        kb.wait("scalar", S["vV"], S["vV"].n)
        kb.op("scalar", lambda: nc.scalar.drain(), inc=S["sS"])
    lg_ev = S["sS"].n

    if layer == 1:
        kb.wait("tensor", S["sS"], lg_ev)
        for half in range(2):
            kb.op("tensor", lambda half=half: nc.tensor.matmul(
                pD[0:4, half * 512:(half + 1) * 512], kb.sb["oh"][0:16, 8:12],
                smalls[0:16, half * 512:(half + 1) * 512], start=True, stop=True),
                inc=S["mT"] if half == 1 else None)
        sl_mm = S["mT"].n
        kb.wait("vector", S["mT"], sl_mm)
        kb.op("vector", lambda: nc.vector.tensor_single_scalar(
            smalls[0:4, 1024:2048], pD[0:4, :], kb.sb["oh"][0:4, 32:33],
            op=ALU.add), inc=S["vV"])
        kb.wait("scalar", S["vV"], S["vV"].n)
        kb.op("scalar", lambda: nc.scalar.drain(), inc=S["sS"])
        lg_ev = S["sS"].n

    nsl = E if layer == 0 else 4
    soff = 0 if layer == 0 else 1024
    kb.wait("tensor", S["sS"], lg_ev)
    for i in range(8):
        kb.op("tensor", lambda i=i, nsl=nsl, soff=soff: nc.tensor.transpose(
            pD[:, 64 + nsl * i: 64 + nsl * (i + 1)],
            smalls[0:nsl, soff + 128 * i: soff + 128 * (i + 1)],
            identf[0:nsl, 0:nsl]),
            inc=S["mT"] if i == 7 else None)
    tr_mm = S["mT"].n
    kb.wait("vector", S["mT"], tr_mm)
    M1, NM1, SS, RS = gw[:, 0:1], gw[:, 1:2], gw[:, 2:3], gw[:, 3:4]
    M2 = gw[:, 4:5]
    EQ, KEEP, EE, WW = gw[:, 8:16], gw[:, 16:24], gw[:, 24:32], gw[:, 32:40]
    tb_done = 0
    for i in range(8):
        lgt = pD[:, 64 + nsl * i: 64 + nsl * (i + 1)]
        if i > 0:
            kb.wait("vector", S["mT"], tb_done)
        kb.op("vector", lambda lgt=lgt: nc.vector.reduce_max(M1, lgt, axis=AX.X))
        kb.drain("vector")
        kb.op("vector", lambda: nc.vector.tensor_scalar(NM1, M1, -1.0, None, op0=ALU.mult))
        if layer == 0:
            kb.op("vector", lambda lgt=lgt: nc.vector.tensor_scalar(
                EQ[:, 0:nsl], lgt, M1[:, 0:1], None, op0=ALU.is_equal))
            kb.drain("vector")
            kb.op("vector", lambda lgt=lgt: nc.vector.scalar_tensor_tensor(
                EQ[:, 0:nsl], EQ[:, 0:nsl], -1e30, lgt, op0=ALU.mult, op1=ALU.add))
            kb.drain("vector")
            kb.op("vector", lambda: nc.vector.reduce_max(M2, EQ[:, 0:nsl], axis=AX.X))
        kb.op("vector", lambda: nc.vector.drain(), inc=S["vV"])
        vv = S["vV"].n
        kb.wait("scalar", S["vV"], vv)
        kb.op("scalar", lambda lgt=lgt: nc.scalar.activation(
            EE[:, 0:nsl], lgt, AF.Exp, bias=NM1[:, 0:1]), inc=S["sS"])
        se = S["sS"].n
        kb.wait("vector", S["sS"], se)
        if layer == 0:
            kb.op("vector", lambda lgt=lgt: nc.vector.tensor_scalar(
                KEEP[:, 0:nsl], lgt, M2[:, 0:1], None, op0=ALU.is_ge))
            kb.drain("vector")
            kb.op("vector", lambda: nc.vector.tensor_mul(WW[:, 0:nsl], EE[:, 0:nsl],
                  KEEP[:, 0:nsl]))
        else:
            kb.op("vector", lambda: nc.vector.tensor_copy(WW[:, 0:nsl], EE[:, 0:nsl]))
        kb.drain("vector")
        kb.op("vector", lambda: nc.vector.reduce_sum(SS, WW[:, 0:nsl], axis=AX.X))
        kb.drain("vector")
        kb.op("vector", lambda: nc.vector.reciprocal(RS, SS))
        kb.drain("vector")
        kb.op("vector", lambda: nc.vector.tensor_scalar(
            gwb[:, 0:nsl], WW[:, 0:nsl], RS[:, 0:1], None, op0=ALU.mult), inc=S["vV"])
        vr = S["vV"].n
        kb.wait("tensor", S["vV"], vr)
        kb.op("tensor", lambda i=i, nsl=nsl: nc.tensor.transpose(
            pB[0:nsl, 128 * i:128 * (i + 1)], gwb[:, 0:nsl], identf[:]),
            inc=S["mT"])
        tb_done = S["mT"].n
    kb.wait("scalar", S["mT"], tb_done)
    if layer == 0:
        kb.op("scalar", lambda: nc.scalar.activation(grow[0:8, 64:1088],
              pB[0:8, 0:1024], AF.Copy), inc=S["sS"])
    else:
        kb.op("scalar", lambda: nc.scalar.activation(kb.sb["attnT"][0:4, 0:1024],
              pB[0:4, 0:1024], AF.Copy), inc=S["sS"])
    rw_ready = S["sS"].n

    if layer == 1:
        kb.wait("tensor", S["sS"], rw_ready)
        for half in range(2):
            kb.op("tensor", lambda half=half: nc.tensor.matmul(
                pC[0:16, half * 512:(half + 1) * 512], grow[0:4, 0:16],
                kb.sb["attnT"][0:4, half * 512:(half + 1) * 512],
                start=True, stop=True), inc=S["mT"] if half == 1 else None)
        af_mm = S["mT"].n
        kb.wait("scalar", S["mT"], af_mm)
        kb.op("scalar", lambda: nc.scalar.activation(grow[0:16, 64:1088],
              pC[0:16, 0:1024], AF.Copy), inc=S["sS"])
        rw_ready = S["sS"].n

    kb.wait("tensor", S["vV"], S["vV"].n)
    for ht in range(NT):
        for half in range(2):
            kb.op("tensor", lambda ht=ht, half=half: nc.tensor.matmul(
                pD[:, half * 512:(half + 1) * 512], wpool[:, 128 * ht:128 * (ht + 1)],
                inbuf[:, ht * 1024 + half * 512: ht * 1024 + (half + 1) * 512],
                start=(ht == 0), stop=(ht == NT - 1)),
                inc=S["mT"] if (ht == NT - 1 and half == 1) else None)
    z_mm = S["mT"].n
    kb.wait("scalar", S["mT"], z_mm)
    kb.op("scalar", lambda: nc.scalar.activation(zsb[:, 0:1024], pD[:], AF.Copy), inc=S["sS"])
    z_ev = S["sS"].n
    kb.wait("tensor", S["sS"], rw_ready)
    krap = kb.sb["kreps"][0:8, 0:128] if layer == 0 else kb.sb["kreps"][0:16, 128:256]
    nK = 8 if layer == 0 else 16
    for half in range(2):
        kb.op("tensor", lambda half=half, krap=krap, nK=nK: nc.tensor.matmul(
            pC[:, half * 512:(half + 1) * 512], krap,
            grow[0:nK, 64 + half * 512: 64 + (half + 1) * 512],
            start=True, stop=True), inc=S["mT"] if half == 1 else None)
    bc_mm = S["mT"].n
    kb.wait("vector", S["mT"], bc_mm)
    kb.wait("vector", S["sS"], z_ev)
    kb.op("vector", lambda: nc.vector.tensor_mul(hwsb[:], zsb[:, 0:1024], pC[:]), inc=S["vV"])
    hw_done = S["vV"].n

    wb_dram = D[wb_name]
    mm_emit = make_wstream_mm_emit(kb, wb_dram, inbuf, extra=("delta", hw_done))
    bcol = 0 if layer == 0 else 32
    bias_cols = lambda t: kb.sb["bias_sb"][:, bcol + t: bcol + t + 1]
    emit_evac_loop(kb, NT, mm_emit, resid_load, normed, outbuf, scr_out,
                   bias_cols=bias_cols)


def make_wstream_mm_emit(kb, wb_dram, inbuf, extra=None):
    S = kb.sems; nc = kb.nc
    wsb, wpool = kb.sb["wsb"], kb.sb["wpool"]
    hwsb = kb.sb.get("hwsb")
    q_mm = [S["mT"].n] * 4
    q_dma = [0, 0, 0, 0]
    def mm_emit(kb2, t, pt):
        for q in range(4):
            kb.wait("gpsimd", S["mT"], q_mm[q])
            q_dma[q] = kb.dma(
                "gpsimd",
                lambda q=q: wsb[:, q * 1024:(q + 1) * 1024],
                lambda t=t, q=q: wb_dram.ap()[t, q], S["dQ" + str(q)])
        if t == 0 and extra is not None:
            kb.wait("tensor", S["vV"], extra[1])
        for q in range(4):
            kb.wait("tensor", S["dQ" + str(q)], q_dma[q])
            for j in range(8):
                u = 8 * q + j
                for half in range(2):
                    kb.op("tensor", lambda u=u, half=half, pt=pt:
                          nc.tensor.matmul(
                              pt[:, half * 512:(half + 1) * 512],
                              wsb[:, u * 128:(u + 1) * 128],
                              inbuf[:, u * 1024 + half * 512:
                                    u * 1024 + (half + 1) * 512],
                              start=(u == 0),
                              stop=(extra is None and u == NT - 1)),
                          inc=S["mT"] if (j == 7 and half == 1) else None)
            q_mm[q] = S["mT"].n
        if extra is not None:
            for half in range(2):
                kb.op("tensor", lambda t=t, half=half, pt=pt: nc.tensor.matmul(
                    pt[:, half * 512:(half + 1) * 512],
                    wpool[:, 4096 + 128 * t: 4096 + 128 * (t + 1)],
                    hwsb[:, half * 512:(half + 1) * 512],
                    start=False, stop=True),
                    inc=S["mT"] if half == 1 else None)
        return S["mT"].n
    return mm_emit


def resid_from_dram(kb, dram):
    S = kb.sems
    def load(kb2, t, sl):
        return kb.dma("sync",
                      lambda sl=sl: kb.sb["xpre"][:, sl * 1024:(sl + 1) * 1024],
                      lambda t=t: dram.ap()[t * 128:(t + 1) * 128, :],
                      S["dX0" if sl == 0 else "dX1"])
    return load


def build_ffn_phase_program(layer):
    nc = bass.Bass("TRN2", target_bir_lowering=False)
    ctx = ExitStack()
    with ctx:
        kb = KB(nc, ctx)
        phase = "p1" if layer == 0 else "p3"
        setup_common(kb, {phase})
        kb.D["xnin"] = nc.dram_tensor("xnin", [HD, TPC], F32, kind="ExternalInput")
        load_consts(kb, {phase})
        emit_cast_load_buf(kb, kb.D["xnin"], kb.sb["bufA"])
        kb.wait("tensor", kb.sems["dW"], kb.sems["dW"].n)
        kb.wait("tensor", kb.sems["dXQ"], kb.sems["dXQ"].n)
        emit_ffn(kb, layer, kb.sb["bufA"], kb.sb["bufB"],
                 resid_from_dram(kb, kb.D["xnin"]), False, kb.D["scr0"])
        emit_stats_finalize(kb)
        emit_final_output(kb, kb.D["scr0"], kb.D["out"])
        kb.finalize_block()
    return nc


def ffn_phase_inputs(layer, inputs, core):
    m = dict(
        ones_bf=NPC["ones_bf"], ones1f=NPC["ones1x128_f"], onesm=NPC["onesm"],
        identb=NPC["ident_bf"], identf=NPC["ident_f"],
        tri=NPC["tri01_bf"], rmat=NPC["rmat_bf"],
    )
    if layer == 0:
        m["wb0t"] = pack_w(np.asarray(inputs["L0_ffn_W_base"]).T)
        m["b0"] = np.ascontiguousarray(inputs["L0_ffn_b_base"].reshape(32, 128).T)
        m["wgt"] = np.ascontiguousarray(inputs["L0_ffn_W_gate"].T)
        m["a0t"] = np.ascontiguousarray(
            inputs["L0_ffn_A"].transpose(2, 0, 1).reshape(HD, 128))
        m["b0f"] = np.ascontiguousarray(
            inputs["L0_ffn_B"].transpose(0, 2, 1).reshape(128, HD))
        m["kr8"] = NPC["krepl8"]
    else:
        m["wb1t"] = pack_w(np.asarray(inputs["L1_ffn_W_base"]).T)
        m["b1"] = np.ascontiguousarray(inputs["L1_ffn_b_base"].reshape(32, 128).T)
        m["rwt"] = np.ascontiguousarray(inputs["L1_ffn_router_W"].T)
        m["la1t"] = np.ascontiguousarray(
            inputs["L1_ffn_lora_A"].transpose(2, 0, 1).reshape(HD, 128))
        m["lb1f"] = np.ascontiguousarray(
            inputs["L1_ffn_lora_B"].transpose(0, 2, 1).reshape(128, HD))
        m["rb"] = np.ascontiguousarray(inputs["L1_ffn_router_b"].reshape(16, 1))
        m["prior"] = np.ascontiguousarray(inputs["L1_ffn_domain_prior"].reshape(16, 1))
        m["active"] = np.ascontiguousarray(inputs["L1_ffn_active"].reshape(1, 4))
        m["kr16"] = NPC["krepl16x2"]
        m["io16c"] = NPC["iota16c"]
        m["io4r"] = NPC["iota4r"]
    return m


def emit_kbuild(kb):
    S = kb.sems; nc = kb.nc; D = kb.D
    gw, gwb, xpre, scf = kb.sb["gw"], kb.sb["gwb"], kb.sb["xpre"], kb.sb["scf"]
    pA, pB, pD = kb.ps["pA"], kb.ps["pB"], kb.ps["pD"]
    ktile = kb.sb["ktile"]
    C1f = kb.sb["cons1f"]

    LR, LI = gw[0:64, 0:1], gw[0:64, 1:2]
    PR, PI = gw[0:64, 2:3], gw[0:64, 3:4]
    QR, QI = gw[0:64, 4:5], gw[0:64, 5:6]
    CTR, CTI = gw[0:64, 6:7], gw[0:64, 7:8]
    BP = gw[0:64, 8:9]
    NLR, NLI, LR2 = gw[0:64, 9:10], gw[0:64, 10:11], gw[0:64, 11:12]
    BR, BI = gw[0:64, 12:13], gw[0:64, 13:14]
    VV = gw[0:64, 16:24]
    STP = gw[0:1, 32:33]
    RS2 = gw[0:1, 33:34]
    DD = gw[0:1, 34:35]

    kb.op("vector", lambda: nc.vector.memset(xpre[:, 0:32], 0.0), inc=S["vV"])
    zv = S["vV"].n
    kb.wait("gpsimd", S["vV"], zv)
    zkp = kb.dma("gpsimd",
                 lambda: bass.AP(D["kpad"], 0, [[32, 128], [1, 32]]),
                 lambda: xpre[:, 0:32], S["dW"])

    kb.dma("sync", gw[0:64, 0:2], D["lam"].ap(), S["dXQ"])
    kb.dma("sync", gw[0:64, 2:4], D["pri"].ap(), S["dXQ"])
    kb.dma("sync", gw[0:64, 4:6], D["qri"].ap(), S["dXQ"])
    kb.dma("sync", gw[0:64, 6:8], D["ctri"].ap(), S["dXQ"])
    kb.dma("sync", gw[0:64, 8:9], D["bp"].ap(), S["dXQ"])
    kb.dma("sync", gw[0:1, 34:35], D["dd"].ap(), S["dXQ"])
    kb.dma("sync", gw[0:1, 35:36], D["lstep"].ap(), S["dXQ"])
    kb.dma("sync", scf[0:64, 2400:2464], D["vcrt"].ap(), S["dXQ"])
    kb.dma("sync", scf[0:64, 2464:2528], D["vcit"].ap(), S["dXQ"])
    kb.dma("sync", scf[:, 2528:2560], D["ccr"].ap(), S["dXQ"])
    kb.dma("sync", scf[:, 2560:2592], D["cci"].ap(), S["dXQ"])
    kb.dma("sync", scf[0:32, 2592:2624], D["w32r"].ap(), S["dXQ"])
    kb.dma("sync", scf[0:32, 2624:2656], D["w32i"].ap(), S["dXQ"])
    kb.dma("sync", scf[0:32, 2656:2688], D["w32in"].ap(), S["dXQ"])
    kb.dma("sync", scf[:, 2688:2720], D["twr"].ap(), S["dXQ"])
    kb.dma("sync", scf[:, 2720:2752], D["twi"].ap(), S["dXQ"])
    kb.dma("sync", scf[:, 2752:2880], D["c128"].ap(), S["dXQ"])
    kb.dma("sync", scf[:, 2880:3008], D["s128n"].ap(), S["dXQ"])
    pdma = S["dXQ"].n

    kb.wait("scalar", S["dXQ"], pdma)
    kb.op("scalar", lambda: nc.scalar.activation(STP, gw[0:1, 35:36], AF.Exp),
          inc=S["sS"])
    kb.wait("vector", S["sS"], S["sS"].n)
    kb.op("vector", lambda: nc.vector.reciprocal(RS2, STP))
    kb.drain("vector")
    kb.op("vector", lambda: nc.vector.tensor_scalar(RS2, RS2, 2.0, None, op0=ALU.mult))
    kb.drain("vector")
    kb.op("vector", lambda: nc.vector.tensor_scalar(NLR, LR, -1.0, None, op0=ALU.mult))
    kb.op("vector", lambda: nc.vector.tensor_scalar(NLI, LI, -1.0, None, op0=ALU.mult))
    kb.op("vector", lambda: nc.vector.tensor_mul(LR2, LR, LR), inc=S["vV"])
    vpar = S["vV"].n

    kb.wait("tensor", S["dXQ"], pdma)
    kb.op("tensor", lambda: nc.tensor.matmul(pA[0:64, 0:1], scf[0:64, 2400:2464],
          BP, start=True, stop=True))
    kb.op("tensor", lambda: nc.tensor.matmul(pA[0:64, 1:2], scf[0:64, 2464:2528],
          BP, start=True, stop=True), inc=S["mT"])
    kb.wait("vector", S["mT"], S["mT"].n)
    kb.op("vector", lambda: nc.vector.tensor_copy(BR, pA[0:64, 0:1]))
    kb.op("vector", lambda: nc.vector.tensor_copy(BI, pA[0:64, 1:2]))
    kb.drain("vector")
    def cmul_conj(ar, ai, br_, bi_, outr, outi):
        kb.op("vector", lambda: nc.vector.tensor_mul(outr, ar, br_))
        kb.op("vector", lambda: nc.vector.tensor_mul(outi, ar, bi_))
        kb.drain("vector")
        kb.op("vector", lambda: nc.vector.scalar_tensor_tensor(
            gw[0:64, 14:15], ai, 1.0, bi_, op0=ALU.mult, op1=ALU.mult))
        kb.op("vector", lambda: nc.vector.scalar_tensor_tensor(
            gw[0:64, 15:16], ai, 1.0, br_, op0=ALU.mult, op1=ALU.mult))
        kb.drain("vector")
        kb.op("vector", lambda: nc.vector.tensor_add(outr, outr, gw[0:64, 14:15]))
        kb.op("vector", lambda: nc.vector.tensor_sub(outi, outi, gw[0:64, 15:16]))
        kb.drain("vector")
    cmul_conj(CTR, CTI, BR, BI, VV[:, 0:1], VV[:, 1:2])
    cmul_conj(CTR, CTI, PR, PI, VV[:, 2:3], VV[:, 3:4])
    cmul_conj(QR, QI, BR, BI, VV[:, 4:5], VV[:, 5:6])
    cmul_conj(QR, QI, PR, PI, VV[:, 6:7], VV[:, 7:8])
    for j in range(4):
        kb.op("vector", lambda j=j: nc.vector.tensor_copy(
            gwb[0:64, 2 * j:2 * j + 1], VV[:, 2 * j:2 * j + 1]))
        kb.op("vector", lambda j=j: nc.vector.tensor_copy(
            gwb[0:64, 2 * j + 1:2 * j + 2], VV[:, 2 * j + 1:2 * j + 2]))
        kb.op("vector", lambda j=j: nc.vector.tensor_scalar(
            gwb[0:64, 8 + 2 * j:9 + 2 * j], VV[:, 2 * j + 1:2 * j + 2], -1.0,
            None, op0=ALU.mult))
        kb.op("vector", lambda j=j: nc.vector.tensor_copy(
            gwb[0:64, 9 + 2 * j:10 + 2 * j], VV[:, 2 * j:2 * j + 1]))
    kb.wait("vector", S["dW"], zkp)
    kb.op("vector", lambda: nc.vector.drain(), inc=S["vV"])
    vl_done = S["vV"].n

    tr_n = 0
    for kh in range(2):
        if kh == 1:
            kb.wait("sync", S["mT"], tr_n)
        th = kb.dma("sync", scf[0:1, 0:2048],
                    D["tim"].ap()[0:1, 2048 * kh:2048 * (kh + 1)], S["dX0"])
        kb.wait("vector", S["dX0"], th)
        kb.wait("vector", S["vV"], vl_done)
        kb.op("vector", lambda: nc.vector.tensor_scalar(
            scf[0:1, 0:2048], scf[0:1, 0:2048], RS2, None, op0=ALU.mult),
            inc=S["vV"])
        gv = S["vV"].n
        for c in range(4):
            kb.wait("tensor", S["vV"], gv)
            if c > 0:
                kb.wait("tensor", S["vV"], S["vV"].n)
            kb.op("tensor", lambda c=c: nc.tensor.matmul(
                pA[0:64, 0:512], C1f[0:1, 0:64],
                scf[0:1, 512 * c:512 * (c + 1)], start=True, stop=True),
                inc=S["mT"])
            kb.wait("vector", S["mT"], S["mT"].n)
            kb.op("vector", lambda c=c: nc.vector.tensor_single_scalar(
                xpre[0:64, 512 * c:512 * (c + 1)], pA[0:64, 0:512], NLI,
                op=ALU.add), inc=S["vV"])
        dIn = S["vV"].n
        kb.wait("scalar", S["vV"], dIn)
        kb.op("scalar", lambda: nc.scalar.activation(
            scf[0:64, 0:2048], xpre[0:64, 0:2048], AF.Square), inc=S["sS"])
        sqn = S["sS"].n
        kb.wait("vector", S["sS"], sqn)
        kb.op("vector", lambda: nc.vector.tensor_scalar(
            scf[0:64, 0:2048], scf[0:64, 0:2048], LR2, None, op0=ALU.add))
        kb.drain("vector")
        kb.op("vector", lambda: nc.vector.reciprocal(
            scf[0:64, 0:2048], scf[0:64, 0:2048]))
        kb.drain("vector")
        kb.op("vector", lambda: nc.vector.scalar_tensor_tensor(
            xpre[0:64, 0:2048], xpre[0:64, 0:2048], -1.0, scf[0:64, 0:2048],
            op0=ALU.mult, op1=ALU.mult))
        kb.drain("vector")
        kb.op("vector", lambda: nc.vector.tensor_scalar(
            scf[0:64, 0:2048], scf[0:64, 0:2048], NLR, None, op0=ALU.mult),
            inc=S["vV"])
        rdone = S["vV"].n
        for c in range(4):
            kb.wait("tensor", S["vV"], rdone)
            kb.wait("tensor", S["sS"], S["sS"].n)
            kb.op("tensor", lambda c=c: nc.tensor.matmul(
                pB[0:8, 0:512], gwb[0:64, 0:8],
                scf[0:64, 512 * c:512 * (c + 1)], start=True, stop=False))
            kb.op("tensor", lambda c=c: nc.tensor.matmul(
                pB[0:8, 0:512], gwb[0:64, 8:16],
                xpre[0:64, 512 * c:512 * (c + 1)], start=False, stop=True),
                inc=S["mT"])
            kb.wait("scalar", S["mT"], S["mT"].n)
            kb.op("scalar", lambda c=c: nc.scalar.activation(
                xpre[0:8, 512 * c:512 * (c + 1)], pB[0:8, 0:512], AF.Copy),
                inc=S["sS"])
        kmn = S["sS"].n
        kb.wait("tensor", S["sS"], kmn)
        for j in range(16):
            F = 16 * kh + j
            kb.op("tensor", lambda j=j, F=F: nc.tensor.transpose(
                pD[:, 8 * F:8 * F + 8], xpre[0:8, 128 * j:128 * (j + 1)],
                kb.sb["identf"][0:8, 0:8]),
                inc=S["mT"] if j == 15 else None)
        tr_n = S["mT"].n

    kb.wait("vector", S["mT"], tr_n)
    AT = lambda j: pD[:, 8 * 0 + j: 256 + j: 8]
    atR, atI = scf[:, 0:32], scf[:, 32:64]
    T1, T2, T3, T4 = (scf[:, 64:96], scf[:, 96:128], scf[:, 128:160],
                      scf[:, 160:192])
    kb.op("vector", lambda: nc.vector.tensor_scalar(T1, AT(6), 1.0, None, op0=ALU.add))
    kb.op("vector", lambda: nc.vector.tensor_copy(T2, AT(7)))
    kb.drain("vector")
    kb.op("vector", lambda: nc.vector.tensor_mul(T3, T1, T1))
    kb.op("vector", lambda: nc.vector.tensor_mul(T4, T2, T2))
    kb.drain("vector")
    kb.op("vector", lambda: nc.vector.tensor_add(T3, T3, T4))
    kb.drain("vector")
    kb.op("vector", lambda: nc.vector.reciprocal(T3, T3))
    kb.drain("vector")
    QRv, QIv = scf[:, 192:224], scf[:, 224:256]
    kb.op("vector", lambda: nc.vector.tensor_mul(QRv, AT(4), T1))
    kb.op("vector", lambda: nc.vector.tensor_mul(QIv, AT(5), T1))
    kb.drain("vector")
    kb.op("vector", lambda: nc.vector.scalar_tensor_tensor(
        T4, AT(5), 1.0, T2, op0=ALU.mult, op1=ALU.mult))
    kb.drain("vector")
    kb.op("vector", lambda: nc.vector.tensor_add(QRv, QRv, T4))
    kb.drain("vector")
    kb.op("vector", lambda: nc.vector.scalar_tensor_tensor(
        T4, AT(4), 1.0, T2, op0=ALU.mult, op1=ALU.mult))
    kb.drain("vector")
    kb.op("vector", lambda: nc.vector.tensor_sub(QIv, QIv, T4))
    kb.drain("vector")
    kb.op("vector", lambda: nc.vector.tensor_mul(QRv, QRv, T3))
    kb.op("vector", lambda: nc.vector.tensor_mul(QIv, QIv, T3))
    kb.drain("vector")
    kb.op("vector", lambda: nc.vector.tensor_mul(T1, AT(2), QRv))
    kb.op("vector", lambda: nc.vector.tensor_mul(T2, AT(3), QIv))
    kb.op("vector", lambda: nc.vector.tensor_mul(T3, AT(2), QIv))
    kb.op("vector", lambda: nc.vector.tensor_mul(T4, AT(3), QRv))
    kb.drain("vector")
    kb.op("vector", lambda: nc.vector.tensor_sub(T1, T1, T2))
    kb.op("vector", lambda: nc.vector.tensor_add(T3, T3, T4))
    kb.drain("vector")
    kb.op("vector", lambda: nc.vector.tensor_sub(T1, AT(0), T1))
    kb.op("vector", lambda: nc.vector.tensor_sub(T3, AT(1), T3))
    kb.drain("vector")
    kb.op("vector", lambda: nc.vector.tensor_mul(atR, scf[:, 2528:2560], T1))
    kb.op("vector", lambda: nc.vector.tensor_mul(atI, scf[:, 2528:2560], T3))
    kb.drain("vector")
    kb.op("vector", lambda: nc.vector.scalar_tensor_tensor(
        T2, scf[:, 2560:2592], 1.0, T3, op0=ALU.mult, op1=ALU.mult))
    kb.op("vector", lambda: nc.vector.scalar_tensor_tensor(
        T4, scf[:, 2560:2592], 1.0, T1, op0=ALU.mult, op1=ALU.mult))
    kb.drain("vector")
    kb.op("vector", lambda: nc.vector.tensor_sub(atR, atR, T2))
    kb.op("vector", lambda: nc.vector.tensor_add(atI, atI, T4), inc=S["vV"])
    at_done = S["vV"].n

    kb.wait("tensor", S["vV"], at_done)
    kb.op("tensor", lambda: nc.tensor.transpose(pA[0:32, 0:128], atR,
          kb.sb["identf"][:]))
    kb.op("tensor", lambda: nc.tensor.transpose(pA[0:32, 128:256], atI,
          kb.sb["identf"][:]), inc=S["mT"])
    kb.wait("scalar", S["mT"], S["mT"].n)
    kb.op("scalar", lambda: nc.scalar.activation(scf[0:32, 256:384],
          pA[0:32, 0:128], AF.Copy))
    kb.op("scalar", lambda: nc.scalar.activation(scf[0:32, 384:512],
          pA[0:32, 128:256], AF.Copy), inc=S["sS"])
    art_done = S["sS"].n
    kb.wait("tensor", S["sS"], art_done)
    kb.op("tensor", lambda: nc.tensor.matmul(pB[:, 0:32], scf[0:32, 256:384],
          scf[0:32, 2592:2624], start=True, stop=False))
    kb.op("tensor", lambda: nc.tensor.matmul(pB[:, 0:32], scf[0:32, 384:512],
          scf[0:32, 2656:2688], start=False, stop=True))
    kb.op("tensor", lambda: nc.tensor.matmul(pB[:, 32:64], scf[0:32, 256:384],
          scf[0:32, 2624:2656], start=True, stop=False))
    kb.op("tensor", lambda: nc.tensor.matmul(pB[:, 32:64], scf[0:32, 384:512],
          scf[0:32, 2592:2624], start=False, stop=True), inc=S["mT"])
    kb.wait("scalar", S["mT"], S["mT"].n)
    kb.op("scalar", lambda: nc.scalar.activation(scf[:, 512:544], pB[:, 0:32],
          AF.Copy))
    kb.op("scalar", lambda: nc.scalar.activation(scf[:, 544:576], pB[:, 32:64],
          AF.Copy), inc=S["sS"])
    h_done = S["sS"].n
    kb.wait("vector", S["sS"], h_done)
    HR, HI = scf[:, 512:544], scf[:, 544:576]
    HpR, HpI = scf[:, 576:608], scf[:, 608:640]
    kb.op("vector", lambda: nc.vector.tensor_mul(HpR, HR, scf[:, 2688:2720]))
    kb.op("vector", lambda: nc.vector.tensor_mul(HpI, HR, scf[:, 2720:2752]))
    kb.drain("vector")
    kb.op("vector", lambda: nc.vector.scalar_tensor_tensor(
        T1, HI, 1.0, scf[:, 2720:2752], op0=ALU.mult, op1=ALU.mult))
    kb.op("vector", lambda: nc.vector.scalar_tensor_tensor(
        T2, HI, 1.0, scf[:, 2688:2720], op0=ALU.mult, op1=ALU.mult))
    kb.drain("vector")
    kb.op("vector", lambda: nc.vector.tensor_sub(HpR, HpR, T1))
    kb.op("vector", lambda: nc.vector.tensor_add(HpI, HpI, T2), inc=S["vV"])
    hp_done = S["vV"].n
    kb.wait("tensor", S["vV"], hp_done)
    kb.op("tensor", lambda: nc.tensor.matmul(pA[:, 0:32], scf[:, 2752:2880],
          HpR, start=True, stop=False))
    kb.op("tensor", lambda: nc.tensor.matmul(pA[:, 0:32], scf[:, 2880:3008],
          HpI, start=False, stop=True), inc=S["mT"])
    k2_done = S["mT"].n
    kb.wait("scalar", S["mT"], k2_done)
    kb.op("scalar", lambda: nc.scalar.activation(ktile[:], pA[:, 0:32],
          AF.Copy, scale=1.0 / 4096.0), inc=S["sS"])
    kt_done = S["sS"].n
    kb.wait("vector", S["sS"], kt_done)
    kb.op("vector", lambda: nc.vector.tensor_scalar(
        ktile[0:1, 0:1], ktile[0:1, 0:1], DD, None, op0=ALU.add), inc=S["vV"])
    kd = S["vV"].n
    kb.wait("gpsimd", S["vV"], kd)
    kb.wait("gpsimd", S["dW"], zkp)
    kpd = kb.dma("gpsimd",
                 lambda: bass.AP(D["kpad"], 4096, [[32, 128], [1, 32]]),
                 lambda: ktile[:], S["dW"])
    return kpd


def emit_conv(kb, inbuf, outbuf, resid_load, normed, scr_out):
    S = kb.sems; nc = kb.nc; D = kb.D
    wsb, rmat = kb.sb["wsb"], kb.sb["rmat"]
    pA, pB = kb.ps["pA"], kb.ps["pB"]

    kpd = emit_kbuild(kb)

    kb.wait("tensor", S["sS"], S["sS"].n)
    fl_mm = [0, 0]; fl_ev = [0, 0]
    for j in range(NT):
        sl = j % 2
        pt = kb.ps["pA" if sl == 0 else "pB"]
        if j >= 2:
            kb.wait("tensor", S["sS"], fl_ev[sl])
        for half in range(2):
            kb.op("tensor", lambda j=j, half=half, pt=pt: nc.tensor.matmul(
                pt[:, half * 512:(half + 1) * 512], rmat[:],
                inbuf[:, j * 1024 + half * 512: j * 1024 + (half + 1) * 512],
                start=True, stop=True),
                inc=S["mT"] if half == 1 else None)
        fl_mm[sl] = S["mT"].n
        kb.wait("scalar", S["mT"], fl_mm[sl])
        kb.op("scalar", lambda j=j, pt=pt: nc.scalar.activation(
            inbuf[:, j * 1024:(j + 1) * 1024], pt[:], AF.Copy), inc=S["sS"])
        fl_ev[sl] = S["sS"].n
    flip_done = S["sS"].n

    kb.wait("sync", S["dW"], kpd)
    bnd = kb.dma("sync", wsb[:, 0:4096],
                 lambda: bass.AP(D["kpad"], 3969, [[1, 128], [128, 32], [1, 128]]),
                 S["dX1"])

    def mm_emit(kb2, t, pt):
        if t == 0:
            kb.wait("tensor", S["dX1"], bnd)
            kb.wait("tensor", S["sS"], flip_done)
        for j in range(t + 1):
            for half in range(2):
                kb.op("tensor", lambda t=t, j=j, half=half, pt=pt:
                      nc.tensor.matmul(
                          pt[:, half * 512:(half + 1) * 512],
                          wsb[:, (t - j) * 128:(t - j + 1) * 128],
                          inbuf[:, j * 1024 + half * 512:
                                j * 1024 + (half + 1) * 512],
                          start=(j == 0), stop=(j == t)),
                      inc=S["mT"] if (j == t and half == 1) else None)
        return S["mT"].n

    emit_evac_loop(kb, NT, mm_emit, resid_load, normed, outbuf, scr_out)


def build_conv_phase_program():
    nc = bass.Bass("TRN2", target_bir_lowering=False)
    ctx = ExitStack()
    with ctx:
        kb = KB(nc, ctx)
        setup_common(kb, {"p2"})
        kb.D["xnin"] = nc.dram_tensor("xnin", [HD, TPC], F32, kind="ExternalInput")
        load_consts(kb, {"p2"})
        emit_cast_load_buf(kb, kb.D["xnin"], kb.sb["bufA"])
        kb.wait("tensor", kb.sems["dW"], kb.sems["dW"].n)
        kb.wait("tensor", kb.sems["dXQ"], kb.sems["dXQ"].n)
        emit_conv(kb, kb.sb["bufA"], kb.sb["bufB"],
                  resid_from_dram(kb, kb.D["xnin"]), False, kb.D["scr0"])
        emit_stats_finalize(kb)
        emit_final_output(kb, kb.D["scr0"], kb.D["out"])
        kb.finalize_block()
    return nc


def conv_phase_inputs(inputs):
    m = dict(
        ones_bf=NPC["ones_bf"], ones1f=NPC["ones1x128_f"], onesm=NPC["onesm"],
        identb=NPC["ident_bf"], identf=NPC["ident_f"],
        tri=NPC["tri01_bf"], rmat=NPC["rmat_bf"],
        lam=np.ascontiguousarray(inputs["L1_attn_Lambda_ri"]),
        pri=np.ascontiguousarray(inputs["L1_attn_p_ri"]),
        qri=np.ascontiguousarray(inputs["L1_attn_q_ri"]),
        vcrt=np.ascontiguousarray(inputs["L1_attn_Vc_ri"][:, :, 0].T),
        vcit=np.ascontiguousarray(inputs["L1_attn_Vc_ri"][:, :, 1].T),
        ctri=np.ascontiguousarray(inputs["L1_attn_Ct_ri"].reshape(64, 2)),
        bp=np.ascontiguousarray(inputs["L1_attn_Bp"]),
        dd=np.ascontiguousarray(inputs["L1_attn_D"].reshape(1, 1)),
        lstep=np.ascontiguousarray(inputs["L1_attn_log_step"].reshape(1, 1)),
        tim=NPC["tim"], ccr=NPC["ccr"], cci=NPC["cci"],
        w32r=NPC["w32r"], w32i=NPC["w32i"], w32in=NPC["w32in"],
        twr=NPC["twr"], twi=NPC["twi"], c128=NPC["c128"], s128n=NPC["s128n"],
    )
    return m


ATTN_HEADS = NHEADS
ATTN_SKIP_SWITCH = False
ATTN_PARTS = 8


def emit_attn(kb, outbuf, AO, resid_load, scr_out):
    S = kb.sems; nc = kb.nc; D = kb.D
    wsb, wpool = kb.sb["wsb"], kb.sb["wpool"]
    qsb, attnT, rbcs = kb.sb["qsb"], kb.sb["attnT"], kb.sb["rbcs"]
    biastab, tri = kb.sb["biastab"], kb.sb["tri"]
    scf = kb.sb["scf"]
    pA, pB, pC, pD = kb.ps["pA"], kb.ps["pB"], kb.ps["pC"], kb.ps["pD"]
    x0buf = outbuf

    xsb = [wsb[:, 0:1024], wsb[:, 1024:2048]]
    wkvs = [wsb[:, 2048:2304], wsb[:, 2304:2560]]
    kv_dma = [0, 0]; kv_mm = [0, 0]
    ev_k = 0; ev_v = 0; tr_ev = [0, 0]
    for kc in range(4):
        for ht in range(NT):
            sl = ht % 2
            if kc + ht > 0:
                kb.wait("gpsimd", S["mT"], kv_mm[sl])
            ksem = S["dK0" if sl == 0 else "dK1"]
            kb.dma("gpsimd", lambda sl=sl: xsb[sl],
                   lambda kc=kc, ht=ht: D["xT"].ap()[128 * ht:128 * (ht + 1),
                                                     1024 * kc:1024 * (kc + 1)],
                   ksem)
            kv_dma[sl] = kb.dma("gpsimd", lambda sl=sl: wkvs[sl],
                                lambda ht=ht: D["wkv"].ap()[128 * ht:128 * (ht + 1), :],
                                ksem)
            kb.wait("tensor", ksem, kv_dma[sl])
            if ht == 0:
                kb.wait("tensor", S["sS"], ev_v)
            for half in range(2):
                kb.op("tensor", lambda sl=sl, half=half, ht=ht: nc.tensor.matmul(
                    pA[:, half * 512:(half + 1) * 512], wkvs[sl][:, 0:128],
                    xsb[sl][:, half * 512:(half + 1) * 512],
                    start=(ht == 0), stop=(ht == NT - 1)))
            for half in range(2):
                kb.op("tensor", lambda sl=sl, half=half, ht=ht: nc.tensor.matmul(
                    pB[:, half * 512:(half + 1) * 512], wkvs[sl][:, 128:256],
                    xsb[sl][:, half * 512:(half + 1) * 512],
                    start=(ht == 0), stop=(ht == NT - 1)),
                    inc=S["mT"] if half == 1 else None)
            kv_mm[sl] = S["mT"].n
        kb.wait("scalar", S["mT"], kv_mm[(NT - 1) % 2])
        kb.op("scalar", lambda kc=kc: nc.scalar.activation(
            wpool[:, 1024 * kc:1024 * (kc + 1)], pA[:], AF.Copy))
        kb.op("scalar", lambda: nc.scalar.activation(
            scf[:, 0:1024], pB[:], AF.Copy), inc=S["sS"])
        ev_v = S["sS"].n
        for i in range(8):
            tsl = i % 2
            kb.wait("tensor", S["sS"], max(ev_v, tr_ev[tsl]))
            kb.op("tensor", lambda i=i, tsl=tsl: nc.tensor.transpose(
                pD[:, 512 * tsl:512 * tsl + 128], scf[:, 128 * i:128 * (i + 1)],
                kb.sb["identf"][:]), inc=S["mT"])
            kb.wait("scalar", S["mT"], S["mT"].n)
            kb.op("scalar", lambda kc=kc, i=i, tsl=tsl: nc.scalar.activation(
                wpool[:, 4096 + (8 * kc + i) * 128: 4096 + (8 * kc + i + 1) * 128],
                pD[:, 512 * tsl:512 * tsl + 128], AF.Copy), inc=S["sS"])
            tr_ev[tsl] = S["sS"].n
    kv_done_s = S["sS"].n

    def emit_variant(v):
        units = attn_units(v)
        by_head = {}
        for uid, d in enumerate(units):
            by_head.setdefault(d["h"], []).append((uid, d))
        Q = [512 * v, 3584 - 512 * v]
        q_ev = [kv_done_s, kv_done_s]
        q_ev_last = kv_done_s
        att_done = [0, 0]
        exp_n = [0, 0]
        trv_n = [0, 0]
        avden = [0, 0]
        avev = [S["vV"].n, S["vV"].n]
        rbcev = [kv_done_s, kv_done_s]
        qq_mm = [S["mT"].n] * 4
        useq = 0
        for h in range(ATTN_HEADS):
            hsl = h % 2
            for q in range(4):
                kb.wait("gpsimd", S["mT"], qq_mm[q])
                qd = kb.dma("gpsimd", lambda q=q: wsb[:, q * 1024:(q + 1) * 1024],
                            lambda h=h, q=q: D["wq"].ap()[h, q], S["dQ" + str(q)])
                kb.wait("tensor", S["dQ" + str(q)], qd)
                if q == 0:
                    kb.wait("tensor", S["sS"], q_ev_last)
                for j in range(8):
                    u = 8 * q + j
                    for half in range(2):
                        kb.op("tensor", lambda u=u, half=half: nc.tensor.matmul(
                            pC[:, half * 512:(half + 1) * 512],
                            wsb[:, u * 128:(u + 1) * 128],
                            x0buf[:, u * 1024 + half * 512: u * 1024 + (half + 1) * 512],
                            start=(u == 0), stop=(u == NT - 1)),
                            inc=S["mT"] if (j == 7 and half == 1) else None)
                qq_mm[q] = S["mT"].n
            qmm = S["mT"].n
            kb.wait("scalar", S["mT"], max(qmm, att_done[hsl]))
            kb.op("scalar", lambda hsl=hsl: nc.scalar.activation(
                qsb[:, 1024 * hsl:1024 * (hsl + 1)], pC[:], AF.Copy), inc=S["sS"])
            q_ev[hsl] = S["sS"].n
            q_ev_last = S["sS"].n
            blocks = {}
            for uid, d in by_head[h]:
                blocks.setdefault(d["b"], []).append((uid, d))
            for b, ulist in sorted(blocks.items()):
                bcol = 512 * b
                first = True
                nu = len(ulist)
                for i, (uid, d) in enumerate(ulist):
                    last = (i == nu - 1)
                    usl = useq % 2
                    useq += 1
                    w = d["w"]
                    if i == 0:
                        kb.wait("tensor", S["vV"], max(avev[0], avev[1]))
                        kb.wait("tensor", S["sS"], max(rbcev[0], rbcev[1]))
                    kb.wait("tensor", S["sS"], q_ev[hsl])
                    kb.wait("tensor", S["aRs"], exp_n[usl])
                    kb.op("tensor", lambda d=d, usl=usl, w=w, hsl=hsl, b=b, Q=Q:
                          nc.tensor.matmul(
                              pA[:, 512 * usl:512 * usl + w],
                              wpool[:, d["t"] * 128:(d["t"] + 1) * 128],
                              qsb[:, 1024 * hsl + 512 * b + (d["A"] + d["off"] - Q[b]):
                                  1024 * hsl + 512 * b + (d["A"] + d["off"] - Q[b]) + w],
                              start=True, stop=True), inc=S["mT"])
                    sc_n = S["mT"].n
                    if ATTN_PARTS < 2:
                        exp_n[usl] = S["aRs"].n
                        first = False
                        continue
                    kb.wait("scalar", S["mT"], max(sc_n, avden[usl]))
                    kb.op("scalar", lambda uid=uid, usl=usl, w=w:
                          nc.scalar.activation(
                              attnT[:, 512 * usl:512 * usl + w],
                              pA[:, 512 * usl:512 * usl + w], AF.Exp,
                              bias=biastab[:, uid:uid + 1], scale=ASCALE),
                          inc=S["aRs"])
                    exp_n[usl] = S["aRs"].n
                    if ATTN_PARTS < 3:
                        first = False
                        continue
                    if ATTN_PARTS >= 3 and d["diag"]:
                        kb.wait("vector", S["aRs"], exp_n[usl])
                        kb.op("vector", lambda d=d, usl=usl:
                              nc.vector.tensor_mul(
                                  attnT[:, 512 * usl:512 * usl + d["tri_w"]],
                                  attnT[:, 512 * usl:512 * usl + d["tri_w"]],
                                  tri[:, d["tri_shift"]:d["tri_shift"] + d["tri_w"]]),
                              inc=S["aRv"])
                        trv_n[usl] = S["aRv"].n
                    if ATTN_PARTS < 4:
                        first = False
                        continue
                    kb.wait("tensor", S["aRs"], exp_n[usl])
                    if d["diag"]:
                        kb.wait("tensor", S["aRv"], trv_n[usl])
                    if first:
                        kb.wait("tensor", S["vV"], max(avev[0], avev[1]))
                        kb.wait("tensor", S["sS"], max(rbcev[0], rbcev[1]))
                    col0 = bcol + (d["A"] + d["off"] - Q[b])
                    kb.op("tensor", lambda d=d, usl=usl, w=w, col0=col0, b=b, first=first, last=last:
                          nc.tensor.matmul(
                              pB[:, col0 - 512 * b:col0 - 512 * b + w],
                              wpool[:, 4096 + d["t"] * 128: 4096 + (d["t"] + 1) * 128],
                              attnT[:, 512 * usl:512 * usl + w],
                              start=first, stop=last))
                    dcol = col0 - 512 * b
                    kb.op("tensor", lambda usl=usl, w=w, dcol=dcol, first=first, last=last:
                          nc.tensor.matmul(
                              pD[:, dcol:dcol + w],
                              kb.sb["onesm"][:],
                              attnT[:, 512 * usl:512 * usl + w],
                              start=first, stop=last), inc=S["mT"])
                    avden[usl] = S["mT"].n
                    first = False
                att_done[hsl] = S["mT"].n
                avsl = scf[:, 1024 * b:1024 * b + 512]
                dnsl = scf[:, 1024 * b + 512:1024 * b + 1024]
                rcp = scf[:, 2048:2560]
                kb.wait("scalar", S["mT"], att_done[hsl])
                kb.wait("scalar", S["vV"], avev[b])
                kb.op("scalar", lambda avsl=avsl: nc.scalar.activation(
                    avsl, pB[:, 0:512], AF.Copy))
                kb.op("scalar", lambda dnsl=dnsl: nc.scalar.activation(
                    dnsl, pD[:, 0:512], AF.Copy), inc=S["sS"])
                rbcev[b] = S["sS"].n
                kb.wait("vector", S["sS"], rbcev[b])
                kb.op("vector", lambda dnsl=dnsl, rcp=rcp:
                      nc.vector.reciprocal(rcp, dnsl))
                kb.drain("vector")
                kb.op("vector", lambda h=h, b=b, avsl=avsl, rcp=rcp:
                      nc.vector.tensor_mul(
                          AO[:, 1024 * h + 512 * b: 1024 * h + 512 * (b + 1)],
                          avsl, rcp), inc=S["vV"])
                avev[b] = S["vV"].n

    if not ATTN_SKIP_SWITCH:
        kb.switch(4, emit_variant, ["tensor", "vector", "scalar", "gpsimd"])

    kb.wait("tensor", S["vV"], S["vV"].n)
    mm_emit = make_wstream_mm_emit(kb, D["wo"], AO, extra=None)
    emit_evac_loop(kb, NT, mm_emit, resid_load, False, outbuf, scr_out)


def build_attn_phase_program():
    nc = bass.Bass("TRN2", target_bir_lowering=False)
    ctx = ExitStack()
    with ctx:
        kb = KB(nc, ctx)
        setup_common(kb, {"p0"})
        load_consts(kb, {"p0"})
        emit_cast_load_buf(kb, kb.D["x0p"], kb.sb["bufA"])
        kb.wait("tensor", kb.sems["dW"], kb.sems["dW"].n)
        kb.wait("tensor", kb.sems["dXQ"], kb.sems["dXQ"].n)
        emit_attn(kb, kb.sb["bufA"], kb.sb["bufB"],
                  resid_from_dram(kb, kb.D["x0p"]), kb.D["scr0"])
        emit_stats_finalize(kb)
        emit_final_output(kb, kb.D["scr0"], kb.D["out"])
        kb.finalize_block()
    return nc


def attn_phase_inputs(inputs, core, x_full_T=None):
    if x_full_T is None:
        x_full_T = np.ascontiguousarray(
            inputs["L0_attn_hidden_states"].transpose(0, 2, 1))
    batch = core // 4
    v = core % 4
    lo, hi = 512 * v, 3584 - 512 * v
    xT = x_full_T[batch]
    x0p = np.concatenate([xT[:, lo:lo + 512], xT[:, hi:hi + 512]], axis=1)
    m = dict(
        ones_bf=NPC["ones_bf"], ones1f=NPC["ones1x128_f"], onesm=NPC["onesm"],
        identb=NPC["ident_bf"], identf=NPC["ident_f"],
        tri=NPC["tri01_bf"], rmat=NPC["rmat_bf"],
        x0p=np.ascontiguousarray(x0p), xT=np.ascontiguousarray(xT),
        wq=pack_w(inputs["L0_attn_Wq"]),
        wkv=np.ascontiguousarray(inputs["L0_attn_Wkv"]),
        wo=pack_w(inputs["L0_attn_Wo"]),
        biastab=bias_table(core),
    )
    return m


_PROG_CACHE = {}


def _get_prog(name, builder):
    if name not in _PROG_CACHE:
        _PROG_CACHE[name] = builder()
    return _PROG_CACHE[name]


def kernel_unfused(**inputs):
    x_full_T = np.ascontiguousarray(
        np.asarray(inputs["L0_attn_hidden_states"]).transpose(0, 2, 1))
    cores = list(range(8))

    nc0 = _get_prog("p0", build_attn_phase_program)
    maps = [attn_phase_inputs(inputs, c, x_full_T) for c in cores]
    r = run_bass_kernel_spmd(nc0, maps, core_ids=cores)
    xn = [np.asarray(r.results[c]["out"]) for c in cores]

    nc1 = _get_prog("p1", lambda: build_ffn_phase_program(0))
    maps = []
    for c in cores:
        m = ffn_phase_inputs(0, inputs, c)
        m["xnin"] = xn[c]
        maps.append(m)
    r = run_bass_kernel_spmd(nc1, maps, core_ids=cores)
    xn = [np.asarray(r.results[c]["out"]) for c in cores]

    nc2 = _get_prog("p2", build_conv_phase_program)
    maps = []
    for c in cores:
        m = conv_phase_inputs(inputs)
        m["xnin"] = xn[c]
        maps.append(m)
    r = run_bass_kernel_spmd(nc2, maps, core_ids=cores)
    xn = [np.asarray(r.results[c]["out"]) for c in cores]

    nc3 = _get_prog("p3", lambda: build_ffn_phase_program(1))
    maps = []
    for c in cores:
        m = ffn_phase_inputs(1, inputs, c)
        m["xnin"] = xn[c]
        maps.append(m)
    r = run_bass_kernel_spmd(nc3, maps, core_ids=cores)

    yT = np.empty((2, HD, HD), dtype=np.float32)
    for c in cores:
        out = np.asarray(r.results[c]["out"])
        b, v = c // 4, c % 4
        lo, hi = 512 * v, 3584 - 512 * v
        yT[b][:, lo:lo + 512] = out[:, 0:512]
        yT[b][:, hi:hi + 512] = out[:, 512:1024]
    return np.ascontiguousarray(yT.transpose(0, 2, 1))


def _phase_barrier(kb):
    S = kb.sems
    snap = [(n, s.n) for n, s in S.items()]
    for eng in ENGS:
        for n, v in snap:
            if v > 0:
                kb.wait(eng, S[n], v)


def build_fused_program():
    nc = bass.Bass("TRN2", target_bir_lowering=False)
    ctx = ExitStack()
    with ctx:
        kb = KB(nc, ctx)
        phases = {"p0", "p1", "p2", "p3"}
        setup_common(kb, phases)
        load_consts(kb, phases)
        D = kb.D
        bufA, bufB = kb.sb["bufA"], kb.sb["bufB"]
        emit_cast_load_buf(kb, D["x0p"], bufA)
        kb.wait("tensor", kb.sems["dW"], kb.sems["dW"].n)
        kb.wait("tensor", kb.sems["dXQ"], kb.sems["dXQ"].n)
        emit_attn(kb, bufA, bufB, resid_from_dram(kb, D["x0p"]), D["scr0"])
        emit_stats_finalize(kb)
        emit_norm_pass(kb, bufA)
        _phase_barrier(kb)
        emit_ffn(kb, 0, bufA, bufB, resid_from_scr(kb, D["scr0"]), True, D["scr1"])
        emit_stats_finalize(kb)
        emit_norm_pass(kb, bufB)
        _phase_barrier(kb)
        emit_conv(kb, bufB, bufA, resid_from_scr(kb, D["scr1"]), True, D["scr0"])
        emit_stats_finalize(kb)
        emit_norm_pass(kb, bufA)
        _phase_barrier(kb)
        emit_ffn(kb, 1, bufA, bufB, resid_from_scr(kb, D["scr0"]), True, D["scr1"])
        emit_stats_finalize(kb)
        _phase_barrier(kb)
        emit_final_output(kb, D["scr1"], D["out"])
        kb.finalize_block()
    return nc


def resid_from_scr(kb, dram):
    return resid_from_dram(kb, dram)


def fused_inputs(inputs, core, x_full_T):
    m = attn_phase_inputs(inputs, core, x_full_T)
    m.update(ffn_phase_inputs(0, inputs, core))
    m.update(ffn_phase_inputs(1, inputs, core))
    m.update(conv_phase_inputs(inputs))
    return m


def kernel(**inputs):
    x_full_T = np.ascontiguousarray(
        np.asarray(inputs["L0_attn_hidden_states"]).transpose(0, 2, 1))
    cores = list(range(8))
    ncf = _get_prog("fused", build_fused_program)
    maps = [fused_inputs(inputs, c, x_full_T) for c in cores]
    r = run_bass_kernel_spmd(ncf, maps, core_ids=cores)
    yT = np.empty((2, HD, HD), dtype=np.float32)
    for c in cores:
        out = np.asarray(r.results[c]["out"])
        b, v = c // 4, c % 4
        lo, hi = 512 * v, 3584 - 512 * v
        yT[b][:, lo:lo + 512] = out[:, 0:512]
        yT[b][:, hi:hi + 512] = out[:, 512:1024]
    return np.ascontiguousarray(yT.transpose(0, 2, 1))


# version 05 (final)
# speedup vs baseline: 1.6030x; 1.2407x over previous
import numpy as np
import ml_dtypes
from contextlib import ExitStack

import concourse.bass as bass
import concourse.mybir as mybir
from concourse.bass_utils import run_bass_kernel_spmd

BF = ml_dtypes.bfloat16
F32 = mybir.dt.float32
BF16 = mybir.dt.bfloat16
I32 = mybir.dt.int32
AF = mybir.ActivationFunctionType
ALU = mybir.AluOpType
AX = mybir.AxisListType

HD = 4096
NT = 32
TPC = 1024
NHEADS = 32
DH = 128
ASCALE = float(DH) ** -0.5
EPS_LN = 1e-5

_start = 2.0 ** (-(2.0 ** -(np.log2(NHEADS) - 3)))
SLOPES = np.array([_start ** (i + 1) for i in range(NHEADS)], dtype=np.float64)

def _chunk_for(slope):
    for c in (512, 256, 128, 64):
        if slope * (c - 1) <= 65.0:
            return c
    return 64

CHUNK = [_chunk_for(s) for s in SLOPES]
CUTOFF = [130.0 / s for s in SLOPES]


def _np_consts():
    c = {}
    c["ones_bf"] = np.ones((128, 1), dtype=BF)
    c["ones1x128_f"] = np.ones((1, 128), dtype=np.float32)
    c["ident_bf"] = np.eye(128, dtype=BF)
    c["ident_f"] = np.eye(128, dtype=np.float32)
    c["tri01_bf"] = np.triu(np.ones((128, 128), dtype=np.float32)).astype(BF)
    c["rmat_bf"] = np.eye(128, dtype=np.float32)[::-1].copy().astype(BF)
    c["onesm"] = np.ones((128, 128), dtype=BF)
    c["krepl8"] = np.repeat(np.eye(8, dtype=np.float32), 16, axis=1).astype(BF)
    c["krepl16x2"] = (2.0 * np.repeat(np.eye(16, dtype=np.float32), 8, axis=1)).astype(BF)
    c["iota16c"] = np.tile(np.arange(16, dtype=np.float32)[:, None], (1, 4))
    c["iota4r"] = np.tile(np.arange(16, dtype=np.float32)[None, :], (4, 1))
    a32 = np.arange(32)
    w32 = np.exp(2j * np.pi * np.outer(a32, a32) / 32.0)
    c["w32r"] = w32.real.astype(np.float32)
    c["w32i"] = w32.imag.astype(np.float32)
    c["w32in"] = (-w32.imag).astype(np.float32)
    p128 = np.arange(128)
    tw = np.exp(2j * np.pi * np.outer(p128, a32) / 4096.0)
    c["twr"] = tw.real.astype(np.float32)
    c["twi"] = tw.imag.astype(np.float32)
    w128 = np.exp(2j * np.pi * np.outer(p128, p128) / 128.0)
    c["c128"] = w128.real.astype(np.float32)
    c["s128n"] = (-w128.imag).astype(np.float32)
    k = (np.arange(32)[None, :] * 128 + np.arange(128)[:, None]).astype(np.float64)
    om = np.exp(-2j * np.pi * k / 4096.0)
    cc = 2.0 / (1.0 + om)
    c["ccr"] = cc.real.astype(np.float32)
    c["cci"] = cc.imag.astype(np.float32)
    tim = np.tan(np.pi * np.arange(4096, dtype=np.float64) / 4096.0)
    c["tim"] = tim[None, :].astype(np.float32)
    return c

NPC = _np_consts()


def attn_units(variant):
    k = variant
    Q = [512 * k, 3584 - 512 * k]
    units = []
    for h in range(NHEADS):
        c = CHUNK[h]
        cut = CUTOFF[h]
        for b in (0, 1):
            q0 = Q[b]
            for ch in range(512 // c):
                A = q0 + ch * c
                t_hi = (A + c - 1) // 128
                t_lo = max(0, int(np.floor((A - cut) / 128.0)))
                for t in range(t_lo, t_hi + 1):
                    off = max(0, t * 128 - A)
                    w = c - off
                    diag = (t * 128 + 127 >= A)
                    tri_shift = max(0, A - t * 128) if diag else 0
                    tri_w = min(128 - tri_shift, w) if diag else 0
                    units.append(dict(h=h, b=b, A=A, c=c, t=t, off=off, w=w,
                                      diag=diag, tri_shift=tri_shift,
                                      tri_w=tri_w))
    return units

N_UNITS = [len(attn_units(v)) for v in range(4)]
NU_MAX = max(N_UNITS)


def pack_w(w):
    return np.ascontiguousarray(
        np.asarray(w).reshape(4, 8, 128, 32, 128).transpose(3, 0, 2, 1, 4))


def bias_table(core):
    v = core % 4
    units = attn_units(v)
    tab = np.zeros((128, NU_MAX), dtype=np.float32)
    p = np.arange(128, dtype=np.float64)
    for u, d in enumerate(units):
        kref = d["A"] + d["c"] - 1
        tab[:, u] = (SLOPES[d["h"]] * (d["t"] * 128 + p - kref)).astype(np.float32)
    return tab


ENGS = ("tensor", "vector", "scalar", "sync", "gpsimd")
ETYPE = {"tensor": mybir.EngineType.PE, "vector": mybir.EngineType.DVE,
         "scalar": mybir.EngineType.Activation, "sync": mybir.EngineType.SP,
         "gpsimd": mybir.EngineType.Pool}


class Sem:
    __slots__ = ("h", "n", "name")
    def __init__(self, h, name):
        self.h = h
        self.n = 0
        self.name = name


class KB:
    def __init__(self, nc, ctx):
        self.nc = nc
        self.ctx = ctx
        self.streams = {e: [] for e in ENGS}
        self._stack = []
        self.sems = {}
        self.sb = {}
        self.ps = {}

    def sem(self, name):
        s = Sem(self.ctx.enter_context(self.nc.semaphore(name)), name)
        self.sems[name] = s
        return s

    def sbuf(self, name, shape, dt):
        t = self.ctx.enter_context(self.nc.sbuf_tensor(name + "_s", shape, dt))
        self.sb[name] = t
        return t

    def psum(self, name, shape, dt):
        t = self.ctx.enter_context(self.nc.psum_tensor(name + "_p", shape, dt))
        self.ps[name] = t
        return t

    def emit(self, eng, fn):
        self.streams[eng].append(fn)

    def op(self, eng, fn, inc=None, inc2=None):
        if inc is not None:
            sem, k = inc if isinstance(inc, tuple) else (inc, 1)
            sem.n += k
            if inc2 is not None:
                s2, k2 = inc2 if isinstance(inc2, tuple) else (inc2, 1)
                s2.n += k2
                self.emit(eng, lambda: fn().then_inc(sem.h, k).then_inc(s2.h, k2))
            else:
                self.emit(eng, lambda: fn().then_inc(sem.h, k))
            return sem.n
        self.emit(eng, fn)
        return None

    def wait(self, eng, sem, val=None):
        v = sem.n if val is None else val
        h = sem.h
        eobj = getattr(self.nc, eng)
        self.emit(eng, lambda: eobj.wait_ge(h, v))

    def dma(self, eng, dst, src, sem):
        sem.n += 16
        eobj = getattr(self.nc, eng)
        d = dst if callable(dst) else (lambda: dst)
        s = src if callable(src) else (lambda: src)
        self.emit(eng, lambda: eobj.dma_start(d(), s()).then_inc(sem.h, 16))
        return sem.n

    def drain(self, eng):
        eobj = getattr(self.nc, eng)
        self.emit(eng, lambda: eobj.drain())

    def switch(self, nvar, emit_fn, engines):
        pre = {n: s.n for n, s in self.sems.items()}
        var_streams = []
        var_counts = []
        for v in range(nvar):
            for n, s in self.sems.items():
                s.n = pre[n]
            saved = self.streams
            self.streams = {e: [] for e in ENGS}
            emit_fn(v)
            var_streams.append(self.streams)
            self.streams = saved
            var_counts.append({n: s.n for n, s in self.sems.items()})
        final = {n: max(vc[n] for vc in var_counts) for n in self.sems}
        for n, s in self.sems.items():
            s.n = final[n]
        for v in range(nvar):
            for e in ENGS:
                if e not in engines:
                    assert not var_streams[v][e], (v, e, "undeclared switch engine")
        nc = self.nc
        for e in engines:
            owned = [n for n in self.sems
                     if any(var_counts[v][n] != pre[n] for v in range(nvar))
                     and SEM_OWNER.get(n) == e]
            bodies = [var_streams[v][e] for v in range(nvar)]
            deficits = [{n: final[n] - var_counts[v][n] for n in owned}
                        for v in range(nvar)]
            eobj = getattr(nc, e)
            et = ETYPE[e]
            semmap = {n: self.sems[n].h for n in owned}
            def do_switch(bodies=bodies, deficits=deficits, eobj=eobj, et=et,
                          semmap=semmap, nvar=nvar):
                pid = nc.partition_id(engines=[et])
                var = pid % nvar
                for i in eobj.Switch(var, nvar):
                    for f in bodies[i]:
                        f()
                    for n, d in deficits[i].items():
                        if d > 0:
                            eobj.sem_inc(semmap[n], d)
            self.emit(e, do_switch)

    def finalize_block(self):
        nc = self.nc
        block = self.ctx.enter_context(nc.Block())
        streams = self.streams
        @block.tensor
        def _(t):
            for f in streams["tensor"]:
                f()
        @block.vector
        def _(v):
            for f in streams["vector"]:
                f()
        @block.scalar
        def _(s):
            for f in streams["scalar"]:
                f()
        @block.sync
        def _(s):
            for f in streams["sync"]:
                f()
        @block.gpsimd
        def _(g):
            for f in streams["gpsimd"]:
                f()


SEM_OWNER = {}


def mk_sems(kb, names_owners):
    out = {}
    for name, owner in names_owners:
        out[name] = kb.sem(name)
        SEM_OWNER[name] = owner
    return out


def setup_common(kb, phases):
    nc = kb.nc
    D = {}
    def din(name, shape, dt=F32):
        D[name] = nc.dram_tensor(name, shape, dt, kind="ExternalInput")
        return D[name]
    din("ones_bf", [128, 1], BF16); din("ones1f", [1, 128])
    din("onesm", [128, 128], BF16)
    din("identb", [128, 128], BF16); din("identf", [128, 128])
    din("tri", [128, 128], BF16); din("rmat", [128, 128], BF16)
    if "p0" in phases:
        din("x0p", [HD, TPC]); din("xT", [HD, HD])
        din("wq", [NT, 4, 128, 8, 128]); din("wkv", [HD, 256]); din("wo", [NT, 4, 128, 8, 128])
        din("biastab", [128, NU_MAX])
    if "p1" in phases:
        din("wb0t", [NT, 4, 128, 8, 128]); din("b0", [128, 32]); din("wgt", [HD, 8])
        din("a0t", [HD, 128]); din("b0f", [128, HD])
        din("kr8", [8, 128], BF16)
    if "p2" in phases:
        din("lam", [64, 2]); din("pri", [64, 2]); din("qri", [64, 2])
        din("vcrt", [64, 64]); din("vcit", [64, 64])
        din("ctri", [64, 2]); din("bp", [64, 1]); din("dd", [1, 1]); din("lstep", [1, 1])
        din("tim", [1, 4096]); din("ccr", [128, 32]); din("cci", [128, 32])
        din("w32r", [32, 32]); din("w32i", [32, 32]); din("w32in", [32, 32])
        din("twr", [128, 32]); din("twi", [128, 32])
        din("c128", [128, 128]); din("s128n", [128, 128])
    if "p3" in phases:
        din("wb1t", [NT, 4, 128, 8, 128]); din("b1", [128, 32]); din("rwt", [HD, 16])
        din("la1t", [HD, 128]); din("lb1f", [128, HD])
        din("rb", [16, 1]); din("prior", [16, 1]); din("active", [1, 4], I32)
        din("kr16", [16, 128], BF16); din("io16c", [16, 4]); din("io4r", [4, 16])
    D["out"] = nc.dram_tensor("out", [HD, TPC], F32, kind="ExternalOutput")
    D["scr0"] = nc.dram_tensor("scr0", [HD, TPC], F32)
    D["scr1"] = nc.dram_tensor("scr1", [HD, TPC], F32)
    if "p2" in phases:
        D["kpad"] = nc.dram_tensor("kpad", [8192], BF16)
    kb.D = D

    kb.sbuf("bufA", [128, NT * TPC], BF16)
    kb.sbuf("bufB", [128, NT * TPC], BF16)
    kb.sbuf("wpool", [128, 8192], BF16)
    kb.sbuf("wsb", [128, 4096], BF16)
    kb.sbuf("scf", [128, 3072], F32)
    kb.sbuf("xpre", [128, 2048], F32)
    kb.sbuf("sqsb", [128, 2048], BF16)
    kb.sbuf("mbc", [128, 1024], F32)
    kb.sbuf("rbc2", [128, 1024], F32)
    kb.sbuf("cons_bf", [128, 1], BF16)
    kb.sbuf("onesm", [128, 128], BF16)
    kb.sbuf("cons1f", [1, 128], F32)
    kb.sbuf("identb", [128, 128], BF16)
    kb.sbuf("identf", [128, 128], F32)
    kb.sbuf("attnT", [128, 1024], BF16)
    if "p0" in phases:
        kb.sbuf("qsb", [128, 2048], BF16)
        kb.sbuf("rbcs", [128, 512], F32)
        kb.sbuf("biastab", [128, NU_MAX], F32)
        kb.sbuf("tri", [128, 128], BF16)
    if phases & {"p1", "p2", "p3"}:
        kb.sbuf("gw", [128, 48], F32)
        kb.sbuf("gwb", [128, 16], F32)
    if "p1" in phases or "p3" in phases:
        kb.sbuf("hwsb", [128, 1024], BF16)
        kb.sbuf("grow", [16, 1152], BF16)
        kb.sbuf("bias_sb", [128, 64], F32)
        kb.sbuf("kreps", [16, 256], BF16)
        kb.sbuf("wgs", [128, 512], BF16)
    if "p3" in phases:
        kb.sbuf("oh", [16, 64], F32)
        kb.sbuf("acti", [1, 4], I32)
        kb.sbuf("ioc", [16, 4], F32)
        kb.sbuf("ior", [4, 16], F32)
    if "p2" in phases:
        kb.sbuf("rmat", [128, 128], BF16)
        kb.sbuf("ktile", [128, 32], BF16)
    kb.alias = {}
    kb.psum("pA", [128, 1024], F32)
    kb.psum("pB", [128, 1024], F32)
    kb.psum("pC", [128, 1024], F32)
    kb.psum("pD", [128, 1024], F32)

    mk_sems(kb, [("dW", "gpsimd"), ("dXQ", "sync"), ("dSC", "sync"),
                 ("dOUT", "sync"), ("mT", "tensor"), ("vV", "vector"),
                 ("sS", "scalar"), ("gG", "gpsimd"), ("dSS", "scalar"),
                 ("aRs", "scalar"), ("aRv", "vector"),
                 ("dP", "gpsimd"),
                 ("dQ0", "gpsimd"), ("dQ1", "gpsimd"),
                 ("dQ2", "gpsimd"), ("dQ3", "gpsimd"),
                 ("dK0", "gpsimd"), ("dK1", "gpsimd"),
                 ("dX0", "sync"), ("dX1", "sync"),
                 ("dS0", "sync"), ("dS1", "sync"),
                 ("dO0", "sync"), ("dO1", "sync")])


def load_consts(kb, phases):
    D, S = kb.D, kb.sems
    kb.dma("sync", kb.sb["cons_bf"][:], D["ones_bf"].ap(), S["dXQ"])
    kb.dma("sync", kb.sb["onesm"][:], D["onesm"].ap(), S["dXQ"])
    kb.dma("sync", kb.sb["cons1f"][:], D["ones1f"].ap(), S["dXQ"])
    kb.dma("sync", kb.sb["identb"][:], D["identb"].ap(), S["dXQ"])
    kb.dma("sync", kb.sb["identf"][:], D["identf"].ap(), S["dXQ"])
    if "p0" in phases:
        kb.dma("sync", kb.sb["biastab"][:], D["biastab"].ap(), S["dXQ"])
        kb.dma("sync", kb.sb["tri"][:], D["tri"].ap(), S["dXQ"])
    if "p2" in phases:
        kb.dma("sync", kb.sb["rmat"][:], D["rmat"].ap(), S["dXQ"])
    if "p1" in phases:
        kb.dma("sync", kb.sb["kreps"][0:8, 0:128], D["kr8"].ap(), S["dXQ"])
        kb.dma("sync", kb.sb["bias_sb"][:, 0:32], D["b0"].ap(), S["dXQ"])
    if "p3" in phases:
        kb.dma("sync", kb.sb["kreps"][0:16, 128:256], D["kr16"].ap(), S["dXQ"])
        kb.dma("sync", kb.sb["bias_sb"][:, 32:64], D["b1"].ap(), S["dXQ"])
        kb.dma("sync", kb.sb["ioc"][:], D["io16c"].ap(), S["dXQ"])
        kb.dma("sync", kb.sb["ior"][:], D["io4r"].ap(), S["dXQ"])
        kb.dma("sync", kb.sb["acti"][:], D["active"].ap(), S["dXQ"])
        kb.dma("sync", kb.sb["oh"][:, 0:1], D["rb"].ap(), S["dXQ"])
        kb.dma("sync", kb.sb["oh"][:, 1:2], D["prior"].ap(), S["dXQ"])
    return S["dXQ"].n


def emit_evac_loop(kb, n_tiles, mm_emit, resid_load, normed, outbuf, scr_out,
                  bias_cols=None, psum_names=("pA", "pB")):
    S = kb.sems
    nc = kb.nc
    scf, xpre, sqsb = kb.sb["scf"], kb.sb["xpre"], kb.sb["sqsb"]
    ev1 = [0, 0]; wdone = [0, 0]; obn = [0, 0]; sqd = [0, 0]; scrd = [0, 0]
    stats_done = [0, 0]
    pending = []

    def emit_stats(t, sl, ob_ap, sq_ap):
        kb.wait("tensor", S["sS"], sqd[sl])
        for half in range(2):
            kb.op("tensor", lambda ob_ap=ob_ap, half=half, t=t:
                  nc.tensor.matmul(kb.ps["pC"][:, half * 512:(half + 1) * 512],
                                   kb.sb["onesm"][:],
                                   ob_ap[:, half * 512:(half + 1) * 512],
                                   start=(t == 0), stop=(t == n_tiles - 1)))
        for half in range(2):
            kb.op("tensor", lambda sq_ap=sq_ap, half=half, t=t:
                  nc.tensor.matmul(kb.ps["pD"][:, half * 512:(half + 1) * 512],
                                   kb.sb["onesm"][:],
                                   sq_ap[:, half * 512:(half + 1) * 512],
                                   start=(t == 0), stop=(t == n_tiles - 1)),
                  inc=(S["mT"], 1) if half == 1 else None)
        stats_done[sl] = S["mT"].n

    for t in range(n_tiles):
        sl = t % 2
        pt = kb.ps[psum_names[sl]]
        if t >= 2:
            kb.wait("tensor", S["sS"], ev1[sl])
        mm_done = mm_emit(kb, t, pt)
        if pending:
            emit_stats(*pending.pop())
        if t >= 2:
            kb.wait("sync", S["vV"], wdone[sl])
        xp = resid_load(kb, t, sl)
        if t >= 2:
            kb.wait("scalar", S["dS0" if sl == 0 else "dS1"], scrd[sl])
        kb.wait("scalar", S["mT"], mm_done)
        wout = scf[:, sl * 1024:(sl + 1) * 1024]
        kb.op("scalar", lambda pt=pt, wout=wout:
              nc.scalar.activation(wout, pt[:], AF.Copy), inc=S["sS"])
        ev1[sl] = S["sS"].n
        bap = bias_cols(t) if bias_cols is not None else None
        kb.wait("vector", S["sS"], ev1[sl])
        kb.wait("vector", S["dX0" if sl == 0 else "dX1"], xp)
        xs = xpre[:, sl * 1024:(sl + 1) * 1024]
        if normed:
            tmp = scf[:, 2048:3072]
            kb.op("vector", lambda xs=xs, tmp=tmp:
                  nc.vector.tensor_sub(tmp, xs, kb.sb["mbc"][:]))
            kb.drain("vector")
            kb.op("vector", lambda tmp=tmp:
                  nc.vector.tensor_mul(tmp, tmp, kb.sb["rbc2"][:]))
            kb.drain("vector")
            other = tmp
        else:
            other = xs
        if bap is not None:
            kb.op("vector", lambda other=other, wout=wout, bap=bap:
                  nc.vector.scalar_tensor_tensor(wout, wout, bap, other,
                                                 op0=ALU.add, op1=ALU.add),
                  inc=S["vV"])
        else:
            kb.op("vector", lambda other=other, wout=wout:
                  nc.vector.tensor_add(wout, other, wout), inc=S["vV"])
        wdone[sl] = S["vV"].n
        ob_ap = outbuf[:, t * 1024:(t + 1) * 1024]
        if t >= 2:
            kb.wait("scalar", S["mT"], stats_done[sl])
        kb.wait("scalar", S["vV"], wdone[sl])
        kb.op("scalar", lambda ob_ap=ob_ap, wout=wout:
              nc.scalar.activation(ob_ap, wout, AF.Copy))
        sq_ap = sqsb[:, sl * 1024:(sl + 1) * 1024]
        kb.op("scalar", lambda sq_ap=sq_ap, wout=wout:
              nc.scalar.activation(sq_ap, wout, AF.Square), inc=S["sS"])
        obn[sl] = S["sS"].n
        sqd[sl] = S["sS"].n
        kb.wait("sync", S["vV"], wdone[sl])
        scrd[sl] = kb.dma("sync", lambda t=t: scr_out.ap()[t * 128:(t + 1) * 128, :],
                          lambda wout=wout: wout, S["dS0" if sl == 0 else "dS1"])
        pending.append((t, sl, ob_ap, sq_ap))
    emit_stats(*pending.pop())
    return S["mT"].n


def emit_stats_finalize(kb):
    S = kb.sems; nc = kb.nc
    scf = kb.sb["scf"]
    kb.wait("vector", S["mT"], S["mT"].n)
    kb.wait("vector", S["dS0"], S["dS0"].n)
    kb.wait("vector", S["dS1"], S["dS1"].n)
    MR = scf[0:1, 0:1024]; VR = scf[0:1, 1024:2048]; T1 = scf[0:1, 2048:3072]
    kb.op("vector", lambda: nc.vector.tensor_scalar(MR, kb.ps["pC"][0:1, :],
          1.0 / HD, None, op0=ALU.mult))
    kb.op("vector", lambda: nc.vector.tensor_scalar(VR, kb.ps["pD"][0:1, :],
          1.0 / HD, None, op0=ALU.mult))
    kb.drain("vector")
    kb.op("vector", lambda: nc.vector.tensor_mul(T1, MR, MR))
    kb.drain("vector")
    kb.op("vector", lambda: nc.vector.tensor_sub(VR, VR, T1), inc=S["vV"])
    kb.drain("vector")
    kb.op("vector", lambda: nc.vector.tensor_scalar(VR, VR, EPS_LN, None,
          op0=ALU.add), inc=S["vV"])
    vv = S["vV"].n
    kb.wait("scalar", S["vV"], vv)
    kb.op("scalar", lambda: nc.scalar.activation(T1, VR, AF.Ln))
    kb.drain("scalar")
    kb.op("scalar", lambda: nc.scalar.activation(VR, T1, AF.Exp, scale=-0.5),
          inc=S["sS"])
    sv = S["sS"].n
    kb.wait("tensor", S["sS"], sv)
    for half in range(2):
        kb.op("tensor", lambda half=half: nc.tensor.matmul(
            kb.ps["pA"][:, half * 512:(half + 1) * 512], kb.sb["cons1f"][:],
            MR[:, half * 512:(half + 1) * 512], start=True, stop=True))
    for half in range(2):
        kb.op("tensor", lambda half=half: nc.tensor.matmul(
            kb.ps["pB"][:, half * 512:(half + 1) * 512], kb.sb["cons1f"][:],
            VR[:, half * 512:(half + 1) * 512], start=True, stop=True),
            inc=(S["mT"], 1) if half == 1 else None)
    mm = S["mT"].n
    kb.wait("scalar", S["mT"], mm)
    kb.op("scalar", lambda: nc.scalar.activation(kb.sb["mbc"][:], kb.ps["pA"][:], AF.Copy))
    kb.op("scalar", lambda: nc.scalar.activation(kb.sb["rbc2"][:], kb.ps["pB"][:], AF.Copy),
          inc=S["sS"])
    return S["sS"].n


def emit_norm_pass(kb, buf):
    S = kb.sems; nc = kb.nc
    ntmp = kb.sb["sqsb"]
    sv0 = S["sS"].n
    kb.wait("vector", S["sS"], sv0)
    kb.wait("gpsimd", S["sS"], sv0)
    gdone = [0, 0]
    for t in range(NT):
        sl = t % 2
        bt = buf[:, t * 1024:(t + 1) * 1024]
        nt = ntmp[:, sl * 1024:(sl + 1) * 1024]
        if t >= 2:
            kb.wait("vector", S["gG"], gdone[sl])
        kb.op("vector", lambda bt=bt, nt=nt:
              nc.vector.tensor_sub(nt, bt, kb.sb["mbc"][:]), inc=S["vV"])
        vv = S["vV"].n
        kb.wait("gpsimd", S["vV"], vv)
        kb.op("gpsimd", lambda bt=bt, nt=nt:
              nc.gpsimd.tensor_mul(bt, nt, kb.sb["rbc2"][:]), inc=S["gG"])
        gdone[sl] = S["gG"].n
    return S["gG"].n


def emit_final_output(kb, scr_in, out_dram):
    S = kb.sems; nc = kb.nc
    xpre, scf = kb.sb["xpre"], kb.sb["scf"]
    vdone = [0, 0]; outd = [0, 0]
    sv0 = S["sS"].n
    kb.wait("vector", S["sS"], sv0)
    for t in range(NT):
        sl = t % 2
        if t >= 2:
            kb.wait("sync", S["dO0" if sl == 0 else "dO1"], outd[sl])
        xp = kb.dma("sync", lambda t=t, sl=sl: xpre[:, sl * 1024:(sl + 1) * 1024],
                    lambda t=t: scr_in.ap()[t * 128:(t + 1) * 128, :],
                    S["dX0" if sl == 0 else "dX1"])
        kb.wait("vector", S["dX0" if sl == 0 else "dX1"], xp)
        xs = xpre[:, sl * 1024:(sl + 1) * 1024]
        tmp = scf[:, sl * 1024:(sl + 1) * 1024]
        if t >= 2:
            kb.wait("vector", S["dO0" if sl == 0 else "dO1"], outd[sl])
        kb.op("vector", lambda xs=xs, tmp=tmp:
              nc.vector.tensor_sub(tmp, xs, kb.sb["mbc"][:]))
        kb.drain("vector")
        kb.op("vector", lambda tmp=tmp:
              nc.vector.tensor_mul(tmp, tmp, kb.sb["rbc2"][:]), inc=S["vV"])
        vdone[sl] = S["vV"].n
        kb.wait("sync", S["vV"], vdone[sl])
        outd[sl] = kb.dma("sync", lambda t=t: out_dram.ap()[t * 128:(t + 1) * 128, :],
                          lambda tmp=tmp: tmp, S["dO0" if sl == 0 else "dO1"])
    kb.wait("sync", S["dO0"], S["dO0"].n)
    kb.wait("sync", S["dO1"], S["dO1"].n)


def emit_cast_load_buf(kb, src_dram, buf):
    S = kb.sems
    nc = kb.nc
    for t in range(NT):
        kb.dma("gpsimd", lambda t=t: buf[:, t * 1024:(t + 1) * 1024],
               lambda t=t: src_dram.ap()[t * 128:(t + 1) * 128, :], S["dW"])
    return S["dW"].n


def emit_ffn(kb, layer, inbuf, outbuf, resid_load, normed, scr_out):
    S = kb.sems; nc = kb.nc; D = kb.D
    wpool, wgs, wsb = kb.sb["wpool"], kb.sb["wgs"], kb.sb["wsb"]
    smalls = kb.sb["scf"]
    gw, gwb, grow = kb.sb["gw"], kb.sb["gwb"], kb.sb["grow"]
    zsb = kb.sb["sqsb"]
    hwsb = kb.sb["hwsb"]
    identf = kb.sb["identf"]
    pA, pB, pC, pD = kb.ps["pA"], kb.ps["pB"], kb.ps["pC"], kb.ps["pD"]
    E = 8 if layer == 0 else 16

    at_name, bf_name, wb_name = (("a0t", "b0f", "wb0t") if layer == 0
                                 else ("la1t", "lb1f", "wb1t"))
    kb.dma("gpsimd", wpool[:, 0:4096],
           lambda: D[at_name].ap().rearrange("(t p) j -> p t j", p=128), S["dP"])
    kb.dma("gpsimd", wpool[:, 4096:8192], lambda: D[bf_name].ap(), S["dP"])
    if layer == 0:
        kb.dma("gpsimd", wgs[:, 0:256],
               lambda: D["wgt"].ap().rearrange("(t p) e -> p t e", p=128), S["dP"])
    else:
        kb.dma("gpsimd", wgs[:, 0:512],
               lambda: D["rwt"].ap().rearrange("(t p) e -> p t e", p=128), S["dP"])
    pool_dma = S["dP"].n

    if layer == 1:
        oh = kb.sb["oh"]
        kb.wait("vector", S["dXQ"], S["dXQ"].n)
        kb.op("vector", lambda: nc.vector.tensor_copy(oh[0:1, 2:6], kb.sb["acti"][:]),
              inc=S["vV"])
        va = S["vV"].n
        kb.wait("tensor", S["vV"], va)
        kb.op("tensor", lambda: nc.tensor.matmul(pA[0:16, 0:4], kb.sb["cons1f"][0:1, 0:16],
              oh[0:1, 2:6], start=True, stop=True))
        kb.op("tensor", lambda: nc.tensor.matmul(pB[0:4, 0:16], oh[0:1, 2:6],
              kb.sb["cons1f"][0:1, 0:16], start=True, stop=True), inc=S["mT"])
        m1 = S["mT"].n
        kb.wait("vector", S["mT"], m1)
        kb.op("vector", lambda: nc.vector.tensor_tensor(oh[0:16, 8:12], kb.sb["ioc"][:],
              pA[0:16, 0:4], op=ALU.is_equal))
        kb.op("vector", lambda: nc.vector.tensor_tensor(oh[0:4, 16:32], kb.sb["ior"][:],
              pB[0:4, 0:16], op=ALU.is_equal), inc=S["vV"])
        v1 = S["vV"].n
        kb.wait("tensor", S["vV"], v1)
        kb.op("tensor", lambda: nc.tensor.matmul(pC[0:4, 0:1], oh[0:16, 8:12],
              oh[0:16, 1:2], start=True, stop=True), inc=S["mT"])
        m2 = S["mT"].n
        kb.wait("vector", S["mT"], m2)
        kb.op("vector", lambda: nc.vector.tensor_scalar(oh[0:4, 33:34],
              pC[0:4, 0:1], 1e-12, None, op0=ALU.add), inc=S["vV"])
        kb.wait("scalar", S["vV"], S["vV"].n)
        kb.op("scalar", lambda: nc.scalar.activation(oh[0:4, 32:33],
              oh[0:4, 33:34], AF.Ln))
        kb.drain("scalar")
        kb.op("scalar", lambda: nc.scalar.activation(grow[0:4, 0:16], oh[0:4, 16:32],
              AF.Copy), inc=S["sS"])
        onehot_done = S["sS"].n

    kb.wait("tensor", S["dP"], pool_dma)
    if layer == 1:
        kb.wait("tensor", S["sS"], onehot_done)
    for ht in range(NT):
        for half in range(2):
            kb.op("tensor", lambda ht=ht, half=half: nc.tensor.matmul(
                pC[0:E, half * 512:(half + 1) * 512],
                wgs[:, E * ht:E * (ht + 1)],
                inbuf[:, ht * 1024 + half * 512: ht * 1024 + (half + 1) * 512],
                start=(ht == 0), stop=(ht == NT - 1)),
                inc=S["mT"] if (ht == NT - 1 and half == 1) else None)
    lg_mm = S["mT"].n
    kb.wait("scalar", S["mT"], lg_mm)
    if layer == 0:
        kb.op("scalar", lambda: nc.scalar.activation(smalls[0:E, 0:1024], pC[0:E, :],
              AF.Copy), inc=S["sS"])
    else:
        kb.wait("vector", S["mT"], lg_mm)
        kb.op("vector", lambda: nc.vector.tensor_single_scalar(
            smalls[0:E, 0:1024], pC[0:E, :], kb.sb["oh"][0:16, 0:1],
            op=ALU.add), inc=S["vV"])
        kb.wait("scalar", S["vV"], S["vV"].n)
        kb.op("scalar", lambda: nc.scalar.drain(), inc=S["sS"])
    lg_ev = S["sS"].n

    if layer == 1:
        kb.wait("tensor", S["sS"], lg_ev)
        for half in range(2):
            kb.op("tensor", lambda half=half: nc.tensor.matmul(
                pD[0:4, half * 512:(half + 1) * 512], kb.sb["oh"][0:16, 8:12],
                smalls[0:16, half * 512:(half + 1) * 512], start=True, stop=True),
                inc=S["mT"] if half == 1 else None)
        sl_mm = S["mT"].n
        kb.wait("vector", S["mT"], sl_mm)
        kb.op("vector", lambda: nc.vector.tensor_single_scalar(
            smalls[0:4, 1024:2048], pD[0:4, :], kb.sb["oh"][0:4, 32:33],
            op=ALU.add), inc=S["vV"])
        kb.wait("scalar", S["vV"], S["vV"].n)
        kb.op("scalar", lambda: nc.scalar.drain(), inc=S["sS"])
        lg_ev = S["sS"].n

    nsl = E if layer == 0 else 4
    soff = 0 if layer == 0 else 1024
    kb.wait("tensor", S["sS"], lg_ev)
    for i in range(8):
        kb.op("tensor", lambda i=i, nsl=nsl, soff=soff: nc.tensor.transpose(
            pD[:, 64 + nsl * i: 64 + nsl * (i + 1)],
            smalls[0:nsl, soff + 128 * i: soff + 128 * (i + 1)],
            identf[0:nsl, 0:nsl]),
            inc=S["mT"] if i == 7 else None)
    tr_mm = S["mT"].n
    kb.wait("vector", S["mT"], tr_mm)
    M1, NM1, SS, RS = gw[:, 0:1], gw[:, 1:2], gw[:, 2:3], gw[:, 3:4]
    M2 = gw[:, 4:5]
    EQ, KEEP, EE, WW = gw[:, 8:16], gw[:, 16:24], gw[:, 24:32], gw[:, 32:40]
    tb_done = 0
    for i in range(8):
        lgt = pD[:, 64 + nsl * i: 64 + nsl * (i + 1)]
        if i > 0:
            kb.wait("vector", S["mT"], tb_done)
        kb.op("vector", lambda lgt=lgt: nc.vector.reduce_max(M1, lgt, axis=AX.X))
        kb.drain("vector")
        kb.op("vector", lambda: nc.vector.tensor_scalar(NM1, M1, -1.0, None, op0=ALU.mult))
        if layer == 0:
            kb.op("vector", lambda lgt=lgt: nc.vector.tensor_scalar(
                EQ[:, 0:nsl], lgt, M1[:, 0:1], None, op0=ALU.is_equal))
            kb.drain("vector")
            kb.op("vector", lambda lgt=lgt: nc.vector.scalar_tensor_tensor(
                EQ[:, 0:nsl], EQ[:, 0:nsl], -1e30, lgt, op0=ALU.mult, op1=ALU.add))
            kb.drain("vector")
            kb.op("vector", lambda: nc.vector.reduce_max(M2, EQ[:, 0:nsl], axis=AX.X))
        kb.op("vector", lambda: nc.vector.drain(), inc=S["vV"])
        vv = S["vV"].n
        kb.wait("scalar", S["vV"], vv)
        kb.op("scalar", lambda lgt=lgt: nc.scalar.activation(
            EE[:, 0:nsl], lgt, AF.Exp, bias=NM1[:, 0:1]), inc=S["sS"])
        se = S["sS"].n
        kb.wait("vector", S["sS"], se)
        if layer == 0:
            kb.op("vector", lambda lgt=lgt: nc.vector.tensor_scalar(
                KEEP[:, 0:nsl], lgt, M2[:, 0:1], None, op0=ALU.is_ge))
            kb.drain("vector")
            kb.op("vector", lambda: nc.vector.tensor_mul(WW[:, 0:nsl], EE[:, 0:nsl],
                  KEEP[:, 0:nsl]))
        else:
            kb.op("vector", lambda: nc.vector.tensor_copy(WW[:, 0:nsl], EE[:, 0:nsl]))
        kb.drain("vector")
        kb.op("vector", lambda: nc.vector.reduce_sum(SS, WW[:, 0:nsl], axis=AX.X))
        kb.drain("vector")
        kb.op("vector", lambda: nc.vector.reciprocal(RS, SS))
        kb.drain("vector")
        kb.op("vector", lambda: nc.vector.tensor_scalar(
            gwb[:, 0:nsl], WW[:, 0:nsl], RS[:, 0:1], None, op0=ALU.mult), inc=S["vV"])
        vr = S["vV"].n
        kb.wait("tensor", S["vV"], vr)
        kb.op("tensor", lambda i=i, nsl=nsl: nc.tensor.transpose(
            pB[0:nsl, 128 * i:128 * (i + 1)], gwb[:, 0:nsl], identf[:]),
            inc=S["mT"])
        tb_done = S["mT"].n
    kb.wait("scalar", S["mT"], tb_done)
    if layer == 0:
        kb.op("scalar", lambda: nc.scalar.activation(grow[0:8, 64:1088],
              pB[0:8, 0:1024], AF.Copy), inc=S["sS"])
    else:
        kb.op("scalar", lambda: nc.scalar.activation(kb.sb["attnT"][0:4, 0:1024],
              pB[0:4, 0:1024], AF.Copy), inc=S["sS"])
    rw_ready = S["sS"].n

    if layer == 1:
        kb.wait("tensor", S["sS"], rw_ready)
        for half in range(2):
            kb.op("tensor", lambda half=half: nc.tensor.matmul(
                pC[0:16, half * 512:(half + 1) * 512], grow[0:4, 0:16],
                kb.sb["attnT"][0:4, half * 512:(half + 1) * 512],
                start=True, stop=True), inc=S["mT"] if half == 1 else None)
        af_mm = S["mT"].n
        kb.wait("scalar", S["mT"], af_mm)
        kb.op("scalar", lambda: nc.scalar.activation(grow[0:16, 64:1088],
              pC[0:16, 0:1024], AF.Copy), inc=S["sS"])
        rw_ready = S["sS"].n

    kb.wait("tensor", S["vV"], S["vV"].n)
    for ht in range(NT):
        for half in range(2):
            kb.op("tensor", lambda ht=ht, half=half: nc.tensor.matmul(
                pD[:, half * 512:(half + 1) * 512], wpool[:, 128 * ht:128 * (ht + 1)],
                inbuf[:, ht * 1024 + half * 512: ht * 1024 + (half + 1) * 512],
                start=(ht == 0), stop=(ht == NT - 1)),
                inc=S["mT"] if (ht == NT - 1 and half == 1) else None)
    z_mm = S["mT"].n
    kb.wait("scalar", S["mT"], z_mm)
    kb.op("scalar", lambda: nc.scalar.activation(zsb[:, 0:1024], pD[:], AF.Copy), inc=S["sS"])
    z_ev = S["sS"].n
    kb.wait("tensor", S["sS"], rw_ready)
    krap = kb.sb["kreps"][0:8, 0:128] if layer == 0 else kb.sb["kreps"][0:16, 128:256]
    nK = 8 if layer == 0 else 16
    for half in range(2):
        kb.op("tensor", lambda half=half, krap=krap, nK=nK: nc.tensor.matmul(
            pC[:, half * 512:(half + 1) * 512], krap,
            grow[0:nK, 64 + half * 512: 64 + (half + 1) * 512],
            start=True, stop=True), inc=S["mT"] if half == 1 else None)
    bc_mm = S["mT"].n
    kb.wait("vector", S["mT"], bc_mm)
    kb.wait("vector", S["sS"], z_ev)
    kb.op("vector", lambda: nc.vector.tensor_mul(hwsb[:], zsb[:, 0:1024], pC[:]), inc=S["vV"])
    hw_done = S["vV"].n

    wb_dram = D[wb_name]
    mm_emit = make_wstream_mm_emit(kb, wb_dram, inbuf, extra=("delta", hw_done))
    bcol = 0 if layer == 0 else 32
    bias_cols = lambda t: kb.sb["bias_sb"][:, bcol + t: bcol + t + 1]
    emit_evac_loop(kb, NT, mm_emit, resid_load, normed, outbuf, scr_out,
                   bias_cols=bias_cols)


def make_wstream_mm_emit(kb, wb_dram, inbuf, extra=None):
    S = kb.sems; nc = kb.nc
    wsb, wpool = kb.sb["wsb"], kb.sb["wpool"]
    hwsb = kb.sb.get("hwsb")
    q_mm = [S["mT"].n] * 4
    q_dma = [0, 0, 0, 0]
    def mm_emit(kb2, t, pt):
        for q in range(4):
            kb.wait("gpsimd", S["mT"], q_mm[q])
            q_dma[q] = kb.dma(
                "gpsimd",
                lambda q=q: wsb[:, q * 1024:(q + 1) * 1024],
                lambda t=t, q=q: wb_dram.ap()[t, q], S["dQ" + str(q)])
        if t == 0 and extra is not None:
            kb.wait("tensor", S["vV"], extra[1])
        for q in range(4):
            kb.wait("tensor", S["dQ" + str(q)], q_dma[q])
            for j in range(8):
                u = 8 * q + j
                for half in range(2):
                    kb.op("tensor", lambda u=u, half=half, pt=pt:
                          nc.tensor.matmul(
                              pt[:, half * 512:(half + 1) * 512],
                              wsb[:, u * 128:(u + 1) * 128],
                              inbuf[:, u * 1024 + half * 512:
                                    u * 1024 + (half + 1) * 512],
                              start=(u == 0),
                              stop=(extra is None and u == NT - 1)),
                          inc=S["mT"] if (j == 7 and half == 1) else None)
            q_mm[q] = S["mT"].n
        if extra is not None:
            for half in range(2):
                kb.op("tensor", lambda t=t, half=half, pt=pt: nc.tensor.matmul(
                    pt[:, half * 512:(half + 1) * 512],
                    wpool[:, 4096 + 128 * t: 4096 + 128 * (t + 1)],
                    hwsb[:, half * 512:(half + 1) * 512],
                    start=False, stop=True),
                    inc=S["mT"] if half == 1 else None)
        return S["mT"].n
    return mm_emit


def resid_from_dram(kb, dram):
    S = kb.sems
    def load(kb2, t, sl):
        return kb.dma("sync",
                      lambda sl=sl: kb.sb["xpre"][:, sl * 1024:(sl + 1) * 1024],
                      lambda t=t: dram.ap()[t * 128:(t + 1) * 128, :],
                      S["dX0" if sl == 0 else "dX1"])
    return load


def build_ffn_phase_program(layer):
    nc = bass.Bass("TRN2", target_bir_lowering=False)
    ctx = ExitStack()
    with ctx:
        kb = KB(nc, ctx)
        phase = "p1" if layer == 0 else "p3"
        setup_common(kb, {phase})
        kb.D["xnin"] = nc.dram_tensor("xnin", [HD, TPC], F32, kind="ExternalInput")
        load_consts(kb, {phase})
        emit_cast_load_buf(kb, kb.D["xnin"], kb.sb["bufA"])
        kb.wait("tensor", kb.sems["dW"], kb.sems["dW"].n)
        kb.wait("tensor", kb.sems["dXQ"], kb.sems["dXQ"].n)
        emit_ffn(kb, layer, kb.sb["bufA"], kb.sb["bufB"],
                 resid_from_dram(kb, kb.D["xnin"]), False, kb.D["scr0"])
        emit_stats_finalize(kb)
        emit_final_output(kb, kb.D["scr0"], kb.D["out"])
        kb.finalize_block()
    return nc


def ffn_phase_inputs(layer, inputs, core):
    m = dict(
        ones_bf=NPC["ones_bf"], ones1f=NPC["ones1x128_f"], onesm=NPC["onesm"],
        identb=NPC["ident_bf"], identf=NPC["ident_f"],
        tri=NPC["tri01_bf"], rmat=NPC["rmat_bf"],
    )
    if layer == 0:
        m["wb0t"] = pack_w(np.asarray(inputs["L0_ffn_W_base"]).T)
        m["b0"] = np.ascontiguousarray(inputs["L0_ffn_b_base"].reshape(32, 128).T)
        m["wgt"] = np.ascontiguousarray(inputs["L0_ffn_W_gate"].T)
        m["a0t"] = np.ascontiguousarray(
            inputs["L0_ffn_A"].transpose(2, 0, 1).reshape(HD, 128))
        m["b0f"] = np.ascontiguousarray(
            inputs["L0_ffn_B"].transpose(0, 2, 1).reshape(128, HD))
        m["kr8"] = NPC["krepl8"]
    else:
        m["wb1t"] = pack_w(np.asarray(inputs["L1_ffn_W_base"]).T)
        m["b1"] = np.ascontiguousarray(inputs["L1_ffn_b_base"].reshape(32, 128).T)
        m["rwt"] = np.ascontiguousarray(inputs["L1_ffn_router_W"].T)
        m["la1t"] = np.ascontiguousarray(
            inputs["L1_ffn_lora_A"].transpose(2, 0, 1).reshape(HD, 128))
        m["lb1f"] = np.ascontiguousarray(
            inputs["L1_ffn_lora_B"].transpose(0, 2, 1).reshape(128, HD))
        m["rb"] = np.ascontiguousarray(inputs["L1_ffn_router_b"].reshape(16, 1))
        m["prior"] = np.ascontiguousarray(inputs["L1_ffn_domain_prior"].reshape(16, 1))
        m["active"] = np.ascontiguousarray(inputs["L1_ffn_active"].reshape(1, 4))
        m["kr16"] = NPC["krepl16x2"]
        m["io16c"] = NPC["iota16c"]
        m["io4r"] = NPC["iota4r"]
    return m


def emit_kbuild(kb):
    S = kb.sems; nc = kb.nc; D = kb.D
    gw, gwb, xpre, scf = kb.sb["gw"], kb.sb["gwb"], kb.sb["xpre"], kb.sb["scf"]
    pA, pB, pD = kb.ps["pA"], kb.ps["pB"], kb.ps["pD"]
    ktile = kb.sb["ktile"]
    C1f = kb.sb["cons1f"]

    LR, LI = gw[0:64, 0:1], gw[0:64, 1:2]
    PR, PI = gw[0:64, 2:3], gw[0:64, 3:4]
    QR, QI = gw[0:64, 4:5], gw[0:64, 5:6]
    CTR, CTI = gw[0:64, 6:7], gw[0:64, 7:8]
    BP = gw[0:64, 8:9]
    NLR, NLI, LR2 = gw[0:64, 9:10], gw[0:64, 10:11], gw[0:64, 11:12]
    BR, BI = gw[0:64, 12:13], gw[0:64, 13:14]
    VV = gw[0:64, 16:24]
    STP = gw[0:1, 32:33]
    RS2 = gw[0:1, 33:34]
    DD = gw[0:1, 34:35]

    kb.op("vector", lambda: nc.vector.memset(xpre[:, 0:32], 0.0), inc=S["vV"])
    zv = S["vV"].n
    kb.wait("gpsimd", S["vV"], zv)
    zkp = kb.dma("gpsimd",
                 lambda: bass.AP(D["kpad"], 0, [[32, 128], [1, 32]]),
                 lambda: xpre[:, 0:32], S["dW"])

    kb.dma("sync", gw[0:64, 0:2], D["lam"].ap(), S["dXQ"])
    kb.dma("sync", gw[0:64, 2:4], D["pri"].ap(), S["dXQ"])
    kb.dma("sync", gw[0:64, 4:6], D["qri"].ap(), S["dXQ"])
    kb.dma("sync", gw[0:64, 6:8], D["ctri"].ap(), S["dXQ"])
    kb.dma("sync", gw[0:64, 8:9], D["bp"].ap(), S["dXQ"])
    kb.dma("sync", gw[0:1, 34:35], D["dd"].ap(), S["dXQ"])
    kb.dma("sync", gw[0:1, 35:36], D["lstep"].ap(), S["dXQ"])
    kb.dma("sync", scf[0:64, 2400:2464], D["vcrt"].ap(), S["dXQ"])
    kb.dma("sync", scf[0:64, 2464:2528], D["vcit"].ap(), S["dXQ"])
    kb.dma("sync", scf[:, 2528:2560], D["ccr"].ap(), S["dXQ"])
    kb.dma("sync", scf[:, 2560:2592], D["cci"].ap(), S["dXQ"])
    kb.dma("sync", scf[0:32, 2592:2624], D["w32r"].ap(), S["dXQ"])
    kb.dma("sync", scf[0:32, 2624:2656], D["w32i"].ap(), S["dXQ"])
    kb.dma("sync", scf[0:32, 2656:2688], D["w32in"].ap(), S["dXQ"])
    kb.dma("sync", scf[:, 2688:2720], D["twr"].ap(), S["dXQ"])
    kb.dma("sync", scf[:, 2720:2752], D["twi"].ap(), S["dXQ"])
    kb.dma("sync", scf[:, 2752:2880], D["c128"].ap(), S["dXQ"])
    kb.dma("sync", scf[:, 2880:3008], D["s128n"].ap(), S["dXQ"])
    pdma = S["dXQ"].n

    kb.wait("scalar", S["dXQ"], pdma)
    kb.op("scalar", lambda: nc.scalar.activation(STP, gw[0:1, 35:36], AF.Exp),
          inc=S["sS"])
    kb.wait("vector", S["sS"], S["sS"].n)
    kb.op("vector", lambda: nc.vector.reciprocal(RS2, STP))
    kb.drain("vector")
    kb.op("vector", lambda: nc.vector.tensor_scalar(RS2, RS2, 2.0, None, op0=ALU.mult))
    kb.drain("vector")
    kb.op("vector", lambda: nc.vector.tensor_scalar(NLR, LR, -1.0, None, op0=ALU.mult))
    kb.op("vector", lambda: nc.vector.tensor_scalar(NLI, LI, -1.0, None, op0=ALU.mult))
    kb.op("vector", lambda: nc.vector.tensor_mul(LR2, LR, LR), inc=S["vV"])
    vpar = S["vV"].n

    kb.wait("tensor", S["dXQ"], pdma)
    kb.op("tensor", lambda: nc.tensor.matmul(pA[0:64, 0:1], scf[0:64, 2400:2464],
          BP, start=True, stop=True))
    kb.op("tensor", lambda: nc.tensor.matmul(pA[0:64, 1:2], scf[0:64, 2464:2528],
          BP, start=True, stop=True), inc=S["mT"])
    kb.wait("vector", S["mT"], S["mT"].n)
    kb.op("vector", lambda: nc.vector.tensor_copy(BR, pA[0:64, 0:1]))
    kb.op("vector", lambda: nc.vector.tensor_copy(BI, pA[0:64, 1:2]))
    kb.drain("vector")
    def cmul_conj(ar, ai, br_, bi_, outr, outi):
        kb.op("vector", lambda: nc.vector.tensor_mul(outr, ar, br_))
        kb.op("vector", lambda: nc.vector.tensor_mul(outi, ar, bi_))
        kb.drain("vector")
        kb.op("vector", lambda: nc.vector.scalar_tensor_tensor(
            gw[0:64, 14:15], ai, 1.0, bi_, op0=ALU.mult, op1=ALU.mult))
        kb.op("vector", lambda: nc.vector.scalar_tensor_tensor(
            gw[0:64, 15:16], ai, 1.0, br_, op0=ALU.mult, op1=ALU.mult))
        kb.drain("vector")
        kb.op("vector", lambda: nc.vector.tensor_add(outr, outr, gw[0:64, 14:15]))
        kb.op("vector", lambda: nc.vector.tensor_sub(outi, outi, gw[0:64, 15:16]))
        kb.drain("vector")
    cmul_conj(CTR, CTI, BR, BI, VV[:, 0:1], VV[:, 1:2])
    cmul_conj(CTR, CTI, PR, PI, VV[:, 2:3], VV[:, 3:4])
    cmul_conj(QR, QI, BR, BI, VV[:, 4:5], VV[:, 5:6])
    cmul_conj(QR, QI, PR, PI, VV[:, 6:7], VV[:, 7:8])
    for j in range(4):
        kb.op("vector", lambda j=j: nc.vector.tensor_copy(
            gwb[0:64, 2 * j:2 * j + 1], VV[:, 2 * j:2 * j + 1]))
        kb.op("vector", lambda j=j: nc.vector.tensor_copy(
            gwb[0:64, 2 * j + 1:2 * j + 2], VV[:, 2 * j + 1:2 * j + 2]))
        kb.op("vector", lambda j=j: nc.vector.tensor_scalar(
            gwb[0:64, 8 + 2 * j:9 + 2 * j], VV[:, 2 * j + 1:2 * j + 2], -1.0,
            None, op0=ALU.mult))
        kb.op("vector", lambda j=j: nc.vector.tensor_copy(
            gwb[0:64, 9 + 2 * j:10 + 2 * j], VV[:, 2 * j:2 * j + 1]))
    kb.wait("vector", S["dW"], zkp)
    kb.op("vector", lambda: nc.vector.drain(), inc=S["vV"])
    vl_done = S["vV"].n

    tr_n = 0
    for kh in range(2):
        if kh == 1:
            kb.wait("sync", S["mT"], tr_n)
        th = kb.dma("sync", scf[0:1, 0:2048],
                    D["tim"].ap()[0:1, 2048 * kh:2048 * (kh + 1)], S["dX0"])
        kb.wait("vector", S["dX0"], th)
        kb.wait("vector", S["vV"], vl_done)
        kb.op("vector", lambda: nc.vector.tensor_scalar(
            scf[0:1, 0:2048], scf[0:1, 0:2048], RS2, None, op0=ALU.mult),
            inc=S["vV"])
        gv = S["vV"].n
        for c in range(4):
            kb.wait("tensor", S["vV"], gv)
            if c > 0:
                kb.wait("tensor", S["vV"], S["vV"].n)
            kb.op("tensor", lambda c=c: nc.tensor.matmul(
                pA[0:64, 0:512], C1f[0:1, 0:64],
                scf[0:1, 512 * c:512 * (c + 1)], start=True, stop=True),
                inc=S["mT"])
            kb.wait("vector", S["mT"], S["mT"].n)
            kb.op("vector", lambda c=c: nc.vector.tensor_single_scalar(
                xpre[0:64, 512 * c:512 * (c + 1)], pA[0:64, 0:512], NLI,
                op=ALU.add), inc=S["vV"])
        dIn = S["vV"].n
        kb.wait("scalar", S["vV"], dIn)
        kb.op("scalar", lambda: nc.scalar.activation(
            scf[0:64, 0:2048], xpre[0:64, 0:2048], AF.Square), inc=S["sS"])
        sqn = S["sS"].n
        kb.wait("vector", S["sS"], sqn)
        kb.op("vector", lambda: nc.vector.tensor_scalar(
            scf[0:64, 0:2048], scf[0:64, 0:2048], LR2, None, op0=ALU.add))
        kb.drain("vector")
        kb.op("vector", lambda: nc.vector.reciprocal(
            scf[0:64, 0:2048], scf[0:64, 0:2048]))
        kb.drain("vector")
        kb.op("vector", lambda: nc.vector.scalar_tensor_tensor(
            xpre[0:64, 0:2048], xpre[0:64, 0:2048], -1.0, scf[0:64, 0:2048],
            op0=ALU.mult, op1=ALU.mult))
        kb.drain("vector")
        kb.op("vector", lambda: nc.vector.tensor_scalar(
            scf[0:64, 0:2048], scf[0:64, 0:2048], NLR, None, op0=ALU.mult),
            inc=S["vV"])
        rdone = S["vV"].n
        for c in range(4):
            kb.wait("tensor", S["vV"], rdone)
            kb.wait("tensor", S["sS"], S["sS"].n)
            kb.op("tensor", lambda c=c: nc.tensor.matmul(
                pB[0:8, 0:512], gwb[0:64, 0:8],
                scf[0:64, 512 * c:512 * (c + 1)], start=True, stop=False))
            kb.op("tensor", lambda c=c: nc.tensor.matmul(
                pB[0:8, 0:512], gwb[0:64, 8:16],
                xpre[0:64, 512 * c:512 * (c + 1)], start=False, stop=True),
                inc=S["mT"])
            kb.wait("scalar", S["mT"], S["mT"].n)
            kb.op("scalar", lambda c=c: nc.scalar.activation(
                xpre[0:8, 512 * c:512 * (c + 1)], pB[0:8, 0:512], AF.Copy),
                inc=S["sS"])
        kmn = S["sS"].n
        kb.wait("tensor", S["sS"], kmn)
        for j in range(16):
            F = 16 * kh + j
            kb.op("tensor", lambda j=j, F=F: nc.tensor.transpose(
                pD[:, 8 * F:8 * F + 8], xpre[0:8, 128 * j:128 * (j + 1)],
                kb.sb["identf"][0:8, 0:8]),
                inc=S["mT"] if j == 15 else None)
        tr_n = S["mT"].n

    kb.wait("vector", S["mT"], tr_n)
    AT = lambda j: pD[:, 8 * 0 + j: 256 + j: 8]
    atR, atI = scf[:, 0:32], scf[:, 32:64]
    T1, T2, T3, T4 = (scf[:, 64:96], scf[:, 96:128], scf[:, 128:160],
                      scf[:, 160:192])
    kb.op("vector", lambda: nc.vector.tensor_scalar(T1, AT(6), 1.0, None, op0=ALU.add))
    kb.op("vector", lambda: nc.vector.tensor_copy(T2, AT(7)))
    kb.drain("vector")
    kb.op("vector", lambda: nc.vector.tensor_mul(T3, T1, T1))
    kb.op("vector", lambda: nc.vector.tensor_mul(T4, T2, T2))
    kb.drain("vector")
    kb.op("vector", lambda: nc.vector.tensor_add(T3, T3, T4))
    kb.drain("vector")
    kb.op("vector", lambda: nc.vector.reciprocal(T3, T3))
    kb.drain("vector")
    QRv, QIv = scf[:, 192:224], scf[:, 224:256]
    kb.op("vector", lambda: nc.vector.tensor_mul(QRv, AT(4), T1))
    kb.op("vector", lambda: nc.vector.tensor_mul(QIv, AT(5), T1))
    kb.drain("vector")
    kb.op("vector", lambda: nc.vector.scalar_tensor_tensor(
        T4, AT(5), 1.0, T2, op0=ALU.mult, op1=ALU.mult))
    kb.drain("vector")
    kb.op("vector", lambda: nc.vector.tensor_add(QRv, QRv, T4))
    kb.drain("vector")
    kb.op("vector", lambda: nc.vector.scalar_tensor_tensor(
        T4, AT(4), 1.0, T2, op0=ALU.mult, op1=ALU.mult))
    kb.drain("vector")
    kb.op("vector", lambda: nc.vector.tensor_sub(QIv, QIv, T4))
    kb.drain("vector")
    kb.op("vector", lambda: nc.vector.tensor_mul(QRv, QRv, T3))
    kb.op("vector", lambda: nc.vector.tensor_mul(QIv, QIv, T3))
    kb.drain("vector")
    kb.op("vector", lambda: nc.vector.tensor_mul(T1, AT(2), QRv))
    kb.op("vector", lambda: nc.vector.tensor_mul(T2, AT(3), QIv))
    kb.op("vector", lambda: nc.vector.tensor_mul(T3, AT(2), QIv))
    kb.op("vector", lambda: nc.vector.tensor_mul(T4, AT(3), QRv))
    kb.drain("vector")
    kb.op("vector", lambda: nc.vector.tensor_sub(T1, T1, T2))
    kb.op("vector", lambda: nc.vector.tensor_add(T3, T3, T4))
    kb.drain("vector")
    kb.op("vector", lambda: nc.vector.tensor_sub(T1, AT(0), T1))
    kb.op("vector", lambda: nc.vector.tensor_sub(T3, AT(1), T3))
    kb.drain("vector")
    kb.op("vector", lambda: nc.vector.tensor_mul(atR, scf[:, 2528:2560], T1))
    kb.op("vector", lambda: nc.vector.tensor_mul(atI, scf[:, 2528:2560], T3))
    kb.drain("vector")
    kb.op("vector", lambda: nc.vector.scalar_tensor_tensor(
        T2, scf[:, 2560:2592], 1.0, T3, op0=ALU.mult, op1=ALU.mult))
    kb.op("vector", lambda: nc.vector.scalar_tensor_tensor(
        T4, scf[:, 2560:2592], 1.0, T1, op0=ALU.mult, op1=ALU.mult))
    kb.drain("vector")
    kb.op("vector", lambda: nc.vector.tensor_sub(atR, atR, T2))
    kb.op("vector", lambda: nc.vector.tensor_add(atI, atI, T4), inc=S["vV"])
    at_done = S["vV"].n

    kb.wait("tensor", S["vV"], at_done)
    kb.op("tensor", lambda: nc.tensor.transpose(pA[0:32, 0:128], atR,
          kb.sb["identf"][:]))
    kb.op("tensor", lambda: nc.tensor.transpose(pA[0:32, 128:256], atI,
          kb.sb["identf"][:]), inc=S["mT"])
    kb.wait("scalar", S["mT"], S["mT"].n)
    kb.op("scalar", lambda: nc.scalar.activation(scf[0:32, 256:384],
          pA[0:32, 0:128], AF.Copy))
    kb.op("scalar", lambda: nc.scalar.activation(scf[0:32, 384:512],
          pA[0:32, 128:256], AF.Copy), inc=S["sS"])
    art_done = S["sS"].n
    kb.wait("tensor", S["sS"], art_done)
    kb.op("tensor", lambda: nc.tensor.matmul(pB[:, 0:32], scf[0:32, 256:384],
          scf[0:32, 2592:2624], start=True, stop=False))
    kb.op("tensor", lambda: nc.tensor.matmul(pB[:, 0:32], scf[0:32, 384:512],
          scf[0:32, 2656:2688], start=False, stop=True))
    kb.op("tensor", lambda: nc.tensor.matmul(pB[:, 32:64], scf[0:32, 256:384],
          scf[0:32, 2624:2656], start=True, stop=False))
    kb.op("tensor", lambda: nc.tensor.matmul(pB[:, 32:64], scf[0:32, 384:512],
          scf[0:32, 2592:2624], start=False, stop=True), inc=S["mT"])
    kb.wait("scalar", S["mT"], S["mT"].n)
    kb.op("scalar", lambda: nc.scalar.activation(scf[:, 512:544], pB[:, 0:32],
          AF.Copy))
    kb.op("scalar", lambda: nc.scalar.activation(scf[:, 544:576], pB[:, 32:64],
          AF.Copy), inc=S["sS"])
    h_done = S["sS"].n
    kb.wait("vector", S["sS"], h_done)
    HR, HI = scf[:, 512:544], scf[:, 544:576]
    HpR, HpI = scf[:, 576:608], scf[:, 608:640]
    kb.op("vector", lambda: nc.vector.tensor_mul(HpR, HR, scf[:, 2688:2720]))
    kb.op("vector", lambda: nc.vector.tensor_mul(HpI, HR, scf[:, 2720:2752]))
    kb.drain("vector")
    kb.op("vector", lambda: nc.vector.scalar_tensor_tensor(
        T1, HI, 1.0, scf[:, 2720:2752], op0=ALU.mult, op1=ALU.mult))
    kb.op("vector", lambda: nc.vector.scalar_tensor_tensor(
        T2, HI, 1.0, scf[:, 2688:2720], op0=ALU.mult, op1=ALU.mult))
    kb.drain("vector")
    kb.op("vector", lambda: nc.vector.tensor_sub(HpR, HpR, T1))
    kb.op("vector", lambda: nc.vector.tensor_add(HpI, HpI, T2), inc=S["vV"])
    hp_done = S["vV"].n
    kb.wait("tensor", S["vV"], hp_done)
    kb.op("tensor", lambda: nc.tensor.matmul(pA[:, 0:32], scf[:, 2752:2880],
          HpR, start=True, stop=False))
    kb.op("tensor", lambda: nc.tensor.matmul(pA[:, 0:32], scf[:, 2880:3008],
          HpI, start=False, stop=True), inc=S["mT"])
    k2_done = S["mT"].n
    kb.wait("scalar", S["mT"], k2_done)
    kb.op("scalar", lambda: nc.scalar.activation(ktile[:], pA[:, 0:32],
          AF.Copy, scale=1.0 / 4096.0), inc=S["sS"])
    kt_done = S["sS"].n
    kb.wait("vector", S["sS"], kt_done)
    kb.op("vector", lambda: nc.vector.tensor_scalar(
        ktile[0:1, 0:1], ktile[0:1, 0:1], DD, None, op0=ALU.add), inc=S["vV"])
    kd = S["vV"].n
    kb.wait("gpsimd", S["vV"], kd)
    kb.wait("gpsimd", S["dW"], zkp)
    kpd = kb.dma("gpsimd",
                 lambda: bass.AP(D["kpad"], 4096, [[32, 128], [1, 32]]),
                 lambda: ktile[:], S["dW"])
    return kpd


def emit_conv(kb, inbuf, outbuf, resid_load, normed, scr_out):
    S = kb.sems; nc = kb.nc; D = kb.D
    wsb, rmat = kb.sb["wsb"], kb.sb["rmat"]
    pA, pB = kb.ps["pA"], kb.ps["pB"]

    kpd = emit_kbuild(kb)

    kb.wait("tensor", S["sS"], S["sS"].n)
    fl_mm = [0, 0]; fl_ev = [0, 0]
    for j in range(NT):
        sl = j % 2
        pt = kb.ps["pA" if sl == 0 else "pB"]
        if j >= 2:
            kb.wait("tensor", S["sS"], fl_ev[sl])
        for half in range(2):
            kb.op("tensor", lambda j=j, half=half, pt=pt: nc.tensor.matmul(
                pt[:, half * 512:(half + 1) * 512], rmat[:],
                inbuf[:, j * 1024 + half * 512: j * 1024 + (half + 1) * 512],
                start=True, stop=True),
                inc=S["mT"] if half == 1 else None)
        fl_mm[sl] = S["mT"].n
        kb.wait("scalar", S["mT"], fl_mm[sl])
        kb.op("scalar", lambda j=j, pt=pt: nc.scalar.activation(
            inbuf[:, j * 1024:(j + 1) * 1024], pt[:], AF.Copy), inc=S["sS"])
        fl_ev[sl] = S["sS"].n
    flip_done = S["sS"].n

    kb.wait("sync", S["dW"], kpd)
    bnd = kb.dma("sync", wsb[:, 0:4096],
                 lambda: bass.AP(D["kpad"], 3969, [[1, 128], [128, 32], [1, 128]]),
                 S["dX1"])

    def mm_emit(kb2, t, pt):
        if t == 0:
            kb.wait("tensor", S["dX1"], bnd)
            kb.wait("tensor", S["sS"], flip_done)
        for j in range(t + 1):
            for half in range(2):
                kb.op("tensor", lambda t=t, j=j, half=half, pt=pt:
                      nc.tensor.matmul(
                          pt[:, half * 512:(half + 1) * 512],
                          wsb[:, (t - j) * 128:(t - j + 1) * 128],
                          inbuf[:, j * 1024 + half * 512:
                                j * 1024 + (half + 1) * 512],
                          start=(j == 0), stop=(j == t)),
                      inc=S["mT"] if (j == t and half == 1) else None)
        return S["mT"].n

    emit_evac_loop(kb, NT, mm_emit, resid_load, normed, outbuf, scr_out)


def build_conv_phase_program():
    nc = bass.Bass("TRN2", target_bir_lowering=False)
    ctx = ExitStack()
    with ctx:
        kb = KB(nc, ctx)
        setup_common(kb, {"p2"})
        kb.D["xnin"] = nc.dram_tensor("xnin", [HD, TPC], F32, kind="ExternalInput")
        load_consts(kb, {"p2"})
        emit_cast_load_buf(kb, kb.D["xnin"], kb.sb["bufA"])
        kb.wait("tensor", kb.sems["dW"], kb.sems["dW"].n)
        kb.wait("tensor", kb.sems["dXQ"], kb.sems["dXQ"].n)
        emit_conv(kb, kb.sb["bufA"], kb.sb["bufB"],
                  resid_from_dram(kb, kb.D["xnin"]), False, kb.D["scr0"])
        emit_stats_finalize(kb)
        emit_final_output(kb, kb.D["scr0"], kb.D["out"])
        kb.finalize_block()
    return nc


def conv_phase_inputs(inputs):
    m = dict(
        ones_bf=NPC["ones_bf"], ones1f=NPC["ones1x128_f"], onesm=NPC["onesm"],
        identb=NPC["ident_bf"], identf=NPC["ident_f"],
        tri=NPC["tri01_bf"], rmat=NPC["rmat_bf"],
        lam=np.ascontiguousarray(inputs["L1_attn_Lambda_ri"]),
        pri=np.ascontiguousarray(inputs["L1_attn_p_ri"]),
        qri=np.ascontiguousarray(inputs["L1_attn_q_ri"]),
        vcrt=np.ascontiguousarray(inputs["L1_attn_Vc_ri"][:, :, 0].T),
        vcit=np.ascontiguousarray(inputs["L1_attn_Vc_ri"][:, :, 1].T),
        ctri=np.ascontiguousarray(inputs["L1_attn_Ct_ri"].reshape(64, 2)),
        bp=np.ascontiguousarray(inputs["L1_attn_Bp"]),
        dd=np.ascontiguousarray(inputs["L1_attn_D"].reshape(1, 1)),
        lstep=np.ascontiguousarray(inputs["L1_attn_log_step"].reshape(1, 1)),
        tim=NPC["tim"], ccr=NPC["ccr"], cci=NPC["cci"],
        w32r=NPC["w32r"], w32i=NPC["w32i"], w32in=NPC["w32in"],
        twr=NPC["twr"], twi=NPC["twi"], c128=NPC["c128"], s128n=NPC["s128n"],
    )
    return m


ATTN_HEADS = NHEADS
ATTN_SKIP_SWITCH = False
ATTN_PARTS = 8


def emit_attn(kb, outbuf, AO, resid_load, scr_out):
    S = kb.sems; nc = kb.nc; D = kb.D
    wsb, wpool = kb.sb["wsb"], kb.sb["wpool"]
    qsb, attnT, rbcs = kb.sb["qsb"], kb.sb["attnT"], kb.sb["rbcs"]
    biastab, tri = kb.sb["biastab"], kb.sb["tri"]
    scf = kb.sb["scf"]
    pA, pB, pC, pD = kb.ps["pA"], kb.ps["pB"], kb.ps["pC"], kb.ps["pD"]
    x0buf = outbuf

    xsb = [wsb[:, 0:1024], wsb[:, 1024:2048]]
    wkvs = [wsb[:, 2048:2304], wsb[:, 2304:2560]]
    kv_dma = [0, 0]; kv_mm = [0, 0]
    ev_k = 0; ev_v = 0; tr_ev = [0, 0]
    for kc in range(4):
        for ht in range(NT):
            sl = ht % 2
            if kc + ht > 0:
                kb.wait("gpsimd", S["mT"], kv_mm[sl])
            ksem = S["dK0" if sl == 0 else "dK1"]
            kb.dma("gpsimd", lambda sl=sl: xsb[sl],
                   lambda kc=kc, ht=ht: D["xT"].ap()[128 * ht:128 * (ht + 1),
                                                     1024 * kc:1024 * (kc + 1)],
                   ksem)
            kv_dma[sl] = kb.dma("gpsimd", lambda sl=sl: wkvs[sl],
                                lambda ht=ht: D["wkv"].ap()[128 * ht:128 * (ht + 1), :],
                                ksem)
            kb.wait("tensor", ksem, kv_dma[sl])
            if ht == 0:
                kb.wait("tensor", S["sS"], ev_v)
            for half in range(2):
                kb.op("tensor", lambda sl=sl, half=half, ht=ht: nc.tensor.matmul(
                    pA[:, half * 512:(half + 1) * 512], wkvs[sl][:, 0:128],
                    xsb[sl][:, half * 512:(half + 1) * 512],
                    start=(ht == 0), stop=(ht == NT - 1)))
            for half in range(2):
                kb.op("tensor", lambda sl=sl, half=half, ht=ht: nc.tensor.matmul(
                    pB[:, half * 512:(half + 1) * 512], wkvs[sl][:, 128:256],
                    xsb[sl][:, half * 512:(half + 1) * 512],
                    start=(ht == 0), stop=(ht == NT - 1)),
                    inc=S["mT"] if half == 1 else None)
            kv_mm[sl] = S["mT"].n
        kb.wait("scalar", S["mT"], kv_mm[(NT - 1) % 2])
        kb.op("scalar", lambda kc=kc: nc.scalar.activation(
            wpool[:, 1024 * kc:1024 * (kc + 1)], pA[:], AF.Copy))
        kb.op("scalar", lambda: nc.scalar.activation(
            scf[:, 0:1024], pB[:], AF.Copy), inc=S["sS"])
        ev_v = S["sS"].n
        for i in range(8):
            tsl = i % 2
            kb.wait("tensor", S["sS"], max(ev_v, tr_ev[tsl]))
            kb.op("tensor", lambda i=i, tsl=tsl: nc.tensor.transpose(
                pD[:, 512 * tsl:512 * tsl + 128], scf[:, 128 * i:128 * (i + 1)],
                kb.sb["identf"][:]), inc=S["mT"])
            kb.wait("scalar", S["mT"], S["mT"].n)
            kb.op("scalar", lambda kc=kc, i=i, tsl=tsl: nc.scalar.activation(
                wpool[:, 4096 + (8 * kc + i) * 128: 4096 + (8 * kc + i + 1) * 128],
                pD[:, 512 * tsl:512 * tsl + 128], AF.Copy), inc=S["sS"])
            tr_ev[tsl] = S["sS"].n
    kv_done_s = S["sS"].n

    def emit_variant(v):
        units = attn_units(v)
        by_head = {}
        for uid, d in enumerate(units):
            by_head.setdefault(d["h"], []).append((uid, d))
        Q = [512 * v, 3584 - 512 * v]
        q_ev = [kv_done_s, kv_done_s]
        q_ev_last = kv_done_s
        att_done = [0, 0]
        exp_n = [0, 0]
        trv_n = [0, 0]
        avden = [0, 0]
        avev = [S["vV"].n, S["vV"].n]
        rbcev = [kv_done_s, kv_done_s]
        qq_mm = [S["mT"].n] * 4
        useq = 0
        for h in range(ATTN_HEADS):
            hsl = h % 2
            for q in range(4):
                kb.wait("gpsimd", S["mT"], qq_mm[q])
                qd = kb.dma("gpsimd", lambda q=q: wsb[:, q * 1024:(q + 1) * 1024],
                            lambda h=h, q=q: D["wq"].ap()[h, q], S["dQ" + str(q)])
                kb.wait("tensor", S["dQ" + str(q)], qd)
                if q == 0:
                    kb.wait("tensor", S["sS"], q_ev_last)
                for j in range(8):
                    u = 8 * q + j
                    for half in range(2):
                        kb.op("tensor", lambda u=u, half=half: nc.tensor.matmul(
                            pC[:, half * 512:(half + 1) * 512],
                            wsb[:, u * 128:(u + 1) * 128],
                            x0buf[:, u * 1024 + half * 512: u * 1024 + (half + 1) * 512],
                            start=(u == 0), stop=(u == NT - 1)),
                            inc=S["mT"] if (j == 7 and half == 1) else None)
                qq_mm[q] = S["mT"].n
            qmm = S["mT"].n
            kb.wait("scalar", S["mT"], max(qmm, att_done[hsl]))
            kb.op("scalar", lambda hsl=hsl: nc.scalar.activation(
                qsb[:, 1024 * hsl:1024 * (hsl + 1)], pC[:], AF.Copy), inc=S["sS"])
            q_ev[hsl] = S["sS"].n
            q_ev_last = S["sS"].n
            blocks = {}
            for uid, d in by_head[h]:
                blocks.setdefault(d["b"], []).append((uid, d))
            for b, ulist in sorted(blocks.items()):
                bcol = 512 * b
                nu = len(ulist)

                def emit_avden(pend):
                    usl, w, dcol, tt, first, last, diag = pend
                    kb.wait("tensor", S["aRs"], exp_n[usl])
                    if diag:
                        kb.wait("tensor", S["aRv"], trv_n[usl])
                    if first:
                        kb.wait("tensor", S["vV"], max(avev[0], avev[1]))
                        kb.wait("tensor", S["sS"], max(rbcev[0], rbcev[1]))
                    kb.op("tensor", lambda usl=usl, w=w, dcol=dcol, tt=tt, first=first, last=last:
                          nc.tensor.matmul(
                              pB[:, dcol:dcol + w],
                              wpool[:, 4096 + tt * 128: 4096 + (tt + 1) * 128],
                              attnT[:, 512 * usl:512 * usl + w],
                              start=first, stop=last))
                    kb.op("tensor", lambda usl=usl, w=w, dcol=dcol, first=first, last=last:
                          nc.tensor.matmul(
                              pD[:, dcol:dcol + w],
                              kb.sb["onesm"][:],
                              attnT[:, 512 * usl:512 * usl + w],
                              start=first, stop=last), inc=S["mT"])
                    avden[usl] = S["mT"].n

                pend = None
                for i, (uid, d) in enumerate(ulist):
                    last = (i == nu - 1)
                    usl = useq % 2
                    useq += 1
                    w = d["w"]
                    kb.wait("tensor", S["sS"], q_ev[hsl])
                    kb.wait("tensor", S["aRs"], exp_n[usl])
                    kb.op("tensor", lambda d=d, usl=usl, w=w, hsl=hsl, b=b, Q=Q:
                          nc.tensor.matmul(
                              pA[:, 512 * usl:512 * usl + w],
                              wpool[:, d["t"] * 128:(d["t"] + 1) * 128],
                              qsb[:, 1024 * hsl + 512 * b + (d["A"] + d["off"] - Q[b]):
                                  1024 * hsl + 512 * b + (d["A"] + d["off"] - Q[b]) + w],
                              start=True, stop=True), inc=S["mT"])
                    sc_n = S["mT"].n
                    if pend is not None:
                        emit_avden(pend)
                    kb.wait("scalar", S["mT"], max(sc_n, avden[usl]))
                    kb.op("scalar", lambda uid=uid, usl=usl, w=w:
                          nc.scalar.activation(
                              attnT[:, 512 * usl:512 * usl + w],
                              pA[:, 512 * usl:512 * usl + w], AF.Exp,
                              bias=biastab[:, uid:uid + 1], scale=ASCALE),
                          inc=S["aRs"])
                    exp_n[usl] = S["aRs"].n
                    if d["diag"]:
                        kb.wait("vector", S["aRs"], exp_n[usl])
                        kb.op("vector", lambda d=d, usl=usl:
                              nc.vector.tensor_mul(
                                  attnT[:, 512 * usl:512 * usl + d["tri_w"]],
                                  attnT[:, 512 * usl:512 * usl + d["tri_w"]],
                                  tri[:, d["tri_shift"]:d["tri_shift"] + d["tri_w"]]),
                              inc=S["aRv"])
                        trv_n[usl] = S["aRv"].n
                    col0 = bcol + (d["A"] + d["off"] - Q[b])
                    pend = (usl, w, col0 - 512 * b, d["t"], i == 0, last, d["diag"])
                emit_avden(pend)
                att_done[hsl] = S["mT"].n
                avsl = scf[:, 1024 * b:1024 * b + 512]
                dnsl = scf[:, 1024 * b + 512:1024 * b + 1024]
                rcp = scf[:, 2048:2560]
                kb.wait("scalar", S["mT"], att_done[hsl])
                kb.wait("scalar", S["vV"], avev[b])
                kb.op("scalar", lambda avsl=avsl: nc.scalar.activation(
                    avsl, pB[:, 0:512], AF.Copy))
                kb.op("scalar", lambda dnsl=dnsl: nc.scalar.activation(
                    dnsl, pD[:, 0:512], AF.Copy), inc=S["sS"])
                rbcev[b] = S["sS"].n
                kb.wait("vector", S["sS"], rbcev[b])
                kb.op("vector", lambda dnsl=dnsl, rcp=rcp:
                      nc.vector.reciprocal(rcp, dnsl))
                kb.drain("vector")
                kb.op("vector", lambda h=h, b=b, avsl=avsl, rcp=rcp:
                      nc.vector.tensor_mul(
                          AO[:, 1024 * h + 512 * b: 1024 * h + 512 * (b + 1)],
                          avsl, rcp), inc=S["vV"])
                avev[b] = S["vV"].n

    if not ATTN_SKIP_SWITCH:
        kb.switch(4, emit_variant, ["tensor", "vector", "scalar", "gpsimd"])

    kb.wait("tensor", S["vV"], S["vV"].n)
    mm_emit = make_wstream_mm_emit(kb, D["wo"], AO, extra=None)
    emit_evac_loop(kb, NT, mm_emit, resid_load, False, outbuf, scr_out)


def build_attn_phase_program():
    nc = bass.Bass("TRN2", target_bir_lowering=False)
    ctx = ExitStack()
    with ctx:
        kb = KB(nc, ctx)
        setup_common(kb, {"p0"})
        load_consts(kb, {"p0"})
        emit_cast_load_buf(kb, kb.D["x0p"], kb.sb["bufA"])
        kb.wait("tensor", kb.sems["dW"], kb.sems["dW"].n)
        kb.wait("tensor", kb.sems["dXQ"], kb.sems["dXQ"].n)
        emit_attn(kb, kb.sb["bufA"], kb.sb["bufB"],
                  resid_from_dram(kb, kb.D["x0p"]), kb.D["scr0"])
        emit_stats_finalize(kb)
        emit_final_output(kb, kb.D["scr0"], kb.D["out"])
        kb.finalize_block()
    return nc


def attn_phase_inputs(inputs, core, x_full_T=None):
    if x_full_T is None:
        x_full_T = np.ascontiguousarray(
            inputs["L0_attn_hidden_states"].transpose(0, 2, 1))
    batch = core // 4
    v = core % 4
    lo, hi = 512 * v, 3584 - 512 * v
    xT = x_full_T[batch]
    x0p = np.concatenate([xT[:, lo:lo + 512], xT[:, hi:hi + 512]], axis=1)
    m = dict(
        ones_bf=NPC["ones_bf"], ones1f=NPC["ones1x128_f"], onesm=NPC["onesm"],
        identb=NPC["ident_bf"], identf=NPC["ident_f"],
        tri=NPC["tri01_bf"], rmat=NPC["rmat_bf"],
        x0p=np.ascontiguousarray(x0p), xT=np.ascontiguousarray(xT),
        wq=pack_w(inputs["L0_attn_Wq"]),
        wkv=np.ascontiguousarray(inputs["L0_attn_Wkv"]),
        wo=pack_w(inputs["L0_attn_Wo"]),
        biastab=bias_table(core),
    )
    return m


_PROG_CACHE = {}


def _get_prog(name, builder):
    if name not in _PROG_CACHE:
        _PROG_CACHE[name] = builder()
    return _PROG_CACHE[name]


def kernel_unfused(**inputs):
    x_full_T = np.ascontiguousarray(
        np.asarray(inputs["L0_attn_hidden_states"]).transpose(0, 2, 1))
    cores = list(range(8))

    nc0 = _get_prog("p0", build_attn_phase_program)
    maps = [attn_phase_inputs(inputs, c, x_full_T) for c in cores]
    r = run_bass_kernel_spmd(nc0, maps, core_ids=cores)
    xn = [np.asarray(r.results[c]["out"]) for c in cores]

    nc1 = _get_prog("p1", lambda: build_ffn_phase_program(0))
    maps = []
    for c in cores:
        m = ffn_phase_inputs(0, inputs, c)
        m["xnin"] = xn[c]
        maps.append(m)
    r = run_bass_kernel_spmd(nc1, maps, core_ids=cores)
    xn = [np.asarray(r.results[c]["out"]) for c in cores]

    nc2 = _get_prog("p2", build_conv_phase_program)
    maps = []
    for c in cores:
        m = conv_phase_inputs(inputs)
        m["xnin"] = xn[c]
        maps.append(m)
    r = run_bass_kernel_spmd(nc2, maps, core_ids=cores)
    xn = [np.asarray(r.results[c]["out"]) for c in cores]

    nc3 = _get_prog("p3", lambda: build_ffn_phase_program(1))
    maps = []
    for c in cores:
        m = ffn_phase_inputs(1, inputs, c)
        m["xnin"] = xn[c]
        maps.append(m)
    r = run_bass_kernel_spmd(nc3, maps, core_ids=cores)

    yT = np.empty((2, HD, HD), dtype=np.float32)
    for c in cores:
        out = np.asarray(r.results[c]["out"])
        b, v = c // 4, c % 4
        lo, hi = 512 * v, 3584 - 512 * v
        yT[b][:, lo:lo + 512] = out[:, 0:512]
        yT[b][:, hi:hi + 512] = out[:, 512:1024]
    return np.ascontiguousarray(yT.transpose(0, 2, 1))


def _phase_barrier(kb):
    S = kb.sems
    snap = [(n, s.n) for n, s in S.items()]
    for eng in ENGS:
        for n, v in snap:
            if v > 0:
                kb.wait(eng, S[n], v)


def build_fused_program():
    nc = bass.Bass("TRN2", target_bir_lowering=False)
    ctx = ExitStack()
    with ctx:
        kb = KB(nc, ctx)
        phases = {"p0", "p1", "p2", "p3"}
        setup_common(kb, phases)
        load_consts(kb, phases)
        D = kb.D
        bufA, bufB = kb.sb["bufA"], kb.sb["bufB"]
        emit_cast_load_buf(kb, D["x0p"], bufA)
        kb.wait("tensor", kb.sems["dW"], kb.sems["dW"].n)
        kb.wait("tensor", kb.sems["dXQ"], kb.sems["dXQ"].n)
        emit_attn(kb, bufA, bufB, resid_from_dram(kb, D["x0p"]), D["scr0"])
        emit_stats_finalize(kb)
        emit_norm_pass(kb, bufA)
        _phase_barrier(kb)
        emit_ffn(kb, 0, bufA, bufB, resid_from_scr(kb, D["scr0"]), True, D["scr1"])
        emit_stats_finalize(kb)
        emit_norm_pass(kb, bufB)
        _phase_barrier(kb)
        emit_conv(kb, bufB, bufA, resid_from_scr(kb, D["scr1"]), True, D["scr0"])
        emit_stats_finalize(kb)
        emit_norm_pass(kb, bufA)
        _phase_barrier(kb)
        emit_ffn(kb, 1, bufA, bufB, resid_from_scr(kb, D["scr0"]), True, D["scr1"])
        emit_stats_finalize(kb)
        _phase_barrier(kb)
        emit_final_output(kb, D["scr1"], D["out"])
        kb.finalize_block()
    return nc


def resid_from_scr(kb, dram):
    return resid_from_dram(kb, dram)


def fused_inputs(inputs, core, x_full_T):
    m = attn_phase_inputs(inputs, core, x_full_T)
    m.update(ffn_phase_inputs(0, inputs, core))
    m.update(ffn_phase_inputs(1, inputs, core))
    m.update(conv_phase_inputs(inputs))
    return m


def kernel(**inputs):
    x_full_T = np.ascontiguousarray(
        np.asarray(inputs["L0_attn_hidden_states"]).transpose(0, 2, 1))
    cores = list(range(8))
    ncf = _get_prog("fused", build_fused_program)
    maps = [fused_inputs(inputs, c, x_full_T) for c in cores]
    r = run_bass_kernel_spmd(ncf, maps, core_ids=cores)
    yT = np.empty((2, HD, HD), dtype=np.float32)
    for c in cores:
        out = np.asarray(r.results[c]["out"])
        b, v = c // 4, c % 4
        lo, hi = 512 * v, 3584 - 512 * v
        yT[b][:, lo:lo + 512] = out[:, 0:512]
        yT[b][:, hi:hi + 512] = out[:, 512:1024]
    return np.ascontiguousarray(yT.transpose(0, 2, 1))
